# version 24
# baseline (speedup 1.0000x reference)
import math
import numpy as np
import concourse.bass as bass
import concourse.mybir as mybir
from concourse.bass_utils import run_bass_kernel_spmd

F32 = mybir.dt.float32
BF16 = mybir.dt.bfloat16
AF = mybir.ActivationFunctionType
ALU = mybir.AluOpType
AX = mybir.AxisListType.X

MASKVAL = -30000.0
EPS_LN = 1e-5
EPS_RMS = 1e-6


def real_cfg():
    return dict(D=2048, NH=16, WINDOW=1024, T_OWN=2048, T_HALO=1024,
                E=8, R=16, debug_taps=False)


def mini_cfg():
    return dict(D=256, NH=2, WINDOW=256, T_OWN=512, T_HALO=256,
                E=8, R=16, debug_taps=True)


class Ctr:
    def __init__(self, handle, name):
        self.h = handle
        self.name = name
        self.c = 0
        self.att = 0

    def reserve(self, n=1):
        self.c += n
        return self.c

    def reserve_dma(self):
        return self.reserve(16)

    def attach(self, instr, n=1):
        self.att += n
        instr.then_inc(self.h, n)
        return instr

    def attach_dma(self, instr):
        return self.attach(instr, 16)


class Pool:

    def __init__(self, nbuf, name, init=None):
        self.nbuf = nbuf
        self.name = name
        self.uses = 0
        self.init = init
        self.rel = []

    def acquire(self):
        u = self.uses
        self.uses += 1
        if u >= self.nbuf:
            assert len(self.rel) > u - self.nbuf, \
                f"pool {self.name}: use {u} before release {u - self.nbuf}"
            w = self.rel[u - self.nbuf]
        else:
            w = self.init
        return u % self.nbuf, w

    def released_at(self, ctr, value):
        self.rel.append((ctr, value))


def build_nc(cfg):
    D = cfg["D"]; NH = cfg["NH"]; W = cfg["WINDOW"]
    T = cfg["T_OWN"]; TH = cfg["T_HALO"]; E = cfg["E"]; R = cfg["R"]
    HA = E * R
    HD = D // NH
    assert HD == 128 and HA == 128 and W == TH
    NDC = D // 128
    TC = T // 128
    HC = TH // 128
    KC = TC + HC
    TKV = T + TH
    WKC = W // 128
    KSPAN = WKC + 1
    KW = KSPAN * 128
    KWP = ((KW + 511) // 512) * 512
    NJ = HD // 2
    taps = cfg["debug_taps"]
    QG = min(512, T)
    NQG = T // QG
    QGC = QG // 128
    CSC = 1.0 / math.sqrt(HD)

    nc = bass.Bass("TRN2", target_bir_lowering=False, debug=False)

    def din(name, shape, dt=F32):
        return nc.dram_tensor(name, list(shape), dt, kind="ExternalInput").ap()

    def dscr(name, shape, dt):
        return nc.dram_tensor(name, list(shape), dt, kind="Internal").ap()

    def dout(name, shape, dt=F32):
        return nc.dram_tensor(name, list(shape), dt, kind="ExternalOutput").ap()

    xh = din("xh", [TKV, D])
    wqT = din("wqT", [D, D]); wkT = din("wkT", [D, D]); wvT = din("wvT", [D, D])
    woT = din("woT", [D, D]); wbT = din("wbT", [D, D])
    waT = din("waT", [D, HA])
    wrT = din("wrT", [D, E])
    bwT = din("bwT", [HA, D])
    b_base = din("b_base", [1, D])
    qnw = din("qnw", [1, D]); knw = din("knw", [1, D])
    posr = din("posr", [128, KC])
    jrow = din("jrow", [1, NJ])
    mW = din("mW", [128, 128]); mC = din("mC", [128, 128])
    mV = din("mV", [128, TH])
    ones_row = din("ones_row", [1, 128])
    ident = din("ident", [128, 128])

    s_ln1h = dscr("s_ln1h", [TKV, D], BF16); s_ln1l = dscr("s_ln1l", [TKV, D], BF16)
    s_qh = dscr("s_qh", [T, D], BF16); s_ql = dscr("s_ql", [T, D], BF16)
    s_kh = dscr("s_kh", [TKV, D], BF16); s_kl = dscr("s_kl", [TKV, D], BF16)
    s_vh = dscr("s_vh", [TKV, D], BF16); s_vl = dscr("s_vl", [TKV, D], BF16)
    s_aoh = dscr("s_aoh", [D, T], BF16); s_aol = dscr("s_aol", [D, T], BF16)
    s_x2 = dscr("s_x2", [T, D], F32)
    s_wkh = dscr("s_wkh", [D, D], BF16); s_wkl = dscr("s_wkl", [D, D], BF16)
    s_wvh = dscr("s_wvh", [D, D], BF16); s_wvl = dscr("s_wvl", [D, D], BF16)
    s_woh = dscr("s_woh", [D, D], BF16); s_wol = dscr("s_wol", [D, D], BF16)
    s_wbh = dscr("s_wbh", [D, D], BF16); s_wbl = dscr("s_wbl", [D, D], BF16)
    s_ln2h = dscr("s_ln2h", [T, D], BF16); s_ln2l = dscr("s_ln2l", [T, D], BF16)

    y = dout("y", [T, D])
    taph = {}
    if taps:
        for nm, shp in (("t_ln1", [TKV, D]), ("t_q", [T, D]), ("t_k", [TKV, D]),
                        ("t_v", [TKV, D]), ("t_ao", [D, T]), ("t_x2", [T, D]),
                        ("t_gate", [128, T]), ("t_ha", [128, T])):
            taph[nm] = dout(nm, shp)

    ops = {"sync": [], "scalar": [], "vector": [], "tensor": [], "gpsimd": []}

    def emit(eng, fn):
        ops[eng].append(fn)

    glob = []

    def sbuf(name, shape, dt, scope):
        g = nc.sbuf_tensor(name, list(shape), dt)
        t = g.__enter__()
        scope.append(g)
        return t

    def psum(name, shape, scope, dt=F32):
        g = nc.psum_tensor(name, list(shape), dt)
        t = g.__enter__()
        scope.append(g)
        return t

    def close_scope(scope):
        for g in reversed(scope):
            g.__exit__(None, None, None)
        scope.clear()

    all_ctrs = []

    def sem(name):
        g = nc.semaphore(name)
        h = g.__enter__()
        glob.append(g)
        c = Ctr(h, name)
        all_ctrs.append(c)
        return c

    sLD = sem("sLD")
    sST = sem("sST")
    sMM = sem("sMM")
    sACT = sem("sACT")
    sDVE = sem("sDVE")
    sOUT = sem("sOUT")

    STORE_CTRS = []

    class DStream:

        def __init__(self, name, n=2, store=False):
            self.s = [sem(f"{name}{i}") for i in range(n)]
            if store:
                STORE_CTRS.extend(self.s)

        def reserve(self, slot, ndma=1):
            return (self.s[slot], self.s[slot].reserve(16 * ndma))

        def attach(self, slot, instr):
            return self.s[slot].attach(instr, 16)

    def phase_barrier():
        waits = [(c, c.c) for c in STORE_CTRS if c.c > 0]

        def b(e, ws=waits):
            for c, v in ws:
                e.wait_ge(c.h, v)
        emit("sync", b)
        emit("scalar", b)

    def SA(instr):
        return sACT.attach(instr)

    def VA(instr):
        return sDVE.attach(instr)

    def sw(sc):
        sc.wait_ge(sACT.h, sACT.att)

    def vw(v):
        v.wait_ge(sDVE.h, sDVE.att)

    def pw(eng, w):
        if w is not None:
            eng.wait_ge(w[0].h, w[1])

    NBN = (D + 511) // 512

    def bn_pair(v, srcbuf, st6, mv):
        for c in range(NBN):
            vw(v)
            VA(v.bn_stats(st6[:, c * 6:(c + 1) * 6],
                          srcbuf[:, c * 512:min((c + 1) * 512, D)]))
        vw(v)
        VA(v.bn_aggr(mv[:], st6[:, 0:6 * NBN]))

    ones_bf = sbuf("ones_bf", [1, 128], BF16, glob)
    epsln_t = sbuf("epsln_t", [128, 1], F32, glob)
    rmssc_t = sbuf("rmssc_t", [128, 1], F32, glob)
    rmseps_t = sbuf("rmseps_t", [128, 1], F32, glob)
    zro_bf = sbuf("zro_bf", [1, 512], BF16, glob)
    qsc_t = sbuf("qsc_t", [128, 1], F32, glob)
    qeps_t = sbuf("qeps_t", [128, 1], F32, glob)
    sc03 = []
    mW_t = sbuf("mW_t", [128, 128], F32, sc03)
    mC_t = sbuf("mC_t", [128, 128], F32, sc03)
    mV_t = sbuf("mV_t", [128, TH], F32, sc03)
    id_bf = sbuf("id_bf", [128, 128], BF16, sc03)
    mW_b = sbuf("mW_b", [128, 128], BF16, sc03)
    mC_b = sbuf("mC_b", [128, 128], BF16, sc03)
    mV_b = sbuf("mV_b", [128, TH], BF16, sc03)
    idf_t = sbuf("idf_t", [128, 128], F32, sc03)
    onesr_b = sbuf("onesr_b", [1, 512], BF16, sc03)
    ones_kv = sbuf("ones_kv", [128, 128], BF16, sc03)
    sc02 = []
    cos_t = sbuf("cos_t", [128, KC * NJ], F32, sc02)
    sin_t = sbuf("sin_t", [128, KC * NJ], F32, sc02)
    HW_ = D // 2
    cvx = [sbuf(f"cvx{i}", [128, HW_], F32, sc02) for i in range(1)]
    cvh = [sbuf(f"cvh{i}", [128, HW_], BF16, sc02) for i in range(1)]
    cvl = [sbuf(f"cvl{i}", [128, HW_], BF16, sc02) for i in range(1)]

    sDVE.reserve(8)

    def v_consts(v):
        VA(v.memset(zro_bf[:], 0.0))
        VA(v.memset(epsln_t[:], EPS_LN))
        VA(v.memset(rmssc_t[:], 1.0 / D))
        VA(v.memset(rmseps_t[:], EPS_RMS))
        VA(v.memset(qsc_t[:], float(HD) / D))
        VA(v.memset(qeps_t[:], float(HD) * EPS_RMS))
        VA(v.memset(onesr_b[:], 1.0))
        VA(v.memset(ones_kv[:], 1.0))
    emit("vector", v_consts)

    def p0():
        sc0 = []
        post = sbuf("post", [128, KC], F32, sc0)
        jbc = sbuf("jbc", [128, NJ], F32, sc0)
        ones_f = sbuf("ones_f", [1, 128], F32, sc0)
        r_ang = sbuf("r_ang", [128, KC * NJ], F32, sc0)
        r_s1 = sbuf("r_s1", [128, KC * NJ], F32, sc0)
        r_s2 = sbuf("r_s2", [128, KC * NJ], F32, sc0)
        r_s3 = sbuf("r_s3", [128, KC * NJ], F32, sc0)
        c_a = sbuf("c_a", [128, 1], F32, sc0)

        w0 = [sLD.reserve_dma() for _ in range(7)][-1]

        def ld(s):
            sLD.attach_dma(s.dma_start(out=ones_f[:], in_=ones_row[:]))
            sLD.attach_dma(s.dma_start(out=idf_t[:], in_=ident[:]))
            sLD.attach_dma(s.dma_start(out=post[:], in_=posr[:]))
            sLD.attach_dma(s.dma_start(out=jbc[:], in_=jrow[:].broadcast_to((128, NJ))))
            sLD.attach_dma(s.dma_start(out=mW_t[:], in_=mW[:]))
            sLD.attach_dma(s.dma_start(out=mC_t[:], in_=mC[:]))
            sLD.attach_dma(s.dma_start(out=mV_t[:], in_=mV[:]))
        emit("sync", ld)

        a_if = sACT.reserve(9)

        def sc_c(sc):
            sc.wait_ge(sLD.h, w0)
            SA(sc.activation(out=ones_bf[:], in_=ones_f[:], func=AF.Copy))
            SA(sc.activation(out=id_bf[:], in_=idf_t[:], func=AF.Copy))
            SA(sc.activation(out=mW_b[:], in_=mW_t[:], func=AF.Copy))
            SA(sc.activation(out=mC_b[:], in_=mC_t[:], func=AF.Copy))
            SA(sc.activation(out=mV_b[:], in_=mV_t[:], func=AF.Copy))
            SA(sc.activation(out=c_a[:], in_=jbc[:, 0:1], func=AF.Copy,
                             scale=0.0, bias=10000.0))
            sw(sc)
            SA(sc.activation(out=c_a[:], in_=c_a[:], func=AF.Ln))
            sw(sc)
            SA(sc.mul(c_a[:], c_a[:], -1.0 / NJ))
            sw(sc)
            SA(sc.activation(out=r_s1[:, 0:NJ], in_=jbc[:], func=AF.Exp,
                             scale=c_a[:]))
        emit("scalar", sc_c)

        mmax = max(0, int(math.ceil(math.log2(max(TKV, 7) / (2 * math.pi)))))
        sDVE.reserve(1 + 3 * (mmax + 1))

        def v_ang(v):
            v.wait_ge(sACT.h, a_if)
            ang = r_ang[:].rearrange("p (kc j) -> p kc j", j=NJ)
            vw(v)
            VA(v.tensor_tensor(
                ang,
                post[:].unsqueeze(2).broadcast_to((128, KC, NJ)),
                r_s1[:, 0:NJ].unsqueeze(1).broadcast_to((128, KC, NJ)),
                ALU.mult))
            for m in range(mmax, -1, -1):
                c = 2 * math.pi * (2 ** m)
                vw(v)
                VA(v.tensor_single_scalar(r_s1[:], r_ang[:], c, ALU.is_ge))
                vw(v)
                VA(v.tensor_scalar(r_s1[:], r_s1[:], c, None, ALU.mult))
                vw(v)
                VA(v.tensor_tensor(r_ang[:], r_ang[:], r_s1[:], ALU.subtract))
        emit("vector", v_ang)

        def make_table(shift, out_tab):
            dv = sDVE.reserve(14)

            def vf(v):
                vw(v)
                VA(v.tensor_scalar(r_s1[:], r_ang[:], shift, None, ALU.add))
                vw(v)
                VA(v.tensor_single_scalar(r_s2[:], r_s1[:], 2 * math.pi, ALU.is_ge))
                vw(v)
                VA(v.tensor_scalar(r_s2[:], r_s2[:], 2 * math.pi, None, ALU.mult))
                vw(v)
                VA(v.tensor_tensor(r_s1[:], r_s1[:], r_s2[:], ALU.subtract))
                vw(v)
                VA(v.tensor_scalar(r_s1[:], r_s1[:], math.pi, None, ALU.subtract))
                vw(v)
                VA(v.tensor_tensor(r_s2[:], r_s1[:], r_s1[:], ALU.mult))
                vw(v)
                VA(v.tensor_single_scalar(r_s2[:], r_s2[:],
                                          (math.pi / 2) ** 2, ALU.is_ge))
                vw(v)
                VA(v.tensor_single_scalar(r_s3[:], r_s1[:], 0.0, ALU.is_ge))
                vw(v)
                VA(v.tensor_scalar(r_s3[:], r_s3[:], 2.0, -1.0, ALU.mult, ALU.add))
                vw(v)
                VA(v.tensor_tensor(r_s3[:], r_s3[:], r_s2[:], ALU.mult))
                vw(v)
                VA(v.tensor_scalar(r_s3[:], r_s3[:], math.pi, None, ALU.mult))
                vw(v)
                VA(v.tensor_scalar(r_s2[:], r_s2[:], -2.0, 1.0, ALU.mult, ALU.add))
                vw(v)
                VA(v.tensor_tensor(r_s1[:], r_s1[:], r_s2[:], ALU.mult))
                vw(v)
                VA(v.tensor_tensor(r_s1[:], r_s1[:], r_s3[:], ALU.add))
            emit("vector", vf)
            av = sACT.reserve()

            def sf(sc):
                sc.wait_ge(sDVE.h, dv)
                sw(sc)
                SA(sc.activation(out=out_tab[:], in_=r_s1[:], func=AF.Sin,
                                 scale=-1.0))
            emit("scalar", sf)
            emit("vector", lambda v, w=av: v.wait_ge(sACT.h, w))

        make_table(0.0, sin_t)
        make_table(math.pi / 2, cos_t)
        fin = sACT.c
        emit("sync", lambda s, w=fin: s.wait_ge(sACT.h, w))
        close_scope(sc0)

    p0()

    dcv = DStream("cvld", n=1)
    dcs = DStream("cvst", n=1, store=True)
    cvpool = Pool(1, "cv")

    def emit_conv_tile(wsrc, dsth, dstl, dc, hf):
        slot, wv = cvpool.acquire()
        lw = dcv.reserve(slot)
        r0, c0 = dc * 128, hf * HW_

        def ld(s, wsrc=wsrc, slot=slot, wv=wv, r0=r0, c0=c0):
            pw(s, wv)
            dcv.attach(slot, s.dma_start(
                out=cvx[slot][:], in_=wsrc[r0:r0 + 128, c0:c0 + HW_]))
        emit("sync", ld)

        aw = sACT.reserve()

        def s_h(sc, slot=slot, w=lw):
            pw(sc, w)
            sw(sc)
            SA(sc.activation(out=cvh[slot][:], in_=cvx[slot][:], func=AF.Copy))
        emit("scalar", s_h)

        dv = sDVE.reserve()

        def v_l(v, slot=slot, w=aw):
            v.wait_ge(sACT.h, w)
            vw(v)
            VA(v.tensor_tensor(cvl[slot][:], cvx[slot][:], cvh[slot][:],
                               ALU.subtract))
        emit("vector", v_l)

        sw_ = dcs.reserve(slot, 2)
        cvpool.released_at(sw_[0], sw_[1])

        def st(s, dsth=dsth, dstl=dstl, slot=slot, w=dv, r0=r0, c0=c0):
            s.wait_ge(sDVE.h, w)
            dcs.attach(slot, s.dma_start(
                out=dsth[r0:r0 + 128, c0:c0 + HW_], in_=cvh[slot][:]))
            dcs.attach(slot, s.dma_start(
                out=dstl[r0:r0 + 128, c0:c0 + HW_], in_=cvl[slot][:]))
        emit("sync", st)

    conv_plan = {
        "p1": [(wkT, s_wkh, s_wkl, dc, hf) for dc in range(NDC)
               for hf in range(2)],
        "q": [(wvT, s_wvh, s_wvl, dc, hf) for dc in range(NDC)
              for hf in range(2)],
        "k": [(woT, s_woh, s_wol, dc, hf) for dc in range(NDC)
              for hf in range(2)],
        "v": [(wbT, s_wbh, s_wbl, dc, hf) for dc in range(NDC)
              for hf in range(2)],
    }

    def emit_conv_two(plan, i):
        for j in (2 * i, 2 * i + 1):
            if j < len(plan):
                emit_conv_tile(*plan[j])


    def p1():
        phase_barrier()
        sc1 = []
        NP1 = 3
        dsx = DStream("p1x", n=NP1)
        dst1 = DStream("st1", n=NP1, store=True)
        xpool = Pool(NP1, "p1x")
        x_in = [sbuf(f"p1x{i}", [128, D], F32, sc1) for i in range(NP1)]
        lnf = [sbuf(f"p1lnf{i}", [128, D], F32, sc1) for i in range(NP1)]
        lnh = [sbuf(f"p1lnh{i}", [128, D], BF16, sc1) for i in range(NP1)]
        lnl = [sbuf(f"p1lnl{i}", [128, D], BF16, sc1) for i in range(NP1)]
        st6 = [sbuf(f"p1st6{i}", [128, 6 * NBN], F32, sc1) for i in range(NP1)]
        mv = [sbuf(f"p1mv{i}", [128, 2], F32, sc1) for i in range(NP1)]
        r_t = [sbuf(f"p1r{i}", [128, 1], F32, sc1) for i in range(NP1)]
        b_t = [sbuf(f"p1b{i}", [128, 1], F32, sc1) for i in range(NP1)]
        stw_par = [None] * NP1

        for t in range(KC):
            p = t % NP1
            slot, wv = xpool.acquire()
            xw = dsx.reserve(slot)

            def ld(s, t=t, slot=slot, wv=wv):
                pw(s, wv)
                dsx.attach(slot, s.dma_start(out=x_in[slot][:],
                                             in_=xh[t * 128:(t + 1) * 128, :]))
            emit("sync", ld)

            spw = stw_par[p]
            d0 = sDVE.reserve(NBN + 1)

            def v_st(v, slot=slot, p=p, w=xw, spw=spw):
                pw(v, w)
                pw(v, spw)
                bn_pair(v, x_in[slot][:], st6[p], mv[p])
            emit("vector", v_st)

            a0 = sACT.reserve()

            def s_r(sc, p=p, w=d0):
                sc.wait_ge(sDVE.h, w)
                sw(sc)
                SA(sc.activation(out=r_t[p][:], in_=mv[p][:, 1:2], func=AF.Sqrt,
                                 bias=epsln_t[:]))
            emit("scalar", s_r)

            d1 = sDVE.reserve(2)

            def v_b(v, p=p, w=a0):
                v.wait_ge(sACT.h, w)
                vw(v)
                VA(v.reciprocal(r_t[p][:], r_t[p][:]))
                vw(v)
                VA(v.tensor_scalar(b_t[p][:], mv[p][:, 0:1], r_t[p][:], -1.0,
                                   ALU.mult, ALU.mult))
            emit("vector", v_b)

            a1 = sACT.reserve(2)
            xpool.released_at(sACT, a1 - 1)

            def s_ln(sc, slot=slot, p=p, w=d1):
                sc.wait_ge(sDVE.h, w)
                sw(sc)
                SA(sc.activation(out=lnf[p][:], in_=x_in[slot][:],
                                 func=AF.Identity,
                                 scale=r_t[p][:], bias=b_t[p][:]))
                sw(sc)
                SA(sc.activation(out=lnh[p][:], in_=lnf[p][:], func=AF.Copy))
            emit("scalar", s_ln)

            dlo = sDVE.reserve()

            def v_fin(v, p=p, w=a1):
                v.wait_ge(sACT.h, w)
                vw(v)
                VA(v.tensor_tensor(lnl[p][:], lnf[p][:], lnh[p][:],
                                   ALU.subtract))
            emit("vector", v_fin)

            nst = 3 if taps else 2
            stw = dst1.reserve(p, nst)
            stw_par[p] = stw

            def g_st(g, t=t, p=p, w=dlo):
                g.wait_ge(sDVE.h, w)
                dst1.attach(p, g.dma_start(out=s_ln1h[t * 128:(t + 1) * 128, :],
                                           in_=lnh[p][:]))
                dst1.attach(p, g.dma_start(out=s_ln1l[t * 128:(t + 1) * 128, :],
                                           in_=lnl[p][:]))
                if taps:
                    dst1.attach(p, g.dma_start(
                        out=taph["t_ln1"][t * 128:(t + 1) * 128, :],
                        in_=lnf[p][:]))
            emit("gpsimd", g_st)
            emit_conv_two(conv_plan["p1"], t)

        close_scope(sc1)

    p1()

    def load_w_bf16(pref, srch, srcl, scope):
        whi = [sbuf(f"{pref}h{i}", [128, D], BF16, scope)
               for i in range(NDC)]
        wlo = [sbuf(f"{pref}l{i}", [128, D], BF16, scope)
               for i in range(NDC)]
        dwb = DStream(pref + "b")
        r0 = dwb.reserve(0, NDC)
        r1 = dwb.reserve(1, NDC)

        def ld(s):
            for dc in range(NDC):
                dwb.attach(dc % 2, s.dma_start(
                    out=whi[dc][:], in_=srch[dc * 128:(dc + 1) * 128, :]))
                dwb.attach(dc % 2, s.dma_start(
                    out=wlo[dc][:], in_=srcl[dc * 128:(dc + 1) * 128, :]))
        emit("sync", ld)
        return whi, wlo, [r0, r1], None

    def load_w_resident(pref, src, ncols, scope):
        whi = [sbuf(f"{pref}h{i}", [128, ncols], BF16, scope) for i in range(NDC)]
        wlo = [sbuf(f"{pref}l{i}", [128, ncols], BF16, scope) for i in range(NDC)]
        tmp = []
        nws = 2
        wsl = [sbuf(f"{pref}t{i}", [128, ncols], F32, tmp) for i in range(nws)]
        dsw = DStream(pref + "d", n=nws)
        pl = Pool(nws, pref)
        for dc in range(NDC):
            slot, wv = pl.acquire()
            ww = dsw.reserve(slot)

            def ld(s, dc=dc, slot=slot, wv=wv):
                pw(s, wv)
                dsw.attach(slot, s.dma_start(out=wsl[slot][:],
                                             in_=src[dc * 128:(dc + 1) * 128, :]))
            emit("sync", ld)

            aw = sACT.reserve()

            def s_h(sc, dc=dc, slot=slot, w=ww):
                pw(sc, w)
                sw(sc)
                SA(sc.activation(out=whi[dc][:], in_=wsl[slot][:], func=AF.Copy))
            emit("scalar", s_h)

            dv = sDVE.reserve()
            pl.released_at(sDVE, dv)

            def v_l(v, dc=dc, slot=slot, w=aw):
                v.wait_ge(sACT.h, w)
                vw(v)
                VA(v.tensor_tensor(wlo[dc][:], wsl[slot][:], whi[dc][:],
                                   ALU.subtract))
            emit("vector", v_l)
        ready = [(sDVE, sDVE.c), (sACT, sACT.c)]
        close_scope(tmp)
        return whi, wlo, ready, (sDVE, ready[0][1])

    def emit_mm_tc(te, ps_ap, n_out, groups):
        ocs = [(i * 512, min(512, n_out - i * 512))
               for i in range((n_out + 511) // 512)]
        plan = []
        for (lh, ll, rh, rl) in groups:
            plan.append((lh, rh))
            plan.append((lh, rl))
            if ll is not None:
                plan.append((ll, rh))
        last = None
        first = [True] * len(ocs)
        for pi, (lt, rt) in enumerate(plan):
            for oi, (o0, on) in enumerate(ocs):
                st = first[oi]
                first[oi] = False
                last = te.matmul(ps_ap[:, o0:o0 + on], lt, rt[:, o0:o0 + on],
                                 start=st, stop=(pi == len(plan) - 1))
        return last

    def p2_og(og):
        phase_barrier()
        sc2 = []
        dstat = DStream(f"p2{og}s")
        dst2 = DStream(f"st2{og}", store=True)
        if og == "q":
            whi, wlo, wready, tguard = load_w_resident("wq", wqT, D, sc2)
        else:
            srch, srcl = {"k": (s_wkh, s_wkl), "v": (s_wvh, s_wvl)}[og]
            whi, wlo, wready, tguard = load_w_bf16(f"w{og}", srch, srcl, sc2)

        wrep_w = None
        if og in ("q", "k"):
            wrep = sbuf(f"p2{og}rep", [128, D], F32, sc2)
            wrep_w = sLD.reserve_dma()

            def ldrep(s, og=og, w=tguard):
                pw(s, w)
                wsrc2 = qnw if og == "q" else knw
                sLD.attach_dma(s.dma_start(out=wrep[:],
                                           in_=wsrc2[:].broadcast_to((128, D))))
            emit("sync", ldrep)

        statpool = Pool(2, f"p2{og}st", init=tguard)
        sth = [sbuf(f"p2{og}sh{i}", [128, D], BF16, sc2) for i in range(2)]
        stl = [sbuf(f"p2{og}sl{i}", [128, D], BF16, sc2) for i in range(2)]
        pspool = Pool(2, f"p2{og}ps")
        ps2 = [psum(f"p2{og}ps{i}", [128, D], sc2) for i in range(2)]
        qn = sbuf(f"p2{og}qn", [128, D], F32, sc2)
        rr = sbuf(f"p2{og}rr", [128, D], F32, sc2)
        ohi = sbuf(f"p2{og}oh", [128, D], BF16, sc2)
        olo = sbuf(f"p2{og}ol", [128, D], BF16, sc2)
        sq_t = sbuf(f"p2{og}sq", [128, 1], F32, sc2)
        rq_t = sbuf(f"p2{og}rq", [128, 1], F32, sc2)

        tcs = list(range(HC, KC)) if og == "q" else list(range(KC))
        dsth, dstl = {"q": (s_qh, s_ql), "k": (s_kh, s_kl),
                      "v": (s_vh, s_vl)}[og]
        tapn = {"q": "t_q", "k": "t_k", "v": "t_v"}[og]

        for tc in tcs:
            sslot, swv = statpool.acquire()
            trw = dstat.reserve(sslot, 2)

            def tl(sc, tc=tc, sslot=sslot, swv=swv):
                pw(sc, swv)
                r0 = tc * 128
                src_h = s_ln1h[r0:r0 + 128, :].rearrange("t (dc d) -> t dc d", d=128)
                src_l = s_ln1l[r0:r0 + 128, :].rearrange("t (dc d) -> t dc d", d=128)
                dst_h = sth[sslot][:].rearrange("p (dc t) -> p dc t", t=128)
                dst_l = stl[sslot][:].rearrange("p (dc t) -> p dc t", t=128)
                dstat.attach(sslot, sc.dma_start(out=dst_h, in_=src_h, transpose=True))
                dstat.attach(sslot, sc.dma_start(out=dst_l, in_=src_l, transpose=True))
            emit("sync", tl)

            pslot, pwv = pspool.acquire()
            mmv = sMM.reserve()
            statpool.released_at(sMM, mmv)

            def mms(te, sslot=sslot, pslot=pslot, pwv=pwv, trw=trw):
                pw(te, pwv)
                pw(te, trw)
                for w_ in wready:
                    pw(te, w_)
                groups = [(sth[sslot][:, dc * 128:(dc + 1) * 128],
                           stl[sslot][:, dc * 128:(dc + 1) * 128],
                           whi[dc][:], wlo[dc][:]) for dc in range(NDC)]
                sMM.attach(emit_mm_tc(te, ps2[pslot][:], D, groups))
            emit("tensor", mms)
            ps = ps2[pslot]
            row0 = (tc - HC) * 128 if og == "q" else tc * 128

            if og in ("q", "k"):
                a0 = sACT.reserve(2)

                def s_e(sc, og=og, ps=ps, w=mmv):
                    sc.wait_ge(sMM.h, w)
                    sw(sc)
                    SA(sc.activation(out=rr[:], in_=ps[:], func=AF.Square,
                                     accum_out=sq_t[:]))
                    sw(sc)
                    if og == "q":
                        SA(sc.activation(out=rq_t[:], in_=sq_t[:], func=AF.Sqrt,
                                         scale=qsc_t[:], bias=qeps_t[:]))
                    else:
                        SA(sc.activation(out=rq_t[:], in_=sq_t[:], func=AF.Sqrt,
                                         scale=rmssc_t[:], bias=rmseps_t[:]))
                emit("scalar", s_e)

                d0 = sDVE.reserve(8)
                pspool.released_at(sDVE, d0 - 6)

                def v_e(v, tc=tc, ps=ps, w=a0, ww=wrep_w):
                    v.wait_ge(sACT.h, w)
                    v.wait_ge(sLD.h, ww)
                    vw(v)
                    VA(v.reciprocal(rq_t[:], rq_t[:]))
                    vw(v)
                    VA(v.scalar_tensor_tensor(qn[:], ps[:], rq_t[:], wrep[:],
                                              ALU.mult, ALU.mult))

                    def hv(buf, half):
                        vvw = buf[:].rearrange("p (h two j) -> p h two j",
                                               two=2, j=NJ)
                        return vvw[:, :, half, :]
                    cosv = cos_t[:, tc * NJ:(tc + 1) * NJ] \
                        .unsqueeze(1).broadcast_to((128, NH, NJ))
                    sinv = sin_t[:, tc * NJ:(tc + 1) * NJ] \
                        .unsqueeze(1).broadcast_to((128, NH, NJ))
                    qt, qb = hv(qn, 0), hv(qn, 1)
                    rt_, rb = hv(rr, 0), hv(rr, 1)
                    vw(v); VA(v.tensor_tensor(rt_, qb, sinv, ALU.mult))
                    vw(v); VA(v.tensor_tensor(rb, qb, cosv, ALU.mult))
                    vw(v); VA(v.tensor_tensor(qb, qt, sinv, ALU.mult))
                    vw(v); VA(v.tensor_tensor(rb, rb, qb, ALU.add))
                    vw(v); VA(v.tensor_tensor(qb, qt, cosv, ALU.mult))
                    vw(v); VA(v.tensor_tensor(rt_, qb, rt_, ALU.subtract))
                emit("vector", v_e)

                a1 = sACT.reserve()

                def s_hi(sc, w=d0):
                    sc.wait_ge(sDVE.h, w)
                    sw(sc)
                    SA(sc.activation(out=ohi[:], in_=rr[:], func=AF.Copy))
                emit("scalar", s_hi)

                dlo = sDVE.reserve()

                def v_lo(v, w=a1):
                    v.wait_ge(sACT.h, w)
                    vw(v)
                    VA(v.tensor_tensor(olo[:], rr[:], ohi[:], ALU.subtract))
                emit("vector", v_lo)
            else:
                a0 = sACT.reserve(2 if taps else 1)

                def s_e(sc, ps=ps, w=mmv):
                    sc.wait_ge(sMM.h, w)
                    sw(sc)
                    SA(sc.activation(out=ohi[:], in_=ps[:], func=AF.Copy))
                    if taps:
                        sw(sc)
                        SA(sc.activation(out=rr[:], in_=ps[:], func=AF.Copy))
                emit("scalar", s_e)

                dlo = sDVE.reserve()
                pspool.released_at(sDVE, dlo)

                def v_lo(v, ps=ps, w=a0):
                    v.wait_ge(sACT.h, w)
                    vw(v)
                    VA(v.tensor_tensor(olo[:], ps[:], ohi[:], ALU.subtract))
                emit("vector", v_lo)

            p = tc % 2
            nst = 3 if taps else 2
            stw = dst2.reserve(p, nst)

            def g_st(g, p=p, row0=row0, w=dlo):
                g.wait_ge(sDVE.h, w)
                dst2.attach(p, g.dma_start(out=dsth[row0:row0 + 128, :],
                                           in_=ohi[:]))
                dst2.attach(p, g.dma_start(out=dstl[row0:row0 + 128, :],
                                           in_=olo[:]))
                if taps:
                    dst2.attach(p, g.dma_start(
                        out=taph[tapn][row0:row0 + 128, :], in_=rr[:]))
            emit("gpsimd", g_st)
            emit("scalar", lambda sc, w=stw: pw(sc, w))
            emit_conv_two(conv_plan[og], tcs.index(tc))

        close_scope(sc2)

    p2_og("q")
    p2_og("k")
    p2_og("v")
    close_scope(sc02)

    def p3():
        phase_barrier()
        sc3 = []
        dqk = DStream("p3qk")
        dst3 = DStream("st3", store=True)
        dv_ = DStream("p3v")
        headpool = Pool(2, "p3head")
        qTh = [sbuf(f"p3qh{i}", [128, T], BF16, sc3) for i in range(2)]
        qTl = [sbuf(f"p3ql{i}", [128, T], BF16, sc3) for i in range(2)]
        kTh = [sbuf(f"p3kh{i}", [128, TKV], BF16, sc3) for i in range(2)]
        kTl = [sbuf(f"p3kl{i}", [128, TKV], BF16, sc3) for i in range(2)]
        vh_ = [sbuf(f"p3vh{i}", [128, TKV], BF16, sc3) for i in range(2)]
        vl_ = [sbuf(f"p3vl{i}", [128, TKV], BF16, sc3) for i in range(2)]
        e32 = [sbuf(f"p3e{i}", [128, 512], F32, sc3) for i in range(2)]
        ehi = [sbuf(f"p3eh{i}", [128, 512], BF16, sc3) for i in range(3)]
        elo = [sbuf(f"p3el{i}", [128, 512], BF16, sc3) for i in range(3)]
        den_sb = [sbuf(f"p3dsb{i}", [128, QG], F32, sc3) for i in range(2)]
        lnt = sbuf("p3lnt", [128, QG], F32, sc3)
        rcpb = sbuf("p3rcpb", [128, QG], F32, sc3)
        aof_ev = sbuf("p3aof", [128, QG], F32, sc3)
        aoh_ev = sbuf("p3aoh", [128, QG], BF16, sc3)
        aol_ev = sbuf("p3aol", [128, QG], BF16, sc3)
        scpool = Pool(3, "p3sc")
        ps_s = [psum(f"p3ps{i}", [128, 512], sc3) for i in range(3)]
        epool = Pool(3, "p3ep")
        aopool = Pool(2, "p3ao")
        ps_ao = [psum(f"p3pao{i}", [128, QG], sc3) for i in range(2)]
        ps_dn = [psum(f"p3pdn{i}", [128, 512], sc3) for i in range(2)]

        e32_free = {0: None, 1: None}
        dn_free = {}
        gci = 0
        prev_stw = None
        prev_dlo2 = None
        pending_evac = None
        for h in range(NH):
            hslot, hwv = headpool.acquire()
            trw = dqk.reserve(hslot, 4)

            def hload_t(sc, h=h, hslot=hslot, hwv=hwv):
                pw(sc, hwv)
                c0 = h * 128
                dqk.attach(hslot, sc.dma_start(out=qTh[hslot][:],
                                               in_=s_qh[:, c0:c0 + 128],
                                               transpose=True))
                dqk.attach(hslot, sc.dma_start(out=qTl[hslot][:],
                                               in_=s_ql[:, c0:c0 + 128],
                                               transpose=True))
                dqk.attach(hslot, sc.dma_start(out=kTh[hslot][:],
                                               in_=s_kh[:, c0:c0 + 128],
                                               transpose=True))
                dqk.attach(hslot, sc.dma_start(out=kTl[hslot][:],
                                               in_=s_kl[:, c0:c0 + 128],
                                               transpose=True))
            emit("sync", hload_t)

            vldw = dv_.reserve(hslot, 2)

            def hload_v(s, h=h, hslot=hslot, hwv=hwv):
                pw(s, hwv)
                c0 = h * 128
                src_h = s_vh[:, c0:c0 + 128].rearrange("(kc t) d -> t kc d", t=128)
                src_l = s_vl[:, c0:c0 + 128].rearrange("(kc t) d -> t kc d", t=128)
                dst_h = vh_[hslot][:].rearrange("p (kc d) -> p kc d", d=128)
                dst_l = vl_[hslot][:].rearrange("p (kc d) -> p kc d", d=128)
                dv_.attach(hslot, s.dma_start(out=dst_h, in_=src_h))
                dv_.attach(hslot, s.dma_start(out=dst_l, in_=src_l))
            emit("sync", hload_v)

            pv_done_prev = None
            for qg in range(NQG):
                qc0, qc1 = qg * QGC, (qg + 1) * QGC - 1
                kvcs = list(range(qc0, qc1 + WKC + 1))
                aoslot, aowv = aopool.acquire()

                def zf(te, aowv=aowv, aoslot=aoslot):
                    pw(te, aowv)
                    te.matmul(ps_ao[aoslot][:, 0:QG], ones_bf[:],
                              zro_bf[:, 0:QG], start=True, stop=False)
                emit("tensor", zf)

                def emit_pv(info, is_last, aowv=aowv, hslot=hslot,
                            vldw=vldw, qc0=qc0, aoslot=aoslot):
                    ci, kvc, eslot, ew, qa, qb, dnf = info
                    nc_ = (qb - qa + 1) * 128
                    pvv = sMM.reserve()
                    epool.released_at(sMM, pvv)

                    def t_pv(te, ci=ci, kvc=kvc, eslot=eslot, ew=ew, qa=qa,
                             qb=qb, nc_=nc_, is_last=is_last, qc0=qc0,
                             dnf=dnf, aoslot=aoslot):
                        te.wait_ge(sDVE.h, ew)
                        pw(te, vldw)
                        if dnf is not None:
                            te.wait_ge(sDVE.h, dnf)
                        off = (qa - qc0) * 128
                        vs_h = vh_[hslot][:, kvc * 128:(kvc + 1) * 128]
                        vs_l = vl_[hslot][:, kvc * 128:(kvc + 1) * 128]
                        eh = ehi[eslot][:, 0:nc_]
                        el = elo[eslot][:, 0:nc_]
                        te.matmul(ps_ao[aoslot][:, off:off + nc_], vs_h, eh,
                                  start=False, stop=False)
                        te.matmul(ps_ao[aoslot][:, off:off + nc_], vs_h, el,
                                  start=False, stop=False)
                        te.matmul(ps_ao[aoslot][:, off:off + nc_], vs_l, eh,
                                  start=False, stop=is_last)
                        te.matmul(ps_dn[ci % 2][:, 0:nc_], ones_kv[:], eh,
                                  start=True, stop=False)
                        lastm = te.matmul(ps_dn[ci % 2][:, 0:nc_], ones_kv[:],
                                          el, start=False, stop=True)
                        sMM.attach(lastm)
                    emit("tensor", t_pv)

                    dnv = sDVE.reserve(2 if ci == 0 else 1)
                    dpar = qg % 2

                    def v_dn(v, ci=ci, qa=qa, nc_=nc_, w=pvv, qc0=qc0,
                             dpar=dpar):
                        v.wait_ge(sMM.h, w)
                        off = (qa - qc0) * 128
                        if ci == 0:
                            vw(v)
                            VA(v.memset(den_sb[dpar][:], 0.0))
                        vw(v)
                        VA(v.tensor_tensor(den_sb[dpar][:, off:off + nc_],
                                           ps_dn[ci % 2][:, 0:nc_],
                                           den_sb[dpar][:, off:off + nc_],
                                           ALU.add))
                    emit("vector", v_dn)
                    return pvv, dnv

                pend = None
                pv_done = None
                dn_last = None
                for ci, kvc in enumerate(kvcs):
                    if ci == 1 and pending_evac is not None:
                        prev_stw, prev_dlo2 = pending_evac()
                        pending_evac = None
                    qa = max(qc0, kvc - WKC)
                    qb = min(qc1, kvc)
                    nc_ = (qb - qa + 1) * 128
                    cslot, cwv = scpool.acquire()
                    mmv = sMM.reserve()

                    def t_sc(te, kvc=kvc, qa=qa, qb=qb, nc_=nc_, hslot=hslot,
                             cslot=cslot, cwv=cwv, trw=trw):
                        pw(te, cwv)
                        pw(te, trw)
                        ps = ps_s[cslot]
                        kh = kTh[hslot][:, kvc * 128:(kvc + 1) * 128]
                        kl = kTl[hslot][:, kvc * 128:(kvc + 1) * 128]
                        qh = qTh[hslot][:, qa * 128:qa * 128 + nc_]
                        ql = qTl[hslot][:, qa * 128:qa * 128 + nc_]
                        te.matmul(ps[:, 0:nc_], kh, qh, start=True, stop=False)
                        te.matmul(ps[:, 0:nc_], kh, ql, start=False, stop=False)
                        if kvc <= qb:
                            o = (kvc - qa) * 128
                            te.matmul(ps[:, o:o + 128], mW_b[:], id_bf[:],
                                      start=False, stop=False)
                        if kvc - WKC >= qa:
                            o = (kvc - WKC - qa) * 128
                            te.matmul(ps[:, o:o + 128], mC_b[:], id_bf[:],
                                      start=False, stop=False)
                        if kvc < HC:
                            te.matmul(ps[:, 0:nc_],
                                      mV_b[0:1, kvc * 128:(kvc + 1) * 128],
                                      onesr_b[:, 0:nc_],
                                      start=False, stop=False)
                        last = te.matmul(ps[:, 0:nc_], kl, qh,
                                         start=False, stop=True)
                        sMM.attach(last)
                    emit("tensor", t_sc)

                    eslot, ewv = epool.acquire()
                    asp = sACT.reserve(2)
                    scpool.released_at(sACT, asp)
                    par = gci % 2
                    e2f = e32_free[par]

                    def s_split(sc, cslot=cslot, eslot=eslot, ewv=ewv,
                                nc_=nc_, w=mmv, e2f=e2f, par=par):
                        sc.wait_ge(sMM.h, w)
                        pw(sc, ewv)
                        if e2f is not None:
                            sc.wait_ge(sDVE.h, e2f)
                        sw(sc)
                        SA(sc.activation(out=e32[par][:, 0:nc_],
                                         in_=ps_s[cslot][:, 0:nc_],
                                         func=AF.Exp))
                        sw(sc)
                        SA(sc.activation(out=ehi[eslot][:, 0:nc_],
                                         in_=ps_s[cslot][:, 0:nc_],
                                         func=AF.Exp))
                    emit("scalar", s_split)

                    dlo = sDVE.reserve(1)

                    def v_lo(v, eslot=eslot, nc_=nc_, w=asp, par=par):
                        v.wait_ge(sACT.h, w)
                        vw(v)
                        VA(v.tensor_tensor(elo[eslot][:, 0:nc_],
                                           e32[par][:, 0:nc_],
                                           ehi[eslot][:, 0:nc_], ALU.subtract))
                    emit("vector", v_lo)
                    e32_free[par] = dlo
                    gci += 1

                    dnf = dn_free.get(ci % 2)
                    cur = (ci, kvc, eslot, dlo, qa, qb, dnf)
                    if pend is not None:
                        _, dnv = emit_pv(pend, False)
                        dn_free[pend[0] % 2] = dnv
                        dn_last = dnv
                    pend = cur
                pv_done, dn_last = emit_pv(pend, True)
                dn_free[pend[0] % 2] = dn_last

                def make_evac(h=h, qg=qg, pv_done=pv_done, aoslot=aoslot,
                              pstw=prev_stw, pdlo2=prev_dlo2, dn_last=dn_last):
                    def do_emit():
                        dpar = qg % 2
                        arc = sACT.reserve(2)

                        def s_rc(sc, w=dn_last, pdlo2=pdlo2):
                            sc.wait_ge(sDVE.h, w)
                            if pdlo2 is not None:
                                sc.wait_ge(sDVE.h, pdlo2)
                            sw(sc)
                            SA(sc.activation(out=lnt[:], in_=den_sb[dpar][:],
                                             func=AF.Ln))
                            sw(sc)
                            SA(sc.activation(out=rcpb[:], in_=lnt[:],
                                             func=AF.Exp, scale=-1.0))
                        emit("scalar", s_rc)

                        dlo2 = sDVE.reserve(3)
                        aopool.released_at(sDVE, dlo2)

                        def v_ev(v, w=pv_done, pstw=pstw, aoslot=aoslot,
                                 arc=arc):
                            v.wait_ge(sMM.h, w)
                            v.wait_ge(sACT.h, arc)
                            if pstw is not None:
                                pw(v, pstw)
                            vw(v)
                            VA(v.tensor_tensor(aof_ev[:], ps_ao[aoslot][:],
                                               rcpb[:], ALU.mult))
                            vw(v)
                            VA(v.tensor_copy(aoh_ev[:], aof_ev[:]))
                            vw(v)
                            VA(v.tensor_tensor(aol_ev[:], aof_ev[:], aoh_ev[:],
                                               ALU.subtract))
                        emit("vector", v_ev)

                        p = (h * NQG + qg) % 2
                        nst = 3 if taps else 2
                        stw = dst3.reserve(p, nst)

                        def g_st(g, h=h, qg=qg, p=p, w=dlo2):
                            g.wait_ge(sDVE.h, w)
                            r0 = h * 128
                            c0 = qg * QG
                            dst3.attach(p, g.dma_start(
                                out=s_aoh[r0:r0 + 128, c0:c0 + QG],
                                in_=aoh_ev[:]))
                            dst3.attach(p, g.dma_start(
                                out=s_aol[r0:r0 + 128, c0:c0 + QG],
                                in_=aol_ev[:]))
                            if taps:
                                dst3.attach(p, g.dma_start(
                                    out=taph["t_ao"][r0:r0 + 128, c0:c0 + QG],
                                    in_=aof_ev[:]))
                        emit("gpsimd", g_st)
                        return stw, dlo2
                    return do_emit

                pending_evac = make_evac()
                pv_done_prev = pv_done

            headpool.released_at(sMM, pv_done_prev)

        if pending_evac is not None:
            prev_stw, prev_dlo2 = pending_evac()
            pending_evac = None

        close_scope(sc3)

    p3()
    close_scope(sc03)

    def p4():
        phase_barrier()
        sc4 = []
        dstat = DStream("p4s")
        dx = DStream("p4x")
        dst4 = DStream("st4", store=True)
        whi, wlo, wready, tguard = load_w_bf16("wo", s_woh, s_wol, sc4)
        statpool = Pool(2, "p4st", init=tguard)
        sth = [sbuf(f"p4sh{i}", [128, D], BF16, sc4) for i in range(2)]
        stl = [sbuf(f"p4sl{i}", [128, D], BF16, sc4) for i in range(2)]
        pspool = Pool(2, "p4ps")
        ps4 = [psum(f"p4ps{i}", [128, D], sc4) for i in range(2)]
        xpool = Pool(2, "p4x", init=tguard)
        xt = [sbuf(f"p4x{i}", [128, D], F32, sc4) for i in range(2)]
        attnb = sbuf("p4attn", [128, D], F32, sc4)
        x2b = sbuf("p4x2", [128, D], F32, sc4)
        lnh = sbuf("p4lnh", [128, D], BF16, sc4)
        lnl = sbuf("p4lnl", [128, D], BF16, sc4)
        st6 = sbuf("p4st6", [128, 6 * NBN], F32, sc4)
        mv = sbuf("p4mv", [128, 2], F32, sc4)
        r3_t = sbuf("p4r3", [128, 1], F32, sc4)
        b3_t = sbuf("p4b3", [128, 1], F32, sc4)
        r2_t = sbuf("p4r2", [128, 1], F32, sc4)
        b2_t = sbuf("p4b2", [128, 1], F32, sc4)

        for tc in range(TC):
            sslot, swv = statpool.acquire()
            ldw = dstat.reserve(sslot, 2)

            def ld(s, tc=tc, sslot=sslot, swv=swv):
                pw(s, swv)
                src_h = s_aoh[:, tc * 128:(tc + 1) * 128] \
                    .rearrange("(dc d) t -> d dc t", d=128)
                src_l = s_aol[:, tc * 128:(tc + 1) * 128] \
                    .rearrange("(dc d) t -> d dc t", d=128)
                dst_h = sth[sslot][:].rearrange("p (dc t) -> p dc t", t=128)
                dst_l = stl[sslot][:].rearrange("p (dc t) -> p dc t", t=128)
                dstat.attach(sslot, s.dma_start(out=dst_h, in_=src_h))
                dstat.attach(sslot, s.dma_start(out=dst_l, in_=src_l))
            emit("sync", ld)

            xslot, xwv = xpool.acquire()
            xw = dx.reserve(xslot)

            def ldx(s, tc=tc, xslot=xslot, xwv=xwv):
                pw(s, xwv)
                r0 = (HC + tc) * 128
                dx.attach(xslot, s.dma_start(out=xt[xslot][:],
                                             in_=xh[r0:r0 + 128, :]))
            emit("sync", ldx)

            pslot, pwv = pspool.acquire()
            mmv = sMM.reserve()
            statpool.released_at(sMM, mmv)

            def mms(te, sslot=sslot, pslot=pslot, pwv=pwv, ldw=ldw):
                pw(te, pwv)
                pw(te, ldw)
                for w_ in wready:
                    pw(te, w_)
                groups = [(sth[sslot][:, dc * 128:(dc + 1) * 128],
                           stl[sslot][:, dc * 128:(dc + 1) * 128],
                           whi[dc][:], wlo[dc][:]) for dc in range(NDC)]
                sMM.attach(emit_mm_tc(te, ps4[pslot][:], D, groups))
            emit("tensor", mms)
            ps = ps4[pslot]

            a0 = sACT.reserve()
            pspool.released_at(sACT, a0)

            def s_c(sc, ps=ps, w=mmv):
                sc.wait_ge(sMM.h, w)
                sw(sc)
                SA(sc.activation(out=attnb[:], in_=ps[:], func=AF.Copy))
            emit("scalar", s_c)

            d0 = sDVE.reserve(NBN + 1)

            def v_st3(v, w=a0):
                v.wait_ge(sACT.h, w)
                bn_pair(v, attnb[:], st6, mv)
            emit("vector", v_st3)

            a1 = sACT.reserve()

            def s_r3(sc, w=d0):
                sc.wait_ge(sDVE.h, w)
                sw(sc)
                SA(sc.activation(out=r3_t[:], in_=mv[:, 1:2], func=AF.Sqrt,
                                 bias=epsln_t[:]))
            emit("scalar", s_r3)

            d1 = sDVE.reserve(4 + NBN + 1)
            xpool.released_at(sDVE, d1 - (NBN + 1))

            def v_x2(v, xslot=xslot, w=a1, xw=xw):
                v.wait_ge(sACT.h, w)
                pw(v, xw)
                vw(v)
                VA(v.reciprocal(r3_t[:], r3_t[:]))
                vw(v)
                VA(v.tensor_scalar(b3_t[:], mv[:, 0:1], r3_t[:], -1.0,
                                   ALU.mult, ALU.mult))
                vw(v)
                VA(v.tensor_scalar_add(x2b[:], xt[xslot][:], b3_t[:]))
                vw(v)
                VA(v.scalar_tensor_tensor(x2b[:], attnb[:], r3_t[:], x2b[:],
                                          ALU.mult, ALU.add))
                bn_pair(v, x2b[:], st6, mv)
            emit("vector", v_x2)

            a2 = sACT.reserve()

            def s_r2(sc, w=d1):
                sc.wait_ge(sDVE.h, w)
                sw(sc)
                SA(sc.activation(out=r2_t[:], in_=mv[:, 1:2], func=AF.Sqrt,
                                 bias=epsln_t[:]))
            emit("scalar", s_r2)

            d2 = sDVE.reserve(2)

            def v_b2(v, w=a2):
                v.wait_ge(sACT.h, w)
                vw(v)
                VA(v.reciprocal(r2_t[:], r2_t[:]))
                vw(v)
                VA(v.tensor_scalar(b2_t[:], mv[:, 0:1], r2_t[:], -1.0,
                                   ALU.mult, ALU.mult))
            emit("vector", v_b2)

            a3 = sACT.reserve(2)

            def s_ln2(sc, w=d2):
                sc.wait_ge(sDVE.h, w)
                sw(sc)
                SA(sc.activation(out=attnb[:], in_=x2b[:], func=AF.Identity,
                                 scale=r2_t[:], bias=b2_t[:]))
                sw(sc)
                SA(sc.activation(out=lnh[:], in_=attnb[:], func=AF.Copy))
            emit("scalar", s_ln2)

            dlo = sDVE.reserve()

            def v_fin(v, w=a3):
                v.wait_ge(sACT.h, w)
                vw(v)
                VA(v.tensor_tensor(lnl[:], attnb[:], lnh[:], ALU.subtract))
            emit("vector", v_fin)

            p = tc % 2
            nst = 4 if taps else 3
            stw = dst4.reserve(p, nst)

            def g_st(g, tc=tc, p=p, w=dlo):
                g.wait_ge(sDVE.h, w)
                r0 = tc * 128
                dst4.attach(p, g.dma_start(out=s_x2[r0:r0 + 128, :], in_=x2b[:]))
                dst4.attach(p, g.dma_start(out=s_ln2h[r0:r0 + 128, :], in_=lnh[:]))
                dst4.attach(p, g.dma_start(out=s_ln2l[r0:r0 + 128, :], in_=lnl[:]))
                if taps:
                    dst4.attach(p, g.dma_start(out=taph["t_x2"][r0:r0 + 128, :],
                                               in_=x2b[:]))
            emit("gpsimd", g_st)
            emit("scalar", lambda sc, w=stw: pw(sc, w))

        close_scope(sc4)

    p4()

    def p5a():
        phase_barrier()
        sc5 = []
        dmv = DStream("p5mv")
        wa_h = [sbuf(f"wa_h{i}", [128, HA], BF16, sc5) for i in range(NDC)]
        wa_l = [sbuf(f"wa_l{i}", [128, HA], BF16, sc5) for i in range(NDC)]
        wr_h = [sbuf(f"wr_h{i}", [128, E], BF16, sc5) for i in range(NDC)]
        wr_l = [sbuf(f"wr_l{i}", [128, E], BF16, sc5) for i in range(NDC)]
        tmp = []
        wfa = [sbuf(f"p5wfa{i}", [128, HA], F32, tmp) for i in range(2)]
        wfr = [sbuf(f"p5wfr{i}", [128, E], F32, tmp) for i in range(2)]
        dwa = DStream("p5wa")
        wapool = Pool(2, "p5wa", init=(sDVE, sDVE.c))
        for dc in range(NDC):
            slot, wv = wapool.acquire()
            ww = dwa.reserve(slot, 2)

            def ldw(s, dc=dc, slot=slot, wv=wv):
                pw(s, wv)
                dwa.attach(slot, s.dma_start(out=wfa[slot][:],
                                             in_=waT[dc * 128:(dc + 1) * 128, :]))
                dwa.attach(slot, s.dma_start(out=wfr[slot][:],
                                             in_=wrT[dc * 128:(dc + 1) * 128, :]))
            emit("sync", ldw)

            aw = sACT.reserve(2)

            def s_h(sc, dc=dc, slot=slot, w=ww):
                pw(sc, w)
                sw(sc)
                SA(sc.activation(out=wa_h[dc][:], in_=wfa[slot][:], func=AF.Copy))
                sw(sc)
                SA(sc.activation(out=wr_h[dc][:], in_=wfr[slot][:], func=AF.Copy))
            emit("scalar", s_h)

            dvv = sDVE.reserve(2)
            wapool.released_at(sDVE, dvv)

            def v_l(v, dc=dc, slot=slot, w=aw):
                v.wait_ge(sACT.h, w)
                vw(v)
                VA(v.tensor_tensor(wa_l[dc][:], wfa[slot][:], wa_h[dc][:],
                                   ALU.subtract))
                vw(v)
                VA(v.tensor_tensor(wr_l[dc][:], wfr[slot][:], wr_h[dc][:],
                                   ALU.subtract))
            emit("vector", v_l)
        w_rdy = (sDVE.c, sACT.c)
        close_scope(tmp)

        dst5 = DStream("st5", store=True)
        mvpool = Pool(2, "p5mv", init=(sDVE, w_rdy[0]))
        mv_h = [sbuf(f"p5mh{i}", [128, T], BF16, sc5) for i in range(2)]
        mv_l = [sbuf(f"p5ml{i}", [128, T], BF16, sc5) for i in range(2)]
        ps_ha = psum("p5pha", [128, T], sc5)
        ps_lg = psum("p5plg", [8, T], sc5)

        ocs = [(i * 512, min(512, T - i * 512)) for i in range((T + 511) // 512)]
        for dc in range(NDC):
            mslot, mwv = mvpool.acquire()
            trw = dmv.reserve(mslot, 2)

            def tl(sc, dc=dc, mslot=mslot, mwv=mwv):
                pw(sc, mwv)
                c0 = dc * 128
                dmv.attach(mslot, sc.dma_start(out=mv_h[mslot][:],
                                               in_=s_ln2h[:, c0:c0 + 128],
                                               transpose=True))
                dmv.attach(mslot, sc.dma_start(out=mv_l[mslot][:],
                                               in_=s_ln2l[:, c0:c0 + 128],
                                               transpose=True))
            emit("sync", tl)

            mm5v = sMM.reserve()
            mvpool.released_at(sMM, mm5v)

            def mm5(te, dc=dc, mslot=mslot, trw=trw):
                pw(te, trw)
                if dc == 0:
                    te.wait_ge(sDVE.h, w_rdy[0])
                    te.wait_ge(sACT.h, w_rdy[1])
                last = None
                for pi, (lt, rt) in enumerate(
                        ((wa_h[dc][:], mv_h[mslot][:]),
                         (wa_h[dc][:], mv_l[mslot][:]),
                         (wa_l[dc][:], mv_h[mslot][:]))):
                    for (o0, on) in ocs:
                        last = te.matmul(ps_ha[:, o0:o0 + on], lt,
                                         rt[:, o0:o0 + on],
                                         start=(dc == 0 and pi == 0),
                                         stop=(dc == NDC - 1 and pi == 2))
                for pi, (lt, rt) in enumerate(
                        ((wr_h[dc][:], mv_h[mslot][:]),
                         (wr_h[dc][:], mv_l[mslot][:]),
                         (wr_l[dc][:], mv_h[mslot][:]))):
                    for (o0, on) in ocs:
                        last = te.matmul(ps_lg[:, o0:o0 + on], lt,
                                         rt[:, o0:o0 + on],
                                         start=(dc == 0 and pi == 0),
                                         stop=(dc == NDC - 1 and pi == 2))
                sMM.attach(last)
            emit("tensor", mm5)
        mm_done = sMM.c

        haf = sbuf("p5haf", [128, T], F32, sc5)
        lgf = sbuf("p5lgf", [32, T], F32, sc5)
        lgT = sbuf("p5lgT", [128, TC * 32], F32, sc5)
        gateb = sbuf("p5gb", [128, TC * 32], BF16, sc5)
        g8f = sbuf("p5g8f", [32, T], BF16, sc5)
        gfm = sbuf("p5gfm", [128, T], BF16, sc5)
        mx_t = sbuf("p5mx", [128, 1], F32, sc5)

        dz = sDVE.reserve(2)

        def v_zero(v):
            vw(v)
            VA(v.memset(lgf[:], 0.0))
            vw(v)
            VA(v.memset(gateb[:], 0.0))
        emit("vector", v_zero)

        aev = sACT.reserve(2)

        def s_ev(sc, w=mm_done, wz=dz):
            sc.wait_ge(sMM.h, w)
            sc.wait_ge(sDVE.h, wz)
            sw(sc)
            SA(sc.activation(out=haf[:], in_=ps_ha[:], func=AF.Copy))
            sw(sc)
            SA(sc.activation(out=lgf[0:8, :], in_=ps_lg[:], func=AF.Copy))
        emit("scalar", s_ev)

        dgate = sDVE.reserve(2 * (T // 32) + 2 * TC)

        def v_gate(v, w=aev):
            v.wait_ge(sACT.h, w)
            for j in range(T // 32):
                tcj, rj = j // 4, (j % 4) * 32
                vw(v)
                VA(v.transpose(lgT[rj:rj + 32, tcj * 32:tcj * 32 + 32],
                               lgf[0:32, j * 32:(j + 1) * 32]))
            for tc in range(TC):
                vw(v)
                VA(v.reduce_max(mx_t[:], lgT[:, tc * 32:tc * 32 + E], axis=AX))
                vw(v)
                VA(v.tensor_scalar(gateb[:, tc * 32:tc * 32 + E],
                                   lgT[:, tc * 32:tc * 32 + E],
                                   mx_t[:], 2.0, ALU.is_ge, ALU.mult))
            for j in range(T // 32):
                tcj, rj = j // 4, (j % 4) * 32
                vw(v)
                VA(v.transpose(g8f[0:32, j * 32:(j + 1) * 32],
                               gateb[rj:rj + 32, tcj * 32:tcj * 32 + 32]))
        emit("vector", v_gate)

        gexp = dst5.reserve(0, R)

        def g_gate(g, w=dgate):
            g.wait_ge(sDVE.h, w)
            for r in range(R):
                dst5.attach(0, g.dma_start(out=gfm[r * E:(r + 1) * E, :],
                                           in_=g8f[0:E, :]))
        emit("gpsimd", g_gate)

        dhag = sDVE.reserve()

        def v_hag(v, w=gexp):
            pw(v, w)
            vw(v)
            VA(v.tensor_tensor(haf[:], haf[:], gfm[:], ALU.mult))
        emit("vector", v_hag)

        if taps:
            dst5.reserve(1, 2)

            def g_tap(g, w=dhag):
                g.wait_ge(sDVE.h, w)
                dst5.attach(1, g.dma_start(out=taph["t_gate"][:, :], in_=gfm[:]))
                dst5.attach(1, g.dma_start(out=taph["t_ha"][:, :], in_=haf[:]))
            emit("gpsimd", g_tap)

        ah = sACT.reserve()

        def s_hi(sc, w=dhag):
            sc.wait_ge(sDVE.h, w)
            sw(sc)
            SA(sc.activation(out=hag_hi[:], in_=haf[:], func=AF.Copy))
        emit("scalar", s_hi)

        hag_rdy = sDVE.reserve()

        def v_lo(v, w=ah):
            v.wait_ge(sACT.h, w)
            vw(v)
            VA(v.tensor_tensor(hag_lo[:], haf[:], hag_hi[:], ALU.subtract))
        emit("vector", v_lo)
        close_scope(sc5)
        return hag_rdy

    phase_barrier()
    sc56 = []
    hag_hi = sbuf("hag_hi", [128, T], BF16, sc56)
    hag_lo = sbuf("hag_lo", [128, T], BF16, sc56)
    wb_whi, wb_wlo, wb_ready, wb_tg = load_w_bf16("wb", s_wbh, s_wbl, sc56)
    bw_h = sbuf("bw_h", [128, D], BF16, sc56)
    bw_l = sbuf("bw_l", [128, D], BF16, sc56)
    b_hi = sbuf("b_hi", [1, D], BF16, sc56)
    b_lo = sbuf("b_lo", [1, D], BF16, sc56)
    tmp56 = []
    bwf = sbuf("p5bwf", [128, D], F32, tmp56)
    b_f32 = sbuf("p5bf32", [1, D], F32, tmp56)

    bww = sLD.reserve(32)

    def ldbw(s, w=wb_tg):
        pw(s, w)
        sLD.attach_dma(s.dma_start(out=bwf[:], in_=bwT[:, :]))
        sLD.attach_dma(s.dma_start(out=b_f32[:], in_=b_base[:]))
    emit("sync", ldbw)

    abw = sACT.reserve(2)

    def s_bw(sc, w=bww):
        sc.wait_ge(sLD.h, w)
        sw(sc)
        SA(sc.activation(out=bw_h[:], in_=bwf[:], func=AF.Copy))
        sw(sc)
        SA(sc.activation(out=b_hi[:], in_=b_f32[:], func=AF.Copy))
    emit("scalar", s_bw)

    bw_rdy = sDVE.reserve(2)

    def v_bw(v, w=abw):
        v.wait_ge(sACT.h, w)
        vw(v)
        VA(v.tensor_tensor(bw_l[:], bwf[:], bw_h[:], ALU.subtract))
        vw(v)
        VA(v.tensor_tensor(b_lo[:], b_f32[:], b_hi[:], ALU.subtract))
    emit("vector", v_bw)
    close_scope(tmp56)
    hag_rdy = p5a()

    def p5b():
        phase_barrier()
        sc6 = []
        dstat = DStream("p5s")
        dx2 = DStream("p5x")
        whi, wlo, wready = wb_whi, wb_wlo, wb_ready

        dsto = DStream("sto", store=True)
        statpool = Pool(2, "p5st", init=(sDVE, bw_rdy))
        sth = [sbuf(f"p5sh{i}", [128, D], BF16, sc6) for i in range(2)]
        stl = [sbuf(f"p5sl{i}", [128, D], BF16, sc6) for i in range(2)]
        pspool = Pool(2, "p5ps")
        ps5 = [psum(f"p5ps{i}", [128, D], sc6) for i in range(2)]
        x2pool = Pool(2, "p5x2", init=(sDVE, bw_rdy))
        x2t = [sbuf(f"p5x2{i}", [128, D], F32, sc6) for i in range(2)]
        ffb = sbuf("p5ffb", [128, D], F32, sc6)
        outb = sbuf("p5out", [128, D], F32, sc6)
        st6 = sbuf("p5st6", [128, 6 * NBN], F32, sc6)
        mv = sbuf("p5mv", [128, 2], F32, sc6)
        r4_t = sbuf("p5r4", [128, 1], F32, sc6)
        b4_t = sbuf("p5b4", [128, 1], F32, sc6)

        for tc in range(TC):
            sslot, swv = statpool.acquire()
            trw = dstat.reserve(sslot, 2)

            def tl(sc, tc=tc, sslot=sslot, swv=swv):
                pw(sc, swv)
                r0 = tc * 128
                src_h = s_ln2h[r0:r0 + 128, :].rearrange("t (dc d) -> t dc d", d=128)
                src_l = s_ln2l[r0:r0 + 128, :].rearrange("t (dc d) -> t dc d", d=128)
                dst_h = sth[sslot][:].rearrange("p (dc t) -> p dc t", t=128)
                dst_l = stl[sslot][:].rearrange("p (dc t) -> p dc t", t=128)
                dstat.attach(sslot, sc.dma_start(out=dst_h, in_=src_h, transpose=True))
                dstat.attach(sslot, sc.dma_start(out=dst_l, in_=src_l, transpose=True))
            emit("sync", tl)

            xslot, xwv = x2pool.acquire()
            xw = dx2.reserve(xslot)

            def ldx(s, tc=tc, xslot=xslot, xwv=xwv):
                pw(s, xwv)
                r0 = tc * 128
                dx2.attach(xslot, s.dma_start(out=x2t[xslot][:],
                                              in_=s_x2[r0:r0 + 128, :]))
            emit("sync", ldx)

            pslot, pwv = pspool.acquire()
            mmv = sMM.reserve()
            statpool.released_at(sMM, mmv)
            dvw = max(bw_rdy, hag_rdy)

            def mms(te, tc=tc, sslot=sslot, pslot=pslot, pwv=pwv, trw=trw,
                    dvw=dvw):
                pw(te, pwv)
                pw(te, trw)
                te.wait_ge(sDVE.h, dvw)
                for w_ in wready:
                    pw(te, w_)
                groups = [(sth[sslot][:, dc * 128:(dc + 1) * 128],
                           stl[sslot][:, dc * 128:(dc + 1) * 128],
                           whi[dc][:], wlo[dc][:]) for dc in range(NDC)]
                groups.append((hag_hi[:, tc * 128:(tc + 1) * 128],
                               hag_lo[:, tc * 128:(tc + 1) * 128],
                               bw_h[:], bw_l[:]))
                groups.append((ones_bf[:], None, b_hi[:], b_lo[:]))
                sMM.attach(emit_mm_tc(te, ps5[pslot][:], D, groups))
            emit("tensor", mms)
            ps = ps5[pslot]

            a0 = sACT.reserve()
            pspool.released_at(sACT, a0)

            def s_c(sc, ps=ps, w=mmv):
                sc.wait_ge(sMM.h, w)
                sw(sc)
                SA(sc.activation(out=ffb[:], in_=ps[:], func=AF.Copy))
            emit("scalar", s_c)

            d0 = sDVE.reserve(NBN + 1)

            def v_st(v, w=a0):
                v.wait_ge(sACT.h, w)
                bn_pair(v, ffb[:], st6, mv)
            emit("vector", v_st)

            a1 = sACT.reserve()

            def s_r4(sc, w=d0):
                sc.wait_ge(sDVE.h, w)
                sw(sc)
                SA(sc.activation(out=r4_t[:], in_=mv[:, 1:2], func=AF.Sqrt,
                                 bias=epsln_t[:]))
            emit("scalar", s_r4)

            dout_ = sDVE.reserve(4)
            x2pool.released_at(sDVE, dout_)

            def v_out(v, xslot=xslot, w=a1, xw=xw):
                v.wait_ge(sACT.h, w)
                pw(v, xw)
                vw(v)
                VA(v.reciprocal(r4_t[:], r4_t[:]))
                vw(v)
                VA(v.tensor_scalar(b4_t[:], mv[:, 0:1], r4_t[:], -1.0,
                                   ALU.mult, ALU.mult))
                vw(v)
                VA(v.tensor_scalar_add(outb[:], x2t[xslot][:], b4_t[:]))
                vw(v)
                VA(v.scalar_tensor_tensor(outb[:], ffb[:], r4_t[:], outb[:],
                                          ALU.mult, ALU.add))
            emit("vector", v_out)

            p = tc % 2
            ow = dsto.reserve(p)

            def g_out(g, tc=tc, p=p, w=dout_):
                g.wait_ge(sDVE.h, w)
                r0 = tc * 128
                dsto.attach(p, g.dma_start(out=y[r0:r0 + 128, :], in_=outb[:]))
            emit("gpsimd", g_out)
            emit("scalar", lambda sc, w=ow: pw(sc, w))
            emit("vector", lambda v, w=ow: pw(v, w))

        close_scope(sc6)

    p5b()
    close_scope(sc56)

    fin_waits = [(c, c.c) for c in STORE_CTRS if c.c > 0]

    def fin(g, ws=fin_waits):
        for c, v in ws:
            g.wait_ge(c.h, v)
    emit("gpsimd", fin)

    with nc.Block() as block:
        @block.sync
        def _(sync):
            for f in ops["sync"]:
                f(sync)

        @block.scalar
        def _(scalar):
            for f in ops["scalar"]:
                f(scalar)

        @block.vector
        def _(vector):
            for f in ops["vector"]:
                f(vector)

        @block.tensor
        def _(tensor):
            for f in ops["tensor"]:
                f(tensor)

        @block.gpsimd
        def _(gpsimd):
            for f in ops["gpsimd"]:
                f(gpsimd)

    for c in all_ctrs:
        assert c.att == c.c, f"sem {c.name}: attached {c.att} != reserved {c.c}"

    close_scope(glob)
    return nc


def stage_core_inputs(cfg, core_id, x_full, w_qkv, q_norm_w, k_norm_w, w_o,
                      W_base, b_base_v, router_W, A, Bw):
    D = cfg["D"]; NH = cfg["NH"]
    T = cfg["T_OWN"]; TH = cfg["T_HALO"]; E = cfg["E"]; R = cfg["R"]
    KC = (T + TH) // 128
    NJ = (D // NH) // 2
    b, half = core_id // 2, core_id % 2

    own0 = half * T
    x_own = np.ascontiguousarray(x_full[b, own0:own0 + T, :], dtype=np.float32)
    if half >= 1:
        x_halo = np.ascontiguousarray(x_full[b, own0 - TH:own0, :],
                                      dtype=np.float32)
        posv = np.arange(own0 - TH, own0 + T, dtype=np.float32)
        mv = np.zeros((128, TH), dtype=np.float32)
    else:
        x_halo = np.zeros((TH, D), dtype=np.float32)
        posv = np.concatenate([np.zeros(TH, dtype=np.float32),
                               np.arange(0, T, dtype=np.float32)])
        mv = np.full((128, TH), MASKVAL, dtype=np.float32)

    xh = np.concatenate([x_halo, x_own], axis=0)
    posr = np.ascontiguousarray(posv.reshape(KC, 128).T)

    i = np.arange(128)[:, None]
    j = np.arange(128)[None, :]
    mW_ = np.where(j >= i + 1, 0.0, MASKVAL).astype(np.float32)
    mC_ = np.where(j <= i, 0.0, MASKVAL).astype(np.float32)
    jr = np.arange(NJ, dtype=np.float32)[None, :]

    return {
        "xh": xh,
        "wqT": np.ascontiguousarray(w_qkv[0:D, :].T, dtype=np.float32),
        "wkT": np.ascontiguousarray(w_qkv[D:2 * D, :].T, dtype=np.float32),
        "wvT": np.ascontiguousarray(w_qkv[2 * D:3 * D, :].T, dtype=np.float32),
        "woT": np.ascontiguousarray(w_o.T, dtype=np.float32),
        "wbT": np.ascontiguousarray(W_base.T, dtype=np.float32),
        "waT": np.ascontiguousarray(A.transpose(2, 1, 0).reshape(D, R * E)),
        "wrT": np.ascontiguousarray(router_W.T, dtype=np.float32),
        "bwT": np.ascontiguousarray(Bw.transpose(2, 0, 1).reshape(R * E, D)),
        "b_base": np.ascontiguousarray(b_base_v.reshape(1, D), dtype=np.float32),
        "qnw": np.ascontiguousarray(q_norm_w.reshape(1, D), dtype=np.float32),
        "knw": np.ascontiguousarray(k_norm_w.reshape(1, D), dtype=np.float32),
        "posr": posr,
        "jrow": np.ascontiguousarray(jr, dtype=np.float32),
        "mW": mW_, "mC": mC_, "mV": mv,
        "ones_row": np.ones((1, 128), dtype=np.float32),
        "ident": np.eye(128, dtype=np.float32),
    }


def kernel(L0_attn_hidden_states, L0_attn_w_qkv, L0_attn_q_norm_w,
           L0_attn_k_norm_w, L0_attn_w_o, L0_ffn_W_base, L0_ffn_b_base,
           L0_ffn_router_W, L0_ffn_A, L0_ffn_Bw):
    cfg = real_cfg()
    x = np.asarray(L0_attn_hidden_states, dtype=np.float32)
    args = (x, np.asarray(L0_attn_w_qkv, np.float32),
            np.asarray(L0_attn_q_norm_w, np.float32),
            np.asarray(L0_attn_k_norm_w, np.float32),
            np.asarray(L0_attn_w_o, np.float32),
            np.asarray(L0_ffn_W_base, np.float32),
            np.asarray(L0_ffn_b_base, np.float32),
            np.asarray(L0_ffn_router_W, np.float32),
            np.asarray(L0_ffn_A, np.float32),
            np.asarray(L0_ffn_Bw, np.float32))
    in_maps = [stage_core_inputs(cfg, c, *args) for c in range(8)]
    nc = build_nc(cfg)
    res = run_bass_kernel_spmd(nc, in_maps, core_ids=list(range(8)))
    B, S, D = x.shape
    out = np.empty((B, S, D), dtype=np.float32)
    T = cfg["T_OWN"]
    for c in range(8):
        b, half = c // 2, c % 2
        out[b, half * T:(half + 1) * T, :] = res.results[c]["y"]
    return out


# version 25
# speedup vs baseline: 1.2313x; 1.2313x over previous
import math
import numpy as np
import concourse.bass as bass
import concourse.mybir as mybir
from concourse.bass_utils import run_bass_kernel_spmd

F32 = mybir.dt.float32
BF16 = mybir.dt.bfloat16
AF = mybir.ActivationFunctionType
ALU = mybir.AluOpType
AX = mybir.AxisListType.X

MASKVAL = -30000.0
EPS_LN = 1e-5
EPS_RMS = 1e-6


def real_cfg():
    return dict(D=2048, NH=16, WINDOW=1024, T_OWN=2048, T_HALO=1024,
                E=8, R=16, debug_taps=False)


def mini_cfg():
    return dict(D=256, NH=2, WINDOW=256, T_OWN=512, T_HALO=256,
                E=8, R=16, debug_taps=True)


class Ctr:
    def __init__(self, handle, name):
        self.h = handle
        self.name = name
        self.c = 0
        self.att = 0

    def reserve(self, n=1):
        self.c += n
        return self.c

    def reserve_dma(self):
        return self.reserve(16)

    def attach(self, instr, n=1):
        self.att += n
        instr.then_inc(self.h, n)
        return instr

    def attach_dma(self, instr):
        return self.attach(instr, 16)


class Pool:

    def __init__(self, nbuf, name, init=None):
        self.nbuf = nbuf
        self.name = name
        self.uses = 0
        self.init = init
        self.rel = []

    def acquire(self):
        u = self.uses
        self.uses += 1
        if u >= self.nbuf:
            assert len(self.rel) > u - self.nbuf, \
                f"pool {self.name}: use {u} before release {u - self.nbuf}"
            w = self.rel[u - self.nbuf]
        else:
            w = self.init
        return u % self.nbuf, w

    def released_at(self, ctr, value):
        self.rel.append((ctr, value))


def build_nc(cfg):
    D = cfg["D"]; NH = cfg["NH"]; W = cfg["WINDOW"]
    T = cfg["T_OWN"]; TH = cfg["T_HALO"]; E = cfg["E"]; R = cfg["R"]
    HA = E * R
    HD = D // NH
    assert HD == 128 and HA == 128 and W == TH
    NDC = D // 128
    TC = T // 128
    HC = TH // 128
    KC = TC + HC
    TKV = T + TH
    WKC = W // 128
    KSPAN = WKC + 1
    KW = KSPAN * 128
    KWP = ((KW + 511) // 512) * 512
    NJ = HD // 2
    taps = cfg["debug_taps"]
    QG = min(512, T)
    NQG = T // QG
    QGC = QG // 128
    CSC = 1.0 / math.sqrt(HD)

    nc = bass.Bass("TRN2", target_bir_lowering=False, debug=False)

    def din(name, shape, dt=F32):
        return nc.dram_tensor(name, list(shape), dt, kind="ExternalInput").ap()

    def dscr(name, shape, dt):
        return nc.dram_tensor(name, list(shape), dt, kind="Internal").ap()

    def dout(name, shape, dt=F32):
        return nc.dram_tensor(name, list(shape), dt, kind="ExternalOutput").ap()

    xh = din("xh", [TKV, D])
    wqT = din("wqT", [D, D]); wkT = din("wkT", [D, D]); wvT = din("wvT", [D, D])
    woT = din("woT", [D, D]); wbT = din("wbT", [D, D])
    waT = din("waT", [D, HA])
    wrT = din("wrT", [D, E])
    bwT = din("bwT", [HA, D])
    b_base = din("b_base", [1, D])
    qnw = din("qnw", [1, D]); knw = din("knw", [1, D])
    posr = din("posr", [128, KC])
    jrow = din("jrow", [1, NJ])
    mW = din("mW", [128, 128]); mC = din("mC", [128, 128])
    mV = din("mV", [128, TH])
    ones_row = din("ones_row", [1, 128])
    ident = din("ident", [128, 128])

    s_ln1h = dscr("s_ln1h", [TKV, D], BF16); s_ln1l = dscr("s_ln1l", [TKV, D], BF16)
    s_qh = dscr("s_qh", [T, D], BF16); s_ql = dscr("s_ql", [T, D], BF16)
    s_kh = dscr("s_kh", [TKV, D], BF16); s_kl = dscr("s_kl", [TKV, D], BF16)
    s_vh = dscr("s_vh", [TKV, D], BF16); s_vl = dscr("s_vl", [TKV, D], BF16)
    s_aoh = dscr("s_aoh", [D, T], BF16); s_aol = dscr("s_aol", [D, T], BF16)
    s_x2 = dscr("s_x2", [T, D], F32)
    s_woh = dscr("s_woh", [D, D], BF16); s_wol = dscr("s_wol", [D, D], BF16)
    s_wbh = dscr("s_wbh", [D, D], BF16); s_wbl = dscr("s_wbl", [D, D], BF16)
    s_ln2h = dscr("s_ln2h", [T, D], BF16); s_ln2l = dscr("s_ln2l", [T, D], BF16)

    y = dout("y", [T, D])
    taph = {}
    if taps:
        for nm, shp in (("t_ln1", [TKV, D]), ("t_q", [T, D]), ("t_k", [TKV, D]),
                        ("t_v", [TKV, D]), ("t_ao", [D, T]), ("t_x2", [T, D]),
                        ("t_gate", [128, T]), ("t_ha", [128, T])):
            taph[nm] = dout(nm, shp)

    ops = {"sync": [], "scalar": [], "vector": [], "tensor": [], "gpsimd": []}

    def emit(eng, fn):
        ops[eng].append(fn)

    glob = []

    def sbuf(name, shape, dt, scope):
        g = nc.sbuf_tensor(name, list(shape), dt)
        t = g.__enter__()
        scope.append(g)
        return t

    def psum(name, shape, scope, dt=F32):
        g = nc.psum_tensor(name, list(shape), dt)
        t = g.__enter__()
        scope.append(g)
        return t

    def close_scope(scope):
        for g in reversed(scope):
            g.__exit__(None, None, None)
        scope.clear()

    all_ctrs = []

    def sem(name):
        g = nc.semaphore(name)
        h = g.__enter__()
        glob.append(g)
        c = Ctr(h, name)
        all_ctrs.append(c)
        return c

    sLD = sem("sLD")
    sST = sem("sST")
    sMM = sem("sMM")
    sACT = sem("sACT")
    sDVE = sem("sDVE")
    sOUT = sem("sOUT")

    STORE_CTRS = []

    class DStream:

        def __init__(self, name, n=2, store=False):
            self.s = [sem(f"{name}{i}") for i in range(n)]
            if store:
                STORE_CTRS.extend(self.s)

        def reserve(self, slot, ndma=1):
            return (self.s[slot], self.s[slot].reserve(16 * ndma))

        def attach(self, slot, instr):
            return self.s[slot].attach(instr, 16)

    def phase_barrier():
        waits = [(c, c.c) for c in STORE_CTRS if c.c > 0]

        def b(e, ws=waits):
            for c, v in ws:
                e.wait_ge(c.h, v)
        emit("sync", b)
        emit("scalar", b)

    def SA(instr):
        return sACT.attach(instr)

    def VA(instr):
        return sDVE.attach(instr)

    def sw(sc):
        sc.wait_ge(sACT.h, sACT.att)

    def vw(v):
        v.wait_ge(sDVE.h, sDVE.att)

    def pw(eng, w):
        if w is not None:
            eng.wait_ge(w[0].h, w[1])

    NBN = (D + 511) // 512

    def bn_pair(v, srcbuf, st6, mv):
        for c in range(NBN):
            vw(v)
            VA(v.bn_stats(st6[:, c * 6:(c + 1) * 6],
                          srcbuf[:, c * 512:min((c + 1) * 512, D)]))
        vw(v)
        VA(v.bn_aggr(mv[:], st6[:, 0:6 * NBN]))

    ones_bf = sbuf("ones_bf", [1, 128], BF16, glob)
    epsln_t = sbuf("epsln_t", [128, 1], F32, glob)
    rmssc_t = sbuf("rmssc_t", [128, 1], F32, glob)
    rmseps_t = sbuf("rmseps_t", [128, 1], F32, glob)
    zro_bf = sbuf("zro_bf", [1, 512], BF16, glob)
    qsc_t = sbuf("qsc_t", [128, 1], F32, glob)
    qeps_t = sbuf("qeps_t", [128, 1], F32, glob)
    sc03 = []
    mW_t = sbuf("mW_t", [128, 128], F32, sc03)
    mC_t = sbuf("mC_t", [128, 128], F32, sc03)
    mV_t = sbuf("mV_t", [128, TH], F32, sc03)
    id_bf = sbuf("id_bf", [128, 128], BF16, sc03)
    mW_b = sbuf("mW_b", [128, 128], BF16, sc03)
    mC_b = sbuf("mC_b", [128, 128], BF16, sc03)
    mV_b = sbuf("mV_b", [128, TH], BF16, sc03)
    idf_t = sbuf("idf_t", [128, 128], F32, sc03)
    HW_ = D // 2
    cvx = [sbuf(f"cvx{i}", [128, HW_], F32, sc03) for i in range(1)]
    cvh = [sbuf(f"cvh{i}", [128, HW_], BF16, sc03) for i in range(1)]
    cvl = [sbuf(f"cvl{i}", [128, HW_], BF16, sc03) for i in range(1)]
    onesr_b = sbuf("onesr_b", [1, 512], BF16, sc03)
    ones_kv = sbuf("ones_kv", [128, 128], BF16, sc03)
    sc02 = []
    cos_t = sbuf("cos_t", [128, KC * NJ], F32, sc02)
    sin_t = sbuf("sin_t", [128, KC * NJ], F32, sc02)


    sDVE.reserve(8)

    def v_consts(v):
        VA(v.memset(zro_bf[:], 0.0))
        VA(v.memset(epsln_t[:], EPS_LN))
        VA(v.memset(rmssc_t[:], 1.0 / D))
        VA(v.memset(rmseps_t[:], EPS_RMS))
        VA(v.memset(qsc_t[:], float(HD) / D))
        VA(v.memset(qeps_t[:], float(HD) * EPS_RMS))
        VA(v.memset(onesr_b[:], 1.0))
        VA(v.memset(ones_kv[:], 1.0))
    emit("vector", v_consts)

    def p0():
        sc0 = []
        post = sbuf("post", [128, KC], F32, sc0)
        jbc = sbuf("jbc", [128, NJ], F32, sc0)
        ones_f = sbuf("ones_f", [1, 128], F32, sc0)
        r_ang = sbuf("r_ang", [128, KC * NJ], F32, sc0)
        r_s1 = sbuf("r_s1", [128, KC * NJ], F32, sc0)
        r_s2 = sbuf("r_s2", [128, KC * NJ], F32, sc0)
        r_s3 = sbuf("r_s3", [128, KC * NJ], F32, sc0)
        c_a = sbuf("c_a", [128, 1], F32, sc0)

        w0 = [sLD.reserve_dma() for _ in range(7)][-1]

        def ld(s):
            sLD.attach_dma(s.dma_start(out=ones_f[:], in_=ones_row[:]))
            sLD.attach_dma(s.dma_start(out=idf_t[:], in_=ident[:]))
            sLD.attach_dma(s.dma_start(out=post[:], in_=posr[:]))
            sLD.attach_dma(s.dma_start(out=jbc[:], in_=jrow[:].broadcast_to((128, NJ))))
            sLD.attach_dma(s.dma_start(out=mW_t[:], in_=mW[:]))
            sLD.attach_dma(s.dma_start(out=mC_t[:], in_=mC[:]))
            sLD.attach_dma(s.dma_start(out=mV_t[:], in_=mV[:]))
        emit("sync", ld)

        a_if = sACT.reserve(9)

        def sc_c(sc):
            sc.wait_ge(sLD.h, w0)
            SA(sc.activation(out=ones_bf[:], in_=ones_f[:], func=AF.Copy))
            SA(sc.activation(out=id_bf[:], in_=idf_t[:], func=AF.Copy))
            SA(sc.activation(out=mW_b[:], in_=mW_t[:], func=AF.Copy))
            SA(sc.activation(out=mC_b[:], in_=mC_t[:], func=AF.Copy))
            SA(sc.activation(out=mV_b[:], in_=mV_t[:], func=AF.Copy))
            SA(sc.activation(out=c_a[:], in_=jbc[:, 0:1], func=AF.Copy,
                             scale=0.0, bias=10000.0))
            sw(sc)
            SA(sc.activation(out=c_a[:], in_=c_a[:], func=AF.Ln))
            sw(sc)
            SA(sc.mul(c_a[:], c_a[:], -1.0 / NJ))
            sw(sc)
            SA(sc.activation(out=r_s1[:, 0:NJ], in_=jbc[:], func=AF.Exp,
                             scale=c_a[:]))
        emit("scalar", sc_c)

        mmax = max(0, int(math.ceil(math.log2(max(TKV, 7) / (2 * math.pi)))))
        sDVE.reserve(1 + 3 * (mmax + 1))

        def v_ang(v):
            v.wait_ge(sACT.h, a_if)
            ang = r_ang[:].rearrange("p (kc j) -> p kc j", j=NJ)
            vw(v)
            VA(v.tensor_tensor(
                ang,
                post[:].unsqueeze(2).broadcast_to((128, KC, NJ)),
                r_s1[:, 0:NJ].unsqueeze(1).broadcast_to((128, KC, NJ)),
                ALU.mult))
            for m in range(mmax, -1, -1):
                c = 2 * math.pi * (2 ** m)
                vw(v)
                VA(v.tensor_single_scalar(r_s1[:], r_ang[:], c, ALU.is_ge))
                vw(v)
                VA(v.tensor_scalar(r_s1[:], r_s1[:], c, None, ALU.mult))
                vw(v)
                VA(v.tensor_tensor(r_ang[:], r_ang[:], r_s1[:], ALU.subtract))
        emit("vector", v_ang)

        def make_table(shift, out_tab):
            dv = sDVE.reserve(14)

            def vf(v):
                vw(v)
                VA(v.tensor_scalar(r_s1[:], r_ang[:], shift, None, ALU.add))
                vw(v)
                VA(v.tensor_single_scalar(r_s2[:], r_s1[:], 2 * math.pi, ALU.is_ge))
                vw(v)
                VA(v.tensor_scalar(r_s2[:], r_s2[:], 2 * math.pi, None, ALU.mult))
                vw(v)
                VA(v.tensor_tensor(r_s1[:], r_s1[:], r_s2[:], ALU.subtract))
                vw(v)
                VA(v.tensor_scalar(r_s1[:], r_s1[:], math.pi, None, ALU.subtract))
                vw(v)
                VA(v.tensor_tensor(r_s2[:], r_s1[:], r_s1[:], ALU.mult))
                vw(v)
                VA(v.tensor_single_scalar(r_s2[:], r_s2[:],
                                          (math.pi / 2) ** 2, ALU.is_ge))
                vw(v)
                VA(v.tensor_single_scalar(r_s3[:], r_s1[:], 0.0, ALU.is_ge))
                vw(v)
                VA(v.tensor_scalar(r_s3[:], r_s3[:], 2.0, -1.0, ALU.mult, ALU.add))
                vw(v)
                VA(v.tensor_tensor(r_s3[:], r_s3[:], r_s2[:], ALU.mult))
                vw(v)
                VA(v.tensor_scalar(r_s3[:], r_s3[:], math.pi, None, ALU.mult))
                vw(v)
                VA(v.tensor_scalar(r_s2[:], r_s2[:], -2.0, 1.0, ALU.mult, ALU.add))
                vw(v)
                VA(v.tensor_tensor(r_s1[:], r_s1[:], r_s2[:], ALU.mult))
                vw(v)
                VA(v.tensor_tensor(r_s1[:], r_s1[:], r_s3[:], ALU.add))
            emit("vector", vf)
            av = sACT.reserve()

            def sf(sc):
                sc.wait_ge(sDVE.h, dv)
                sw(sc)
                SA(sc.activation(out=out_tab[:], in_=r_s1[:], func=AF.Sin,
                                 scale=-1.0))
            emit("scalar", sf)
            emit("vector", lambda v, w=av: v.wait_ge(sACT.h, w))

        make_table(0.0, sin_t)
        make_table(math.pi / 2, cos_t)
        fin = sACT.c
        emit("sync", lambda s, w=fin: s.wait_ge(sACT.h, w))
        close_scope(sc0)

    p0()

    dcv = DStream("cvld", n=1)
    dcs = DStream("cvst", n=1, store=True)
    cvpool = Pool(1, "cv")

    def emit_conv_tile(wsrc, dsth, dstl, dc, hf):
        slot, wv = cvpool.acquire()
        lw = dcv.reserve(slot)
        r0, c0 = dc * 128, hf * HW_

        def ld(s, wsrc=wsrc, slot=slot, wv=wv, r0=r0, c0=c0):
            pw(s, wv)
            dcv.attach(slot, s.dma_start(
                out=cvx[slot][:], in_=wsrc[r0:r0 + 128, c0:c0 + HW_]))
        emit("sync", ld)

        aw = sACT.reserve()

        def s_h(sc, slot=slot, w=lw):
            pw(sc, w)
            sw(sc)
            SA(sc.activation(out=cvh[slot][:], in_=cvx[slot][:], func=AF.Copy))
        emit("scalar", s_h)

        dv = sDVE.reserve()

        def v_l(v, slot=slot, w=aw):
            v.wait_ge(sACT.h, w)
            vw(v)
            VA(v.tensor_tensor(cvl[slot][:], cvx[slot][:], cvh[slot][:],
                               ALU.subtract))
        emit("vector", v_l)

        sw_ = dcs.reserve(slot, 2)
        cvpool.released_at(sw_[0], sw_[1])

        def st(s, dsth=dsth, dstl=dstl, slot=slot, w=dv, r0=r0, c0=c0):
            s.wait_ge(sDVE.h, w)
            dcs.attach(slot, s.dma_start(
                out=dsth[r0:r0 + 128, c0:c0 + HW_], in_=cvh[slot][:]))
            dcs.attach(slot, s.dma_start(
                out=dstl[r0:r0 + 128, c0:c0 + HW_], in_=cvl[slot][:]))
        emit("sync", st)

    conv_plan = ([(woT, s_woh, s_wol, dc, hf) for dc in range(NDC)
                  for hf in range(2)] +
                 [(wbT, s_wbh, s_wbl, dc, hf) for dc in range(NDC)
                  for hf in range(2)])

    def emit_conv_some(i0, n):
        for j in range(i0, min(i0 + n, len(conv_plan))):
            emit_conv_tile(*conv_plan[j])


    def p1():
        phase_barrier()
        sc1 = []
        NP1 = 3
        dsx = DStream("p1x", n=NP1)
        dst1 = DStream("st1", n=NP1, store=True)
        xpool = Pool(NP1, "p1x")
        x_in = [sbuf(f"p1x{i}", [128, D], F32, sc1) for i in range(NP1)]
        lnf = [sbuf(f"p1lnf{i}", [128, D], F32, sc1) for i in range(NP1)]
        lnh = [sbuf(f"p1lnh{i}", [128, D], BF16, sc1) for i in range(NP1)]
        lnl = [sbuf(f"p1lnl{i}", [128, D], BF16, sc1) for i in range(NP1)]
        st6 = [sbuf(f"p1st6{i}", [128, 6 * NBN], F32, sc1) for i in range(NP1)]
        mv = [sbuf(f"p1mv{i}", [128, 2], F32, sc1) for i in range(NP1)]
        r_t = [sbuf(f"p1r{i}", [128, 1], F32, sc1) for i in range(NP1)]
        b_t = [sbuf(f"p1b{i}", [128, 1], F32, sc1) for i in range(NP1)]
        stw_par = [None] * NP1

        for t in range(KC):
            p = t % NP1
            slot, wv = xpool.acquire()
            xw = dsx.reserve(slot)

            def ld(s, t=t, slot=slot, wv=wv):
                pw(s, wv)
                dsx.attach(slot, s.dma_start(out=x_in[slot][:],
                                             in_=xh[t * 128:(t + 1) * 128, :]))
            emit("sync", ld)

            spw = stw_par[p]
            d0 = sDVE.reserve(NBN + 1)

            def v_st(v, slot=slot, p=p, w=xw, spw=spw):
                pw(v, w)
                pw(v, spw)
                bn_pair(v, x_in[slot][:], st6[p], mv[p])
            emit("vector", v_st)

            a0 = sACT.reserve()

            def s_r(sc, p=p, w=d0):
                sc.wait_ge(sDVE.h, w)
                sw(sc)
                SA(sc.activation(out=r_t[p][:], in_=mv[p][:, 1:2], func=AF.Sqrt,
                                 bias=epsln_t[:]))
            emit("scalar", s_r)

            d1 = sDVE.reserve(2)

            def v_b(v, p=p, w=a0):
                v.wait_ge(sACT.h, w)
                vw(v)
                VA(v.reciprocal(r_t[p][:], r_t[p][:]))
                vw(v)
                VA(v.tensor_scalar(b_t[p][:], mv[p][:, 0:1], r_t[p][:], -1.0,
                                   ALU.mult, ALU.mult))
            emit("vector", v_b)

            a1 = sACT.reserve(2)
            xpool.released_at(sACT, a1 - 1)

            def s_ln(sc, slot=slot, p=p, w=d1):
                sc.wait_ge(sDVE.h, w)
                sw(sc)
                SA(sc.activation(out=lnf[p][:], in_=x_in[slot][:],
                                 func=AF.Identity,
                                 scale=r_t[p][:], bias=b_t[p][:]))
                sw(sc)
                SA(sc.activation(out=lnh[p][:], in_=lnf[p][:], func=AF.Copy))
            emit("scalar", s_ln)

            dlo = sDVE.reserve()

            def v_fin(v, p=p, w=a1):
                v.wait_ge(sACT.h, w)
                vw(v)
                VA(v.tensor_tensor(lnl[p][:], lnf[p][:], lnh[p][:],
                                   ALU.subtract))
            emit("vector", v_fin)

            nst = 3 if taps else 2
            stw = dst1.reserve(p, nst)
            stw_par[p] = stw

            def g_st(g, t=t, p=p, w=dlo):
                g.wait_ge(sDVE.h, w)
                dst1.attach(p, g.dma_start(out=s_ln1h[t * 128:(t + 1) * 128, :],
                                           in_=lnh[p][:]))
                dst1.attach(p, g.dma_start(out=s_ln1l[t * 128:(t + 1) * 128, :],
                                           in_=lnl[p][:]))
                if taps:
                    dst1.attach(p, g.dma_start(
                        out=taph["t_ln1"][t * 128:(t + 1) * 128, :],
                        in_=lnf[p][:]))
            emit("gpsimd", g_st)

        close_scope(sc1)

    p1()

    def load_w_bf16(pref, srch, srcl, scope):
        whi = [sbuf(f"{pref}h{i}", [128, D], BF16, scope)
               for i in range(NDC)]
        wlo = [sbuf(f"{pref}l{i}", [128, D], BF16, scope)
               for i in range(NDC)]
        dwb = DStream(pref + "b")
        r0 = dwb.reserve(0, NDC)
        r1 = dwb.reserve(1, NDC)

        def ld(s):
            for dc in range(NDC):
                dwb.attach(dc % 2, s.dma_start(
                    out=whi[dc][:], in_=srch[dc * 128:(dc + 1) * 128, :]))
                dwb.attach(dc % 2, s.dma_start(
                    out=wlo[dc][:], in_=srcl[dc * 128:(dc + 1) * 128, :]))
        emit("sync", ld)
        return whi, wlo, [r0, r1], None

    def load_w_resident(pref, src, ncols, scope):
        whi = [sbuf(f"{pref}h{i}", [128, ncols], BF16, scope) for i in range(NDC)]
        wlo = [sbuf(f"{pref}l{i}", [128, ncols], BF16, scope) for i in range(NDC)]
        tmp = []
        nws = 4
        wsl = [sbuf(f"{pref}t{i}", [128, ncols], F32, tmp) for i in range(nws)]
        dsw = DStream(pref + "d", n=nws)
        pl = Pool(nws, pref)
        for dc in range(NDC):
            slot, wv = pl.acquire()
            ww = dsw.reserve(slot)

            def ld(s, dc=dc, slot=slot, wv=wv):
                pw(s, wv)
                dsw.attach(slot, s.dma_start(out=wsl[slot][:],
                                             in_=src[dc * 128:(dc + 1) * 128, :]))
            emit("sync", ld)

            aw = sACT.reserve()

            def s_h(sc, dc=dc, slot=slot, w=ww):
                pw(sc, w)
                sw(sc)
                SA(sc.activation(out=whi[dc][:], in_=wsl[slot][:], func=AF.Copy))
            emit("scalar", s_h)

            dv = sDVE.reserve()
            pl.released_at(sDVE, dv)

            def v_l(v, dc=dc, slot=slot, w=aw):
                v.wait_ge(sACT.h, w)
                vw(v)
                VA(v.tensor_tensor(wlo[dc][:], wsl[slot][:], whi[dc][:],
                                   ALU.subtract))
            emit("vector", v_l)
        ready = [(sDVE, sDVE.c), (sACT, sACT.c)]
        close_scope(tmp)
        return whi, wlo, ready, (sDVE, ready[0][1])

    def emit_mm_tc(te, ps_ap, n_out, groups):
        ocs = [(i * 512, min(512, n_out - i * 512))
               for i in range((n_out + 511) // 512)]
        plan = []
        for (lh, ll, rh, rl) in groups:
            plan.append((lh, rh))
            plan.append((lh, rl))
            if ll is not None:
                plan.append((ll, rh))
        last = None
        first = [True] * len(ocs)
        for pi, (lt, rt) in enumerate(plan):
            for oi, (o0, on) in enumerate(ocs):
                st = first[oi]
                first[oi] = False
                last = te.matmul(ps_ap[:, o0:o0 + on], lt, rt[:, o0:o0 + on],
                                 start=st, stop=(pi == len(plan) - 1))
        return last

    def p2_og(og):
        phase_barrier()
        sc2 = []
        dstat = DStream(f"p2{og}s")
        dst2 = DStream(f"st2{og}", store=True)
        wsrc = {"q": wqT, "k": wkT, "v": wvT}[og]
        whi, wlo, wready, tguard = load_w_resident(f"w{og}", wsrc, D, sc2)

        wrep_w = None
        if og in ("q", "k"):
            wrep = sbuf(f"p2{og}rep", [128, D], F32, sc2)
            wrep_w = sLD.reserve_dma()

            def ldrep(s, og=og, w=tguard):
                pw(s, w)
                wsrc2 = qnw if og == "q" else knw
                sLD.attach_dma(s.dma_start(out=wrep[:],
                                           in_=wsrc2[:].broadcast_to((128, D))))
            emit("sync", ldrep)

        statpool = Pool(2, f"p2{og}st", init=tguard)
        sth = [sbuf(f"p2{og}sh{i}", [128, D], BF16, sc2) for i in range(2)]
        stl = [sbuf(f"p2{og}sl{i}", [128, D], BF16, sc2) for i in range(2)]
        pspool = Pool(2, f"p2{og}ps")
        ps2 = [psum(f"p2{og}ps{i}", [128, D], sc2) for i in range(2)]
        qn = sbuf(f"p2{og}qn", [128, D], F32, sc2)
        rr = sbuf(f"p2{og}rr", [128, D], F32, sc2)
        ohi = sbuf(f"p2{og}oh", [128, D], BF16, sc2)
        olo = sbuf(f"p2{og}ol", [128, D], BF16, sc2)
        sq_t = sbuf(f"p2{og}sq", [128, 1], F32, sc2)
        rq_t = sbuf(f"p2{og}rq", [128, 1], F32, sc2)

        tcs = list(range(HC, KC)) if og == "q" else list(range(KC))
        dsth, dstl = {"q": (s_qh, s_ql), "k": (s_kh, s_kl),
                      "v": (s_vh, s_vl)}[og]
        tapn = {"q": "t_q", "k": "t_k", "v": "t_v"}[og]

        for tc in tcs:
            sslot, swv = statpool.acquire()
            trw = dstat.reserve(sslot, 2)

            def tl(sc, tc=tc, sslot=sslot, swv=swv):
                pw(sc, swv)
                r0 = tc * 128
                src_h = s_ln1h[r0:r0 + 128, :].rearrange("t (dc d) -> t dc d", d=128)
                src_l = s_ln1l[r0:r0 + 128, :].rearrange("t (dc d) -> t dc d", d=128)
                dst_h = sth[sslot][:].rearrange("p (dc t) -> p dc t", t=128)
                dst_l = stl[sslot][:].rearrange("p (dc t) -> p dc t", t=128)
                dstat.attach(sslot, sc.dma_start(out=dst_h, in_=src_h, transpose=True))
                dstat.attach(sslot, sc.dma_start(out=dst_l, in_=src_l, transpose=True))
            emit("sync", tl)

            pslot, pwv = pspool.acquire()
            mmv = sMM.reserve()
            statpool.released_at(sMM, mmv)

            def mms(te, sslot=sslot, pslot=pslot, pwv=pwv, trw=trw):
                pw(te, pwv)
                pw(te, trw)
                for w_ in wready:
                    pw(te, w_)
                groups = [(sth[sslot][:, dc * 128:(dc + 1) * 128],
                           stl[sslot][:, dc * 128:(dc + 1) * 128],
                           whi[dc][:], wlo[dc][:]) for dc in range(NDC)]
                sMM.attach(emit_mm_tc(te, ps2[pslot][:], D, groups))
            emit("tensor", mms)
            ps = ps2[pslot]
            row0 = (tc - HC) * 128 if og == "q" else tc * 128

            if og in ("q", "k"):
                a0 = sACT.reserve(2)

                def s_e(sc, og=og, ps=ps, w=mmv):
                    sc.wait_ge(sMM.h, w)
                    sw(sc)
                    SA(sc.activation(out=rr[:], in_=ps[:], func=AF.Square,
                                     accum_out=sq_t[:]))
                    sw(sc)
                    if og == "q":
                        SA(sc.activation(out=rq_t[:], in_=sq_t[:], func=AF.Sqrt,
                                         scale=qsc_t[:], bias=qeps_t[:]))
                    else:
                        SA(sc.activation(out=rq_t[:], in_=sq_t[:], func=AF.Sqrt,
                                         scale=rmssc_t[:], bias=rmseps_t[:]))
                emit("scalar", s_e)

                d0 = sDVE.reserve(8)
                pspool.released_at(sDVE, d0 - 6)

                def v_e(v, tc=tc, ps=ps, w=a0, ww=wrep_w):
                    v.wait_ge(sACT.h, w)
                    v.wait_ge(sLD.h, ww)
                    vw(v)
                    VA(v.reciprocal(rq_t[:], rq_t[:]))
                    vw(v)
                    VA(v.scalar_tensor_tensor(qn[:], ps[:], rq_t[:], wrep[:],
                                              ALU.mult, ALU.mult))

                    def hv(buf, half):
                        vvw = buf[:].rearrange("p (h two j) -> p h two j",
                                               two=2, j=NJ)
                        return vvw[:, :, half, :]
                    cosv = cos_t[:, tc * NJ:(tc + 1) * NJ] \
                        .unsqueeze(1).broadcast_to((128, NH, NJ))
                    sinv = sin_t[:, tc * NJ:(tc + 1) * NJ] \
                        .unsqueeze(1).broadcast_to((128, NH, NJ))
                    qt, qb = hv(qn, 0), hv(qn, 1)
                    rt_, rb = hv(rr, 0), hv(rr, 1)
                    vw(v); VA(v.tensor_tensor(rt_, qb, sinv, ALU.mult))
                    vw(v); VA(v.tensor_tensor(rb, qb, cosv, ALU.mult))
                    vw(v); VA(v.tensor_tensor(qb, qt, sinv, ALU.mult))
                    vw(v); VA(v.tensor_tensor(rb, rb, qb, ALU.add))
                    vw(v); VA(v.tensor_tensor(qb, qt, cosv, ALU.mult))
                    vw(v); VA(v.tensor_tensor(rt_, qb, rt_, ALU.subtract))
                emit("vector", v_e)

                a1 = sACT.reserve()

                def s_hi(sc, w=d0):
                    sc.wait_ge(sDVE.h, w)
                    sw(sc)
                    SA(sc.activation(out=ohi[:], in_=rr[:], func=AF.Copy))
                emit("scalar", s_hi)

                dlo = sDVE.reserve()

                def v_lo(v, w=a1):
                    v.wait_ge(sACT.h, w)
                    vw(v)
                    VA(v.tensor_tensor(olo[:], rr[:], ohi[:], ALU.subtract))
                emit("vector", v_lo)
            else:
                a0 = sACT.reserve(2 if taps else 1)

                def s_e(sc, ps=ps, w=mmv):
                    sc.wait_ge(sMM.h, w)
                    sw(sc)
                    SA(sc.activation(out=ohi[:], in_=ps[:], func=AF.Copy))
                    if taps:
                        sw(sc)
                        SA(sc.activation(out=rr[:], in_=ps[:], func=AF.Copy))
                emit("scalar", s_e)

                dlo = sDVE.reserve()
                pspool.released_at(sDVE, dlo)

                def v_lo(v, ps=ps, w=a0):
                    v.wait_ge(sACT.h, w)
                    vw(v)
                    VA(v.tensor_tensor(olo[:], ps[:], ohi[:], ALU.subtract))
                emit("vector", v_lo)

            p = tc % 2
            nst = 3 if taps else 2
            stw = dst2.reserve(p, nst)

            def g_st(g, p=p, row0=row0, w=dlo):
                g.wait_ge(sDVE.h, w)
                dst2.attach(p, g.dma_start(out=dsth[row0:row0 + 128, :],
                                           in_=ohi[:]))
                dst2.attach(p, g.dma_start(out=dstl[row0:row0 + 128, :],
                                           in_=olo[:]))
                if taps:
                    dst2.attach(p, g.dma_start(
                        out=taph[tapn][row0:row0 + 128, :], in_=rr[:]))
            emit("gpsimd", g_st)
            emit("scalar", lambda sc, w=stw: pw(sc, w))

        close_scope(sc2)

    p2_og("q")
    p2_og("k")
    p2_og("v")
    close_scope(sc02)

    def p3():
        phase_barrier()
        sc3 = []
        dqk = DStream("p3qk")
        dst3 = DStream("st3", store=True)
        dv_ = DStream("p3v")
        headpool = Pool(2, "p3head")
        qTh = [sbuf(f"p3qh{i}", [128, T], BF16, sc3) for i in range(2)]
        qTl = [sbuf(f"p3ql{i}", [128, T], BF16, sc3) for i in range(2)]
        kTh = [sbuf(f"p3kh{i}", [128, TKV], BF16, sc3) for i in range(2)]
        kTl = [sbuf(f"p3kl{i}", [128, TKV], BF16, sc3) for i in range(2)]
        vh_ = [sbuf(f"p3vh{i}", [128, TKV], BF16, sc3) for i in range(2)]
        vl_ = [sbuf(f"p3vl{i}", [128, TKV], BF16, sc3) for i in range(2)]
        e32 = [sbuf(f"p3e{i}", [128, 512], F32, sc3) for i in range(2)]
        ehi = [sbuf(f"p3eh{i}", [128, 512], BF16, sc3) for i in range(3)]
        elo = [sbuf(f"p3el{i}", [128, 512], BF16, sc3) for i in range(3)]
        den_sb = [sbuf(f"p3dsb{i}", [128, QG], F32, sc3) for i in range(2)]
        lnt = sbuf("p3lnt", [128, QG], F32, sc3)
        rcpb = sbuf("p3rcpb", [128, QG], F32, sc3)
        aof_ev = sbuf("p3aof", [128, QG], F32, sc3)
        aoh_ev = sbuf("p3aoh", [128, QG], BF16, sc3)
        aol_ev = sbuf("p3aol", [128, QG], BF16, sc3)
        scpool = Pool(3, "p3sc")
        ps_s = [psum(f"p3ps{i}", [128, 512], sc3) for i in range(3)]
        epool = Pool(3, "p3ep")
        aopool = Pool(2, "p3ao")
        ps_ao = [psum(f"p3pao{i}", [128, QG], sc3) for i in range(2)]
        ps_dn = [psum(f"p3pdn{i}", [128, 512], sc3) for i in range(2)]

        e32_free = {0: None, 1: None}
        dn_free = {}
        gci = 0
        prev_stw = None
        prev_dlo2 = None
        pending_evac = None
        for h in range(NH):
            hslot, hwv = headpool.acquire()
            trw = dqk.reserve(hslot, 4)

            def hload_t(sc, h=h, hslot=hslot, hwv=hwv):
                pw(sc, hwv)
                c0 = h * 128
                dqk.attach(hslot, sc.dma_start(out=qTh[hslot][:],
                                               in_=s_qh[:, c0:c0 + 128],
                                               transpose=True))
                dqk.attach(hslot, sc.dma_start(out=qTl[hslot][:],
                                               in_=s_ql[:, c0:c0 + 128],
                                               transpose=True))
                dqk.attach(hslot, sc.dma_start(out=kTh[hslot][:],
                                               in_=s_kh[:, c0:c0 + 128],
                                               transpose=True))
                dqk.attach(hslot, sc.dma_start(out=kTl[hslot][:],
                                               in_=s_kl[:, c0:c0 + 128],
                                               transpose=True))
            emit("sync", hload_t)

            vldw = dv_.reserve(hslot, 2)

            def hload_v(s, h=h, hslot=hslot, hwv=hwv):
                pw(s, hwv)
                c0 = h * 128
                src_h = s_vh[:, c0:c0 + 128].rearrange("(kc t) d -> t kc d", t=128)
                src_l = s_vl[:, c0:c0 + 128].rearrange("(kc t) d -> t kc d", t=128)
                dst_h = vh_[hslot][:].rearrange("p (kc d) -> p kc d", d=128)
                dst_l = vl_[hslot][:].rearrange("p (kc d) -> p kc d", d=128)
                dv_.attach(hslot, s.dma_start(out=dst_h, in_=src_h))
                dv_.attach(hslot, s.dma_start(out=dst_l, in_=src_l))
            emit("sync", hload_v)
            emit_conv_some(4 * h, 4)

            pv_done_prev = None
            for qg in range(NQG):
                qc0, qc1 = qg * QGC, (qg + 1) * QGC - 1
                kvcs = list(range(qc0, qc1 + WKC + 1))
                aoslot, aowv = aopool.acquire()

                def zf(te, aowv=aowv, aoslot=aoslot):
                    pw(te, aowv)
                    te.matmul(ps_ao[aoslot][:, 0:QG], ones_bf[:],
                              zro_bf[:, 0:QG], start=True, stop=False)
                emit("tensor", zf)

                def emit_pv(info, is_last, aowv=aowv, hslot=hslot,
                            vldw=vldw, qc0=qc0, aoslot=aoslot):
                    ci, kvc, eslot, ew, qa, qb, dnf = info
                    nc_ = (qb - qa + 1) * 128
                    pvv = sMM.reserve()
                    epool.released_at(sMM, pvv)

                    def t_pv(te, ci=ci, kvc=kvc, eslot=eslot, ew=ew, qa=qa,
                             qb=qb, nc_=nc_, is_last=is_last, qc0=qc0,
                             dnf=dnf, aoslot=aoslot):
                        te.wait_ge(sDVE.h, ew)
                        pw(te, vldw)
                        if dnf is not None:
                            te.wait_ge(sDVE.h, dnf)
                        off = (qa - qc0) * 128
                        vs_h = vh_[hslot][:, kvc * 128:(kvc + 1) * 128]
                        vs_l = vl_[hslot][:, kvc * 128:(kvc + 1) * 128]
                        eh = ehi[eslot][:, 0:nc_]
                        el = elo[eslot][:, 0:nc_]
                        te.matmul(ps_ao[aoslot][:, off:off + nc_], vs_h, eh,
                                  start=False, stop=False)
                        te.matmul(ps_ao[aoslot][:, off:off + nc_], vs_h, el,
                                  start=False, stop=False)
                        te.matmul(ps_ao[aoslot][:, off:off + nc_], vs_l, eh,
                                  start=False, stop=is_last)
                        te.matmul(ps_dn[ci % 2][:, 0:nc_], ones_kv[:], eh,
                                  start=True, stop=False)
                        lastm = te.matmul(ps_dn[ci % 2][:, 0:nc_], ones_kv[:],
                                          el, start=False, stop=True)
                        sMM.attach(lastm)
                    emit("tensor", t_pv)

                    dnv = sDVE.reserve(2 if ci == 0 else 1)
                    dpar = qg % 2

                    def v_dn(v, ci=ci, qa=qa, nc_=nc_, w=pvv, qc0=qc0,
                             dpar=dpar):
                        v.wait_ge(sMM.h, w)
                        off = (qa - qc0) * 128
                        if ci == 0:
                            vw(v)
                            VA(v.memset(den_sb[dpar][:], 0.0))
                        vw(v)
                        VA(v.tensor_tensor(den_sb[dpar][:, off:off + nc_],
                                           ps_dn[ci % 2][:, 0:nc_],
                                           den_sb[dpar][:, off:off + nc_],
                                           ALU.add))
                    emit("vector", v_dn)
                    return pvv, dnv

                pend = None
                pv_done = None
                dn_last = None
                for ci, kvc in enumerate(kvcs):
                    if ci == 1 and pending_evac is not None:
                        prev_stw, prev_dlo2 = pending_evac()
                        pending_evac = None
                    qa = max(qc0, kvc - WKC)
                    qb = min(qc1, kvc)
                    nc_ = (qb - qa + 1) * 128
                    cslot, cwv = scpool.acquire()
                    mmv = sMM.reserve()

                    def t_sc(te, kvc=kvc, qa=qa, qb=qb, nc_=nc_, hslot=hslot,
                             cslot=cslot, cwv=cwv, trw=trw):
                        pw(te, cwv)
                        pw(te, trw)
                        ps = ps_s[cslot]
                        kh = kTh[hslot][:, kvc * 128:(kvc + 1) * 128]
                        kl = kTl[hslot][:, kvc * 128:(kvc + 1) * 128]
                        qh = qTh[hslot][:, qa * 128:qa * 128 + nc_]
                        ql = qTl[hslot][:, qa * 128:qa * 128 + nc_]
                        te.matmul(ps[:, 0:nc_], kh, qh, start=True, stop=False)
                        te.matmul(ps[:, 0:nc_], kh, ql, start=False, stop=False)
                        if kvc <= qb:
                            o = (kvc - qa) * 128
                            te.matmul(ps[:, o:o + 128], mW_b[:], id_bf[:],
                                      start=False, stop=False)
                        if kvc - WKC >= qa:
                            o = (kvc - WKC - qa) * 128
                            te.matmul(ps[:, o:o + 128], mC_b[:], id_bf[:],
                                      start=False, stop=False)
                        if kvc < HC:
                            te.matmul(ps[:, 0:nc_],
                                      mV_b[0:1, kvc * 128:(kvc + 1) * 128],
                                      onesr_b[:, 0:nc_],
                                      start=False, stop=False)
                        last = te.matmul(ps[:, 0:nc_], kl, qh,
                                         start=False, stop=True)
                        sMM.attach(last)
                    emit("tensor", t_sc)

                    eslot, ewv = epool.acquire()
                    asp = sACT.reserve(2)
                    scpool.released_at(sACT, asp)
                    par = gci % 2
                    e2f = e32_free[par]

                    def s_split(sc, cslot=cslot, eslot=eslot, ewv=ewv,
                                nc_=nc_, w=mmv, e2f=e2f, par=par):
                        sc.wait_ge(sMM.h, w)
                        pw(sc, ewv)
                        if e2f is not None:
                            sc.wait_ge(sDVE.h, e2f)
                        sw(sc)
                        SA(sc.activation(out=e32[par][:, 0:nc_],
                                         in_=ps_s[cslot][:, 0:nc_],
                                         func=AF.Exp))
                        sw(sc)
                        SA(sc.activation(out=ehi[eslot][:, 0:nc_],
                                         in_=ps_s[cslot][:, 0:nc_],
                                         func=AF.Exp))
                    emit("scalar", s_split)

                    dlo = sDVE.reserve(1)

                    def v_lo(v, eslot=eslot, nc_=nc_, w=asp, par=par):
                        v.wait_ge(sACT.h, w)
                        vw(v)
                        VA(v.tensor_tensor(elo[eslot][:, 0:nc_],
                                           e32[par][:, 0:nc_],
                                           ehi[eslot][:, 0:nc_], ALU.subtract))
                    emit("vector", v_lo)
                    e32_free[par] = dlo
                    gci += 1

                    dnf = dn_free.get(ci % 2)
                    cur = (ci, kvc, eslot, dlo, qa, qb, dnf)
                    if pend is not None:
                        _, dnv = emit_pv(pend, False)
                        dn_free[pend[0] % 2] = dnv
                        dn_last = dnv
                    pend = cur
                pv_done, dn_last = emit_pv(pend, True)
                dn_free[pend[0] % 2] = dn_last

                def make_evac(h=h, qg=qg, pv_done=pv_done, aoslot=aoslot,
                              pstw=prev_stw, pdlo2=prev_dlo2, dn_last=dn_last):
                    def do_emit():
                        dpar = qg % 2
                        arc = sACT.reserve(2)

                        def s_rc(sc, w=dn_last, pdlo2=pdlo2):
                            sc.wait_ge(sDVE.h, w)
                            if pdlo2 is not None:
                                sc.wait_ge(sDVE.h, pdlo2)
                            sw(sc)
                            SA(sc.activation(out=lnt[:], in_=den_sb[dpar][:],
                                             func=AF.Ln))
                            sw(sc)
                            SA(sc.activation(out=rcpb[:], in_=lnt[:],
                                             func=AF.Exp, scale=-1.0))
                        emit("scalar", s_rc)

                        dlo2 = sDVE.reserve(3)
                        aopool.released_at(sDVE, dlo2)

                        def v_ev(v, w=pv_done, pstw=pstw, aoslot=aoslot,
                                 arc=arc):
                            v.wait_ge(sMM.h, w)
                            v.wait_ge(sACT.h, arc)
                            if pstw is not None:
                                pw(v, pstw)
                            vw(v)
                            VA(v.tensor_tensor(aof_ev[:], ps_ao[aoslot][:],
                                               rcpb[:], ALU.mult))
                            vw(v)
                            VA(v.tensor_copy(aoh_ev[:], aof_ev[:]))
                            vw(v)
                            VA(v.tensor_tensor(aol_ev[:], aof_ev[:], aoh_ev[:],
                                               ALU.subtract))
                        emit("vector", v_ev)

                        p = (h * NQG + qg) % 2
                        nst = 3 if taps else 2
                        stw = dst3.reserve(p, nst)

                        def g_st(g, h=h, qg=qg, p=p, w=dlo2):
                            g.wait_ge(sDVE.h, w)
                            r0 = h * 128
                            c0 = qg * QG
                            dst3.attach(p, g.dma_start(
                                out=s_aoh[r0:r0 + 128, c0:c0 + QG],
                                in_=aoh_ev[:]))
                            dst3.attach(p, g.dma_start(
                                out=s_aol[r0:r0 + 128, c0:c0 + QG],
                                in_=aol_ev[:]))
                            if taps:
                                dst3.attach(p, g.dma_start(
                                    out=taph["t_ao"][r0:r0 + 128, c0:c0 + QG],
                                    in_=aof_ev[:]))
                        emit("gpsimd", g_st)
                        return stw, dlo2
                    return do_emit

                pending_evac = make_evac()
                pv_done_prev = pv_done

            headpool.released_at(sMM, pv_done_prev)

        if pending_evac is not None:
            prev_stw, prev_dlo2 = pending_evac()
            pending_evac = None

        close_scope(sc3)

    p3()
    close_scope(sc03)

    def p4():
        phase_barrier()
        sc4 = []
        dstat = DStream("p4s")
        dx = DStream("p4x")
        dst4 = DStream("st4", store=True)
        whi, wlo, wready, tguard = load_w_bf16("wo", s_woh, s_wol, sc4)
        statpool = Pool(2, "p4st", init=tguard)
        sth = [sbuf(f"p4sh{i}", [128, D], BF16, sc4) for i in range(2)]
        stl = [sbuf(f"p4sl{i}", [128, D], BF16, sc4) for i in range(2)]
        pspool = Pool(2, "p4ps")
        ps4 = [psum(f"p4ps{i}", [128, D], sc4) for i in range(2)]
        xpool = Pool(2, "p4x", init=tguard)
        xt = [sbuf(f"p4x{i}", [128, D], F32, sc4) for i in range(2)]
        attnb = sbuf("p4attn", [128, D], F32, sc4)
        x2b = sbuf("p4x2", [128, D], F32, sc4)
        lnh = sbuf("p4lnh", [128, D], BF16, sc4)
        lnl = sbuf("p4lnl", [128, D], BF16, sc4)
        st6 = sbuf("p4st6", [128, 6 * NBN], F32, sc4)
        mv = sbuf("p4mv", [128, 2], F32, sc4)
        r3_t = sbuf("p4r3", [128, 1], F32, sc4)
        b3_t = sbuf("p4b3", [128, 1], F32, sc4)
        r2_t = sbuf("p4r2", [128, 1], F32, sc4)
        b2_t = sbuf("p4b2", [128, 1], F32, sc4)

        for tc in range(TC):
            sslot, swv = statpool.acquire()
            ldw = dstat.reserve(sslot, 2)

            def ld(s, tc=tc, sslot=sslot, swv=swv):
                pw(s, swv)
                src_h = s_aoh[:, tc * 128:(tc + 1) * 128] \
                    .rearrange("(dc d) t -> d dc t", d=128)
                src_l = s_aol[:, tc * 128:(tc + 1) * 128] \
                    .rearrange("(dc d) t -> d dc t", d=128)
                dst_h = sth[sslot][:].rearrange("p (dc t) -> p dc t", t=128)
                dst_l = stl[sslot][:].rearrange("p (dc t) -> p dc t", t=128)
                dstat.attach(sslot, s.dma_start(out=dst_h, in_=src_h))
                dstat.attach(sslot, s.dma_start(out=dst_l, in_=src_l))
            emit("sync", ld)

            xslot, xwv = xpool.acquire()
            xw = dx.reserve(xslot)

            def ldx(s, tc=tc, xslot=xslot, xwv=xwv):
                pw(s, xwv)
                r0 = (HC + tc) * 128
                dx.attach(xslot, s.dma_start(out=xt[xslot][:],
                                             in_=xh[r0:r0 + 128, :]))
            emit("sync", ldx)

            pslot, pwv = pspool.acquire()
            mmv = sMM.reserve()
            statpool.released_at(sMM, mmv)

            def mms(te, sslot=sslot, pslot=pslot, pwv=pwv, ldw=ldw):
                pw(te, pwv)
                pw(te, ldw)
                for w_ in wready:
                    pw(te, w_)
                groups = [(sth[sslot][:, dc * 128:(dc + 1) * 128],
                           stl[sslot][:, dc * 128:(dc + 1) * 128],
                           whi[dc][:], wlo[dc][:]) for dc in range(NDC)]
                sMM.attach(emit_mm_tc(te, ps4[pslot][:], D, groups))
            emit("tensor", mms)
            ps = ps4[pslot]

            a0 = sACT.reserve()
            pspool.released_at(sACT, a0)

            def s_c(sc, ps=ps, w=mmv):
                sc.wait_ge(sMM.h, w)
                sw(sc)
                SA(sc.activation(out=attnb[:], in_=ps[:], func=AF.Copy))
            emit("scalar", s_c)

            d0 = sDVE.reserve(NBN + 1)

            def v_st3(v, w=a0):
                v.wait_ge(sACT.h, w)
                bn_pair(v, attnb[:], st6, mv)
            emit("vector", v_st3)

            a1 = sACT.reserve()

            def s_r3(sc, w=d0):
                sc.wait_ge(sDVE.h, w)
                sw(sc)
                SA(sc.activation(out=r3_t[:], in_=mv[:, 1:2], func=AF.Sqrt,
                                 bias=epsln_t[:]))
            emit("scalar", s_r3)

            d1 = sDVE.reserve(4 + NBN + 1)
            xpool.released_at(sDVE, d1 - (NBN + 1))

            def v_x2(v, xslot=xslot, w=a1, xw=xw):
                v.wait_ge(sACT.h, w)
                pw(v, xw)
                vw(v)
                VA(v.reciprocal(r3_t[:], r3_t[:]))
                vw(v)
                VA(v.tensor_scalar(b3_t[:], mv[:, 0:1], r3_t[:], -1.0,
                                   ALU.mult, ALU.mult))
                vw(v)
                VA(v.tensor_scalar_add(x2b[:], xt[xslot][:], b3_t[:]))
                vw(v)
                VA(v.scalar_tensor_tensor(x2b[:], attnb[:], r3_t[:], x2b[:],
                                          ALU.mult, ALU.add))
                bn_pair(v, x2b[:], st6, mv)
            emit("vector", v_x2)

            a2 = sACT.reserve()

            def s_r2(sc, w=d1):
                sc.wait_ge(sDVE.h, w)
                sw(sc)
                SA(sc.activation(out=r2_t[:], in_=mv[:, 1:2], func=AF.Sqrt,
                                 bias=epsln_t[:]))
            emit("scalar", s_r2)

            d2 = sDVE.reserve(2)

            def v_b2(v, w=a2):
                v.wait_ge(sACT.h, w)
                vw(v)
                VA(v.reciprocal(r2_t[:], r2_t[:]))
                vw(v)
                VA(v.tensor_scalar(b2_t[:], mv[:, 0:1], r2_t[:], -1.0,
                                   ALU.mult, ALU.mult))
            emit("vector", v_b2)

            a3 = sACT.reserve(2)

            def s_ln2(sc, w=d2):
                sc.wait_ge(sDVE.h, w)
                sw(sc)
                SA(sc.activation(out=attnb[:], in_=x2b[:], func=AF.Identity,
                                 scale=r2_t[:], bias=b2_t[:]))
                sw(sc)
                SA(sc.activation(out=lnh[:], in_=attnb[:], func=AF.Copy))
            emit("scalar", s_ln2)

            dlo = sDVE.reserve()

            def v_fin(v, w=a3):
                v.wait_ge(sACT.h, w)
                vw(v)
                VA(v.tensor_tensor(lnl[:], attnb[:], lnh[:], ALU.subtract))
            emit("vector", v_fin)

            p = tc % 2
            nst = 4 if taps else 3
            stw = dst4.reserve(p, nst)

            def g_st(g, tc=tc, p=p, w=dlo):
                g.wait_ge(sDVE.h, w)
                r0 = tc * 128
                dst4.attach(p, g.dma_start(out=s_x2[r0:r0 + 128, :], in_=x2b[:]))
                dst4.attach(p, g.dma_start(out=s_ln2h[r0:r0 + 128, :], in_=lnh[:]))
                dst4.attach(p, g.dma_start(out=s_ln2l[r0:r0 + 128, :], in_=lnl[:]))
                if taps:
                    dst4.attach(p, g.dma_start(out=taph["t_x2"][r0:r0 + 128, :],
                                               in_=x2b[:]))
            emit("gpsimd", g_st)
            emit("scalar", lambda sc, w=stw: pw(sc, w))

        close_scope(sc4)

    p4()

    def p5a():
        phase_barrier()
        sc5 = []
        dmv = DStream("p5mv")
        wa_h = [sbuf(f"wa_h{i}", [128, HA], BF16, sc5) for i in range(NDC)]
        wa_l = [sbuf(f"wa_l{i}", [128, HA], BF16, sc5) for i in range(NDC)]
        wr_h = [sbuf(f"wr_h{i}", [128, E], BF16, sc5) for i in range(NDC)]
        wr_l = [sbuf(f"wr_l{i}", [128, E], BF16, sc5) for i in range(NDC)]
        tmp = []
        wfa = [sbuf(f"p5wfa{i}", [128, HA], F32, tmp) for i in range(2)]
        wfr = [sbuf(f"p5wfr{i}", [128, E], F32, tmp) for i in range(2)]
        dwa = DStream("p5wa")
        wapool = Pool(2, "p5wa", init=(sDVE, sDVE.c))
        for dc in range(NDC):
            slot, wv = wapool.acquire()
            ww = dwa.reserve(slot, 2)

            def ldw(s, dc=dc, slot=slot, wv=wv):
                pw(s, wv)
                dwa.attach(slot, s.dma_start(out=wfa[slot][:],
                                             in_=waT[dc * 128:(dc + 1) * 128, :]))
                dwa.attach(slot, s.dma_start(out=wfr[slot][:],
                                             in_=wrT[dc * 128:(dc + 1) * 128, :]))
            emit("sync", ldw)

            aw = sACT.reserve(2)

            def s_h(sc, dc=dc, slot=slot, w=ww):
                pw(sc, w)
                sw(sc)
                SA(sc.activation(out=wa_h[dc][:], in_=wfa[slot][:], func=AF.Copy))
                sw(sc)
                SA(sc.activation(out=wr_h[dc][:], in_=wfr[slot][:], func=AF.Copy))
            emit("scalar", s_h)

            dvv = sDVE.reserve(2)
            wapool.released_at(sDVE, dvv)

            def v_l(v, dc=dc, slot=slot, w=aw):
                v.wait_ge(sACT.h, w)
                vw(v)
                VA(v.tensor_tensor(wa_l[dc][:], wfa[slot][:], wa_h[dc][:],
                                   ALU.subtract))
                vw(v)
                VA(v.tensor_tensor(wr_l[dc][:], wfr[slot][:], wr_h[dc][:],
                                   ALU.subtract))
            emit("vector", v_l)
        w_rdy = (sDVE.c, sACT.c)
        close_scope(tmp)

        dst5 = DStream("st5", store=True)
        mvpool = Pool(2, "p5mv", init=(sDVE, w_rdy[0]))
        mv_h = [sbuf(f"p5mh{i}", [128, T], BF16, sc5) for i in range(2)]
        mv_l = [sbuf(f"p5ml{i}", [128, T], BF16, sc5) for i in range(2)]
        ps_ha = psum("p5pha", [128, T], sc5)
        ps_lg = psum("p5plg", [8, T], sc5)

        ocs = [(i * 512, min(512, T - i * 512)) for i in range((T + 511) // 512)]
        for dc in range(NDC):
            mslot, mwv = mvpool.acquire()
            trw = dmv.reserve(mslot, 2)

            def tl(sc, dc=dc, mslot=mslot, mwv=mwv):
                pw(sc, mwv)
                c0 = dc * 128
                dmv.attach(mslot, sc.dma_start(out=mv_h[mslot][:],
                                               in_=s_ln2h[:, c0:c0 + 128],
                                               transpose=True))
                dmv.attach(mslot, sc.dma_start(out=mv_l[mslot][:],
                                               in_=s_ln2l[:, c0:c0 + 128],
                                               transpose=True))
            emit("sync", tl)

            mm5v = sMM.reserve()
            mvpool.released_at(sMM, mm5v)

            def mm5(te, dc=dc, mslot=mslot, trw=trw):
                pw(te, trw)
                if dc == 0:
                    te.wait_ge(sDVE.h, w_rdy[0])
                    te.wait_ge(sACT.h, w_rdy[1])
                last = None
                for pi, (lt, rt) in enumerate(
                        ((wa_h[dc][:], mv_h[mslot][:]),
                         (wa_h[dc][:], mv_l[mslot][:]),
                         (wa_l[dc][:], mv_h[mslot][:]))):
                    for (o0, on) in ocs:
                        last = te.matmul(ps_ha[:, o0:o0 + on], lt,
                                         rt[:, o0:o0 + on],
                                         start=(dc == 0 and pi == 0),
                                         stop=(dc == NDC - 1 and pi == 2))
                for pi, (lt, rt) in enumerate(
                        ((wr_h[dc][:], mv_h[mslot][:]),
                         (wr_h[dc][:], mv_l[mslot][:]),
                         (wr_l[dc][:], mv_h[mslot][:]))):
                    for (o0, on) in ocs:
                        last = te.matmul(ps_lg[:, o0:o0 + on], lt,
                                         rt[:, o0:o0 + on],
                                         start=(dc == 0 and pi == 0),
                                         stop=(dc == NDC - 1 and pi == 2))
                sMM.attach(last)
            emit("tensor", mm5)
        mm_done = sMM.c

        haf = sbuf("p5haf", [128, T], F32, sc5)
        lgf = sbuf("p5lgf", [32, T], F32, sc5)
        lgT = sbuf("p5lgT", [128, TC * 32], F32, sc5)
        gateb = sbuf("p5gb", [128, TC * 32], BF16, sc5)
        g8f = sbuf("p5g8f", [32, T], BF16, sc5)
        gfm = sbuf("p5gfm", [128, T], BF16, sc5)
        mx_t = sbuf("p5mx", [128, 1], F32, sc5)

        dz = sDVE.reserve(2)

        def v_zero(v):
            vw(v)
            VA(v.memset(lgf[:], 0.0))
            vw(v)
            VA(v.memset(gateb[:], 0.0))
        emit("vector", v_zero)

        aev = sACT.reserve(2)

        def s_ev(sc, w=mm_done, wz=dz):
            sc.wait_ge(sMM.h, w)
            sc.wait_ge(sDVE.h, wz)
            sw(sc)
            SA(sc.activation(out=haf[:], in_=ps_ha[:], func=AF.Copy))
            sw(sc)
            SA(sc.activation(out=lgf[0:8, :], in_=ps_lg[:], func=AF.Copy))
        emit("scalar", s_ev)

        dgate = sDVE.reserve(2 * (T // 32) + 2 * TC)

        def v_gate(v, w=aev):
            v.wait_ge(sACT.h, w)
            for j in range(T // 32):
                tcj, rj = j // 4, (j % 4) * 32
                vw(v)
                VA(v.transpose(lgT[rj:rj + 32, tcj * 32:tcj * 32 + 32],
                               lgf[0:32, j * 32:(j + 1) * 32]))
            for tc in range(TC):
                vw(v)
                VA(v.reduce_max(mx_t[:], lgT[:, tc * 32:tc * 32 + E], axis=AX))
                vw(v)
                VA(v.tensor_scalar(gateb[:, tc * 32:tc * 32 + E],
                                   lgT[:, tc * 32:tc * 32 + E],
                                   mx_t[:], 2.0, ALU.is_ge, ALU.mult))
            for j in range(T // 32):
                tcj, rj = j // 4, (j % 4) * 32
                vw(v)
                VA(v.transpose(g8f[0:32, j * 32:(j + 1) * 32],
                               gateb[rj:rj + 32, tcj * 32:tcj * 32 + 32]))
        emit("vector", v_gate)

        gexp = dst5.reserve(0, R)

        def g_gate(g, w=dgate):
            g.wait_ge(sDVE.h, w)
            for r in range(R):
                dst5.attach(0, g.dma_start(out=gfm[r * E:(r + 1) * E, :],
                                           in_=g8f[0:E, :]))
        emit("gpsimd", g_gate)

        dhag = sDVE.reserve()

        def v_hag(v, w=gexp):
            pw(v, w)
            vw(v)
            VA(v.tensor_tensor(haf[:], haf[:], gfm[:], ALU.mult))
        emit("vector", v_hag)

        if taps:
            dst5.reserve(1, 2)

            def g_tap(g, w=dhag):
                g.wait_ge(sDVE.h, w)
                dst5.attach(1, g.dma_start(out=taph["t_gate"][:, :], in_=gfm[:]))
                dst5.attach(1, g.dma_start(out=taph["t_ha"][:, :], in_=haf[:]))
            emit("gpsimd", g_tap)

        ah = sACT.reserve()

        def s_hi(sc, w=dhag):
            sc.wait_ge(sDVE.h, w)
            sw(sc)
            SA(sc.activation(out=hag_hi[:], in_=haf[:], func=AF.Copy))
        emit("scalar", s_hi)

        hag_rdy = sDVE.reserve()

        def v_lo(v, w=ah):
            v.wait_ge(sACT.h, w)
            vw(v)
            VA(v.tensor_tensor(hag_lo[:], haf[:], hag_hi[:], ALU.subtract))
        emit("vector", v_lo)
        close_scope(sc5)
        return hag_rdy

    phase_barrier()
    sc56 = []
    hag_hi = sbuf("hag_hi", [128, T], BF16, sc56)
    hag_lo = sbuf("hag_lo", [128, T], BF16, sc56)
    wb_whi, wb_wlo, wb_ready, wb_tg = load_w_bf16("wb", s_wbh, s_wbl, sc56)
    bw_h = sbuf("bw_h", [128, D], BF16, sc56)
    bw_l = sbuf("bw_l", [128, D], BF16, sc56)
    b_hi = sbuf("b_hi", [1, D], BF16, sc56)
    b_lo = sbuf("b_lo", [1, D], BF16, sc56)
    tmp56 = []
    bwf = sbuf("p5bwf", [128, D], F32, tmp56)
    b_f32 = sbuf("p5bf32", [1, D], F32, tmp56)

    bww = sLD.reserve(32)

    def ldbw(s, w=wb_tg):
        pw(s, w)
        sLD.attach_dma(s.dma_start(out=bwf[:], in_=bwT[:, :]))
        sLD.attach_dma(s.dma_start(out=b_f32[:], in_=b_base[:]))
    emit("sync", ldbw)

    abw = sACT.reserve(2)

    def s_bw(sc, w=bww):
        sc.wait_ge(sLD.h, w)
        sw(sc)
        SA(sc.activation(out=bw_h[:], in_=bwf[:], func=AF.Copy))
        sw(sc)
        SA(sc.activation(out=b_hi[:], in_=b_f32[:], func=AF.Copy))
    emit("scalar", s_bw)

    bw_rdy = sDVE.reserve(2)

    def v_bw(v, w=abw):
        v.wait_ge(sACT.h, w)
        vw(v)
        VA(v.tensor_tensor(bw_l[:], bwf[:], bw_h[:], ALU.subtract))
        vw(v)
        VA(v.tensor_tensor(b_lo[:], b_f32[:], b_hi[:], ALU.subtract))
    emit("vector", v_bw)
    close_scope(tmp56)
    hag_rdy = p5a()

    def p5b():
        phase_barrier()
        sc6 = []
        dstat = DStream("p5s")
        dx2 = DStream("p5x")
        whi, wlo, wready = wb_whi, wb_wlo, wb_ready

        dsto = DStream("sto", store=True)
        statpool = Pool(2, "p5st", init=(sDVE, bw_rdy))
        sth = [sbuf(f"p5sh{i}", [128, D], BF16, sc6) for i in range(2)]
        stl = [sbuf(f"p5sl{i}", [128, D], BF16, sc6) for i in range(2)]
        pspool = Pool(2, "p5ps")
        ps5 = [psum(f"p5ps{i}", [128, D], sc6) for i in range(2)]
        x2pool = Pool(2, "p5x2", init=(sDVE, bw_rdy))
        x2t = [sbuf(f"p5x2{i}", [128, D], F32, sc6) for i in range(2)]
        ffb = sbuf("p5ffb", [128, D], F32, sc6)
        outb = sbuf("p5out", [128, D], F32, sc6)
        st6 = sbuf("p5st6", [128, 6 * NBN], F32, sc6)
        mv = sbuf("p5mv", [128, 2], F32, sc6)
        r4_t = sbuf("p5r4", [128, 1], F32, sc6)
        b4_t = sbuf("p5b4", [128, 1], F32, sc6)

        for tc in range(TC):
            sslot, swv = statpool.acquire()
            trw = dstat.reserve(sslot, 2)

            def tl(sc, tc=tc, sslot=sslot, swv=swv):
                pw(sc, swv)
                r0 = tc * 128
                src_h = s_ln2h[r0:r0 + 128, :].rearrange("t (dc d) -> t dc d", d=128)
                src_l = s_ln2l[r0:r0 + 128, :].rearrange("t (dc d) -> t dc d", d=128)
                dst_h = sth[sslot][:].rearrange("p (dc t) -> p dc t", t=128)
                dst_l = stl[sslot][:].rearrange("p (dc t) -> p dc t", t=128)
                dstat.attach(sslot, sc.dma_start(out=dst_h, in_=src_h, transpose=True))
                dstat.attach(sslot, sc.dma_start(out=dst_l, in_=src_l, transpose=True))
            emit("sync", tl)

            xslot, xwv = x2pool.acquire()
            xw = dx2.reserve(xslot)

            def ldx(s, tc=tc, xslot=xslot, xwv=xwv):
                pw(s, xwv)
                r0 = tc * 128
                dx2.attach(xslot, s.dma_start(out=x2t[xslot][:],
                                              in_=s_x2[r0:r0 + 128, :]))
            emit("sync", ldx)

            pslot, pwv = pspool.acquire()
            mmv = sMM.reserve()
            statpool.released_at(sMM, mmv)
            dvw = max(bw_rdy, hag_rdy)

            def mms(te, tc=tc, sslot=sslot, pslot=pslot, pwv=pwv, trw=trw,
                    dvw=dvw):
                pw(te, pwv)
                pw(te, trw)
                te.wait_ge(sDVE.h, dvw)
                for w_ in wready:
                    pw(te, w_)
                groups = [(sth[sslot][:, dc * 128:(dc + 1) * 128],
                           stl[sslot][:, dc * 128:(dc + 1) * 128],
                           whi[dc][:], wlo[dc][:]) for dc in range(NDC)]
                groups.append((hag_hi[:, tc * 128:(tc + 1) * 128],
                               hag_lo[:, tc * 128:(tc + 1) * 128],
                               bw_h[:], bw_l[:]))
                groups.append((ones_bf[:], None, b_hi[:], b_lo[:]))
                sMM.attach(emit_mm_tc(te, ps5[pslot][:], D, groups))
            emit("tensor", mms)
            ps = ps5[pslot]

            a0 = sACT.reserve()
            pspool.released_at(sACT, a0)

            def s_c(sc, ps=ps, w=mmv):
                sc.wait_ge(sMM.h, w)
                sw(sc)
                SA(sc.activation(out=ffb[:], in_=ps[:], func=AF.Copy))
            emit("scalar", s_c)

            d0 = sDVE.reserve(NBN + 1)

            def v_st(v, w=a0):
                v.wait_ge(sACT.h, w)
                bn_pair(v, ffb[:], st6, mv)
            emit("vector", v_st)

            a1 = sACT.reserve()

            def s_r4(sc, w=d0):
                sc.wait_ge(sDVE.h, w)
                sw(sc)
                SA(sc.activation(out=r4_t[:], in_=mv[:, 1:2], func=AF.Sqrt,
                                 bias=epsln_t[:]))
            emit("scalar", s_r4)

            dout_ = sDVE.reserve(4)
            x2pool.released_at(sDVE, dout_)

            def v_out(v, xslot=xslot, w=a1, xw=xw):
                v.wait_ge(sACT.h, w)
                pw(v, xw)
                vw(v)
                VA(v.reciprocal(r4_t[:], r4_t[:]))
                vw(v)
                VA(v.tensor_scalar(b4_t[:], mv[:, 0:1], r4_t[:], -1.0,
                                   ALU.mult, ALU.mult))
                vw(v)
                VA(v.tensor_scalar_add(outb[:], x2t[xslot][:], b4_t[:]))
                vw(v)
                VA(v.scalar_tensor_tensor(outb[:], ffb[:], r4_t[:], outb[:],
                                          ALU.mult, ALU.add))
            emit("vector", v_out)

            p = tc % 2
            ow = dsto.reserve(p)

            def g_out(g, tc=tc, p=p, w=dout_):
                g.wait_ge(sDVE.h, w)
                r0 = tc * 128
                dsto.attach(p, g.dma_start(out=y[r0:r0 + 128, :], in_=outb[:]))
            emit("gpsimd", g_out)
            emit("scalar", lambda sc, w=ow: pw(sc, w))
            emit("vector", lambda v, w=ow: pw(v, w))

        close_scope(sc6)

    p5b()
    close_scope(sc56)

    fin_waits = [(c, c.c) for c in STORE_CTRS if c.c > 0]

    def fin(g, ws=fin_waits):
        for c, v in ws:
            g.wait_ge(c.h, v)
    emit("gpsimd", fin)

    with nc.Block() as block:
        @block.sync
        def _(sync):
            for f in ops["sync"]:
                f(sync)

        @block.scalar
        def _(scalar):
            for f in ops["scalar"]:
                f(scalar)

        @block.vector
        def _(vector):
            for f in ops["vector"]:
                f(vector)

        @block.tensor
        def _(tensor):
            for f in ops["tensor"]:
                f(tensor)

        @block.gpsimd
        def _(gpsimd):
            for f in ops["gpsimd"]:
                f(gpsimd)

    for c in all_ctrs:
        assert c.att == c.c, f"sem {c.name}: attached {c.att} != reserved {c.c}"

    close_scope(glob)
    return nc


def stage_core_inputs(cfg, core_id, x_full, w_qkv, q_norm_w, k_norm_w, w_o,
                      W_base, b_base_v, router_W, A, Bw):
    D = cfg["D"]; NH = cfg["NH"]
    T = cfg["T_OWN"]; TH = cfg["T_HALO"]; E = cfg["E"]; R = cfg["R"]
    KC = (T + TH) // 128
    NJ = (D // NH) // 2
    b, half = core_id // 2, core_id % 2

    own0 = half * T
    x_own = np.ascontiguousarray(x_full[b, own0:own0 + T, :], dtype=np.float32)
    if half >= 1:
        x_halo = np.ascontiguousarray(x_full[b, own0 - TH:own0, :],
                                      dtype=np.float32)
        posv = np.arange(own0 - TH, own0 + T, dtype=np.float32)
        mv = np.zeros((128, TH), dtype=np.float32)
    else:
        x_halo = np.zeros((TH, D), dtype=np.float32)
        posv = np.concatenate([np.zeros(TH, dtype=np.float32),
                               np.arange(0, T, dtype=np.float32)])
        mv = np.full((128, TH), MASKVAL, dtype=np.float32)

    xh = np.concatenate([x_halo, x_own], axis=0)
    posr = np.ascontiguousarray(posv.reshape(KC, 128).T)

    i = np.arange(128)[:, None]
    j = np.arange(128)[None, :]
    mW_ = np.where(j >= i + 1, 0.0, MASKVAL).astype(np.float32)
    mC_ = np.where(j <= i, 0.0, MASKVAL).astype(np.float32)
    jr = np.arange(NJ, dtype=np.float32)[None, :]

    return {
        "xh": xh,
        "wqT": np.ascontiguousarray(w_qkv[0:D, :].T, dtype=np.float32),
        "wkT": np.ascontiguousarray(w_qkv[D:2 * D, :].T, dtype=np.float32),
        "wvT": np.ascontiguousarray(w_qkv[2 * D:3 * D, :].T, dtype=np.float32),
        "woT": np.ascontiguousarray(w_o.T, dtype=np.float32),
        "wbT": np.ascontiguousarray(W_base.T, dtype=np.float32),
        "waT": np.ascontiguousarray(A.transpose(2, 1, 0).reshape(D, R * E)),
        "wrT": np.ascontiguousarray(router_W.T, dtype=np.float32),
        "bwT": np.ascontiguousarray(Bw.transpose(2, 0, 1).reshape(R * E, D)),
        "b_base": np.ascontiguousarray(b_base_v.reshape(1, D), dtype=np.float32),
        "qnw": np.ascontiguousarray(q_norm_w.reshape(1, D), dtype=np.float32),
        "knw": np.ascontiguousarray(k_norm_w.reshape(1, D), dtype=np.float32),
        "posr": posr,
        "jrow": np.ascontiguousarray(jr, dtype=np.float32),
        "mW": mW_, "mC": mC_, "mV": mv,
        "ones_row": np.ones((1, 128), dtype=np.float32),
        "ident": np.eye(128, dtype=np.float32),
    }


def kernel(L0_attn_hidden_states, L0_attn_w_qkv, L0_attn_q_norm_w,
           L0_attn_k_norm_w, L0_attn_w_o, L0_ffn_W_base, L0_ffn_b_base,
           L0_ffn_router_W, L0_ffn_A, L0_ffn_Bw):
    cfg = real_cfg()
    x = np.asarray(L0_attn_hidden_states, dtype=np.float32)
    args = (x, np.asarray(L0_attn_w_qkv, np.float32),
            np.asarray(L0_attn_q_norm_w, np.float32),
            np.asarray(L0_attn_k_norm_w, np.float32),
            np.asarray(L0_attn_w_o, np.float32),
            np.asarray(L0_ffn_W_base, np.float32),
            np.asarray(L0_ffn_b_base, np.float32),
            np.asarray(L0_ffn_router_W, np.float32),
            np.asarray(L0_ffn_A, np.float32),
            np.asarray(L0_ffn_Bw, np.float32))
    in_maps = [stage_core_inputs(cfg, c, *args) for c in range(8)]
    nc = build_nc(cfg)
    res = run_bass_kernel_spmd(nc, in_maps, core_ids=list(range(8)))
    B, S, D = x.shape
    out = np.empty((B, S, D), dtype=np.float32)
    T = cfg["T_OWN"]
    for c in range(8):
        b, half = c // 2, c % 2
        out[b, half * T:(half + 1) * T, :] = res.results[c]["y"]
    return out


# version 26
# speedup vs baseline: 1.3120x; 1.0656x over previous
import math
import numpy as np
import concourse.bass as bass
import concourse.mybir as mybir
from concourse.bass_utils import run_bass_kernel_spmd

F32 = mybir.dt.float32
BF16 = mybir.dt.bfloat16
AF = mybir.ActivationFunctionType
ALU = mybir.AluOpType
AX = mybir.AxisListType.X

MASKVAL = -30000.0
EPS_LN = 1e-5
EPS_RMS = 1e-6


def real_cfg():
    return dict(D=2048, NH=16, WINDOW=1024, T_OWN=2048, T_HALO=1024,
                E=8, R=16, debug_taps=False)


def mini_cfg():
    return dict(D=256, NH=2, WINDOW=256, T_OWN=512, T_HALO=256,
                E=8, R=16, debug_taps=True)


class Ctr:
    def __init__(self, handle, name):
        self.h = handle
        self.name = name
        self.c = 0
        self.att = 0

    def reserve(self, n=1):
        self.c += n
        return self.c

    def reserve_dma(self):
        return self.reserve(16)

    def attach(self, instr, n=1):
        self.att += n
        instr.then_inc(self.h, n)
        return instr

    def attach_dma(self, instr):
        return self.attach(instr, 16)


class Pool:

    def __init__(self, nbuf, name, init=None):
        self.nbuf = nbuf
        self.name = name
        self.uses = 0
        self.init = init
        self.rel = []

    def acquire(self):
        u = self.uses
        self.uses += 1
        if u >= self.nbuf:
            assert len(self.rel) > u - self.nbuf, \
                f"pool {self.name}: use {u} before release {u - self.nbuf}"
            w = self.rel[u - self.nbuf]
        else:
            w = self.init
        return u % self.nbuf, w

    def released_at(self, ctr, value):
        self.rel.append((ctr, value))


def build_nc(cfg):
    D = cfg["D"]; NH = cfg["NH"]; W = cfg["WINDOW"]
    T = cfg["T_OWN"]; TH = cfg["T_HALO"]; E = cfg["E"]; R = cfg["R"]
    HA = E * R
    HD = D // NH
    assert HD == 128 and HA == 128 and W == TH
    NDC = D // 128
    TC = T // 128
    HC = TH // 128
    KC = TC + HC
    TKV = T + TH
    WKC = W // 128
    KSPAN = WKC + 1
    KW = KSPAN * 128
    KWP = ((KW + 511) // 512) * 512
    NJ = HD // 2
    taps = cfg["debug_taps"]
    QG = min(512, T)
    NQG = T // QG
    QGC = QG // 128
    CSC = 1.0 / math.sqrt(HD)

    nc = bass.Bass("TRN2", target_bir_lowering=False, debug=False)

    def din(name, shape, dt=F32):
        return nc.dram_tensor(name, list(shape), dt, kind="ExternalInput").ap()

    def dscr(name, shape, dt):
        return nc.dram_tensor(name, list(shape), dt, kind="Internal").ap()

    def dout(name, shape, dt=F32):
        return nc.dram_tensor(name, list(shape), dt, kind="ExternalOutput").ap()

    xh = din("xh", [TKV, D])
    wqT = din("wqT", [D, D]); wkT = din("wkT", [D, D]); wvT = din("wvT", [D, D])
    woT = din("woT", [D, D]); wbT = din("wbT", [D, D])
    waT = din("waT", [D, HA])
    wrT = din("wrT", [D, E])
    bwT = din("bwT", [HA, D])
    b_base = din("b_base", [1, D])
    qnw = din("qnw", [1, D]); knw = din("knw", [1, D])
    posr = din("posr", [128, KC])
    jrow = din("jrow", [1, NJ])
    mW = din("mW", [128, 128]); mC = din("mC", [128, 128])
    mV = din("mV", [128, TH])
    ones_row = din("ones_row", [1, 128])
    ident = din("ident", [128, 128])

    s_ln1h = dscr("s_ln1h", [TKV, D], BF16); s_ln1l = dscr("s_ln1l", [TKV, D], BF16)
    s_qh = dscr("s_qh", [T, D], BF16); s_ql = dscr("s_ql", [T, D], BF16)
    s_kh = dscr("s_kh", [TKV, D], BF16); s_kl = dscr("s_kl", [TKV, D], BF16)
    s_vh = dscr("s_vh", [TKV, D], BF16); s_vl = dscr("s_vl", [TKV, D], BF16)
    s_aoh = dscr("s_aoh", [D, T], BF16); s_aol = dscr("s_aol", [D, T], BF16)
    s_x2 = dscr("s_x2", [T, D], F32)
    s_ln2h = dscr("s_ln2h", [T, D], BF16); s_ln2l = dscr("s_ln2l", [T, D], BF16)

    y = dout("y", [T, D])
    taph = {}
    if taps:
        for nm, shp in (("t_ln1", [TKV, D]), ("t_q", [T, D]), ("t_k", [TKV, D]),
                        ("t_v", [TKV, D]), ("t_ao", [D, T]), ("t_x2", [T, D]),
                        ("t_gate", [128, T]), ("t_ha", [128, T])):
            taph[nm] = dout(nm, shp)

    ops = {"sync": [], "scalar": [], "vector": [], "tensor": [], "gpsimd": []}

    def emit(eng, fn):
        ops[eng].append(fn)

    glob = []

    def sbuf(name, shape, dt, scope):
        g = nc.sbuf_tensor(name, list(shape), dt)
        t = g.__enter__()
        scope.append(g)
        return t

    def psum(name, shape, scope, dt=F32):
        g = nc.psum_tensor(name, list(shape), dt)
        t = g.__enter__()
        scope.append(g)
        return t

    def close_scope(scope):
        for g in reversed(scope):
            g.__exit__(None, None, None)
        scope.clear()

    all_ctrs = []

    def sem(name):
        g = nc.semaphore(name)
        h = g.__enter__()
        glob.append(g)
        c = Ctr(h, name)
        all_ctrs.append(c)
        return c

    sLD = sem("sLD")
    sST = sem("sST")
    sMM = sem("sMM")
    sACT = sem("sACT")
    sDVE = sem("sDVE")
    sOUT = sem("sOUT")

    STORE_CTRS = []

    class DStream:

        def __init__(self, name, n=2, store=False):
            self.s = [sem(f"{name}{i}") for i in range(n)]
            if store:
                STORE_CTRS.extend(self.s)

        def reserve(self, slot, ndma=1):
            return (self.s[slot], self.s[slot].reserve(16 * ndma))

        def attach(self, slot, instr):
            return self.s[slot].attach(instr, 16)

    def phase_barrier():
        waits = [(c, c.c) for c in STORE_CTRS if c.c > 0]

        def b(e, ws=waits):
            for c, v in ws:
                e.wait_ge(c.h, v)
        emit("sync", b)
        emit("scalar", b)

    def SA(instr):
        return sACT.attach(instr)

    def VA(instr):
        return sDVE.attach(instr)

    def sw(sc):
        sc.wait_ge(sACT.h, sACT.att)

    def vw(v):
        v.wait_ge(sDVE.h, sDVE.att)

    def pw(eng, w):
        if w is not None:
            eng.wait_ge(w[0].h, w[1])

    NBN = (D + 511) // 512

    def bn_pair(v, srcbuf, st6, mv):
        for c in range(NBN):
            vw(v)
            VA(v.bn_stats(st6[:, c * 6:(c + 1) * 6],
                          srcbuf[:, c * 512:min((c + 1) * 512, D)]))
        vw(v)
        VA(v.bn_aggr(mv[:], st6[:, 0:6 * NBN]))

    ones_bf = sbuf("ones_bf", [1, 128], BF16, glob)
    epsln_t = sbuf("epsln_t", [128, 1], F32, glob)
    rmssc_t = sbuf("rmssc_t", [128, 1], F32, glob)
    rmseps_t = sbuf("rmseps_t", [128, 1], F32, glob)
    zro_bf = sbuf("zro_bf", [1, 512], BF16, glob)
    qsc_t = sbuf("qsc_t", [128, 1], F32, glob)
    qeps_t = sbuf("qeps_t", [128, 1], F32, glob)
    sc03 = []
    mW_t = sbuf("mW_t", [128, 128], F32, sc03)
    mC_t = sbuf("mC_t", [128, 128], F32, sc03)
    mV_t = sbuf("mV_t", [128, TH], F32, sc03)
    id_bf = sbuf("id_bf", [128, 128], BF16, sc03)
    mW_b = sbuf("mW_b", [128, 128], BF16, sc03)
    mC_b = sbuf("mC_b", [128, 128], BF16, sc03)
    mV_b = sbuf("mV_b", [128, TH], BF16, sc03)
    idf_t = sbuf("idf_t", [128, 128], F32, sc03)
    onesr_b = sbuf("onesr_b", [1, 512], BF16, sc03)
    ones_kv = sbuf("ones_kv", [128, 128], BF16, sc03)
    sc02 = []
    cos_t = sbuf("cos_t", [128, KC * NJ], F32, sc02)
    sin_t = sbuf("sin_t", [128, KC * NJ], F32, sc02)

    sDVE.reserve(8)

    def v_consts(v):
        VA(v.memset(zro_bf[:], 0.0))
        VA(v.memset(epsln_t[:], EPS_LN))
        VA(v.memset(rmssc_t[:], 1.0 / D))
        VA(v.memset(rmseps_t[:], EPS_RMS))
        VA(v.memset(qsc_t[:], float(HD) / D))
        VA(v.memset(qeps_t[:], float(HD) * EPS_RMS))
        VA(v.memset(onesr_b[:], 1.0))
        VA(v.memset(ones_kv[:], 1.0))
    emit("vector", v_consts)

    def p0():
        sc0 = []
        post = sbuf("post", [128, KC], F32, sc0)
        jbc = sbuf("jbc", [128, NJ], F32, sc0)
        ones_f = sbuf("ones_f", [1, 128], F32, sc0)
        r_ang = sbuf("r_ang", [128, KC * NJ], F32, sc0)
        r_s1 = sbuf("r_s1", [128, KC * NJ], F32, sc0)
        r_s2 = sbuf("r_s2", [128, KC * NJ], F32, sc0)
        r_s3 = sbuf("r_s3", [128, KC * NJ], F32, sc0)
        c_a = sbuf("c_a", [128, 1], F32, sc0)

        w0 = [sLD.reserve_dma() for _ in range(7)][-1]

        def ld(s):
            sLD.attach_dma(s.dma_start(out=ones_f[:], in_=ones_row[:]))
            sLD.attach_dma(s.dma_start(out=idf_t[:], in_=ident[:]))
            sLD.attach_dma(s.dma_start(out=post[:], in_=posr[:]))
            sLD.attach_dma(s.dma_start(out=jbc[:], in_=jrow[:].broadcast_to((128, NJ))))
            sLD.attach_dma(s.dma_start(out=mW_t[:], in_=mW[:]))
            sLD.attach_dma(s.dma_start(out=mC_t[:], in_=mC[:]))
            sLD.attach_dma(s.dma_start(out=mV_t[:], in_=mV[:]))
        emit("sync", ld)

        a_if = sACT.reserve(9)

        def sc_c(sc):
            sc.wait_ge(sLD.h, w0)
            SA(sc.activation(out=ones_bf[:], in_=ones_f[:], func=AF.Copy))
            SA(sc.activation(out=id_bf[:], in_=idf_t[:], func=AF.Copy))
            SA(sc.activation(out=mW_b[:], in_=mW_t[:], func=AF.Copy))
            SA(sc.activation(out=mC_b[:], in_=mC_t[:], func=AF.Copy))
            SA(sc.activation(out=mV_b[:], in_=mV_t[:], func=AF.Copy))
            SA(sc.activation(out=c_a[:], in_=jbc[:, 0:1], func=AF.Copy,
                             scale=0.0, bias=10000.0))
            sw(sc)
            SA(sc.activation(out=c_a[:], in_=c_a[:], func=AF.Ln))
            sw(sc)
            SA(sc.mul(c_a[:], c_a[:], -1.0 / NJ))
            sw(sc)
            SA(sc.activation(out=r_s1[:, 0:NJ], in_=jbc[:], func=AF.Exp,
                             scale=c_a[:]))
        emit("scalar", sc_c)

        mmax = max(0, int(math.ceil(math.log2(max(TKV, 7) / (2 * math.pi)))))
        sDVE.reserve(1 + 3 * (mmax + 1))

        def v_ang(v):
            v.wait_ge(sACT.h, a_if)
            ang = r_ang[:].rearrange("p (kc j) -> p kc j", j=NJ)
            vw(v)
            VA(v.tensor_tensor(
                ang,
                post[:].unsqueeze(2).broadcast_to((128, KC, NJ)),
                r_s1[:, 0:NJ].unsqueeze(1).broadcast_to((128, KC, NJ)),
                ALU.mult))
            for m in range(mmax, -1, -1):
                c = 2 * math.pi * (2 ** m)
                vw(v)
                VA(v.tensor_single_scalar(r_s1[:], r_ang[:], c, ALU.is_ge))
                vw(v)
                VA(v.tensor_scalar(r_s1[:], r_s1[:], c, None, ALU.mult))
                vw(v)
                VA(v.tensor_tensor(r_ang[:], r_ang[:], r_s1[:], ALU.subtract))
        emit("vector", v_ang)

        def make_table(shift, out_tab):
            dv = sDVE.reserve(14)

            def vf(v):
                vw(v)
                VA(v.tensor_scalar(r_s1[:], r_ang[:], shift, None, ALU.add))
                vw(v)
                VA(v.tensor_single_scalar(r_s2[:], r_s1[:], 2 * math.pi, ALU.is_ge))
                vw(v)
                VA(v.tensor_scalar(r_s2[:], r_s2[:], 2 * math.pi, None, ALU.mult))
                vw(v)
                VA(v.tensor_tensor(r_s1[:], r_s1[:], r_s2[:], ALU.subtract))
                vw(v)
                VA(v.tensor_scalar(r_s1[:], r_s1[:], math.pi, None, ALU.subtract))
                vw(v)
                VA(v.tensor_tensor(r_s2[:], r_s1[:], r_s1[:], ALU.mult))
                vw(v)
                VA(v.tensor_single_scalar(r_s2[:], r_s2[:],
                                          (math.pi / 2) ** 2, ALU.is_ge))
                vw(v)
                VA(v.tensor_single_scalar(r_s3[:], r_s1[:], 0.0, ALU.is_ge))
                vw(v)
                VA(v.tensor_scalar(r_s3[:], r_s3[:], 2.0, -1.0, ALU.mult, ALU.add))
                vw(v)
                VA(v.tensor_tensor(r_s3[:], r_s3[:], r_s2[:], ALU.mult))
                vw(v)
                VA(v.tensor_scalar(r_s3[:], r_s3[:], math.pi, None, ALU.mult))
                vw(v)
                VA(v.tensor_scalar(r_s2[:], r_s2[:], -2.0, 1.0, ALU.mult, ALU.add))
                vw(v)
                VA(v.tensor_tensor(r_s1[:], r_s1[:], r_s2[:], ALU.mult))
                vw(v)
                VA(v.tensor_tensor(r_s1[:], r_s1[:], r_s3[:], ALU.add))
            emit("vector", vf)
            av = sACT.reserve()

            def sf(sc):
                sc.wait_ge(sDVE.h, dv)
                sw(sc)
                SA(sc.activation(out=out_tab[:], in_=r_s1[:], func=AF.Sin,
                                 scale=-1.0))
            emit("scalar", sf)
            emit("vector", lambda v, w=av: v.wait_ge(sACT.h, w))

        make_table(0.0, sin_t)
        make_table(math.pi / 2, cos_t)
        fin = sACT.c
        emit("sync", lambda s, w=fin: s.wait_ge(sACT.h, w))
        close_scope(sc0)

    p0()

    def p1():
        phase_barrier()
        sc1 = []
        NP1 = 3
        dsx = DStream("p1x", n=NP1)
        dst1 = DStream("st1", n=NP1, store=True)
        xpool = Pool(NP1, "p1x")
        x_in = [sbuf(f"p1x{i}", [128, D], F32, sc1) for i in range(NP1)]
        lnf = [sbuf(f"p1lnf{i}", [128, D], F32, sc1) for i in range(NP1)]
        lnh = [sbuf(f"p1lnh{i}", [128, D], BF16, sc1) for i in range(NP1)]
        lnl = [sbuf(f"p1lnl{i}", [128, D], BF16, sc1) for i in range(NP1)]
        st6 = [sbuf(f"p1st6{i}", [128, 6 * NBN], F32, sc1) for i in range(NP1)]
        mv = [sbuf(f"p1mv{i}", [128, 2], F32, sc1) for i in range(NP1)]
        r_t = [sbuf(f"p1r{i}", [128, 1], F32, sc1) for i in range(NP1)]
        b_t = [sbuf(f"p1b{i}", [128, 1], F32, sc1) for i in range(NP1)]
        stw_par = [None] * NP1

        for t in range(KC):
            p = t % NP1
            slot, wv = xpool.acquire()
            xw = dsx.reserve(slot)

            def ld(s, t=t, slot=slot, wv=wv):
                pw(s, wv)
                dsx.attach(slot, s.dma_start(out=x_in[slot][:],
                                             in_=xh[t * 128:(t + 1) * 128, :]))
            emit("sync", ld)

            spw = stw_par[p]
            d0 = sDVE.reserve(NBN + 1)

            def v_st(v, slot=slot, p=p, w=xw, spw=spw):
                pw(v, w)
                pw(v, spw)
                bn_pair(v, x_in[slot][:], st6[p], mv[p])
            emit("vector", v_st)

            a0 = sACT.reserve()

            def s_r(sc, p=p, w=d0):
                sc.wait_ge(sDVE.h, w)
                sw(sc)
                SA(sc.activation(out=r_t[p][:], in_=mv[p][:, 1:2], func=AF.Sqrt,
                                 bias=epsln_t[:]))
            emit("scalar", s_r)

            d1 = sDVE.reserve(2)

            def v_b(v, p=p, w=a0):
                v.wait_ge(sACT.h, w)
                vw(v)
                VA(v.reciprocal(r_t[p][:], r_t[p][:]))
                vw(v)
                VA(v.tensor_scalar(b_t[p][:], mv[p][:, 0:1], r_t[p][:], -1.0,
                                   ALU.mult, ALU.mult))
            emit("vector", v_b)

            a1 = sACT.reserve(2)
            xpool.released_at(sACT, a1 - 1)

            def s_ln(sc, slot=slot, p=p, w=d1):
                sc.wait_ge(sDVE.h, w)
                sw(sc)
                SA(sc.activation(out=lnf[p][:], in_=x_in[slot][:],
                                 func=AF.Identity,
                                 scale=r_t[p][:], bias=b_t[p][:]))
                sw(sc)
                SA(sc.activation(out=lnh[p][:], in_=lnf[p][:], func=AF.Copy))
            emit("scalar", s_ln)

            dlo = sDVE.reserve()

            def v_fin(v, p=p, w=a1):
                v.wait_ge(sACT.h, w)
                vw(v)
                VA(v.tensor_tensor(lnl[p][:], lnf[p][:], lnh[p][:],
                                   ALU.subtract))
            emit("vector", v_fin)

            nst = 3 if taps else 2
            stw = dst1.reserve(p, nst)
            stw_par[p] = stw

            def g_st(g, t=t, p=p, w=dlo):
                g.wait_ge(sDVE.h, w)
                dst1.attach(p, g.dma_start(out=s_ln1h[t * 128:(t + 1) * 128, :],
                                           in_=lnh[p][:]))
                dst1.attach(p, g.dma_start(out=s_ln1l[t * 128:(t + 1) * 128, :],
                                           in_=lnl[p][:]))
                if taps:
                    dst1.attach(p, g.dma_start(
                        out=taph["t_ln1"][t * 128:(t + 1) * 128, :],
                        in_=lnf[p][:]))
            emit("gpsimd", g_st)

        close_scope(sc1)

    p1()

    def load_w_resident(pref, src, ncols, scope):
        whi = [sbuf(f"{pref}h{i}", [128, ncols], BF16, scope) for i in range(NDC)]
        wlo = [sbuf(f"{pref}l{i}", [128, ncols], BF16, scope) for i in range(NDC)]
        tmp = []
        nws = 6
        wsl = [sbuf(f"{pref}t{i}", [128, ncols], F32, tmp) for i in range(nws)]
        dsw = DStream(pref + "d", n=nws)
        pl = Pool(nws, pref)
        for dc in range(NDC):
            slot, wv = pl.acquire()
            ww = dsw.reserve(slot)

            def ld(s, dc=dc, slot=slot, wv=wv):
                pw(s, wv)
                dsw.attach(slot, s.dma_start(out=wsl[slot][:],
                                             in_=src[dc * 128:(dc + 1) * 128, :]))
            emit("sync", ld)

            aw = sACT.reserve()

            def s_h(sc, dc=dc, slot=slot, w=ww):
                pw(sc, w)
                sw(sc)
                SA(sc.activation(out=whi[dc][:], in_=wsl[slot][:], func=AF.Copy))
            emit("scalar", s_h)

            dv = sDVE.reserve()
            pl.released_at(sDVE, dv)

            def v_l(v, dc=dc, slot=slot, w=aw):
                v.wait_ge(sACT.h, w)
                vw(v)
                VA(v.tensor_tensor(wlo[dc][:], wsl[slot][:], whi[dc][:],
                                   ALU.subtract))
            emit("vector", v_l)
        ready = (sDVE.c, sACT.c)
        close_scope(tmp)
        return whi, wlo, ready

    def emit_mm_tc(te, ps_ap, n_out, groups):
        ocs = [(i * 512, min(512, n_out - i * 512))
               for i in range((n_out + 511) // 512)]
        plan = []
        for (lh, ll, rh, rl) in groups:
            plan.append((lh, rh))
            plan.append((lh, rl))
            if ll is not None:
                plan.append((ll, rh))
        last = None
        first = [True] * len(ocs)
        for pi, (lt, rt) in enumerate(plan):
            for oi, (o0, on) in enumerate(ocs):
                st = first[oi]
                first[oi] = False
                last = te.matmul(ps_ap[:, o0:o0 + on], lt, rt[:, o0:o0 + on],
                                 start=st, stop=(pi == len(plan) - 1))
        return last

    def p2_og(og):
        phase_barrier()
        sc2 = []
        dstat = DStream(f"p2{og}s")
        dst2 = DStream(f"st2{og}", store=True)
        wsrc = {"q": wqT, "k": wkT, "v": wvT}[og]
        whi, wlo, wready = load_w_resident(f"w{og}", wsrc, D, sc2)

        wrep_w = None
        if og in ("q", "k"):
            wrep = sbuf(f"p2{og}rep", [128, D], F32, sc2)
            wrep_w = sLD.reserve_dma()

            def ldrep(s, og=og, w=wready[0]):
                s.wait_ge(sDVE.h, w)
                wsrc2 = qnw if og == "q" else knw
                sLD.attach_dma(s.dma_start(out=wrep[:],
                                           in_=wsrc2[:].broadcast_to((128, D))))
            emit("sync", ldrep)

        statpool = Pool(2, f"p2{og}st", init=(sDVE, wready[0]))
        sth = [sbuf(f"p2{og}sh{i}", [128, D], BF16, sc2) for i in range(2)]
        stl = [sbuf(f"p2{og}sl{i}", [128, D], BF16, sc2) for i in range(2)]
        pspool = Pool(2, f"p2{og}ps")
        ps2 = [psum(f"p2{og}ps{i}", [128, D], sc2) for i in range(2)]
        qn = sbuf(f"p2{og}qn", [128, D], F32, sc2)
        rr = sbuf(f"p2{og}rr", [128, D], F32, sc2)
        ohi = sbuf(f"p2{og}oh", [128, D], BF16, sc2)
        olo = sbuf(f"p2{og}ol", [128, D], BF16, sc2)
        sq_t = sbuf(f"p2{og}sq", [128, 1], F32, sc2)
        rq_t = sbuf(f"p2{og}rq", [128, 1], F32, sc2)

        tcs = list(range(HC, KC)) if og == "q" else list(range(KC))
        dsth, dstl = {"q": (s_qh, s_ql), "k": (s_kh, s_kl),
                      "v": (s_vh, s_vl)}[og]
        tapn = {"q": "t_q", "k": "t_k", "v": "t_v"}[og]

        for tc in tcs:
            sslot, swv = statpool.acquire()
            trw = dstat.reserve(sslot, 2)

            def tl(sc, tc=tc, sslot=sslot, swv=swv):
                pw(sc, swv)
                r0 = tc * 128
                src_h = s_ln1h[r0:r0 + 128, :].rearrange("t (dc d) -> t dc d", d=128)
                src_l = s_ln1l[r0:r0 + 128, :].rearrange("t (dc d) -> t dc d", d=128)
                dst_h = sth[sslot][:].rearrange("p (dc t) -> p dc t", t=128)
                dst_l = stl[sslot][:].rearrange("p (dc t) -> p dc t", t=128)
                dstat.attach(sslot, sc.dma_start(out=dst_h, in_=src_h, transpose=True))
                dstat.attach(sslot, sc.dma_start(out=dst_l, in_=src_l, transpose=True))
            emit("sync", tl)

            pslot, pwv = pspool.acquire()
            mmv = sMM.reserve()
            statpool.released_at(sMM, mmv)

            def mms(te, sslot=sslot, pslot=pslot, pwv=pwv, trw=trw):
                pw(te, pwv)
                pw(te, trw)
                te.wait_ge(sDVE.h, wready[0])
                te.wait_ge(sACT.h, wready[1])
                groups = [(sth[sslot][:, dc * 128:(dc + 1) * 128],
                           stl[sslot][:, dc * 128:(dc + 1) * 128],
                           whi[dc][:], wlo[dc][:]) for dc in range(NDC)]
                sMM.attach(emit_mm_tc(te, ps2[pslot][:], D, groups))
            emit("tensor", mms)
            ps = ps2[pslot]
            row0 = (tc - HC) * 128 if og == "q" else tc * 128

            if og in ("q", "k"):
                a0 = sACT.reserve(2)

                def s_e(sc, og=og, ps=ps, w=mmv):
                    sc.wait_ge(sMM.h, w)
                    sw(sc)
                    SA(sc.activation(out=rr[:], in_=ps[:], func=AF.Square,
                                     accum_out=sq_t[:]))
                    sw(sc)
                    if og == "q":
                        SA(sc.activation(out=rq_t[:], in_=sq_t[:], func=AF.Sqrt,
                                         scale=qsc_t[:], bias=qeps_t[:]))
                    else:
                        SA(sc.activation(out=rq_t[:], in_=sq_t[:], func=AF.Sqrt,
                                         scale=rmssc_t[:], bias=rmseps_t[:]))
                emit("scalar", s_e)

                d0 = sDVE.reserve(8)
                pspool.released_at(sDVE, d0 - 6)

                def v_e(v, tc=tc, ps=ps, w=a0, ww=wrep_w):
                    v.wait_ge(sACT.h, w)
                    v.wait_ge(sLD.h, ww)
                    vw(v)
                    VA(v.reciprocal(rq_t[:], rq_t[:]))
                    vw(v)
                    VA(v.scalar_tensor_tensor(qn[:], ps[:], rq_t[:], wrep[:],
                                              ALU.mult, ALU.mult))

                    def hv(buf, half):
                        vvw = buf[:].rearrange("p (h two j) -> p h two j",
                                               two=2, j=NJ)
                        return vvw[:, :, half, :]
                    cosv = cos_t[:, tc * NJ:(tc + 1) * NJ] \
                        .unsqueeze(1).broadcast_to((128, NH, NJ))
                    sinv = sin_t[:, tc * NJ:(tc + 1) * NJ] \
                        .unsqueeze(1).broadcast_to((128, NH, NJ))
                    qt, qb = hv(qn, 0), hv(qn, 1)
                    rt_, rb = hv(rr, 0), hv(rr, 1)
                    vw(v); VA(v.tensor_tensor(rt_, qb, sinv, ALU.mult))
                    vw(v); VA(v.tensor_tensor(rb, qb, cosv, ALU.mult))
                    vw(v); VA(v.tensor_tensor(qb, qt, sinv, ALU.mult))
                    vw(v); VA(v.tensor_tensor(rb, rb, qb, ALU.add))
                    vw(v); VA(v.tensor_tensor(qb, qt, cosv, ALU.mult))
                    vw(v); VA(v.tensor_tensor(rt_, qb, rt_, ALU.subtract))
                emit("vector", v_e)

                a1 = sACT.reserve()

                def s_hi(sc, w=d0):
                    sc.wait_ge(sDVE.h, w)
                    sw(sc)
                    SA(sc.activation(out=ohi[:], in_=rr[:], func=AF.Copy))
                emit("scalar", s_hi)

                dlo = sDVE.reserve()

                def v_lo(v, w=a1):
                    v.wait_ge(sACT.h, w)
                    vw(v)
                    VA(v.tensor_tensor(olo[:], rr[:], ohi[:], ALU.subtract))
                emit("vector", v_lo)
            else:
                a0 = sACT.reserve(2 if taps else 1)

                def s_e(sc, ps=ps, w=mmv):
                    sc.wait_ge(sMM.h, w)
                    sw(sc)
                    SA(sc.activation(out=ohi[:], in_=ps[:], func=AF.Copy))
                    if taps:
                        sw(sc)
                        SA(sc.activation(out=rr[:], in_=ps[:], func=AF.Copy))
                emit("scalar", s_e)

                dlo = sDVE.reserve()
                pspool.released_at(sDVE, dlo)

                def v_lo(v, ps=ps, w=a0):
                    v.wait_ge(sACT.h, w)
                    vw(v)
                    VA(v.tensor_tensor(olo[:], ps[:], ohi[:], ALU.subtract))
                emit("vector", v_lo)

            p = tc % 2
            nst = 3 if taps else 2
            stw = dst2.reserve(p, nst)

            def g_st(g, p=p, row0=row0, w=dlo):
                g.wait_ge(sDVE.h, w)
                dst2.attach(p, g.dma_start(out=dsth[row0:row0 + 128, :],
                                           in_=ohi[:]))
                dst2.attach(p, g.dma_start(out=dstl[row0:row0 + 128, :],
                                           in_=olo[:]))
                if taps:
                    dst2.attach(p, g.dma_start(
                        out=taph[tapn][row0:row0 + 128, :], in_=rr[:]))
            emit("gpsimd", g_st)
            emit("scalar", lambda sc, w=stw: pw(sc, w))

        close_scope(sc2)

    p2_og("q")
    p2_og("k")
    p2_og("v")
    close_scope(sc02)

    def p3():
        phase_barrier()
        sc3 = []
        dqk = DStream("p3qk")
        dst3 = DStream("st3", store=True)
        dv_ = DStream("p3v")
        headpool = Pool(2, "p3head")
        qTh = [sbuf(f"p3qh{i}", [128, T], BF16, sc3) for i in range(2)]
        qTl = [sbuf(f"p3ql{i}", [128, T], BF16, sc3) for i in range(2)]
        kTh = [sbuf(f"p3kh{i}", [128, TKV], BF16, sc3) for i in range(2)]
        kTl = [sbuf(f"p3kl{i}", [128, TKV], BF16, sc3) for i in range(2)]
        vh_ = [sbuf(f"p3vh{i}", [128, TKV], BF16, sc3) for i in range(2)]
        vl_ = [sbuf(f"p3vl{i}", [128, TKV], BF16, sc3) for i in range(2)]
        e32 = [sbuf(f"p3e{i}", [128, 512], F32, sc3) for i in range(2)]
        ehi = [sbuf(f"p3eh{i}", [128, 512], BF16, sc3) for i in range(3)]
        elo = [sbuf(f"p3el{i}", [128, 512], BF16, sc3) for i in range(3)]
        den_sb = [sbuf(f"p3dsb{i}", [128, QG], F32, sc3) for i in range(2)]
        lnt = sbuf("p3lnt", [128, QG], F32, sc3)
        rcpb = sbuf("p3rcpb", [128, QG], F32, sc3)
        aof_ev = sbuf("p3aof", [128, QG], F32, sc3)
        aoh_ev = sbuf("p3aoh", [128, QG], BF16, sc3)
        aol_ev = sbuf("p3aol", [128, QG], BF16, sc3)
        scpool = Pool(3, "p3sc")
        ps_s = [psum(f"p3ps{i}", [128, 512], sc3) for i in range(3)]
        epool = Pool(3, "p3ep")
        aopool = Pool(2, "p3ao")
        ps_ao = [psum(f"p3pao{i}", [128, QG], sc3) for i in range(2)]
        ps_dn = [psum(f"p3pdn{i}", [128, 512], sc3) for i in range(2)]

        e32_free = {0: None, 1: None}
        dn_free = {}
        gci = 0
        prev_stw = None
        prev_dlo2 = None
        pending_evac = None
        for h in range(NH):
            hslot, hwv = headpool.acquire()
            trw = dqk.reserve(hslot, 4)

            def hload_t(sc, h=h, hslot=hslot, hwv=hwv):
                pw(sc, hwv)
                c0 = h * 128
                dqk.attach(hslot, sc.dma_start(out=qTh[hslot][:],
                                               in_=s_qh[:, c0:c0 + 128],
                                               transpose=True))
                dqk.attach(hslot, sc.dma_start(out=qTl[hslot][:],
                                               in_=s_ql[:, c0:c0 + 128],
                                               transpose=True))
                dqk.attach(hslot, sc.dma_start(out=kTh[hslot][:],
                                               in_=s_kh[:, c0:c0 + 128],
                                               transpose=True))
                dqk.attach(hslot, sc.dma_start(out=kTl[hslot][:],
                                               in_=s_kl[:, c0:c0 + 128],
                                               transpose=True))
            emit("sync", hload_t)

            vldw = dv_.reserve(hslot, 2)

            def hload_v(s, h=h, hslot=hslot, hwv=hwv):
                pw(s, hwv)
                c0 = h * 128
                src_h = s_vh[:, c0:c0 + 128].rearrange("(kc t) d -> t kc d", t=128)
                src_l = s_vl[:, c0:c0 + 128].rearrange("(kc t) d -> t kc d", t=128)
                dst_h = vh_[hslot][:].rearrange("p (kc d) -> p kc d", d=128)
                dst_l = vl_[hslot][:].rearrange("p (kc d) -> p kc d", d=128)
                dv_.attach(hslot, s.dma_start(out=dst_h, in_=src_h))
                dv_.attach(hslot, s.dma_start(out=dst_l, in_=src_l))
            emit("sync", hload_v)

            pv_done_prev = None
            for qg in range(NQG):
                qc0, qc1 = qg * QGC, (qg + 1) * QGC - 1
                kvcs = list(range(qc0, qc1 + WKC + 1))
                aoslot, aowv = aopool.acquire()

                def zf(te, aowv=aowv, aoslot=aoslot):
                    pw(te, aowv)
                    te.matmul(ps_ao[aoslot][:, 0:QG], ones_bf[:],
                              zro_bf[:, 0:QG], start=True, stop=False)
                emit("tensor", zf)

                def emit_pv(info, is_last, aowv=aowv, hslot=hslot,
                            vldw=vldw, qc0=qc0, aoslot=aoslot):
                    ci, kvc, eslot, ew, qa, qb, dnf = info
                    nc_ = (qb - qa + 1) * 128
                    pvv = sMM.reserve()
                    epool.released_at(sMM, pvv)

                    def t_pv(te, ci=ci, kvc=kvc, eslot=eslot, ew=ew, qa=qa,
                             qb=qb, nc_=nc_, is_last=is_last, qc0=qc0,
                             dnf=dnf, aoslot=aoslot):
                        te.wait_ge(sDVE.h, ew)
                        pw(te, vldw)
                        if dnf is not None:
                            te.wait_ge(sDVE.h, dnf)
                        off = (qa - qc0) * 128
                        vs_h = vh_[hslot][:, kvc * 128:(kvc + 1) * 128]
                        vs_l = vl_[hslot][:, kvc * 128:(kvc + 1) * 128]
                        eh = ehi[eslot][:, 0:nc_]
                        el = elo[eslot][:, 0:nc_]
                        te.matmul(ps_ao[aoslot][:, off:off + nc_], vs_h, eh,
                                  start=False, stop=False)
                        te.matmul(ps_ao[aoslot][:, off:off + nc_], vs_h, el,
                                  start=False, stop=False)
                        te.matmul(ps_ao[aoslot][:, off:off + nc_], vs_l, eh,
                                  start=False, stop=is_last)
                        te.matmul(ps_dn[ci % 2][:, 0:nc_], ones_kv[:], eh,
                                  start=True, stop=False)
                        lastm = te.matmul(ps_dn[ci % 2][:, 0:nc_], ones_kv[:],
                                          el, start=False, stop=True)
                        sMM.attach(lastm)
                    emit("tensor", t_pv)

                    dnv = sDVE.reserve(2 if ci == 0 else 1)
                    dpar = qg % 2

                    def v_dn(v, ci=ci, qa=qa, nc_=nc_, w=pvv, qc0=qc0,
                             dpar=dpar):
                        v.wait_ge(sMM.h, w)
                        off = (qa - qc0) * 128
                        if ci == 0:
                            vw(v)
                            VA(v.memset(den_sb[dpar][:], 0.0))
                        vw(v)
                        VA(v.tensor_tensor(den_sb[dpar][:, off:off + nc_],
                                           ps_dn[ci % 2][:, 0:nc_],
                                           den_sb[dpar][:, off:off + nc_],
                                           ALU.add))
                    emit("vector", v_dn)
                    return pvv, dnv

                pend = None
                pv_done = None
                dn_last = None
                for ci, kvc in enumerate(kvcs):
                    if ci == 1 and pending_evac is not None:
                        prev_stw, prev_dlo2 = pending_evac()
                        pending_evac = None
                    qa = max(qc0, kvc - WKC)
                    qb = min(qc1, kvc)
                    nc_ = (qb - qa + 1) * 128
                    cslot, cwv = scpool.acquire()
                    mmv = sMM.reserve()

                    def t_sc(te, kvc=kvc, qa=qa, qb=qb, nc_=nc_, hslot=hslot,
                             cslot=cslot, cwv=cwv, trw=trw):
                        pw(te, cwv)
                        pw(te, trw)
                        ps = ps_s[cslot]
                        kh = kTh[hslot][:, kvc * 128:(kvc + 1) * 128]
                        kl = kTl[hslot][:, kvc * 128:(kvc + 1) * 128]
                        qh = qTh[hslot][:, qa * 128:qa * 128 + nc_]
                        ql = qTl[hslot][:, qa * 128:qa * 128 + nc_]
                        te.matmul(ps[:, 0:nc_], kh, qh, start=True, stop=False)
                        te.matmul(ps[:, 0:nc_], kh, ql, start=False, stop=False)
                        if kvc <= qb:
                            o = (kvc - qa) * 128
                            te.matmul(ps[:, o:o + 128], mW_b[:], id_bf[:],
                                      start=False, stop=False)
                        if kvc - WKC >= qa:
                            o = (kvc - WKC - qa) * 128
                            te.matmul(ps[:, o:o + 128], mC_b[:], id_bf[:],
                                      start=False, stop=False)
                        if kvc < HC:
                            te.matmul(ps[:, 0:nc_],
                                      mV_b[0:1, kvc * 128:(kvc + 1) * 128],
                                      onesr_b[:, 0:nc_],
                                      start=False, stop=False)
                        last = te.matmul(ps[:, 0:nc_], kl, qh,
                                         start=False, stop=True)
                        sMM.attach(last)
                    emit("tensor", t_sc)

                    eslot, ewv = epool.acquire()
                    asp = sACT.reserve(2)
                    scpool.released_at(sACT, asp)
                    par = gci % 2
                    e2f = e32_free[par]

                    def s_split(sc, cslot=cslot, eslot=eslot, ewv=ewv,
                                nc_=nc_, w=mmv, e2f=e2f, par=par):
                        sc.wait_ge(sMM.h, w)
                        pw(sc, ewv)
                        if e2f is not None:
                            sc.wait_ge(sDVE.h, e2f)
                        sw(sc)
                        SA(sc.activation(out=e32[par][:, 0:nc_],
                                         in_=ps_s[cslot][:, 0:nc_],
                                         func=AF.Exp))
                        sw(sc)
                        SA(sc.activation(out=ehi[eslot][:, 0:nc_],
                                         in_=ps_s[cslot][:, 0:nc_],
                                         func=AF.Exp))
                    emit("scalar", s_split)

                    dlo = sDVE.reserve(1)

                    def v_lo(v, eslot=eslot, nc_=nc_, w=asp, par=par):
                        v.wait_ge(sACT.h, w)
                        vw(v)
                        VA(v.tensor_tensor(elo[eslot][:, 0:nc_],
                                           e32[par][:, 0:nc_],
                                           ehi[eslot][:, 0:nc_], ALU.subtract))
                    emit("vector", v_lo)
                    e32_free[par] = dlo
                    gci += 1

                    dnf = dn_free.get(ci % 2)
                    cur = (ci, kvc, eslot, dlo, qa, qb, dnf)
                    if pend is not None:
                        _, dnv = emit_pv(pend, False)
                        dn_free[pend[0] % 2] = dnv
                        dn_last = dnv
                    pend = cur
                pv_done, dn_last = emit_pv(pend, True)
                dn_free[pend[0] % 2] = dn_last

                def make_evac(h=h, qg=qg, pv_done=pv_done, aoslot=aoslot,
                              pstw=prev_stw, pdlo2=prev_dlo2, dn_last=dn_last):
                    def do_emit():
                        dpar = qg % 2
                        arc = sACT.reserve(2)

                        def s_rc(sc, w=dn_last, pdlo2=pdlo2):
                            sc.wait_ge(sDVE.h, w)
                            if pdlo2 is not None:
                                sc.wait_ge(sDVE.h, pdlo2)
                            sw(sc)
                            SA(sc.activation(out=lnt[:], in_=den_sb[dpar][:],
                                             func=AF.Ln))
                            sw(sc)
                            SA(sc.activation(out=rcpb[:], in_=lnt[:],
                                             func=AF.Exp, scale=-1.0))
                        emit("scalar", s_rc)

                        dlo2 = sDVE.reserve(3)
                        aopool.released_at(sDVE, dlo2)

                        def v_ev(v, w=pv_done, pstw=pstw, aoslot=aoslot,
                                 arc=arc):
                            v.wait_ge(sMM.h, w)
                            v.wait_ge(sACT.h, arc)
                            if pstw is not None:
                                pw(v, pstw)
                            vw(v)
                            VA(v.tensor_tensor(aof_ev[:], ps_ao[aoslot][:],
                                               rcpb[:], ALU.mult))
                            vw(v)
                            VA(v.tensor_copy(aoh_ev[:], aof_ev[:]))
                            vw(v)
                            VA(v.tensor_tensor(aol_ev[:], aof_ev[:], aoh_ev[:],
                                               ALU.subtract))
                        emit("vector", v_ev)

                        p = (h * NQG + qg) % 2
                        nst = 3 if taps else 2
                        stw = dst3.reserve(p, nst)

                        def g_st(g, h=h, qg=qg, p=p, w=dlo2):
                            g.wait_ge(sDVE.h, w)
                            r0 = h * 128
                            c0 = qg * QG
                            dst3.attach(p, g.dma_start(
                                out=s_aoh[r0:r0 + 128, c0:c0 + QG],
                                in_=aoh_ev[:]))
                            dst3.attach(p, g.dma_start(
                                out=s_aol[r0:r0 + 128, c0:c0 + QG],
                                in_=aol_ev[:]))
                            if taps:
                                dst3.attach(p, g.dma_start(
                                    out=taph["t_ao"][r0:r0 + 128, c0:c0 + QG],
                                    in_=aof_ev[:]))
                        emit("gpsimd", g_st)
                        return stw, dlo2
                    return do_emit

                pending_evac = make_evac()
                pv_done_prev = pv_done

            headpool.released_at(sMM, pv_done_prev)

        if pending_evac is not None:
            prev_stw, prev_dlo2 = pending_evac()
            pending_evac = None

        close_scope(sc3)

    p3()
    close_scope(sc03)

    def p4():
        phase_barrier()
        sc4 = []
        dstat = DStream("p4s")
        dx = DStream("p4x")
        dst4 = DStream("st4", store=True)
        whi, wlo, wready = load_w_resident("wo", woT, D, sc4)
        statpool = Pool(2, "p4st", init=(sDVE, wready[0]))
        sth = [sbuf(f"p4sh{i}", [128, D], BF16, sc4) for i in range(2)]
        stl = [sbuf(f"p4sl{i}", [128, D], BF16, sc4) for i in range(2)]
        pspool = Pool(2, "p4ps")
        ps4 = [psum(f"p4ps{i}", [128, D], sc4) for i in range(2)]
        xpool = Pool(2, "p4x", init=(sDVE, wready[0]))
        xt = [sbuf(f"p4x{i}", [128, D], F32, sc4) for i in range(2)]
        attnb = sbuf("p4attn", [128, D], F32, sc4)
        x2b = sbuf("p4x2", [128, D], F32, sc4)
        lnh = sbuf("p4lnh", [128, D], BF16, sc4)
        lnl = sbuf("p4lnl", [128, D], BF16, sc4)
        st6 = sbuf("p4st6", [128, 6 * NBN], F32, sc4)
        mv = sbuf("p4mv", [128, 2], F32, sc4)
        r3_t = sbuf("p4r3", [128, 1], F32, sc4)
        b3_t = sbuf("p4b3", [128, 1], F32, sc4)
        r2_t = sbuf("p4r2", [128, 1], F32, sc4)
        b2_t = sbuf("p4b2", [128, 1], F32, sc4)

        for tc in range(TC):
            sslot, swv = statpool.acquire()
            ldw = dstat.reserve(sslot, 2)

            def ld(s, tc=tc, sslot=sslot, swv=swv):
                pw(s, swv)
                src_h = s_aoh[:, tc * 128:(tc + 1) * 128] \
                    .rearrange("(dc d) t -> d dc t", d=128)
                src_l = s_aol[:, tc * 128:(tc + 1) * 128] \
                    .rearrange("(dc d) t -> d dc t", d=128)
                dst_h = sth[sslot][:].rearrange("p (dc t) -> p dc t", t=128)
                dst_l = stl[sslot][:].rearrange("p (dc t) -> p dc t", t=128)
                dstat.attach(sslot, s.dma_start(out=dst_h, in_=src_h))
                dstat.attach(sslot, s.dma_start(out=dst_l, in_=src_l))
            emit("sync", ld)

            xslot, xwv = xpool.acquire()
            xw = dx.reserve(xslot)

            def ldx(s, tc=tc, xslot=xslot, xwv=xwv):
                pw(s, xwv)
                r0 = (HC + tc) * 128
                dx.attach(xslot, s.dma_start(out=xt[xslot][:],
                                             in_=xh[r0:r0 + 128, :]))
            emit("sync", ldx)

            pslot, pwv = pspool.acquire()
            mmv = sMM.reserve()
            statpool.released_at(sMM, mmv)

            def mms(te, sslot=sslot, pslot=pslot, pwv=pwv, ldw=ldw):
                pw(te, pwv)
                pw(te, ldw)
                te.wait_ge(sDVE.h, wready[0])
                te.wait_ge(sACT.h, wready[1])
                groups = [(sth[sslot][:, dc * 128:(dc + 1) * 128],
                           stl[sslot][:, dc * 128:(dc + 1) * 128],
                           whi[dc][:], wlo[dc][:]) for dc in range(NDC)]
                sMM.attach(emit_mm_tc(te, ps4[pslot][:], D, groups))
            emit("tensor", mms)
            ps = ps4[pslot]

            a0 = sACT.reserve()
            pspool.released_at(sACT, a0)

            def s_c(sc, ps=ps, w=mmv):
                sc.wait_ge(sMM.h, w)
                sw(sc)
                SA(sc.activation(out=attnb[:], in_=ps[:], func=AF.Copy))
            emit("scalar", s_c)

            d0 = sDVE.reserve(NBN + 1)

            def v_st3(v, w=a0):
                v.wait_ge(sACT.h, w)
                bn_pair(v, attnb[:], st6, mv)
            emit("vector", v_st3)

            a1 = sACT.reserve()

            def s_r3(sc, w=d0):
                sc.wait_ge(sDVE.h, w)
                sw(sc)
                SA(sc.activation(out=r3_t[:], in_=mv[:, 1:2], func=AF.Sqrt,
                                 bias=epsln_t[:]))
            emit("scalar", s_r3)

            d1 = sDVE.reserve(4 + NBN + 1)
            xpool.released_at(sDVE, d1 - (NBN + 1))

            def v_x2(v, xslot=xslot, w=a1, xw=xw):
                v.wait_ge(sACT.h, w)
                pw(v, xw)
                vw(v)
                VA(v.reciprocal(r3_t[:], r3_t[:]))
                vw(v)
                VA(v.tensor_scalar(b3_t[:], mv[:, 0:1], r3_t[:], -1.0,
                                   ALU.mult, ALU.mult))
                vw(v)
                VA(v.tensor_scalar_add(x2b[:], xt[xslot][:], b3_t[:]))
                vw(v)
                VA(v.scalar_tensor_tensor(x2b[:], attnb[:], r3_t[:], x2b[:],
                                          ALU.mult, ALU.add))
                bn_pair(v, x2b[:], st6, mv)
            emit("vector", v_x2)

            a2 = sACT.reserve()

            def s_r2(sc, w=d1):
                sc.wait_ge(sDVE.h, w)
                sw(sc)
                SA(sc.activation(out=r2_t[:], in_=mv[:, 1:2], func=AF.Sqrt,
                                 bias=epsln_t[:]))
            emit("scalar", s_r2)

            d2 = sDVE.reserve(2)

            def v_b2(v, w=a2):
                v.wait_ge(sACT.h, w)
                vw(v)
                VA(v.reciprocal(r2_t[:], r2_t[:]))
                vw(v)
                VA(v.tensor_scalar(b2_t[:], mv[:, 0:1], r2_t[:], -1.0,
                                   ALU.mult, ALU.mult))
            emit("vector", v_b2)

            a3 = sACT.reserve(2)

            def s_ln2(sc, w=d2):
                sc.wait_ge(sDVE.h, w)
                sw(sc)
                SA(sc.activation(out=attnb[:], in_=x2b[:], func=AF.Identity,
                                 scale=r2_t[:], bias=b2_t[:]))
                sw(sc)
                SA(sc.activation(out=lnh[:], in_=attnb[:], func=AF.Copy))
            emit("scalar", s_ln2)

            dlo = sDVE.reserve()

            def v_fin(v, w=a3):
                v.wait_ge(sACT.h, w)
                vw(v)
                VA(v.tensor_tensor(lnl[:], attnb[:], lnh[:], ALU.subtract))
            emit("vector", v_fin)

            p = tc % 2
            nst = 4 if taps else 3
            stw = dst4.reserve(p, nst)

            def g_st(g, tc=tc, p=p, w=dlo):
                g.wait_ge(sDVE.h, w)
                r0 = tc * 128
                dst4.attach(p, g.dma_start(out=s_x2[r0:r0 + 128, :], in_=x2b[:]))
                dst4.attach(p, g.dma_start(out=s_ln2h[r0:r0 + 128, :], in_=lnh[:]))
                dst4.attach(p, g.dma_start(out=s_ln2l[r0:r0 + 128, :], in_=lnl[:]))
                if taps:
                    dst4.attach(p, g.dma_start(out=taph["t_x2"][r0:r0 + 128, :],
                                               in_=x2b[:]))
            emit("gpsimd", g_st)
            emit("scalar", lambda sc, w=stw: pw(sc, w))

        close_scope(sc4)

    p4()

    def p5a():
        phase_barrier()
        sc5 = []
        dmv = DStream("p5mv")
        wa_h = [sbuf(f"wa_h{i}", [128, HA], BF16, sc5) for i in range(NDC)]
        wa_l = [sbuf(f"wa_l{i}", [128, HA], BF16, sc5) for i in range(NDC)]
        wr_h = [sbuf(f"wr_h{i}", [128, E], BF16, sc5) for i in range(NDC)]
        wr_l = [sbuf(f"wr_l{i}", [128, E], BF16, sc5) for i in range(NDC)]
        tmp = []
        wfa = [sbuf(f"p5wfa{i}", [128, HA], F32, tmp) for i in range(2)]
        wfr = [sbuf(f"p5wfr{i}", [128, E], F32, tmp) for i in range(2)]
        dwa = DStream("p5wa")
        wapool = Pool(2, "p5wa", init=(sDVE, sDVE.c))
        for dc in range(NDC):
            slot, wv = wapool.acquire()
            ww = dwa.reserve(slot, 2)

            def ldw(s, dc=dc, slot=slot, wv=wv):
                pw(s, wv)
                dwa.attach(slot, s.dma_start(out=wfa[slot][:],
                                             in_=waT[dc * 128:(dc + 1) * 128, :]))
                dwa.attach(slot, s.dma_start(out=wfr[slot][:],
                                             in_=wrT[dc * 128:(dc + 1) * 128, :]))
            emit("sync", ldw)

            aw = sACT.reserve(2)

            def s_h(sc, dc=dc, slot=slot, w=ww):
                pw(sc, w)
                sw(sc)
                SA(sc.activation(out=wa_h[dc][:], in_=wfa[slot][:], func=AF.Copy))
                sw(sc)
                SA(sc.activation(out=wr_h[dc][:], in_=wfr[slot][:], func=AF.Copy))
            emit("scalar", s_h)

            dvv = sDVE.reserve(2)
            wapool.released_at(sDVE, dvv)

            def v_l(v, dc=dc, slot=slot, w=aw):
                v.wait_ge(sACT.h, w)
                vw(v)
                VA(v.tensor_tensor(wa_l[dc][:], wfa[slot][:], wa_h[dc][:],
                                   ALU.subtract))
                vw(v)
                VA(v.tensor_tensor(wr_l[dc][:], wfr[slot][:], wr_h[dc][:],
                                   ALU.subtract))
            emit("vector", v_l)
        w_rdy = (sDVE.c, sACT.c)
        close_scope(tmp)

        dst5 = DStream("st5", store=True)
        mvpool = Pool(2, "p5mv", init=(sDVE, w_rdy[0]))
        mv_h = [sbuf(f"p5mh{i}", [128, T], BF16, sc5) for i in range(2)]
        mv_l = [sbuf(f"p5ml{i}", [128, T], BF16, sc5) for i in range(2)]
        ps_ha = psum("p5pha", [128, T], sc5)
        ps_lg = psum("p5plg", [8, T], sc5)

        ocs = [(i * 512, min(512, T - i * 512)) for i in range((T + 511) // 512)]
        for dc in range(NDC):
            mslot, mwv = mvpool.acquire()
            trw = dmv.reserve(mslot, 2)

            def tl(sc, dc=dc, mslot=mslot, mwv=mwv):
                pw(sc, mwv)
                c0 = dc * 128
                dmv.attach(mslot, sc.dma_start(out=mv_h[mslot][:],
                                               in_=s_ln2h[:, c0:c0 + 128],
                                               transpose=True))
                dmv.attach(mslot, sc.dma_start(out=mv_l[mslot][:],
                                               in_=s_ln2l[:, c0:c0 + 128],
                                               transpose=True))
            emit("sync", tl)

            mm5v = sMM.reserve()
            mvpool.released_at(sMM, mm5v)

            def mm5(te, dc=dc, mslot=mslot, trw=trw):
                pw(te, trw)
                if dc == 0:
                    te.wait_ge(sDVE.h, w_rdy[0])
                    te.wait_ge(sACT.h, w_rdy[1])
                last = None
                for pi, (lt, rt) in enumerate(
                        ((wa_h[dc][:], mv_h[mslot][:]),
                         (wa_h[dc][:], mv_l[mslot][:]),
                         (wa_l[dc][:], mv_h[mslot][:]))):
                    for (o0, on) in ocs:
                        last = te.matmul(ps_ha[:, o0:o0 + on], lt,
                                         rt[:, o0:o0 + on],
                                         start=(dc == 0 and pi == 0),
                                         stop=(dc == NDC - 1 and pi == 2))
                for pi, (lt, rt) in enumerate(
                        ((wr_h[dc][:], mv_h[mslot][:]),
                         (wr_h[dc][:], mv_l[mslot][:]),
                         (wr_l[dc][:], mv_h[mslot][:]))):
                    for (o0, on) in ocs:
                        last = te.matmul(ps_lg[:, o0:o0 + on], lt,
                                         rt[:, o0:o0 + on],
                                         start=(dc == 0 and pi == 0),
                                         stop=(dc == NDC - 1 and pi == 2))
                sMM.attach(last)
            emit("tensor", mm5)
        mm_done = sMM.c

        haf = sbuf("p5haf", [128, T], F32, sc5)
        lgf = sbuf("p5lgf", [32, T], F32, sc5)
        lgT = sbuf("p5lgT", [128, TC * 32], F32, sc5)
        gateb = sbuf("p5gb", [128, TC * 32], BF16, sc5)
        g8f = sbuf("p5g8f", [32, T], BF16, sc5)
        gfm = sbuf("p5gfm", [128, T], BF16, sc5)
        mx_t = sbuf("p5mx", [128, 1], F32, sc5)

        dz = sDVE.reserve(2)

        def v_zero(v):
            vw(v)
            VA(v.memset(lgf[:], 0.0))
            vw(v)
            VA(v.memset(gateb[:], 0.0))
        emit("vector", v_zero)

        aev = sACT.reserve(2)

        def s_ev(sc, w=mm_done, wz=dz):
            sc.wait_ge(sMM.h, w)
            sc.wait_ge(sDVE.h, wz)
            sw(sc)
            SA(sc.activation(out=haf[:], in_=ps_ha[:], func=AF.Copy))
            sw(sc)
            SA(sc.activation(out=lgf[0:8, :], in_=ps_lg[:], func=AF.Copy))
        emit("scalar", s_ev)

        dgate = sDVE.reserve(2 * (T // 32) + 2 * TC)

        def v_gate(v, w=aev):
            v.wait_ge(sACT.h, w)
            for j in range(T // 32):
                tcj, rj = j // 4, (j % 4) * 32
                vw(v)
                VA(v.transpose(lgT[rj:rj + 32, tcj * 32:tcj * 32 + 32],
                               lgf[0:32, j * 32:(j + 1) * 32]))
            for tc in range(TC):
                vw(v)
                VA(v.reduce_max(mx_t[:], lgT[:, tc * 32:tc * 32 + E], axis=AX))
                vw(v)
                VA(v.tensor_scalar(gateb[:, tc * 32:tc * 32 + E],
                                   lgT[:, tc * 32:tc * 32 + E],
                                   mx_t[:], 2.0, ALU.is_ge, ALU.mult))
            for j in range(T // 32):
                tcj, rj = j // 4, (j % 4) * 32
                vw(v)
                VA(v.transpose(g8f[0:32, j * 32:(j + 1) * 32],
                               gateb[rj:rj + 32, tcj * 32:tcj * 32 + 32]))
        emit("vector", v_gate)

        gexp = dst5.reserve(0, R)

        def g_gate(g, w=dgate):
            g.wait_ge(sDVE.h, w)
            for r in range(R):
                dst5.attach(0, g.dma_start(out=gfm[r * E:(r + 1) * E, :],
                                           in_=g8f[0:E, :]))
        emit("gpsimd", g_gate)

        dhag = sDVE.reserve()

        def v_hag(v, w=gexp):
            pw(v, w)
            vw(v)
            VA(v.tensor_tensor(haf[:], haf[:], gfm[:], ALU.mult))
        emit("vector", v_hag)

        if taps:
            dst5.reserve(1, 2)

            def g_tap(g, w=dhag):
                g.wait_ge(sDVE.h, w)
                dst5.attach(1, g.dma_start(out=taph["t_gate"][:, :], in_=gfm[:]))
                dst5.attach(1, g.dma_start(out=taph["t_ha"][:, :], in_=haf[:]))
            emit("gpsimd", g_tap)

        ah = sACT.reserve()

        def s_hi(sc, w=dhag):
            sc.wait_ge(sDVE.h, w)
            sw(sc)
            SA(sc.activation(out=hag_hi[:], in_=haf[:], func=AF.Copy))
        emit("scalar", s_hi)

        hag_rdy = sDVE.reserve()

        def v_lo(v, w=ah):
            v.wait_ge(sACT.h, w)
            vw(v)
            VA(v.tensor_tensor(hag_lo[:], haf[:], hag_hi[:], ALU.subtract))
        emit("vector", v_lo)
        close_scope(sc5)
        return hag_rdy

    phase_barrier()
    sc56 = []
    hag_hi = sbuf("hag_hi", [128, T], BF16, sc56)
    hag_lo = sbuf("hag_lo", [128, T], BF16, sc56)
    wb_whi, wb_wlo, wb_ready = load_w_resident("wb", wbT, D, sc56)
    bw_h = sbuf("bw_h", [128, D], BF16, sc56)
    bw_l = sbuf("bw_l", [128, D], BF16, sc56)
    b_hi = sbuf("b_hi", [1, D], BF16, sc56)
    b_lo = sbuf("b_lo", [1, D], BF16, sc56)
    tmp56 = []
    bwf = sbuf("p5bwf", [128, D], F32, tmp56)
    b_f32 = sbuf("p5bf32", [1, D], F32, tmp56)

    bww = sLD.reserve(32)

    def ldbw(s, w=wb_ready[0]):
        s.wait_ge(sDVE.h, w)
        sLD.attach_dma(s.dma_start(out=bwf[:], in_=bwT[:, :]))
        sLD.attach_dma(s.dma_start(out=b_f32[:], in_=b_base[:]))
    emit("sync", ldbw)

    abw = sACT.reserve(2)

    def s_bw(sc, w=bww):
        sc.wait_ge(sLD.h, w)
        sw(sc)
        SA(sc.activation(out=bw_h[:], in_=bwf[:], func=AF.Copy))
        sw(sc)
        SA(sc.activation(out=b_hi[:], in_=b_f32[:], func=AF.Copy))
    emit("scalar", s_bw)

    bw_rdy = sDVE.reserve(2)

    def v_bw(v, w=abw):
        v.wait_ge(sACT.h, w)
        vw(v)
        VA(v.tensor_tensor(bw_l[:], bwf[:], bw_h[:], ALU.subtract))
        vw(v)
        VA(v.tensor_tensor(b_lo[:], b_f32[:], b_hi[:], ALU.subtract))
    emit("vector", v_bw)
    close_scope(tmp56)
    hag_rdy = p5a()

    def p5b():
        phase_barrier()
        sc6 = []
        dstat = DStream("p5s")
        dx2 = DStream("p5x")
        whi, wlo, wready = wb_whi, wb_wlo, wb_ready

        dsto = DStream("sto", store=True)
        statpool = Pool(2, "p5st", init=(sDVE, bw_rdy))
        sth = [sbuf(f"p5sh{i}", [128, D], BF16, sc6) for i in range(2)]
        stl = [sbuf(f"p5sl{i}", [128, D], BF16, sc6) for i in range(2)]
        pspool = Pool(2, "p5ps")
        ps5 = [psum(f"p5ps{i}", [128, D], sc6) for i in range(2)]
        x2pool = Pool(2, "p5x2", init=(sDVE, bw_rdy))
        x2t = [sbuf(f"p5x2{i}", [128, D], F32, sc6) for i in range(2)]
        ffb = sbuf("p5ffb", [128, D], F32, sc6)
        outb = sbuf("p5out", [128, D], F32, sc6)
        st6 = sbuf("p5st6", [128, 6 * NBN], F32, sc6)
        mv = sbuf("p5mv", [128, 2], F32, sc6)
        r4_t = sbuf("p5r4", [128, 1], F32, sc6)
        b4_t = sbuf("p5b4", [128, 1], F32, sc6)

        for tc in range(TC):
            sslot, swv = statpool.acquire()
            trw = dstat.reserve(sslot, 2)

            def tl(sc, tc=tc, sslot=sslot, swv=swv):
                pw(sc, swv)
                r0 = tc * 128
                src_h = s_ln2h[r0:r0 + 128, :].rearrange("t (dc d) -> t dc d", d=128)
                src_l = s_ln2l[r0:r0 + 128, :].rearrange("t (dc d) -> t dc d", d=128)
                dst_h = sth[sslot][:].rearrange("p (dc t) -> p dc t", t=128)
                dst_l = stl[sslot][:].rearrange("p (dc t) -> p dc t", t=128)
                dstat.attach(sslot, sc.dma_start(out=dst_h, in_=src_h, transpose=True))
                dstat.attach(sslot, sc.dma_start(out=dst_l, in_=src_l, transpose=True))
            emit("sync", tl)

            xslot, xwv = x2pool.acquire()
            xw = dx2.reserve(xslot)

            def ldx(s, tc=tc, xslot=xslot, xwv=xwv):
                pw(s, xwv)
                r0 = tc * 128
                dx2.attach(xslot, s.dma_start(out=x2t[xslot][:],
                                              in_=s_x2[r0:r0 + 128, :]))
            emit("sync", ldx)

            pslot, pwv = pspool.acquire()
            mmv = sMM.reserve()
            statpool.released_at(sMM, mmv)
            dvw = max(wready[0], bw_rdy, hag_rdy)

            def mms(te, tc=tc, sslot=sslot, pslot=pslot, pwv=pwv, trw=trw,
                    dvw=dvw):
                pw(te, pwv)
                pw(te, trw)
                te.wait_ge(sDVE.h, dvw)
                te.wait_ge(sACT.h, wready[1])
                groups = [(sth[sslot][:, dc * 128:(dc + 1) * 128],
                           stl[sslot][:, dc * 128:(dc + 1) * 128],
                           whi[dc][:], wlo[dc][:]) for dc in range(NDC)]
                groups.append((hag_hi[:, tc * 128:(tc + 1) * 128],
                               hag_lo[:, tc * 128:(tc + 1) * 128],
                               bw_h[:], bw_l[:]))
                groups.append((ones_bf[:], None, b_hi[:], b_lo[:]))
                sMM.attach(emit_mm_tc(te, ps5[pslot][:], D, groups))
            emit("tensor", mms)
            ps = ps5[pslot]

            a0 = sACT.reserve()
            pspool.released_at(sACT, a0)

            def s_c(sc, ps=ps, w=mmv):
                sc.wait_ge(sMM.h, w)
                sw(sc)
                SA(sc.activation(out=ffb[:], in_=ps[:], func=AF.Copy))
            emit("scalar", s_c)

            d0 = sDVE.reserve(NBN + 1)

            def v_st(v, w=a0):
                v.wait_ge(sACT.h, w)
                bn_pair(v, ffb[:], st6, mv)
            emit("vector", v_st)

            a1 = sACT.reserve()

            def s_r4(sc, w=d0):
                sc.wait_ge(sDVE.h, w)
                sw(sc)
                SA(sc.activation(out=r4_t[:], in_=mv[:, 1:2], func=AF.Sqrt,
                                 bias=epsln_t[:]))
            emit("scalar", s_r4)

            dout_ = sDVE.reserve(4)
            x2pool.released_at(sDVE, dout_)

            def v_out(v, xslot=xslot, w=a1, xw=xw):
                v.wait_ge(sACT.h, w)
                pw(v, xw)
                vw(v)
                VA(v.reciprocal(r4_t[:], r4_t[:]))
                vw(v)
                VA(v.tensor_scalar(b4_t[:], mv[:, 0:1], r4_t[:], -1.0,
                                   ALU.mult, ALU.mult))
                vw(v)
                VA(v.tensor_scalar_add(outb[:], x2t[xslot][:], b4_t[:]))
                vw(v)
                VA(v.scalar_tensor_tensor(outb[:], ffb[:], r4_t[:], outb[:],
                                          ALU.mult, ALU.add))
            emit("vector", v_out)

            p = tc % 2
            ow = dsto.reserve(p)

            def g_out(g, tc=tc, p=p, w=dout_):
                g.wait_ge(sDVE.h, w)
                r0 = tc * 128
                dsto.attach(p, g.dma_start(out=y[r0:r0 + 128, :], in_=outb[:]))
            emit("gpsimd", g_out)
            emit("scalar", lambda sc, w=ow: pw(sc, w))
            emit("vector", lambda v, w=ow: pw(v, w))

        close_scope(sc6)

    p5b()
    close_scope(sc56)

    fin_waits = [(c, c.c) for c in STORE_CTRS if c.c > 0]

    def fin(g, ws=fin_waits):
        for c, v in ws:
            g.wait_ge(c.h, v)
    emit("gpsimd", fin)

    with nc.Block() as block:
        @block.sync
        def _(sync):
            for f in ops["sync"]:
                f(sync)

        @block.scalar
        def _(scalar):
            for f in ops["scalar"]:
                f(scalar)

        @block.vector
        def _(vector):
            for f in ops["vector"]:
                f(vector)

        @block.tensor
        def _(tensor):
            for f in ops["tensor"]:
                f(tensor)

        @block.gpsimd
        def _(gpsimd):
            for f in ops["gpsimd"]:
                f(gpsimd)

    for c in all_ctrs:
        assert c.att == c.c, f"sem {c.name}: attached {c.att} != reserved {c.c}"

    close_scope(glob)
    return nc


def stage_core_inputs(cfg, core_id, x_full, w_qkv, q_norm_w, k_norm_w, w_o,
                      W_base, b_base_v, router_W, A, Bw):
    D = cfg["D"]; NH = cfg["NH"]
    T = cfg["T_OWN"]; TH = cfg["T_HALO"]; E = cfg["E"]; R = cfg["R"]
    KC = (T + TH) // 128
    NJ = (D // NH) // 2
    b, half = core_id // 2, core_id % 2

    own0 = half * T
    x_own = np.ascontiguousarray(x_full[b, own0:own0 + T, :], dtype=np.float32)
    if half >= 1:
        x_halo = np.ascontiguousarray(x_full[b, own0 - TH:own0, :],
                                      dtype=np.float32)
        posv = np.arange(own0 - TH, own0 + T, dtype=np.float32)
        mv = np.zeros((128, TH), dtype=np.float32)
    else:
        x_halo = np.zeros((TH, D), dtype=np.float32)
        posv = np.concatenate([np.zeros(TH, dtype=np.float32),
                               np.arange(0, T, dtype=np.float32)])
        mv = np.full((128, TH), MASKVAL, dtype=np.float32)

    xh = np.concatenate([x_halo, x_own], axis=0)
    posr = np.ascontiguousarray(posv.reshape(KC, 128).T)

    i = np.arange(128)[:, None]
    j = np.arange(128)[None, :]
    mW_ = np.where(j >= i + 1, 0.0, MASKVAL).astype(np.float32)
    mC_ = np.where(j <= i, 0.0, MASKVAL).astype(np.float32)
    jr = np.arange(NJ, dtype=np.float32)[None, :]

    return {
        "xh": xh,
        "wqT": np.ascontiguousarray(w_qkv[0:D, :].T, dtype=np.float32),
        "wkT": np.ascontiguousarray(w_qkv[D:2 * D, :].T, dtype=np.float32),
        "wvT": np.ascontiguousarray(w_qkv[2 * D:3 * D, :].T, dtype=np.float32),
        "woT": np.ascontiguousarray(w_o.T, dtype=np.float32),
        "wbT": np.ascontiguousarray(W_base.T, dtype=np.float32),
        "waT": np.ascontiguousarray(A.transpose(2, 1, 0).reshape(D, R * E)),
        "wrT": np.ascontiguousarray(router_W.T, dtype=np.float32),
        "bwT": np.ascontiguousarray(Bw.transpose(2, 0, 1).reshape(R * E, D)),
        "b_base": np.ascontiguousarray(b_base_v.reshape(1, D), dtype=np.float32),
        "qnw": np.ascontiguousarray(q_norm_w.reshape(1, D), dtype=np.float32),
        "knw": np.ascontiguousarray(k_norm_w.reshape(1, D), dtype=np.float32),
        "posr": posr,
        "jrow": np.ascontiguousarray(jr, dtype=np.float32),
        "mW": mW_, "mC": mC_, "mV": mv,
        "ones_row": np.ones((1, 128), dtype=np.float32),
        "ident": np.eye(128, dtype=np.float32),
    }


def kernel(L0_attn_hidden_states, L0_attn_w_qkv, L0_attn_q_norm_w,
           L0_attn_k_norm_w, L0_attn_w_o, L0_ffn_W_base, L0_ffn_b_base,
           L0_ffn_router_W, L0_ffn_A, L0_ffn_Bw):
    cfg = real_cfg()
    x = np.asarray(L0_attn_hidden_states, dtype=np.float32)
    args = (x, np.asarray(L0_attn_w_qkv, np.float32),
            np.asarray(L0_attn_q_norm_w, np.float32),
            np.asarray(L0_attn_k_norm_w, np.float32),
            np.asarray(L0_attn_w_o, np.float32),
            np.asarray(L0_ffn_W_base, np.float32),
            np.asarray(L0_ffn_b_base, np.float32),
            np.asarray(L0_ffn_router_W, np.float32),
            np.asarray(L0_ffn_A, np.float32),
            np.asarray(L0_ffn_Bw, np.float32))
    in_maps = [stage_core_inputs(cfg, c, *args) for c in range(8)]
    nc = build_nc(cfg)
    res = run_bass_kernel_spmd(nc, in_maps, core_ids=list(range(8)))
    B, S, D = x.shape
    out = np.empty((B, S, D), dtype=np.float32)
    T = cfg["T_OWN"]
    for c in range(8):
        b, half = c // 2, c % 2
        out[b, half * T:(half + 1) * T, :] = res.results[c]["y"]
    return out


# version 27
# speedup vs baseline: 1.3340x; 1.0167x over previous
import math
import numpy as np
import concourse.bass as bass
import concourse.mybir as mybir
from concourse.bass_utils import run_bass_kernel_spmd

F32 = mybir.dt.float32
BF16 = mybir.dt.bfloat16
AF = mybir.ActivationFunctionType
ALU = mybir.AluOpType
AX = mybir.AxisListType.X

MASKVAL = -30000.0
EPS_LN = 1e-5
EPS_RMS = 1e-6


def real_cfg():
    return dict(D=2048, NH=16, WINDOW=1024, T_OWN=2048, T_HALO=1024,
                E=8, R=16, debug_taps=False)


def mini_cfg():
    return dict(D=256, NH=2, WINDOW=256, T_OWN=512, T_HALO=256,
                E=8, R=16, debug_taps=True)


class Ctr:
    def __init__(self, handle, name):
        self.h = handle
        self.name = name
        self.c = 0
        self.att = 0

    def reserve(self, n=1):
        self.c += n
        return self.c

    def reserve_dma(self):
        return self.reserve(16)

    def attach(self, instr, n=1):
        self.att += n
        instr.then_inc(self.h, n)
        return instr

    def attach_dma(self, instr):
        return self.attach(instr, 16)


class Pool:

    def __init__(self, nbuf, name, init=None):
        self.nbuf = nbuf
        self.name = name
        self.uses = 0
        self.init = init
        self.rel = []

    def acquire(self):
        u = self.uses
        self.uses += 1
        if u >= self.nbuf:
            assert len(self.rel) > u - self.nbuf, \
                f"pool {self.name}: use {u} before release {u - self.nbuf}"
            w = self.rel[u - self.nbuf]
        else:
            w = self.init
        return u % self.nbuf, w

    def released_at(self, ctr, value):
        self.rel.append((ctr, value))


def build_nc(cfg):
    D = cfg["D"]; NH = cfg["NH"]; W = cfg["WINDOW"]
    T = cfg["T_OWN"]; TH = cfg["T_HALO"]; E = cfg["E"]; R = cfg["R"]
    HA = E * R
    HD = D // NH
    assert HD == 128 and HA == 128 and W == TH
    NDC = D // 128
    TC = T // 128
    HC = TH // 128
    KC = TC + HC
    TKV = T + TH
    WKC = W // 128
    KSPAN = WKC + 1
    KW = KSPAN * 128
    KWP = ((KW + 511) // 512) * 512
    NJ = HD // 2
    taps = cfg["debug_taps"]
    QG = min(512, T)
    NQG = T // QG
    QGC = QG // 128
    CSC = 1.0 / math.sqrt(HD)

    nc = bass.Bass("TRN2", target_bir_lowering=False, debug=False)

    def din(name, shape, dt=F32):
        return nc.dram_tensor(name, list(shape), dt, kind="ExternalInput").ap()

    def dscr(name, shape, dt):
        return nc.dram_tensor(name, list(shape), dt, kind="Internal").ap()

    def dout(name, shape, dt=F32):
        return nc.dram_tensor(name, list(shape), dt, kind="ExternalOutput").ap()

    xh = din("xh", [TKV, D])
    wqT = din("wqT", [D, D]); wkT = din("wkT", [D, D]); wvT = din("wvT", [D, D])
    woT = din("woT", [D, D]); wbT = din("wbT", [D, D])
    waT = din("waT", [D, HA])
    wrT = din("wrT", [D, E])
    bwT = din("bwT", [HA, D])
    b_base = din("b_base", [1, D])
    qnw = din("qnw", [1, D]); knw = din("knw", [1, D])
    posr = din("posr", [128, KC])
    jrow = din("jrow", [1, NJ])
    mW = din("mW", [128, 128]); mC = din("mC", [128, 128])
    mV = din("mV", [128, TH])
    ones_row = din("ones_row", [1, 128])
    ident = din("ident", [128, 128])

    s_ln1h = dscr("s_ln1h", [TKV, D], BF16); s_ln1l = dscr("s_ln1l", [TKV, D], BF16)
    s_qh = dscr("s_qh", [T, D], BF16); s_ql = dscr("s_ql", [T, D], BF16)
    s_kh = dscr("s_kh", [TKV, D], BF16); s_kl = dscr("s_kl", [TKV, D], BF16)
    s_vh = dscr("s_vh", [TKV, D], BF16); s_vl = dscr("s_vl", [TKV, D], BF16)
    s_aoh = dscr("s_aoh", [D, T], BF16); s_aol = dscr("s_aol", [D, T], BF16)
    s_x2 = dscr("s_x2", [T, D], F32)
    s_ln2h = dscr("s_ln2h", [T, D], BF16); s_ln2l = dscr("s_ln2l", [T, D], BF16)

    y = dout("y", [T, D])
    taph = {}
    if taps:
        for nm, shp in (("t_ln1", [TKV, D]), ("t_q", [T, D]), ("t_k", [TKV, D]),
                        ("t_v", [TKV, D]), ("t_ao", [D, T]), ("t_x2", [T, D]),
                        ("t_gate", [128, T]), ("t_ha", [128, T])):
            taph[nm] = dout(nm, shp)

    ops = {"sync": [], "scalar": [], "vector": [], "tensor": [], "gpsimd": []}

    def emit(eng, fn):
        ops[eng].append(fn)

    glob = []

    def sbuf(name, shape, dt, scope):
        g = nc.sbuf_tensor(name, list(shape), dt)
        t = g.__enter__()
        scope.append(g)
        return t

    def psum(name, shape, scope, dt=F32):
        g = nc.psum_tensor(name, list(shape), dt)
        t = g.__enter__()
        scope.append(g)
        return t

    def close_scope(scope):
        for g in reversed(scope):
            g.__exit__(None, None, None)
        scope.clear()

    all_ctrs = []

    def sem(name):
        g = nc.semaphore(name)
        h = g.__enter__()
        glob.append(g)
        c = Ctr(h, name)
        all_ctrs.append(c)
        return c

    sLD = sem("sLD")
    sST = sem("sST")
    sMM = sem("sMM")
    sACT = sem("sACT")
    sDVE = sem("sDVE")
    sOUT = sem("sOUT")

    STORE_CTRS = []

    class DStream:

        def __init__(self, name, n=2, store=False):
            self.s = [sem(f"{name}{i}") for i in range(n)]
            if store:
                STORE_CTRS.extend(self.s)

        def reserve(self, slot, ndma=1):
            return (self.s[slot], self.s[slot].reserve(16 * ndma))

        def attach(self, slot, instr):
            return self.s[slot].attach(instr, 16)

    def phase_barrier():
        waits = [(c, c.c) for c in STORE_CTRS if c.c > 0]

        def b(e, ws=waits):
            for c, v in ws:
                e.wait_ge(c.h, v)
        emit("sync", b)
        emit("scalar", b)

    def SA(instr):
        return sACT.attach(instr)

    def VA(instr):
        return sDVE.attach(instr)

    def sw(sc):
        sc.wait_ge(sACT.h, sACT.att)

    def vw(v):
        v.wait_ge(sDVE.h, sDVE.att)

    def pw(eng, w):
        if w is not None:
            eng.wait_ge(w[0].h, w[1])

    NBN = (D + 511) // 512

    def bn_pair(v, srcbuf, st6, mv):
        for c in range(NBN):
            vw(v)
            VA(v.bn_stats(st6[:, c * 6:(c + 1) * 6],
                          srcbuf[:, c * 512:min((c + 1) * 512, D)]))
        vw(v)
        VA(v.bn_aggr(mv[:], st6[:, 0:6 * NBN]))

    ones_bf = sbuf("ones_bf", [1, 128], BF16, glob)
    epsln_t = sbuf("epsln_t", [128, 1], F32, glob)
    rmssc_t = sbuf("rmssc_t", [128, 1], F32, glob)
    rmseps_t = sbuf("rmseps_t", [128, 1], F32, glob)
    zro_bf = sbuf("zro_bf", [1, 512], BF16, glob)
    qsc_t = sbuf("qsc_t", [128, 1], F32, glob)
    qeps_t = sbuf("qeps_t", [128, 1], F32, glob)
    sc03 = []
    mW_t = sbuf("mW_t", [128, 128], F32, sc03)
    mC_t = sbuf("mC_t", [128, 128], F32, sc03)
    mV_t = sbuf("mV_t", [128, TH], F32, sc03)
    id_bf = sbuf("id_bf", [128, 128], BF16, sc03)
    mW_b = sbuf("mW_b", [128, 128], BF16, sc03)
    mC_b = sbuf("mC_b", [128, 128], BF16, sc03)
    mV_b = sbuf("mV_b", [128, TH], BF16, sc03)
    idf_t = sbuf("idf_t", [128, 128], F32, sc03)
    onesr_b = sbuf("onesr_b", [1, 512], BF16, sc03)
    ones_kv = sbuf("ones_kv", [128, 128], BF16, sc03)
    sc02 = []
    cos_t = sbuf("cos_t", [128, KC * NJ], F32, sc02)
    sin_t = sbuf("sin_t", [128, KC * NJ], F32, sc02)

    sDVE.reserve(8)

    def v_consts(v):
        VA(v.memset(zro_bf[:], 0.0))
        VA(v.memset(epsln_t[:], EPS_LN))
        VA(v.memset(rmssc_t[:], 1.0 / D))
        VA(v.memset(rmseps_t[:], EPS_RMS))
        VA(v.memset(qsc_t[:], float(HD) / D))
        VA(v.memset(qeps_t[:], float(HD) * EPS_RMS))
        VA(v.memset(onesr_b[:], 1.0))
        VA(v.memset(ones_kv[:], 1.0))
    emit("vector", v_consts)

    def p0():
        sc0 = []
        post = sbuf("post", [128, KC], F32, sc0)
        jbc = sbuf("jbc", [128, NJ], F32, sc0)
        ones_f = sbuf("ones_f", [1, 128], F32, sc0)
        r_ang = sbuf("r_ang", [128, KC * NJ], F32, sc0)
        r_s1 = sbuf("r_s1", [128, KC * NJ], F32, sc0)
        r_s2 = sbuf("r_s2", [128, KC * NJ], F32, sc0)
        r_s3 = sbuf("r_s3", [128, KC * NJ], F32, sc0)
        c_a = sbuf("c_a", [128, 1], F32, sc0)

        w0 = [sLD.reserve_dma() for _ in range(7)][-1]

        def ld(s):
            sLD.attach_dma(s.dma_start(out=ones_f[:], in_=ones_row[:]))
            sLD.attach_dma(s.dma_start(out=idf_t[:], in_=ident[:]))
            sLD.attach_dma(s.dma_start(out=post[:], in_=posr[:]))
            sLD.attach_dma(s.dma_start(out=jbc[:], in_=jrow[:].broadcast_to((128, NJ))))
            sLD.attach_dma(s.dma_start(out=mW_t[:], in_=mW[:]))
            sLD.attach_dma(s.dma_start(out=mC_t[:], in_=mC[:]))
            sLD.attach_dma(s.dma_start(out=mV_t[:], in_=mV[:]))
        emit("sync", ld)

        a_if = sACT.reserve(9)

        def sc_c(sc):
            sc.wait_ge(sLD.h, w0)
            SA(sc.activation(out=ones_bf[:], in_=ones_f[:], func=AF.Copy))
            SA(sc.activation(out=id_bf[:], in_=idf_t[:], func=AF.Copy))
            SA(sc.activation(out=mW_b[:], in_=mW_t[:], func=AF.Copy))
            SA(sc.activation(out=mC_b[:], in_=mC_t[:], func=AF.Copy))
            SA(sc.activation(out=mV_b[:], in_=mV_t[:], func=AF.Copy))
            SA(sc.activation(out=c_a[:], in_=jbc[:, 0:1], func=AF.Copy,
                             scale=0.0, bias=10000.0))
            sw(sc)
            SA(sc.activation(out=c_a[:], in_=c_a[:], func=AF.Ln))
            sw(sc)
            SA(sc.mul(c_a[:], c_a[:], -1.0 / NJ))
            sw(sc)
            SA(sc.activation(out=r_s1[:, 0:NJ], in_=jbc[:], func=AF.Exp,
                             scale=c_a[:]))
        emit("scalar", sc_c)

        mmax = max(0, int(math.ceil(math.log2(max(TKV, 7) / (2 * math.pi)))))
        sDVE.reserve(1 + 3 * (mmax + 1))

        def v_ang(v):
            v.wait_ge(sACT.h, a_if)
            ang = r_ang[:].rearrange("p (kc j) -> p kc j", j=NJ)
            vw(v)
            VA(v.tensor_tensor(
                ang,
                post[:].unsqueeze(2).broadcast_to((128, KC, NJ)),
                r_s1[:, 0:NJ].unsqueeze(1).broadcast_to((128, KC, NJ)),
                ALU.mult))
            for m in range(mmax, -1, -1):
                c = 2 * math.pi * (2 ** m)
                vw(v)
                VA(v.tensor_single_scalar(r_s1[:], r_ang[:], c, ALU.is_ge))
                vw(v)
                VA(v.tensor_scalar(r_s1[:], r_s1[:], c, None, ALU.mult))
                vw(v)
                VA(v.tensor_tensor(r_ang[:], r_ang[:], r_s1[:], ALU.subtract))
        emit("vector", v_ang)

        def make_table(shift, out_tab):
            dv = sDVE.reserve(14)

            def vf(v):
                vw(v)
                VA(v.tensor_scalar(r_s1[:], r_ang[:], shift, None, ALU.add))
                vw(v)
                VA(v.tensor_single_scalar(r_s2[:], r_s1[:], 2 * math.pi, ALU.is_ge))
                vw(v)
                VA(v.tensor_scalar(r_s2[:], r_s2[:], 2 * math.pi, None, ALU.mult))
                vw(v)
                VA(v.tensor_tensor(r_s1[:], r_s1[:], r_s2[:], ALU.subtract))
                vw(v)
                VA(v.tensor_scalar(r_s1[:], r_s1[:], math.pi, None, ALU.subtract))
                vw(v)
                VA(v.tensor_tensor(r_s2[:], r_s1[:], r_s1[:], ALU.mult))
                vw(v)
                VA(v.tensor_single_scalar(r_s2[:], r_s2[:],
                                          (math.pi / 2) ** 2, ALU.is_ge))
                vw(v)
                VA(v.tensor_single_scalar(r_s3[:], r_s1[:], 0.0, ALU.is_ge))
                vw(v)
                VA(v.tensor_scalar(r_s3[:], r_s3[:], 2.0, -1.0, ALU.mult, ALU.add))
                vw(v)
                VA(v.tensor_tensor(r_s3[:], r_s3[:], r_s2[:], ALU.mult))
                vw(v)
                VA(v.tensor_scalar(r_s3[:], r_s3[:], math.pi, None, ALU.mult))
                vw(v)
                VA(v.tensor_scalar(r_s2[:], r_s2[:], -2.0, 1.0, ALU.mult, ALU.add))
                vw(v)
                VA(v.tensor_tensor(r_s1[:], r_s1[:], r_s2[:], ALU.mult))
                vw(v)
                VA(v.tensor_tensor(r_s1[:], r_s1[:], r_s3[:], ALU.add))
            emit("vector", vf)
            av = sACT.reserve()

            def sf(sc):
                sc.wait_ge(sDVE.h, dv)
                sw(sc)
                SA(sc.activation(out=out_tab[:], in_=r_s1[:], func=AF.Sin,
                                 scale=-1.0))
            emit("scalar", sf)
            emit("vector", lambda v, w=av: v.wait_ge(sACT.h, w))

        make_table(0.0, sin_t)
        make_table(math.pi / 2, cos_t)
        fin = sACT.c
        emit("sync", lambda s, w=fin: s.wait_ge(sACT.h, w))
        close_scope(sc0)

    p0()

    def p1():
        phase_barrier()
        sc1 = []
        NP1 = 3
        dsx = DStream("p1x", n=NP1)
        dst1 = DStream("st1", n=NP1, store=True)
        xpool = Pool(NP1, "p1x")
        x_in = [sbuf(f"p1x{i}", [128, D], F32, sc1) for i in range(NP1)]
        lnf = [sbuf(f"p1lnf{i}", [128, D], F32, sc1) for i in range(NP1)]
        lnh = [sbuf(f"p1lnh{i}", [128, D], BF16, sc1) for i in range(NP1)]
        lnl = [sbuf(f"p1lnl{i}", [128, D], BF16, sc1) for i in range(NP1)]
        st6 = [sbuf(f"p1st6{i}", [128, 6 * NBN], F32, sc1) for i in range(NP1)]
        mv = [sbuf(f"p1mv{i}", [128, 2], F32, sc1) for i in range(NP1)]
        r_t = [sbuf(f"p1r{i}", [128, 1], F32, sc1) for i in range(NP1)]
        b_t = [sbuf(f"p1b{i}", [128, 1], F32, sc1) for i in range(NP1)]
        stw_par = [None] * NP1

        for t in range(KC):
            p = t % NP1
            slot, wv = xpool.acquire()
            xw = dsx.reserve(slot)

            def ld(s, t=t, slot=slot, wv=wv):
                pw(s, wv)
                dsx.attach(slot, s.dma_start(out=x_in[slot][:],
                                             in_=xh[t * 128:(t + 1) * 128, :]))
            emit("sync", ld)

            spw = stw_par[p]
            d0 = sDVE.reserve(NBN + 1)

            def v_st(v, slot=slot, p=p, w=xw, spw=spw):
                pw(v, w)
                pw(v, spw)
                bn_pair(v, x_in[slot][:], st6[p], mv[p])
            emit("vector", v_st)

            a0 = sACT.reserve()

            def s_r(sc, p=p, w=d0):
                sc.wait_ge(sDVE.h, w)
                sw(sc)
                SA(sc.activation(out=r_t[p][:], in_=mv[p][:, 1:2], func=AF.Sqrt,
                                 bias=epsln_t[:]))
            emit("scalar", s_r)

            d1 = sDVE.reserve(2)

            def v_b(v, p=p, w=a0):
                v.wait_ge(sACT.h, w)
                vw(v)
                VA(v.reciprocal(r_t[p][:], r_t[p][:]))
                vw(v)
                VA(v.tensor_scalar(b_t[p][:], mv[p][:, 0:1], r_t[p][:], -1.0,
                                   ALU.mult, ALU.mult))
            emit("vector", v_b)

            a1 = sACT.reserve(2)
            xpool.released_at(sACT, a1 - 1)

            def s_ln(sc, slot=slot, p=p, w=d1):
                sc.wait_ge(sDVE.h, w)
                sw(sc)
                SA(sc.activation(out=lnf[p][:], in_=x_in[slot][:],
                                 func=AF.Identity,
                                 scale=r_t[p][:], bias=b_t[p][:]))
                sw(sc)
                SA(sc.activation(out=lnh[p][:], in_=lnf[p][:], func=AF.Copy))
            emit("scalar", s_ln)

            dlo = sDVE.reserve()

            def v_fin(v, p=p, w=a1):
                v.wait_ge(sACT.h, w)
                vw(v)
                VA(v.tensor_tensor(lnl[p][:], lnf[p][:], lnh[p][:],
                                   ALU.subtract))
            emit("vector", v_fin)

            nst = 3 if taps else 2
            stw = dst1.reserve(p, nst)
            stw_par[p] = stw

            def g_st(g, t=t, p=p, w=dlo):
                g.wait_ge(sDVE.h, w)
                dst1.attach(p, g.dma_start(out=s_ln1h[t * 128:(t + 1) * 128, :],
                                           in_=lnh[p][:]))
                dst1.attach(p, g.dma_start(out=s_ln1l[t * 128:(t + 1) * 128, :],
                                           in_=lnl[p][:]))
                if taps:
                    dst1.attach(p, g.dma_start(
                        out=taph["t_ln1"][t * 128:(t + 1) * 128, :],
                        in_=lnf[p][:]))
            emit("gpsimd", g_st)

        close_scope(sc1)

    p1()

    def emit_w_load(pref, src, whi, wlo, ncols, tmp, init=None):
        nws = 4
        wsl = [sbuf(f"{pref}t{i}", [128, ncols], F32, tmp) for i in range(nws)]
        dsw = DStream(pref + "d", n=nws)
        pl = Pool(nws, pref, init=init)
        for dc in range(NDC):
            slot, wv = pl.acquire()
            ww = dsw.reserve(slot)

            def ld(s, dc=dc, slot=slot, wv=wv):
                pw(s, wv)
                dsw.attach(slot, s.dma_start(out=wsl[slot][:],
                                             in_=src[dc * 128:(dc + 1) * 128, :]))
            emit("sync", ld)

            aw = sACT.reserve()

            def s_h(sc, dc=dc, slot=slot, w=ww):
                pw(sc, w)
                sw(sc)
                SA(sc.activation(out=whi[dc][:], in_=wsl[slot][:], func=AF.Copy))
            emit("scalar", s_h)

            dv = sDVE.reserve()
            pl.released_at(sDVE, dv)

            def v_l(v, dc=dc, slot=slot, w=aw):
                v.wait_ge(sACT.h, w)
                vw(v)
                VA(v.tensor_tensor(wlo[dc][:], wsl[slot][:], whi[dc][:],
                                   ALU.subtract))
            emit("vector", v_l)
        return (sDVE.c, sACT.c)

    def load_w_resident(pref, src, ncols, scope):
        whi = [sbuf(f"{pref}h{i}", [128, ncols], BF16, scope) for i in range(NDC)]
        wlo = [sbuf(f"{pref}l{i}", [128, ncols], BF16, scope) for i in range(NDC)]
        tmp = []
        ready = emit_w_load(pref, src, whi, wlo, ncols, tmp)
        close_scope(tmp)
        return whi, wlo, ready

    def emit_mm_tc(te, ps_ap, n_out, groups):
        ocs = [(i * 512, min(512, n_out - i * 512))
               for i in range((n_out + 511) // 512)]
        plan = []
        for (lh, ll, rh, rl) in groups:
            plan.append((lh, rh))
            plan.append((lh, rl))
            if ll is not None:
                plan.append((ll, rh))
        last = None
        first = [True] * len(ocs)
        for pi, (lt, rt) in enumerate(plan):
            for oi, (o0, on) in enumerate(ocs):
                st = first[oi]
                first[oi] = False
                last = te.matmul(ps_ap[:, o0:o0 + on], lt, rt[:, o0:o0 + on],
                                 start=st, stop=(pi == len(plan) - 1))
        return last

    def p2_og(og):
        phase_barrier()
        sc2 = []
        dstat = DStream(f"p2{og}s")
        dst2 = DStream(f"st2{og}", store=True)
        wsrc = {"q": wqT, "k": wkT, "v": wvT}[og]
        whi, wlo, wready = load_w_resident(f"w{og}", wsrc, D, sc2)

        wrep_w = None
        if og in ("q", "k"):
            wrep = sbuf(f"p2{og}rep", [128, D], F32, sc2)
            wrep_w = sLD.reserve_dma()

            def ldrep(s, og=og, w=wready[0]):
                s.wait_ge(sDVE.h, w)
                wsrc2 = qnw if og == "q" else knw
                sLD.attach_dma(s.dma_start(out=wrep[:],
                                           in_=wsrc2[:].broadcast_to((128, D))))
            emit("sync", ldrep)

        statpool = Pool(2, f"p2{og}st", init=(sDVE, wready[0]))
        sth = [sbuf(f"p2{og}sh{i}", [128, D], BF16, sc2) for i in range(2)]
        stl = [sbuf(f"p2{og}sl{i}", [128, D], BF16, sc2) for i in range(2)]
        pspool = Pool(2, f"p2{og}ps")
        ps2 = [psum(f"p2{og}ps{i}", [128, D], sc2) for i in range(2)]
        qn = sbuf(f"p2{og}qn", [128, D], F32, sc2)
        rr = sbuf(f"p2{og}rr", [128, D], F32, sc2)
        ohi = sbuf(f"p2{og}oh", [128, D], BF16, sc2)
        olo = sbuf(f"p2{og}ol", [128, D], BF16, sc2)
        sq_t = sbuf(f"p2{og}sq", [128, 1], F32, sc2)
        rq_t = sbuf(f"p2{og}rq", [128, 1], F32, sc2)

        tcs = list(range(HC, KC)) if og == "q" else list(range(KC))
        dsth, dstl = {"q": (s_qh, s_ql), "k": (s_kh, s_kl),
                      "v": (s_vh, s_vl)}[og]
        tapn = {"q": "t_q", "k": "t_k", "v": "t_v"}[og]

        for tc in tcs:
            sslot, swv = statpool.acquire()
            trw = dstat.reserve(sslot, 2)

            def tl(sc, tc=tc, sslot=sslot, swv=swv):
                pw(sc, swv)
                r0 = tc * 128
                src_h = s_ln1h[r0:r0 + 128, :].rearrange("t (dc d) -> t dc d", d=128)
                src_l = s_ln1l[r0:r0 + 128, :].rearrange("t (dc d) -> t dc d", d=128)
                dst_h = sth[sslot][:].rearrange("p (dc t) -> p dc t", t=128)
                dst_l = stl[sslot][:].rearrange("p (dc t) -> p dc t", t=128)
                dstat.attach(sslot, sc.dma_start(out=dst_h, in_=src_h, transpose=True))
                dstat.attach(sslot, sc.dma_start(out=dst_l, in_=src_l, transpose=True))
            emit("sync", tl)

            pslot, pwv = pspool.acquire()
            mmv = sMM.reserve()
            statpool.released_at(sMM, mmv)

            def mms(te, sslot=sslot, pslot=pslot, pwv=pwv, trw=trw):
                pw(te, pwv)
                pw(te, trw)
                te.wait_ge(sDVE.h, wready[0])
                te.wait_ge(sACT.h, wready[1])
                groups = [(sth[sslot][:, dc * 128:(dc + 1) * 128],
                           stl[sslot][:, dc * 128:(dc + 1) * 128],
                           whi[dc][:], wlo[dc][:]) for dc in range(NDC)]
                sMM.attach(emit_mm_tc(te, ps2[pslot][:], D, groups))
            emit("tensor", mms)
            ps = ps2[pslot]
            row0 = (tc - HC) * 128 if og == "q" else tc * 128

            if og in ("q", "k"):
                a0 = sACT.reserve(2)

                def s_e(sc, og=og, ps=ps, w=mmv):
                    sc.wait_ge(sMM.h, w)
                    sw(sc)
                    SA(sc.activation(out=rr[:], in_=ps[:], func=AF.Square,
                                     accum_out=sq_t[:]))
                    sw(sc)
                    if og == "q":
                        SA(sc.activation(out=rq_t[:], in_=sq_t[:], func=AF.Sqrt,
                                         scale=qsc_t[:], bias=qeps_t[:]))
                    else:
                        SA(sc.activation(out=rq_t[:], in_=sq_t[:], func=AF.Sqrt,
                                         scale=rmssc_t[:], bias=rmseps_t[:]))
                emit("scalar", s_e)

                d0 = sDVE.reserve(8)
                pspool.released_at(sDVE, d0 - 6)

                def v_e(v, tc=tc, ps=ps, w=a0, ww=wrep_w):
                    v.wait_ge(sACT.h, w)
                    v.wait_ge(sLD.h, ww)
                    vw(v)
                    VA(v.reciprocal(rq_t[:], rq_t[:]))
                    vw(v)
                    VA(v.scalar_tensor_tensor(qn[:], ps[:], rq_t[:], wrep[:],
                                              ALU.mult, ALU.mult))

                    def hv(buf, half):
                        vvw = buf[:].rearrange("p (h two j) -> p h two j",
                                               two=2, j=NJ)
                        return vvw[:, :, half, :]
                    cosv = cos_t[:, tc * NJ:(tc + 1) * NJ] \
                        .unsqueeze(1).broadcast_to((128, NH, NJ))
                    sinv = sin_t[:, tc * NJ:(tc + 1) * NJ] \
                        .unsqueeze(1).broadcast_to((128, NH, NJ))
                    qt, qb = hv(qn, 0), hv(qn, 1)
                    rt_, rb = hv(rr, 0), hv(rr, 1)
                    vw(v); VA(v.tensor_tensor(rt_, qb, sinv, ALU.mult))
                    vw(v); VA(v.tensor_tensor(rb, qb, cosv, ALU.mult))
                    vw(v); VA(v.tensor_tensor(qb, qt, sinv, ALU.mult))
                    vw(v); VA(v.tensor_tensor(rb, rb, qb, ALU.add))
                    vw(v); VA(v.tensor_tensor(qb, qt, cosv, ALU.mult))
                    vw(v); VA(v.tensor_tensor(rt_, qb, rt_, ALU.subtract))
                emit("vector", v_e)

                a1 = sACT.reserve()

                def s_hi(sc, w=d0):
                    sc.wait_ge(sDVE.h, w)
                    sw(sc)
                    SA(sc.activation(out=ohi[:], in_=rr[:], func=AF.Copy))
                emit("scalar", s_hi)

                dlo = sDVE.reserve()

                def v_lo(v, w=a1):
                    v.wait_ge(sACT.h, w)
                    vw(v)
                    VA(v.tensor_tensor(olo[:], rr[:], ohi[:], ALU.subtract))
                emit("vector", v_lo)
            else:
                a0 = sACT.reserve(2 if taps else 1)

                def s_e(sc, ps=ps, w=mmv):
                    sc.wait_ge(sMM.h, w)
                    sw(sc)
                    SA(sc.activation(out=ohi[:], in_=ps[:], func=AF.Copy))
                    if taps:
                        sw(sc)
                        SA(sc.activation(out=rr[:], in_=ps[:], func=AF.Copy))
                emit("scalar", s_e)

                dlo = sDVE.reserve()
                pspool.released_at(sDVE, dlo)

                def v_lo(v, ps=ps, w=a0):
                    v.wait_ge(sACT.h, w)
                    vw(v)
                    VA(v.tensor_tensor(olo[:], ps[:], ohi[:], ALU.subtract))
                emit("vector", v_lo)

            p = tc % 2
            nst = 3 if taps else 2
            stw = dst2.reserve(p, nst)

            def g_st(g, p=p, row0=row0, w=dlo):
                g.wait_ge(sDVE.h, w)
                dst2.attach(p, g.dma_start(out=dsth[row0:row0 + 128, :],
                                           in_=ohi[:]))
                dst2.attach(p, g.dma_start(out=dstl[row0:row0 + 128, :],
                                           in_=olo[:]))
                if taps:
                    dst2.attach(p, g.dma_start(
                        out=taph[tapn][row0:row0 + 128, :], in_=rr[:]))
            emit("gpsimd", g_st)
            emit("scalar", lambda sc, w=stw: pw(sc, w))

        close_scope(sc2)

    p2_og("q")
    p2_og("k")
    p2_og("v")
    close_scope(sc02)

    def p3():
        phase_barrier()
        sc3 = []
        dqk = DStream("p3qk")
        dst3 = DStream("st3", store=True)
        dv_ = DStream("p3v")
        headpool = Pool(2, "p3head")
        qTh = [sbuf(f"p3qh{i}", [128, T], BF16, sc3) for i in range(2)]
        qTl = [sbuf(f"p3ql{i}", [128, T], BF16, sc3) for i in range(2)]
        kTh = [sbuf(f"p3kh{i}", [128, TKV], BF16, sc3) for i in range(2)]
        kTl = [sbuf(f"p3kl{i}", [128, TKV], BF16, sc3) for i in range(2)]
        vh_ = [sbuf(f"p3vh{i}", [128, TKV], BF16, sc3) for i in range(2)]
        vl_ = [sbuf(f"p3vl{i}", [128, TKV], BF16, sc3) for i in range(2)]
        e32 = [sbuf(f"p3e{i}", [128, 512], F32, sc3) for i in range(2)]
        ehi = [sbuf(f"p3eh{i}", [128, 512], BF16, sc3) for i in range(3)]
        elo = [sbuf(f"p3el{i}", [128, 512], BF16, sc3) for i in range(3)]
        den_sb = [sbuf(f"p3dsb{i}", [128, QG], F32, sc3) for i in range(2)]
        lnt = sbuf("p3lnt", [128, QG], F32, sc3)
        rcpb = sbuf("p3rcpb", [128, QG], F32, sc3)
        aof_ev = sbuf("p3aof", [128, QG], F32, sc3)
        aoh_ev = sbuf("p3aoh", [128, QG], BF16, sc3)
        aol_ev = sbuf("p3aol", [128, QG], BF16, sc3)
        scpool = Pool(3, "p3sc")
        ps_s = [psum(f"p3ps{i}", [128, 512], sc3) for i in range(3)]
        epool = Pool(3, "p3ep")
        aopool = Pool(2, "p3ao")
        ps_ao = [psum(f"p3pao{i}", [128, QG], sc3) for i in range(2)]
        ps_dn = [psum(f"p3pdn{i}", [128, 512], sc3) for i in range(2)]

        e32_free = {0: None, 1: None}
        dn_free = {}
        gci = 0
        prev_stw = None
        prev_dlo2 = None
        pending_evac = None
        for h in range(NH):
            hslot, hwv = headpool.acquire()
            trw = dqk.reserve(hslot, 4)

            def hload_t(sc, h=h, hslot=hslot, hwv=hwv):
                pw(sc, hwv)
                c0 = h * 128
                dqk.attach(hslot, sc.dma_start(out=qTh[hslot][:],
                                               in_=s_qh[:, c0:c0 + 128],
                                               transpose=True))
                dqk.attach(hslot, sc.dma_start(out=qTl[hslot][:],
                                               in_=s_ql[:, c0:c0 + 128],
                                               transpose=True))
                dqk.attach(hslot, sc.dma_start(out=kTh[hslot][:],
                                               in_=s_kh[:, c0:c0 + 128],
                                               transpose=True))
                dqk.attach(hslot, sc.dma_start(out=kTl[hslot][:],
                                               in_=s_kl[:, c0:c0 + 128],
                                               transpose=True))
            emit("sync", hload_t)

            vldw = dv_.reserve(hslot, 2)

            def hload_v(s, h=h, hslot=hslot, hwv=hwv):
                pw(s, hwv)
                c0 = h * 128
                src_h = s_vh[:, c0:c0 + 128].rearrange("(kc t) d -> t kc d", t=128)
                src_l = s_vl[:, c0:c0 + 128].rearrange("(kc t) d -> t kc d", t=128)
                dst_h = vh_[hslot][:].rearrange("p (kc d) -> p kc d", d=128)
                dst_l = vl_[hslot][:].rearrange("p (kc d) -> p kc d", d=128)
                dv_.attach(hslot, s.dma_start(out=dst_h, in_=src_h))
                dv_.attach(hslot, s.dma_start(out=dst_l, in_=src_l))
            emit("sync", hload_v)

            pv_done_prev = None
            for qg in range(NQG):
                qc0, qc1 = qg * QGC, (qg + 1) * QGC - 1
                kvcs = list(range(qc0, qc1 + WKC + 1))
                aoslot, aowv = aopool.acquire()

                def zf(te, aowv=aowv, aoslot=aoslot):
                    pw(te, aowv)
                    te.matmul(ps_ao[aoslot][:, 0:QG], ones_bf[:],
                              zro_bf[:, 0:QG], start=True, stop=False)
                emit("tensor", zf)

                def emit_pv(info, is_last, aowv=aowv, hslot=hslot,
                            vldw=vldw, qc0=qc0, aoslot=aoslot):
                    ci, kvc, eslot, ew, qa, qb, dnf = info
                    nc_ = (qb - qa + 1) * 128
                    pvv = sMM.reserve()
                    epool.released_at(sMM, pvv)

                    def t_pv(te, ci=ci, kvc=kvc, eslot=eslot, ew=ew, qa=qa,
                             qb=qb, nc_=nc_, is_last=is_last, qc0=qc0,
                             dnf=dnf, aoslot=aoslot):
                        te.wait_ge(sDVE.h, ew)
                        pw(te, vldw)
                        if dnf is not None:
                            te.wait_ge(sDVE.h, dnf)
                        off = (qa - qc0) * 128
                        vs_h = vh_[hslot][:, kvc * 128:(kvc + 1) * 128]
                        vs_l = vl_[hslot][:, kvc * 128:(kvc + 1) * 128]
                        eh = ehi[eslot][:, 0:nc_]
                        el = elo[eslot][:, 0:nc_]
                        te.matmul(ps_ao[aoslot][:, off:off + nc_], vs_h, eh,
                                  start=False, stop=False)
                        te.matmul(ps_ao[aoslot][:, off:off + nc_], vs_h, el,
                                  start=False, stop=False)
                        te.matmul(ps_ao[aoslot][:, off:off + nc_], vs_l, eh,
                                  start=False, stop=is_last)
                        te.matmul(ps_dn[ci % 2][:, 0:nc_], ones_kv[:], eh,
                                  start=True, stop=False)
                        lastm = te.matmul(ps_dn[ci % 2][:, 0:nc_], ones_kv[:],
                                          el, start=False, stop=True)
                        sMM.attach(lastm)
                    emit("tensor", t_pv)

                    dnv = sDVE.reserve(2 if ci == 0 else 1)
                    dpar = qg % 2

                    def v_dn(v, ci=ci, qa=qa, nc_=nc_, w=pvv, qc0=qc0,
                             dpar=dpar):
                        v.wait_ge(sMM.h, w)
                        off = (qa - qc0) * 128
                        if ci == 0:
                            vw(v)
                            VA(v.memset(den_sb[dpar][:], 0.0))
                        vw(v)
                        VA(v.tensor_tensor(den_sb[dpar][:, off:off + nc_],
                                           ps_dn[ci % 2][:, 0:nc_],
                                           den_sb[dpar][:, off:off + nc_],
                                           ALU.add))
                    emit("vector", v_dn)
                    return pvv, dnv

                pend = None
                pv_done = None
                dn_last = None
                for ci, kvc in enumerate(kvcs):
                    if ci == 1 and pending_evac is not None:
                        prev_stw, prev_dlo2 = pending_evac()
                        pending_evac = None
                    qa = max(qc0, kvc - WKC)
                    qb = min(qc1, kvc)
                    nc_ = (qb - qa + 1) * 128
                    cslot, cwv = scpool.acquire()
                    mmv = sMM.reserve()

                    def t_sc(te, kvc=kvc, qa=qa, qb=qb, nc_=nc_, hslot=hslot,
                             cslot=cslot, cwv=cwv, trw=trw):
                        pw(te, cwv)
                        pw(te, trw)
                        ps = ps_s[cslot]
                        kh = kTh[hslot][:, kvc * 128:(kvc + 1) * 128]
                        kl = kTl[hslot][:, kvc * 128:(kvc + 1) * 128]
                        qh = qTh[hslot][:, qa * 128:qa * 128 + nc_]
                        ql = qTl[hslot][:, qa * 128:qa * 128 + nc_]
                        te.matmul(ps[:, 0:nc_], kh, qh, start=True, stop=False)
                        te.matmul(ps[:, 0:nc_], kh, ql, start=False, stop=False)
                        if kvc <= qb:
                            o = (kvc - qa) * 128
                            te.matmul(ps[:, o:o + 128], mW_b[:], id_bf[:],
                                      start=False, stop=False)
                        if kvc - WKC >= qa:
                            o = (kvc - WKC - qa) * 128
                            te.matmul(ps[:, o:o + 128], mC_b[:], id_bf[:],
                                      start=False, stop=False)
                        if kvc < HC:
                            te.matmul(ps[:, 0:nc_],
                                      mV_b[0:1, kvc * 128:(kvc + 1) * 128],
                                      onesr_b[:, 0:nc_],
                                      start=False, stop=False)
                        last = te.matmul(ps[:, 0:nc_], kl, qh,
                                         start=False, stop=True)
                        sMM.attach(last)
                    emit("tensor", t_sc)

                    eslot, ewv = epool.acquire()
                    asp = sACT.reserve(2)
                    scpool.released_at(sACT, asp)
                    par = gci % 2
                    e2f = e32_free[par]

                    def s_split(sc, cslot=cslot, eslot=eslot, ewv=ewv,
                                nc_=nc_, w=mmv, e2f=e2f, par=par):
                        sc.wait_ge(sMM.h, w)
                        pw(sc, ewv)
                        if e2f is not None:
                            sc.wait_ge(sDVE.h, e2f)
                        sw(sc)
                        SA(sc.activation(out=e32[par][:, 0:nc_],
                                         in_=ps_s[cslot][:, 0:nc_],
                                         func=AF.Exp))
                        sw(sc)
                        SA(sc.activation(out=ehi[eslot][:, 0:nc_],
                                         in_=ps_s[cslot][:, 0:nc_],
                                         func=AF.Exp))
                    emit("scalar", s_split)

                    dlo = sDVE.reserve(1)

                    def v_lo(v, eslot=eslot, nc_=nc_, w=asp, par=par):
                        v.wait_ge(sACT.h, w)
                        vw(v)
                        VA(v.tensor_tensor(elo[eslot][:, 0:nc_],
                                           e32[par][:, 0:nc_],
                                           ehi[eslot][:, 0:nc_], ALU.subtract))
                    emit("vector", v_lo)
                    e32_free[par] = dlo
                    gci += 1

                    dnf = dn_free.get(ci % 2)
                    cur = (ci, kvc, eslot, dlo, qa, qb, dnf)
                    if pend is not None:
                        _, dnv = emit_pv(pend, False)
                        dn_free[pend[0] % 2] = dnv
                        dn_last = dnv
                    pend = cur
                pv_done, dn_last = emit_pv(pend, True)
                dn_free[pend[0] % 2] = dn_last

                def make_evac(h=h, qg=qg, pv_done=pv_done, aoslot=aoslot,
                              pstw=prev_stw, pdlo2=prev_dlo2, dn_last=dn_last):
                    def do_emit():
                        dpar = qg % 2
                        arc = sACT.reserve(2)

                        def s_rc(sc, w=dn_last, pdlo2=pdlo2):
                            sc.wait_ge(sDVE.h, w)
                            if pdlo2 is not None:
                                sc.wait_ge(sDVE.h, pdlo2)
                            sw(sc)
                            SA(sc.activation(out=lnt[:], in_=den_sb[dpar][:],
                                             func=AF.Ln))
                            sw(sc)
                            SA(sc.activation(out=rcpb[:], in_=lnt[:],
                                             func=AF.Exp, scale=-1.0))
                        emit("scalar", s_rc)

                        dlo2 = sDVE.reserve(3)
                        aopool.released_at(sDVE, dlo2)

                        def v_ev(v, w=pv_done, pstw=pstw, aoslot=aoslot,
                                 arc=arc):
                            v.wait_ge(sMM.h, w)
                            v.wait_ge(sACT.h, arc)
                            if pstw is not None:
                                pw(v, pstw)
                            vw(v)
                            VA(v.tensor_tensor(aof_ev[:], ps_ao[aoslot][:],
                                               rcpb[:], ALU.mult))
                            vw(v)
                            VA(v.tensor_copy(aoh_ev[:], aof_ev[:]))
                            vw(v)
                            VA(v.tensor_tensor(aol_ev[:], aof_ev[:], aoh_ev[:],
                                               ALU.subtract))
                        emit("vector", v_ev)

                        p = (h * NQG + qg) % 2
                        nst = 3 if taps else 2
                        stw = dst3.reserve(p, nst)

                        def g_st(g, h=h, qg=qg, p=p, w=dlo2):
                            g.wait_ge(sDVE.h, w)
                            r0 = h * 128
                            c0 = qg * QG
                            dst3.attach(p, g.dma_start(
                                out=s_aoh[r0:r0 + 128, c0:c0 + QG],
                                in_=aoh_ev[:]))
                            dst3.attach(p, g.dma_start(
                                out=s_aol[r0:r0 + 128, c0:c0 + QG],
                                in_=aol_ev[:]))
                            if taps:
                                dst3.attach(p, g.dma_start(
                                    out=taph["t_ao"][r0:r0 + 128, c0:c0 + QG],
                                    in_=aof_ev[:]))
                        emit("gpsimd", g_st)
                        return stw, dlo2
                    return do_emit

                pending_evac = make_evac()
                pv_done_prev = pv_done

            headpool.released_at(sMM, pv_done_prev)

        if pending_evac is not None:
            prev_stw, prev_dlo2 = pending_evac()
            pending_evac = None

        close_scope(sc3)

    p3()
    close_scope(sc03)

    def p4():
        phase_barrier()
        sc4 = []
        dstat = DStream("p4s")
        dx = DStream("p4x")
        dst4 = DStream("st4", store=True)
        whi, wlo, wready = load_w_resident("wo", woT, D, sc4)
        statpool = Pool(2, "p4st", init=(sDVE, wready[0]))
        sth = [sbuf(f"p4sh{i}", [128, D], BF16, sc4) for i in range(2)]
        stl = [sbuf(f"p4sl{i}", [128, D], BF16, sc4) for i in range(2)]
        pspool = Pool(2, "p4ps")
        ps4 = [psum(f"p4ps{i}", [128, D], sc4) for i in range(2)]
        xpool = Pool(2, "p4x", init=(sDVE, wready[0]))
        xt = [sbuf(f"p4x{i}", [128, D], F32, sc4) for i in range(2)]
        attnb = sbuf("p4attn", [128, D], F32, sc4)
        x2b = sbuf("p4x2", [128, D], F32, sc4)
        lnh = sbuf("p4lnh", [128, D], BF16, sc4)
        lnl = sbuf("p4lnl", [128, D], BF16, sc4)
        st6 = sbuf("p4st6", [128, 6 * NBN], F32, sc4)
        mv = sbuf("p4mv", [128, 2], F32, sc4)
        r3_t = sbuf("p4r3", [128, 1], F32, sc4)
        b3_t = sbuf("p4b3", [128, 1], F32, sc4)
        r2_t = sbuf("p4r2", [128, 1], F32, sc4)
        b2_t = sbuf("p4b2", [128, 1], F32, sc4)

        for tc in range(TC):
            sslot, swv = statpool.acquire()
            ldw = dstat.reserve(sslot, 2)

            def ld(s, tc=tc, sslot=sslot, swv=swv):
                pw(s, swv)
                src_h = s_aoh[:, tc * 128:(tc + 1) * 128] \
                    .rearrange("(dc d) t -> d dc t", d=128)
                src_l = s_aol[:, tc * 128:(tc + 1) * 128] \
                    .rearrange("(dc d) t -> d dc t", d=128)
                dst_h = sth[sslot][:].rearrange("p (dc t) -> p dc t", t=128)
                dst_l = stl[sslot][:].rearrange("p (dc t) -> p dc t", t=128)
                dstat.attach(sslot, s.dma_start(out=dst_h, in_=src_h))
                dstat.attach(sslot, s.dma_start(out=dst_l, in_=src_l))
            emit("sync", ld)

            xslot, xwv = xpool.acquire()
            xw = dx.reserve(xslot)

            def ldx(s, tc=tc, xslot=xslot, xwv=xwv):
                pw(s, xwv)
                r0 = (HC + tc) * 128
                dx.attach(xslot, s.dma_start(out=xt[xslot][:],
                                             in_=xh[r0:r0 + 128, :]))
            emit("sync", ldx)

            pslot, pwv = pspool.acquire()
            mmv = sMM.reserve()
            statpool.released_at(sMM, mmv)

            def mms(te, sslot=sslot, pslot=pslot, pwv=pwv, ldw=ldw):
                pw(te, pwv)
                pw(te, ldw)
                te.wait_ge(sDVE.h, wready[0])
                te.wait_ge(sACT.h, wready[1])
                groups = [(sth[sslot][:, dc * 128:(dc + 1) * 128],
                           stl[sslot][:, dc * 128:(dc + 1) * 128],
                           whi[dc][:], wlo[dc][:]) for dc in range(NDC)]
                sMM.attach(emit_mm_tc(te, ps4[pslot][:], D, groups))
            emit("tensor", mms)
            ps = ps4[pslot]

            a0 = sACT.reserve()
            pspool.released_at(sACT, a0)

            def s_c(sc, ps=ps, w=mmv):
                sc.wait_ge(sMM.h, w)
                sw(sc)
                SA(sc.activation(out=attnb[:], in_=ps[:], func=AF.Copy))
            emit("scalar", s_c)

            d0 = sDVE.reserve(NBN + 1)

            def v_st3(v, w=a0):
                v.wait_ge(sACT.h, w)
                bn_pair(v, attnb[:], st6, mv)
            emit("vector", v_st3)

            a1 = sACT.reserve()

            def s_r3(sc, w=d0):
                sc.wait_ge(sDVE.h, w)
                sw(sc)
                SA(sc.activation(out=r3_t[:], in_=mv[:, 1:2], func=AF.Sqrt,
                                 bias=epsln_t[:]))
            emit("scalar", s_r3)

            d1 = sDVE.reserve(4 + NBN + 1)
            xpool.released_at(sDVE, d1 - (NBN + 1))

            def v_x2(v, xslot=xslot, w=a1, xw=xw):
                v.wait_ge(sACT.h, w)
                pw(v, xw)
                vw(v)
                VA(v.reciprocal(r3_t[:], r3_t[:]))
                vw(v)
                VA(v.tensor_scalar(b3_t[:], mv[:, 0:1], r3_t[:], -1.0,
                                   ALU.mult, ALU.mult))
                vw(v)
                VA(v.tensor_scalar_add(x2b[:], xt[xslot][:], b3_t[:]))
                vw(v)
                VA(v.scalar_tensor_tensor(x2b[:], attnb[:], r3_t[:], x2b[:],
                                          ALU.mult, ALU.add))
                bn_pair(v, x2b[:], st6, mv)
            emit("vector", v_x2)

            a2 = sACT.reserve()

            def s_r2(sc, w=d1):
                sc.wait_ge(sDVE.h, w)
                sw(sc)
                SA(sc.activation(out=r2_t[:], in_=mv[:, 1:2], func=AF.Sqrt,
                                 bias=epsln_t[:]))
            emit("scalar", s_r2)

            d2 = sDVE.reserve(2)

            def v_b2(v, w=a2):
                v.wait_ge(sACT.h, w)
                vw(v)
                VA(v.reciprocal(r2_t[:], r2_t[:]))
                vw(v)
                VA(v.tensor_scalar(b2_t[:], mv[:, 0:1], r2_t[:], -1.0,
                                   ALU.mult, ALU.mult))
            emit("vector", v_b2)

            a3 = sACT.reserve(2)

            def s_ln2(sc, w=d2):
                sc.wait_ge(sDVE.h, w)
                sw(sc)
                SA(sc.activation(out=attnb[:], in_=x2b[:], func=AF.Identity,
                                 scale=r2_t[:], bias=b2_t[:]))
                sw(sc)
                SA(sc.activation(out=lnh[:], in_=attnb[:], func=AF.Copy))
            emit("scalar", s_ln2)

            dlo = sDVE.reserve()

            def v_fin(v, w=a3):
                v.wait_ge(sACT.h, w)
                vw(v)
                VA(v.tensor_tensor(lnl[:], attnb[:], lnh[:], ALU.subtract))
            emit("vector", v_fin)

            p = tc % 2
            nst = 4 if taps else 3
            stw = dst4.reserve(p, nst)

            def g_st(g, tc=tc, p=p, w=dlo):
                g.wait_ge(sDVE.h, w)
                r0 = tc * 128
                dst4.attach(p, g.dma_start(out=s_x2[r0:r0 + 128, :], in_=x2b[:]))
                dst4.attach(p, g.dma_start(out=s_ln2h[r0:r0 + 128, :], in_=lnh[:]))
                dst4.attach(p, g.dma_start(out=s_ln2l[r0:r0 + 128, :], in_=lnl[:]))
                if taps:
                    dst4.attach(p, g.dma_start(out=taph["t_x2"][r0:r0 + 128, :],
                                               in_=x2b[:]))
            emit("gpsimd", g_st)
            emit("scalar", lambda sc, w=stw: pw(sc, w))

        close_scope(sc4)

    p4()

    def p5a():
        phase_barrier()
        sc5 = []
        dmv = DStream("p5mv")
        wa_h = [sbuf(f"wa_h{i}", [128, HA], BF16, sc5) for i in range(NDC)]
        wa_l = [sbuf(f"wa_l{i}", [128, HA], BF16, sc5) for i in range(NDC)]
        wr_h = [sbuf(f"wr_h{i}", [128, E], BF16, sc5) for i in range(NDC)]
        wr_l = [sbuf(f"wr_l{i}", [128, E], BF16, sc5) for i in range(NDC)]
        tmp = []
        wfa = [sbuf(f"p5wfa{i}", [128, HA], F32, tmp) for i in range(2)]
        wfr = [sbuf(f"p5wfr{i}", [128, E], F32, tmp) for i in range(2)]
        dwa = DStream("p5wa")
        wapool = Pool(2, "p5wa", init=(sDVE, sDVE.c))
        for dc in range(NDC):
            slot, wv = wapool.acquire()
            ww = dwa.reserve(slot, 2)

            def ldw(s, dc=dc, slot=slot, wv=wv):
                pw(s, wv)
                dwa.attach(slot, s.dma_start(out=wfa[slot][:],
                                             in_=waT[dc * 128:(dc + 1) * 128, :]))
                dwa.attach(slot, s.dma_start(out=wfr[slot][:],
                                             in_=wrT[dc * 128:(dc + 1) * 128, :]))
            emit("sync", ldw)

            aw = sACT.reserve(2)

            def s_h(sc, dc=dc, slot=slot, w=ww):
                pw(sc, w)
                sw(sc)
                SA(sc.activation(out=wa_h[dc][:], in_=wfa[slot][:], func=AF.Copy))
                sw(sc)
                SA(sc.activation(out=wr_h[dc][:], in_=wfr[slot][:], func=AF.Copy))
            emit("scalar", s_h)

            dvv = sDVE.reserve(2)
            wapool.released_at(sDVE, dvv)

            def v_l(v, dc=dc, slot=slot, w=aw):
                v.wait_ge(sACT.h, w)
                vw(v)
                VA(v.tensor_tensor(wa_l[dc][:], wfa[slot][:], wa_h[dc][:],
                                   ALU.subtract))
                vw(v)
                VA(v.tensor_tensor(wr_l[dc][:], wfr[slot][:], wr_h[dc][:],
                                   ALU.subtract))
            emit("vector", v_l)
        w_rdy = (sDVE.c, sACT.c)
        close_scope(tmp)

        dst5 = DStream("st5", store=True)
        mvpool = Pool(2, "p5mv", init=(sDVE, w_rdy[0]))
        mv_h = [sbuf(f"p5mh{i}", [128, T], BF16, sc5) for i in range(2)]
        mv_l = [sbuf(f"p5ml{i}", [128, T], BF16, sc5) for i in range(2)]
        ps_ha = psum("p5pha", [128, T], sc5)
        ps_lg = psum("p5plg", [8, T], sc5)

        ocs = [(i * 512, min(512, T - i * 512)) for i in range((T + 511) // 512)]
        for dc in range(NDC):
            mslot, mwv = mvpool.acquire()
            trw = dmv.reserve(mslot, 2)

            def tl(sc, dc=dc, mslot=mslot, mwv=mwv):
                pw(sc, mwv)
                c0 = dc * 128
                dmv.attach(mslot, sc.dma_start(out=mv_h[mslot][:],
                                               in_=s_ln2h[:, c0:c0 + 128],
                                               transpose=True))
                dmv.attach(mslot, sc.dma_start(out=mv_l[mslot][:],
                                               in_=s_ln2l[:, c0:c0 + 128],
                                               transpose=True))
            emit("sync", tl)

            mm5v = sMM.reserve()
            mvpool.released_at(sMM, mm5v)

            def mm5(te, dc=dc, mslot=mslot, trw=trw):
                pw(te, trw)
                if dc == 0:
                    te.wait_ge(sDVE.h, w_rdy[0])
                    te.wait_ge(sACT.h, w_rdy[1])
                last = None
                for pi, (lt, rt) in enumerate(
                        ((wa_h[dc][:], mv_h[mslot][:]),
                         (wa_h[dc][:], mv_l[mslot][:]),
                         (wa_l[dc][:], mv_h[mslot][:]))):
                    for (o0, on) in ocs:
                        last = te.matmul(ps_ha[:, o0:o0 + on], lt,
                                         rt[:, o0:o0 + on],
                                         start=(dc == 0 and pi == 0),
                                         stop=(dc == NDC - 1 and pi == 2))
                for pi, (lt, rt) in enumerate(
                        ((wr_h[dc][:], mv_h[mslot][:]),
                         (wr_h[dc][:], mv_l[mslot][:]),
                         (wr_l[dc][:], mv_h[mslot][:]))):
                    for (o0, on) in ocs:
                        last = te.matmul(ps_lg[:, o0:o0 + on], lt,
                                         rt[:, o0:o0 + on],
                                         start=(dc == 0 and pi == 0),
                                         stop=(dc == NDC - 1 and pi == 2))
                sMM.attach(last)
            emit("tensor", mm5)
        mm_done = sMM.c

        haf = sbuf("p5haf", [128, T], F32, sc5)
        lgf = sbuf("p5lgf", [32, T], F32, sc5)
        lgT = sbuf("p5lgT", [128, TC * 32], F32, sc5)
        gateb = sbuf("p5gb", [128, TC * 32], BF16, sc5)
        g8f = sbuf("p5g8f", [32, T], BF16, sc5)
        gfm = sbuf("p5gfm", [128, T], BF16, sc5)
        mx_t = sbuf("p5mx", [128, 1], F32, sc5)

        dz = sDVE.reserve(2)

        def v_zero(v):
            vw(v)
            VA(v.memset(lgf[:], 0.0))
            vw(v)
            VA(v.memset(gateb[:], 0.0))
        emit("vector", v_zero)

        aev = sACT.reserve(2)

        def s_ev(sc, w=mm_done, wz=dz):
            sc.wait_ge(sMM.h, w)
            sc.wait_ge(sDVE.h, wz)
            sw(sc)
            SA(sc.activation(out=haf[:], in_=ps_ha[:], func=AF.Copy))
            sw(sc)
            SA(sc.activation(out=lgf[0:8, :], in_=ps_lg[:], func=AF.Copy))
        emit("scalar", s_ev)

        dgate = sDVE.reserve(2 * (T // 32) + 2 * TC)

        def v_gate(v, w=aev):
            v.wait_ge(sACT.h, w)
            for j in range(T // 32):
                tcj, rj = j // 4, (j % 4) * 32
                vw(v)
                VA(v.transpose(lgT[rj:rj + 32, tcj * 32:tcj * 32 + 32],
                               lgf[0:32, j * 32:(j + 1) * 32]))
            for tc in range(TC):
                vw(v)
                VA(v.reduce_max(mx_t[:], lgT[:, tc * 32:tc * 32 + E], axis=AX))
                vw(v)
                VA(v.tensor_scalar(gateb[:, tc * 32:tc * 32 + E],
                                   lgT[:, tc * 32:tc * 32 + E],
                                   mx_t[:], 2.0, ALU.is_ge, ALU.mult))
            for j in range(T // 32):
                tcj, rj = j // 4, (j % 4) * 32
                vw(v)
                VA(v.transpose(g8f[0:32, j * 32:(j + 1) * 32],
                               gateb[rj:rj + 32, tcj * 32:tcj * 32 + 32]))
        emit("vector", v_gate)

        gexp = dst5.reserve(0, R)

        def g_gate(g, w=dgate):
            g.wait_ge(sDVE.h, w)
            for r in range(R):
                dst5.attach(0, g.dma_start(out=gfm[r * E:(r + 1) * E, :],
                                           in_=g8f[0:E, :]))
        emit("gpsimd", g_gate)

        dhag = sDVE.reserve()

        def v_hag(v, w=gexp):
            pw(v, w)
            vw(v)
            VA(v.tensor_tensor(haf[:], haf[:], gfm[:], ALU.mult))
        emit("vector", v_hag)

        if taps:
            dst5.reserve(1, 2)

            def g_tap(g, w=dhag):
                g.wait_ge(sDVE.h, w)
                dst5.attach(1, g.dma_start(out=taph["t_gate"][:, :], in_=gfm[:]))
                dst5.attach(1, g.dma_start(out=taph["t_ha"][:, :], in_=haf[:]))
            emit("gpsimd", g_tap)

        ah = sACT.reserve()

        def s_hi(sc, w=dhag):
            sc.wait_ge(sDVE.h, w)
            sw(sc)
            SA(sc.activation(out=hag_hi[:], in_=haf[:], func=AF.Copy))
        emit("scalar", s_hi)

        hag_rdy = sDVE.reserve()

        def v_lo(v, w=ah):
            v.wait_ge(sACT.h, w)
            vw(v)
            VA(v.tensor_tensor(hag_lo[:], haf[:], hag_hi[:], ALU.subtract))
        emit("vector", v_lo)
        close_scope(sc5)
        return hag_rdy

    phase_barrier()
    sc56 = []
    hag_hi = sbuf("hag_hi", [128, T], BF16, sc56)
    hag_lo = sbuf("hag_lo", [128, T], BF16, sc56)
    wb_whi = [sbuf(f"wbh{i}", [128, D], BF16, sc56) for i in range(NDC)]
    wb_wlo = [sbuf(f"wbl{i}", [128, D], BF16, sc56) for i in range(NDC)]
    bw_h = sbuf("bw_h", [128, D], BF16, sc56)
    bw_l = sbuf("bw_l", [128, D], BF16, sc56)
    b_hi = sbuf("b_hi", [1, D], BF16, sc56)
    b_lo = sbuf("b_lo", [1, D], BF16, sc56)
    tmp56 = []
    bwf = sbuf("p5bwf", [128, D], F32, tmp56)
    b_f32 = sbuf("p5bf32", [1, D], F32, tmp56)

    bww = sLD.reserve(32)

    def ldbw(s):
        sLD.attach_dma(s.dma_start(out=bwf[:], in_=bwT[:, :]))
        sLD.attach_dma(s.dma_start(out=b_f32[:], in_=b_base[:]))
    emit("sync", ldbw)

    abw = sACT.reserve(2)

    def s_bw(sc, w=bww):
        sc.wait_ge(sLD.h, w)
        sw(sc)
        SA(sc.activation(out=bw_h[:], in_=bwf[:], func=AF.Copy))
        sw(sc)
        SA(sc.activation(out=b_hi[:], in_=b_f32[:], func=AF.Copy))
    emit("scalar", s_bw)

    bw_rdy = sDVE.reserve(2)

    def v_bw(v, w=abw):
        v.wait_ge(sACT.h, w)
        vw(v)
        VA(v.tensor_tensor(bw_l[:], bwf[:], bw_h[:], ALU.subtract))
        vw(v)
        VA(v.tensor_tensor(b_lo[:], b_f32[:], b_hi[:], ALU.subtract))
    emit("vector", v_bw)
    close_scope(tmp56)
    hag_rdy = p5a()
    tmpwb = []
    wb_ready = emit_w_load("wb", wbT, wb_whi, wb_wlo, D, tmpwb,
                           init=(sDVE, max(bw_rdy, hag_rdy)))
    close_scope(tmpwb)

    def p5b():
        phase_barrier()
        sc6 = []
        dstat = DStream("p5s")
        dx2 = DStream("p5x")
        whi, wlo, wready = wb_whi, wb_wlo, wb_ready

        dsto = DStream("sto", store=True)
        statpool = Pool(2, "p5st", init=(sDVE, max(bw_rdy, wready[0])))
        sth = [sbuf(f"p5sh{i}", [128, D], BF16, sc6) for i in range(2)]
        stl = [sbuf(f"p5sl{i}", [128, D], BF16, sc6) for i in range(2)]
        pspool = Pool(2, "p5ps")
        ps5 = [psum(f"p5ps{i}", [128, D], sc6) for i in range(2)]
        x2pool = Pool(2, "p5x2", init=(sDVE, bw_rdy))
        x2t = [sbuf(f"p5x2{i}", [128, D], F32, sc6) for i in range(2)]
        ffb = sbuf("p5ffb", [128, D], F32, sc6)
        outb = sbuf("p5out", [128, D], F32, sc6)
        st6 = sbuf("p5st6", [128, 6 * NBN], F32, sc6)
        mv = sbuf("p5mv", [128, 2], F32, sc6)
        r4_t = sbuf("p5r4", [128, 1], F32, sc6)
        b4_t = sbuf("p5b4", [128, 1], F32, sc6)

        for tc in range(TC):
            sslot, swv = statpool.acquire()
            trw = dstat.reserve(sslot, 2)

            def tl(sc, tc=tc, sslot=sslot, swv=swv):
                pw(sc, swv)
                r0 = tc * 128
                src_h = s_ln2h[r0:r0 + 128, :].rearrange("t (dc d) -> t dc d", d=128)
                src_l = s_ln2l[r0:r0 + 128, :].rearrange("t (dc d) -> t dc d", d=128)
                dst_h = sth[sslot][:].rearrange("p (dc t) -> p dc t", t=128)
                dst_l = stl[sslot][:].rearrange("p (dc t) -> p dc t", t=128)
                dstat.attach(sslot, sc.dma_start(out=dst_h, in_=src_h, transpose=True))
                dstat.attach(sslot, sc.dma_start(out=dst_l, in_=src_l, transpose=True))
            emit("sync", tl)

            xslot, xwv = x2pool.acquire()
            xw = dx2.reserve(xslot)

            def ldx(s, tc=tc, xslot=xslot, xwv=xwv):
                pw(s, xwv)
                r0 = tc * 128
                dx2.attach(xslot, s.dma_start(out=x2t[xslot][:],
                                              in_=s_x2[r0:r0 + 128, :]))
            emit("sync", ldx)

            pslot, pwv = pspool.acquire()
            mmv = sMM.reserve()
            statpool.released_at(sMM, mmv)
            dvw = max(wready[0], bw_rdy, hag_rdy)

            def mms(te, tc=tc, sslot=sslot, pslot=pslot, pwv=pwv, trw=trw,
                    dvw=dvw):
                pw(te, pwv)
                pw(te, trw)
                te.wait_ge(sDVE.h, dvw)
                te.wait_ge(sACT.h, wready[1])
                groups = [(sth[sslot][:, dc * 128:(dc + 1) * 128],
                           stl[sslot][:, dc * 128:(dc + 1) * 128],
                           whi[dc][:], wlo[dc][:]) for dc in range(NDC)]
                groups.append((hag_hi[:, tc * 128:(tc + 1) * 128],
                               hag_lo[:, tc * 128:(tc + 1) * 128],
                               bw_h[:], bw_l[:]))
                groups.append((ones_bf[:], None, b_hi[:], b_lo[:]))
                sMM.attach(emit_mm_tc(te, ps5[pslot][:], D, groups))
            emit("tensor", mms)
            ps = ps5[pslot]

            a0 = sACT.reserve()
            pspool.released_at(sACT, a0)

            def s_c(sc, ps=ps, w=mmv):
                sc.wait_ge(sMM.h, w)
                sw(sc)
                SA(sc.activation(out=ffb[:], in_=ps[:], func=AF.Copy))
            emit("scalar", s_c)

            d0 = sDVE.reserve(NBN + 1)

            def v_st(v, w=a0):
                v.wait_ge(sACT.h, w)
                bn_pair(v, ffb[:], st6, mv)
            emit("vector", v_st)

            a1 = sACT.reserve()

            def s_r4(sc, w=d0):
                sc.wait_ge(sDVE.h, w)
                sw(sc)
                SA(sc.activation(out=r4_t[:], in_=mv[:, 1:2], func=AF.Sqrt,
                                 bias=epsln_t[:]))
            emit("scalar", s_r4)

            dout_ = sDVE.reserve(4)
            x2pool.released_at(sDVE, dout_)

            def v_out(v, xslot=xslot, w=a1, xw=xw):
                v.wait_ge(sACT.h, w)
                pw(v, xw)
                vw(v)
                VA(v.reciprocal(r4_t[:], r4_t[:]))
                vw(v)
                VA(v.tensor_scalar(b4_t[:], mv[:, 0:1], r4_t[:], -1.0,
                                   ALU.mult, ALU.mult))
                vw(v)
                VA(v.tensor_scalar_add(outb[:], x2t[xslot][:], b4_t[:]))
                vw(v)
                VA(v.scalar_tensor_tensor(outb[:], ffb[:], r4_t[:], outb[:],
                                          ALU.mult, ALU.add))
            emit("vector", v_out)

            p = tc % 2
            ow = dsto.reserve(p)

            def g_out(g, tc=tc, p=p, w=dout_):
                g.wait_ge(sDVE.h, w)
                r0 = tc * 128
                dsto.attach(p, g.dma_start(out=y[r0:r0 + 128, :], in_=outb[:]))
            emit("gpsimd", g_out)
            emit("scalar", lambda sc, w=ow: pw(sc, w))
            emit("vector", lambda v, w=ow: pw(v, w))

        close_scope(sc6)

    p5b()
    close_scope(sc56)

    fin_waits = [(c, c.c) for c in STORE_CTRS if c.c > 0]

    def fin(g, ws=fin_waits):
        for c, v in ws:
            g.wait_ge(c.h, v)
    emit("gpsimd", fin)

    with nc.Block() as block:
        @block.sync
        def _(sync):
            for f in ops["sync"]:
                f(sync)

        @block.scalar
        def _(scalar):
            for f in ops["scalar"]:
                f(scalar)

        @block.vector
        def _(vector):
            for f in ops["vector"]:
                f(vector)

        @block.tensor
        def _(tensor):
            for f in ops["tensor"]:
                f(tensor)

        @block.gpsimd
        def _(gpsimd):
            for f in ops["gpsimd"]:
                f(gpsimd)

    for c in all_ctrs:
        assert c.att == c.c, f"sem {c.name}: attached {c.att} != reserved {c.c}"

    close_scope(glob)
    return nc


def stage_core_inputs(cfg, core_id, x_full, w_qkv, q_norm_w, k_norm_w, w_o,
                      W_base, b_base_v, router_W, A, Bw):
    D = cfg["D"]; NH = cfg["NH"]
    T = cfg["T_OWN"]; TH = cfg["T_HALO"]; E = cfg["E"]; R = cfg["R"]
    KC = (T + TH) // 128
    NJ = (D // NH) // 2
    b, half = core_id // 2, core_id % 2

    own0 = half * T
    x_own = np.ascontiguousarray(x_full[b, own0:own0 + T, :], dtype=np.float32)
    if half >= 1:
        x_halo = np.ascontiguousarray(x_full[b, own0 - TH:own0, :],
                                      dtype=np.float32)
        posv = np.arange(own0 - TH, own0 + T, dtype=np.float32)
        mv = np.zeros((128, TH), dtype=np.float32)
    else:
        x_halo = np.zeros((TH, D), dtype=np.float32)
        posv = np.concatenate([np.zeros(TH, dtype=np.float32),
                               np.arange(0, T, dtype=np.float32)])
        mv = np.full((128, TH), MASKVAL, dtype=np.float32)

    xh = np.concatenate([x_halo, x_own], axis=0)
    posr = np.ascontiguousarray(posv.reshape(KC, 128).T)

    i = np.arange(128)[:, None]
    j = np.arange(128)[None, :]
    mW_ = np.where(j >= i + 1, 0.0, MASKVAL).astype(np.float32)
    mC_ = np.where(j <= i, 0.0, MASKVAL).astype(np.float32)
    jr = np.arange(NJ, dtype=np.float32)[None, :]

    return {
        "xh": xh,
        "wqT": np.ascontiguousarray(w_qkv[0:D, :].T, dtype=np.float32),
        "wkT": np.ascontiguousarray(w_qkv[D:2 * D, :].T, dtype=np.float32),
        "wvT": np.ascontiguousarray(w_qkv[2 * D:3 * D, :].T, dtype=np.float32),
        "woT": np.ascontiguousarray(w_o.T, dtype=np.float32),
        "wbT": np.ascontiguousarray(W_base.T, dtype=np.float32),
        "waT": np.ascontiguousarray(A.transpose(2, 1, 0).reshape(D, R * E)),
        "wrT": np.ascontiguousarray(router_W.T, dtype=np.float32),
        "bwT": np.ascontiguousarray(Bw.transpose(2, 0, 1).reshape(R * E, D)),
        "b_base": np.ascontiguousarray(b_base_v.reshape(1, D), dtype=np.float32),
        "qnw": np.ascontiguousarray(q_norm_w.reshape(1, D), dtype=np.float32),
        "knw": np.ascontiguousarray(k_norm_w.reshape(1, D), dtype=np.float32),
        "posr": posr,
        "jrow": np.ascontiguousarray(jr, dtype=np.float32),
        "mW": mW_, "mC": mC_, "mV": mv,
        "ones_row": np.ones((1, 128), dtype=np.float32),
        "ident": np.eye(128, dtype=np.float32),
    }


def kernel(L0_attn_hidden_states, L0_attn_w_qkv, L0_attn_q_norm_w,
           L0_attn_k_norm_w, L0_attn_w_o, L0_ffn_W_base, L0_ffn_b_base,
           L0_ffn_router_W, L0_ffn_A, L0_ffn_Bw):
    cfg = real_cfg()
    x = np.asarray(L0_attn_hidden_states, dtype=np.float32)
    args = (x, np.asarray(L0_attn_w_qkv, np.float32),
            np.asarray(L0_attn_q_norm_w, np.float32),
            np.asarray(L0_attn_k_norm_w, np.float32),
            np.asarray(L0_attn_w_o, np.float32),
            np.asarray(L0_ffn_W_base, np.float32),
            np.asarray(L0_ffn_b_base, np.float32),
            np.asarray(L0_ffn_router_W, np.float32),
            np.asarray(L0_ffn_A, np.float32),
            np.asarray(L0_ffn_Bw, np.float32))
    in_maps = [stage_core_inputs(cfg, c, *args) for c in range(8)]
    nc = build_nc(cfg)
    res = run_bass_kernel_spmd(nc, in_maps, core_ids=list(range(8)))
    B, S, D = x.shape
    out = np.empty((B, S, D), dtype=np.float32)
    T = cfg["T_OWN"]
    for c in range(8):
        b, half = c // 2, c % 2
        out[b, half * T:(half + 1) * T, :] = res.results[c]["y"]
    return out


# version 28 (final)
# speedup vs baseline: 1.4316x; 1.0731x over previous
import math
import numpy as np
import concourse.bass as bass
import concourse.mybir as mybir
from concourse.bass_utils import run_bass_kernel_spmd

F32 = mybir.dt.float32
BF16 = mybir.dt.bfloat16
AF = mybir.ActivationFunctionType
ALU = mybir.AluOpType
AX = mybir.AxisListType.X

MASKVAL = -30000.0
EPS_LN = 1e-5
EPS_RMS = 1e-6


def real_cfg():
    return dict(D=2048, NH=16, WINDOW=1024, T_OWN=2048, T_HALO=1024,
                E=8, R=16, debug_taps=False)


def mini_cfg():
    return dict(D=256, NH=2, WINDOW=256, T_OWN=512, T_HALO=256,
                E=8, R=16, debug_taps=True)


class Ctr:
    def __init__(self, handle, name):
        self.h = handle
        self.name = name
        self.c = 0
        self.att = 0

    def reserve(self, n=1):
        self.c += n
        return self.c

    def reserve_dma(self):
        return self.reserve(16)

    def attach(self, instr, n=1):
        self.att += n
        instr.then_inc(self.h, n)
        return instr

    def attach_dma(self, instr):
        return self.attach(instr, 16)


class Pool:

    def __init__(self, nbuf, name, init=None):
        self.nbuf = nbuf
        self.name = name
        self.uses = 0
        self.init = init
        self.rel = []

    def acquire(self):
        u = self.uses
        self.uses += 1
        if u >= self.nbuf:
            assert len(self.rel) > u - self.nbuf, \
                f"pool {self.name}: use {u} before release {u - self.nbuf}"
            w = self.rel[u - self.nbuf]
        else:
            w = self.init
        return u % self.nbuf, w

    def released_at(self, ctr, value):
        self.rel.append((ctr, value))


def build_nc(cfg):
    D = cfg["D"]; NH = cfg["NH"]; W = cfg["WINDOW"]
    T = cfg["T_OWN"]; TH = cfg["T_HALO"]; E = cfg["E"]; R = cfg["R"]
    HA = E * R
    HD = D // NH
    assert HD == 128 and HA == 128 and W == TH
    NDC = D // 128
    TC = T // 128
    HC = TH // 128
    KC = TC + HC
    TKV = T + TH
    WKC = W // 128
    KSPAN = WKC + 1
    KW = KSPAN * 128
    KWP = ((KW + 511) // 512) * 512
    NJ = HD // 2
    taps = cfg["debug_taps"]
    QG = min(512, T)
    NQG = T // QG
    QGC = QG // 128
    CSC = 1.0 / math.sqrt(HD)

    nc = bass.Bass("TRN2", target_bir_lowering=False, debug=False)

    def din(name, shape, dt=F32):
        return nc.dram_tensor(name, list(shape), dt, kind="ExternalInput").ap()

    def dscr(name, shape, dt):
        return nc.dram_tensor(name, list(shape), dt, kind="Internal").ap()

    def dout(name, shape, dt=F32):
        return nc.dram_tensor(name, list(shape), dt, kind="ExternalOutput").ap()

    xh = din("xh", [TKV, D])
    wqT = din("wqT", [D, D]); wkT = din("wkT", [D, D]); wvT = din("wvT", [D, D])
    woT = din("woT", [D, D]); wbT = din("wbT", [D, D])
    waT = din("waT", [D, HA])
    wrT = din("wrT", [D, E])
    bwT = din("bwT", [HA, D])
    b_base = din("b_base", [1, D])
    qnw = din("qnw", [1, D]); knw = din("knw", [1, D])
    posr = din("posr", [128, KC])
    jrow = din("jrow", [1, NJ])
    mW = din("mW", [128, 128]); mC = din("mC", [128, 128])
    mV = din("mV", [128, TH])
    ones_row = din("ones_row", [1, 128])
    ident = din("ident", [128, 128])

    s_ln1h = dscr("s_ln1h", [TKV, D], BF16); s_ln1l = dscr("s_ln1l", [TKV, D], BF16)
    s_qh = dscr("s_qh", [T, D], BF16); s_ql = dscr("s_ql", [T, D], BF16)
    s_kh = dscr("s_kh", [TKV, D], BF16); s_kl = dscr("s_kl", [TKV, D], BF16)
    s_vh = dscr("s_vh", [TKV, D], BF16); s_vl = dscr("s_vl", [TKV, D], BF16)
    s_aoh = dscr("s_aoh", [D, T], BF16); s_aol = dscr("s_aol", [D, T], BF16)
    s_x2 = dscr("s_x2", [T, D], F32)
    s_ln2h = dscr("s_ln2h", [T, D], BF16); s_ln2l = dscr("s_ln2l", [T, D], BF16)

    y = dout("y", [T, D])
    taph = {}
    if taps:
        for nm, shp in (("t_ln1", [TKV, D]), ("t_q", [T, D]), ("t_k", [TKV, D]),
                        ("t_v", [TKV, D]), ("t_ao", [D, T]), ("t_x2", [T, D]),
                        ("t_gate", [128, T]), ("t_ha", [128, T])):
            taph[nm] = dout(nm, shp)

    ops = {"sync": [], "scalar": [], "vector": [], "tensor": [], "gpsimd": []}

    def emit(eng, fn):
        ops[eng].append(fn)

    glob = []

    def sbuf(name, shape, dt, scope):
        g = nc.sbuf_tensor(name, list(shape), dt)
        t = g.__enter__()
        scope.append(g)
        return t

    def psum(name, shape, scope, dt=F32):
        g = nc.psum_tensor(name, list(shape), dt)
        t = g.__enter__()
        scope.append(g)
        return t

    def close_scope(scope):
        for g in reversed(scope):
            g.__exit__(None, None, None)
        scope.clear()

    all_ctrs = []

    def sem(name):
        g = nc.semaphore(name)
        h = g.__enter__()
        glob.append(g)
        c = Ctr(h, name)
        all_ctrs.append(c)
        return c

    sLD = sem("sLD")
    sST = sem("sST")
    sMM = sem("sMM")
    sACT = sem("sACT")
    sDVE = sem("sDVE")
    sOUT = sem("sOUT")

    STORE_CTRS = []

    class DStream:

        def __init__(self, name, n=2, store=False):
            self.s = [sem(f"{name}{i}") for i in range(n)]
            if store:
                STORE_CTRS.extend(self.s)

        def reserve(self, slot, ndma=1):
            return (self.s[slot], self.s[slot].reserve(16 * ndma))

        def attach(self, slot, instr):
            return self.s[slot].attach(instr, 16)

    def phase_barrier():
        waits = [(c, c.c) for c in STORE_CTRS if c.c > 0]

        def b(e, ws=waits):
            for c, v in ws:
                e.wait_ge(c.h, v)
        emit("sync", b)
        emit("scalar", b)

    def SA(instr):
        return sACT.attach(instr)

    def VA(instr):
        return sDVE.attach(instr)

    def sw(sc):
        sc.wait_ge(sACT.h, sACT.att)

    def vw(v):
        v.wait_ge(sDVE.h, sDVE.att)

    def pw(eng, w):
        if w is not None:
            eng.wait_ge(w[0].h, w[1])

    NBN = (D + 511) // 512

    def bn_pair(v, srcbuf, st6, mv):
        for c in range(NBN):
            vw(v)
            VA(v.bn_stats(st6[:, c * 6:(c + 1) * 6],
                          srcbuf[:, c * 512:min((c + 1) * 512, D)]))
        vw(v)
        VA(v.bn_aggr(mv[:], st6[:, 0:6 * NBN]))

    ones_bf = sbuf("ones_bf", [1, 128], BF16, glob)
    epsln_t = sbuf("epsln_t", [128, 1], F32, glob)
    rmssc_t = sbuf("rmssc_t", [128, 1], F32, glob)
    rmseps_t = sbuf("rmseps_t", [128, 1], F32, glob)
    zro_bf = sbuf("zro_bf", [1, 512], BF16, glob)
    qsc_t = sbuf("qsc_t", [128, 1], F32, glob)
    qeps_t = sbuf("qeps_t", [128, 1], F32, glob)
    sc03 = []
    mW_t = sbuf("mW_t", [128, 128], F32, sc03)
    mC_t = sbuf("mC_t", [128, 128], F32, sc03)
    mV_t = sbuf("mV_t", [128, TH], F32, sc03)
    id_bf = sbuf("id_bf", [128, 128], BF16, sc03)
    mW_b = sbuf("mW_b", [128, 128], BF16, sc03)
    mC_b = sbuf("mC_b", [128, 128], BF16, sc03)
    mV_b = sbuf("mV_b", [128, TH], BF16, sc03)
    idf_t = sbuf("idf_t", [128, 128], F32, sc03)
    onesr_b = sbuf("onesr_b", [1, 512], BF16, sc03)
    ones_kv = sbuf("ones_kv", [128, 128], BF16, sc03)
    sc02 = []
    cos_t = sbuf("cos_t", [128, KC * NJ], F32, sc02)
    sin_t = sbuf("sin_t", [128, KC * NJ], F32, sc02)

    sDVE.reserve(8)

    def v_consts(v):
        VA(v.memset(zro_bf[:], 0.0))
        VA(v.memset(epsln_t[:], EPS_LN))
        VA(v.memset(rmssc_t[:], 1.0 / D))
        VA(v.memset(rmseps_t[:], EPS_RMS))
        VA(v.memset(qsc_t[:], float(HD) / D))
        VA(v.memset(qeps_t[:], float(HD) * EPS_RMS))
        VA(v.memset(onesr_b[:], 1.0))
        VA(v.memset(ones_kv[:], 1.0))
    emit("vector", v_consts)

    def p0():
        sc0 = []
        post = sbuf("post", [128, KC], F32, sc0)
        jbc = sbuf("jbc", [128, NJ], F32, sc0)
        ones_f = sbuf("ones_f", [1, 128], F32, sc0)
        r_ang = sbuf("r_ang", [128, KC * NJ], F32, sc0)
        r_s1 = sbuf("r_s1", [128, KC * NJ], F32, sc0)
        r_s2 = sbuf("r_s2", [128, KC * NJ], F32, sc0)
        r_s3 = sbuf("r_s3", [128, KC * NJ], F32, sc0)
        c_a = sbuf("c_a", [128, 1], F32, sc0)

        w0 = [sLD.reserve_dma() for _ in range(7)][-1]

        def ld(s):
            sLD.attach_dma(s.dma_start(out=ones_f[:], in_=ones_row[:]))
            sLD.attach_dma(s.dma_start(out=idf_t[:], in_=ident[:]))
            sLD.attach_dma(s.dma_start(out=post[:], in_=posr[:]))
            sLD.attach_dma(s.dma_start(out=jbc[:], in_=jrow[:].broadcast_to((128, NJ))))
            sLD.attach_dma(s.dma_start(out=mW_t[:], in_=mW[:]))
            sLD.attach_dma(s.dma_start(out=mC_t[:], in_=mC[:]))
            sLD.attach_dma(s.dma_start(out=mV_t[:], in_=mV[:]))
        emit("sync", ld)

        a_if = sACT.reserve(9)

        def sc_c(sc):
            sc.wait_ge(sLD.h, w0)
            SA(sc.activation(out=ones_bf[:], in_=ones_f[:], func=AF.Copy))
            SA(sc.activation(out=id_bf[:], in_=idf_t[:], func=AF.Copy))
            SA(sc.activation(out=mW_b[:], in_=mW_t[:], func=AF.Copy))
            SA(sc.activation(out=mC_b[:], in_=mC_t[:], func=AF.Copy))
            SA(sc.activation(out=mV_b[:], in_=mV_t[:], func=AF.Copy))
            SA(sc.activation(out=c_a[:], in_=jbc[:, 0:1], func=AF.Copy,
                             scale=0.0, bias=10000.0))
            sw(sc)
            SA(sc.activation(out=c_a[:], in_=c_a[:], func=AF.Ln))
            sw(sc)
            SA(sc.mul(c_a[:], c_a[:], -1.0 / NJ))
            sw(sc)
            SA(sc.activation(out=r_s1[:, 0:NJ], in_=jbc[:], func=AF.Exp,
                             scale=c_a[:]))
        emit("scalar", sc_c)

        mmax = max(0, int(math.ceil(math.log2(max(TKV, 7) / (2 * math.pi)))))
        sDVE.reserve(1 + 3 * (mmax + 1))

        def v_ang(v):
            v.wait_ge(sACT.h, a_if)
            ang = r_ang[:].rearrange("p (kc j) -> p kc j", j=NJ)
            vw(v)
            VA(v.tensor_tensor(
                ang,
                post[:].unsqueeze(2).broadcast_to((128, KC, NJ)),
                r_s1[:, 0:NJ].unsqueeze(1).broadcast_to((128, KC, NJ)),
                ALU.mult))
            for m in range(mmax, -1, -1):
                c = 2 * math.pi * (2 ** m)
                vw(v)
                VA(v.tensor_single_scalar(r_s1[:], r_ang[:], c, ALU.is_ge))
                vw(v)
                VA(v.tensor_scalar(r_s1[:], r_s1[:], c, None, ALU.mult))
                vw(v)
                VA(v.tensor_tensor(r_ang[:], r_ang[:], r_s1[:], ALU.subtract))
        emit("vector", v_ang)

        def make_table(shift, out_tab):
            dv = sDVE.reserve(14)

            def vf(v):
                vw(v)
                VA(v.tensor_scalar(r_s1[:], r_ang[:], shift, None, ALU.add))
                vw(v)
                VA(v.tensor_single_scalar(r_s2[:], r_s1[:], 2 * math.pi, ALU.is_ge))
                vw(v)
                VA(v.tensor_scalar(r_s2[:], r_s2[:], 2 * math.pi, None, ALU.mult))
                vw(v)
                VA(v.tensor_tensor(r_s1[:], r_s1[:], r_s2[:], ALU.subtract))
                vw(v)
                VA(v.tensor_scalar(r_s1[:], r_s1[:], math.pi, None, ALU.subtract))
                vw(v)
                VA(v.tensor_tensor(r_s2[:], r_s1[:], r_s1[:], ALU.mult))
                vw(v)
                VA(v.tensor_single_scalar(r_s2[:], r_s2[:],
                                          (math.pi / 2) ** 2, ALU.is_ge))
                vw(v)
                VA(v.tensor_single_scalar(r_s3[:], r_s1[:], 0.0, ALU.is_ge))
                vw(v)
                VA(v.tensor_scalar(r_s3[:], r_s3[:], 2.0, -1.0, ALU.mult, ALU.add))
                vw(v)
                VA(v.tensor_tensor(r_s3[:], r_s3[:], r_s2[:], ALU.mult))
                vw(v)
                VA(v.tensor_scalar(r_s3[:], r_s3[:], math.pi, None, ALU.mult))
                vw(v)
                VA(v.tensor_scalar(r_s2[:], r_s2[:], -2.0, 1.0, ALU.mult, ALU.add))
                vw(v)
                VA(v.tensor_tensor(r_s1[:], r_s1[:], r_s2[:], ALU.mult))
                vw(v)
                VA(v.tensor_tensor(r_s1[:], r_s1[:], r_s3[:], ALU.add))
            emit("vector", vf)
            av = sACT.reserve()

            def sf(sc):
                sc.wait_ge(sDVE.h, dv)
                sw(sc)
                SA(sc.activation(out=out_tab[:], in_=r_s1[:], func=AF.Sin,
                                 scale=-1.0))
            emit("scalar", sf)
            emit("vector", lambda v, w=av: v.wait_ge(sACT.h, w))

        make_table(0.0, sin_t)
        make_table(math.pi / 2, cos_t)
        fin = sACT.c
        emit("sync", lambda s, w=fin: s.wait_ge(sACT.h, w))
        close_scope(sc0)

    p0()

    def p1():
        phase_barrier()
        sc1 = []
        NP1 = 3
        dsx = DStream("p1x", n=NP1)
        dst1 = DStream("st1", n=NP1, store=True)
        xpool = Pool(NP1, "p1x")
        x_in = [sbuf(f"p1x{i}", [128, D], F32, sc1) for i in range(NP1)]
        lnf = [sbuf(f"p1lnf{i}", [128, D], F32, sc1) for i in range(NP1)]
        lnh = [sbuf(f"p1lnh{i}", [128, D], BF16, sc1) for i in range(NP1)]
        lnl = [sbuf(f"p1lnl{i}", [128, D], BF16, sc1) for i in range(NP1)]
        st6 = [sbuf(f"p1st6{i}", [128, 6 * NBN], F32, sc1) for i in range(NP1)]
        mv = [sbuf(f"p1mv{i}", [128, 2], F32, sc1) for i in range(NP1)]
        r_t = [sbuf(f"p1r{i}", [128, 1], F32, sc1) for i in range(NP1)]
        b_t = [sbuf(f"p1b{i}", [128, 1], F32, sc1) for i in range(NP1)]
        stw_par = [None] * NP1

        for t in range(KC):
            p = t % NP1
            slot, wv = xpool.acquire()
            xw = dsx.reserve(slot)

            def ld(s, t=t, slot=slot, wv=wv):
                pw(s, wv)
                dsx.attach(slot, s.dma_start(out=x_in[slot][:],
                                             in_=xh[t * 128:(t + 1) * 128, :]))
            emit("sync", ld)

            spw = stw_par[p]
            d0 = sDVE.reserve(NBN + 1)

            def v_st(v, slot=slot, p=p, w=xw, spw=spw):
                pw(v, w)
                pw(v, spw)
                bn_pair(v, x_in[slot][:], st6[p], mv[p])
            emit("vector", v_st)

            a0 = sACT.reserve()

            def s_r(sc, p=p, w=d0):
                sc.wait_ge(sDVE.h, w)
                sw(sc)
                SA(sc.activation(out=r_t[p][:], in_=mv[p][:, 1:2], func=AF.Sqrt,
                                 bias=epsln_t[:]))
            emit("scalar", s_r)

            d1 = sDVE.reserve(2)

            def v_b(v, p=p, w=a0):
                v.wait_ge(sACT.h, w)
                vw(v)
                VA(v.reciprocal(r_t[p][:], r_t[p][:]))
                vw(v)
                VA(v.tensor_scalar(b_t[p][:], mv[p][:, 0:1], r_t[p][:], -1.0,
                                   ALU.mult, ALU.mult))
            emit("vector", v_b)

            a1 = sACT.reserve(2)
            xpool.released_at(sACT, a1 - 1)

            def s_ln(sc, slot=slot, p=p, w=d1):
                sc.wait_ge(sDVE.h, w)
                sw(sc)
                SA(sc.activation(out=lnf[p][:], in_=x_in[slot][:],
                                 func=AF.Identity,
                                 scale=r_t[p][:], bias=b_t[p][:]))
                sw(sc)
                SA(sc.activation(out=lnh[p][:], in_=lnf[p][:], func=AF.Copy))
            emit("scalar", s_ln)

            dlo = sDVE.reserve()

            def v_fin(v, p=p, w=a1):
                v.wait_ge(sACT.h, w)
                vw(v)
                VA(v.tensor_tensor(lnl[p][:], lnf[p][:], lnh[p][:],
                                   ALU.subtract))
            emit("vector", v_fin)

            nst = 3 if taps else 2
            stw = dst1.reserve(p, nst)
            stw_par[p] = stw

            def g_st(g, t=t, p=p, w=dlo):
                g.wait_ge(sDVE.h, w)
                dst1.attach(p, g.dma_start(out=s_ln1h[t * 128:(t + 1) * 128, :],
                                           in_=lnh[p][:]))
                dst1.attach(p, g.dma_start(out=s_ln1l[t * 128:(t + 1) * 128, :],
                                           in_=lnl[p][:]))
                if taps:
                    dst1.attach(p, g.dma_start(
                        out=taph["t_ln1"][t * 128:(t + 1) * 128, :],
                        in_=lnf[p][:]))
            emit("gpsimd", g_st)

        close_scope(sc1)

    p1()

    def load_w_resident(pref, src, ncols, scope):
        whi = [sbuf(f"{pref}h{i}", [128, ncols], BF16, scope) for i in range(NDC)]
        wlo = [sbuf(f"{pref}l{i}", [128, ncols], BF16, scope) for i in range(NDC)]
        tmp = []
        nws = 4
        wsl = [sbuf(f"{pref}t{i}", [128, ncols], F32, tmp) for i in range(nws)]
        dsw = DStream(pref + "d", n=nws)
        pl = Pool(nws, pref)
        for dc in range(NDC):
            slot, wv = pl.acquire()
            ww = dsw.reserve(slot)

            def ld(s, dc=dc, slot=slot, wv=wv):
                pw(s, wv)
                dsw.attach(slot, s.dma_start(out=wsl[slot][:],
                                             in_=src[dc * 128:(dc + 1) * 128, :]))
            emit("sync", ld)

            aw = sACT.reserve()

            def s_h(sc, dc=dc, slot=slot, w=ww):
                pw(sc, w)
                sw(sc)
                SA(sc.activation(out=whi[dc][:], in_=wsl[slot][:], func=AF.Copy))
            emit("scalar", s_h)

            dv = sDVE.reserve()
            pl.released_at(sDVE, dv)

            def v_l(v, dc=dc, slot=slot, w=aw):
                v.wait_ge(sACT.h, w)
                vw(v)
                VA(v.tensor_tensor(wlo[dc][:], wsl[slot][:], whi[dc][:],
                                   ALU.subtract))
            emit("vector", v_l)
        ready = (sDVE.c, sACT.c)
        close_scope(tmp)
        return whi, wlo, ready

    def emit_mm_tc(te, ps_ap, n_out, groups):
        ocs = [(i * 512, min(512, n_out - i * 512))
               for i in range((n_out + 511) // 512)]
        plan = []
        for (lh, ll, rh, rl) in groups:
            plan.append((lh, rh))
            plan.append((lh, rl))
            if ll is not None:
                plan.append((ll, rh))
        last = None
        first = [True] * len(ocs)
        for pi, (lt, rt) in enumerate(plan):
            for oi, (o0, on) in enumerate(ocs):
                st = first[oi]
                first[oi] = False
                last = te.matmul(ps_ap[:, o0:o0 + on], lt, rt[:, o0:o0 + on],
                                 start=st, stop=(pi == len(plan) - 1))
        return last

    def p2_og(og):
        phase_barrier()
        sc2 = []
        dstat = DStream(f"p2{og}s")
        dst2 = DStream(f"st2{og}", store=True)
        wsrc = {"q": wqT, "k": wkT, "v": wvT}[og]
        whi, wlo, wready = load_w_resident(f"w{og}", wsrc, D, sc2)

        wrep_w = None
        if og in ("q", "k"):
            wrep = sbuf(f"p2{og}rep", [128, D], F32, sc2)
            wrep_w = sLD.reserve_dma()

            def ldrep(s, og=og, w=wready[0]):
                s.wait_ge(sDVE.h, w)
                wsrc2 = qnw if og == "q" else knw
                sLD.attach_dma(s.dma_start(out=wrep[:],
                                           in_=wsrc2[:].broadcast_to((128, D))))
            emit("sync", ldrep)

        statpool = Pool(2, f"p2{og}st", init=(sDVE, wready[0]))
        sth = [sbuf(f"p2{og}sh{i}", [128, D], BF16, sc2) for i in range(2)]
        stl = [sbuf(f"p2{og}sl{i}", [128, D], BF16, sc2) for i in range(2)]
        pspool = Pool(2, f"p2{og}ps")
        ps2 = [psum(f"p2{og}ps{i}", [128, D], sc2) for i in range(2)]
        qn = sbuf(f"p2{og}qn", [128, D], F32, sc2)
        rr = sbuf(f"p2{og}rr", [128, D], F32, sc2)
        ohi = sbuf(f"p2{og}oh", [128, D], BF16, sc2)
        olo = sbuf(f"p2{og}ol", [128, D], BF16, sc2)
        sq_t = sbuf(f"p2{og}sq", [128, 1], F32, sc2)
        rq_t = sbuf(f"p2{og}rq", [128, 1], F32, sc2)

        tcs = list(range(HC, KC)) if og == "q" else list(range(KC))
        dsth, dstl = {"q": (s_qh, s_ql), "k": (s_kh, s_kl),
                      "v": (s_vh, s_vl)}[og]
        tapn = {"q": "t_q", "k": "t_k", "v": "t_v"}[og]

        for tc in tcs:
            sslot, swv = statpool.acquire()
            trw = dstat.reserve(sslot, 2)

            def tl(sc, tc=tc, sslot=sslot, swv=swv):
                pw(sc, swv)
                r0 = tc * 128
                src_h = s_ln1h[r0:r0 + 128, :].rearrange("t (dc d) -> t dc d", d=128)
                src_l = s_ln1l[r0:r0 + 128, :].rearrange("t (dc d) -> t dc d", d=128)
                dst_h = sth[sslot][:].rearrange("p (dc t) -> p dc t", t=128)
                dst_l = stl[sslot][:].rearrange("p (dc t) -> p dc t", t=128)
                dstat.attach(sslot, sc.dma_start(out=dst_h, in_=src_h, transpose=True))
                dstat.attach(sslot, sc.dma_start(out=dst_l, in_=src_l, transpose=True))
            emit("sync", tl)

            pslot, pwv = pspool.acquire()
            mmv = sMM.reserve()
            statpool.released_at(sMM, mmv)

            def mms(te, sslot=sslot, pslot=pslot, pwv=pwv, trw=trw):
                pw(te, pwv)
                pw(te, trw)
                te.wait_ge(sDVE.h, wready[0])
                te.wait_ge(sACT.h, wready[1])
                groups = [(sth[sslot][:, dc * 128:(dc + 1) * 128],
                           stl[sslot][:, dc * 128:(dc + 1) * 128],
                           whi[dc][:], wlo[dc][:]) for dc in range(NDC)]
                sMM.attach(emit_mm_tc(te, ps2[pslot][:], D, groups))
            emit("tensor", mms)
            ps = ps2[pslot]
            row0 = (tc - HC) * 128 if og == "q" else tc * 128

            if og in ("q", "k"):
                a0 = sACT.reserve(2)

                def s_e(sc, og=og, ps=ps, w=mmv):
                    sc.wait_ge(sMM.h, w)
                    sw(sc)
                    SA(sc.activation(out=rr[:], in_=ps[:], func=AF.Square,
                                     accum_out=sq_t[:]))
                    sw(sc)
                    if og == "q":
                        SA(sc.activation(out=rq_t[:], in_=sq_t[:], func=AF.Sqrt,
                                         scale=qsc_t[:], bias=qeps_t[:]))
                    else:
                        SA(sc.activation(out=rq_t[:], in_=sq_t[:], func=AF.Sqrt,
                                         scale=rmssc_t[:], bias=rmseps_t[:]))
                emit("scalar", s_e)

                d0 = sDVE.reserve(8)
                pspool.released_at(sDVE, d0 - 6)

                def v_e(v, tc=tc, ps=ps, w=a0, ww=wrep_w):
                    v.wait_ge(sACT.h, w)
                    v.wait_ge(sLD.h, ww)
                    vw(v)
                    VA(v.reciprocal(rq_t[:], rq_t[:]))
                    vw(v)
                    VA(v.scalar_tensor_tensor(qn[:], ps[:], rq_t[:], wrep[:],
                                              ALU.mult, ALU.mult))

                    def hv(buf, half):
                        vvw = buf[:].rearrange("p (h two j) -> p h two j",
                                               two=2, j=NJ)
                        return vvw[:, :, half, :]
                    cosv = cos_t[:, tc * NJ:(tc + 1) * NJ] \
                        .unsqueeze(1).broadcast_to((128, NH, NJ))
                    sinv = sin_t[:, tc * NJ:(tc + 1) * NJ] \
                        .unsqueeze(1).broadcast_to((128, NH, NJ))
                    qt, qb = hv(qn, 0), hv(qn, 1)
                    rt_, rb = hv(rr, 0), hv(rr, 1)
                    vw(v); VA(v.tensor_tensor(rt_, qb, sinv, ALU.mult))
                    vw(v); VA(v.tensor_tensor(rb, qb, cosv, ALU.mult))
                    vw(v); VA(v.tensor_tensor(qb, qt, sinv, ALU.mult))
                    vw(v); VA(v.tensor_tensor(rb, rb, qb, ALU.add))
                    vw(v); VA(v.tensor_tensor(qb, qt, cosv, ALU.mult))
                    vw(v); VA(v.tensor_tensor(rt_, qb, rt_, ALU.subtract))
                emit("vector", v_e)

                a1 = sACT.reserve()

                def s_hi(sc, w=d0):
                    sc.wait_ge(sDVE.h, w)
                    sw(sc)
                    SA(sc.activation(out=ohi[:], in_=rr[:], func=AF.Copy))
                emit("scalar", s_hi)

                dlo = sDVE.reserve()

                def v_lo(v, w=a1):
                    v.wait_ge(sACT.h, w)
                    vw(v)
                    VA(v.tensor_tensor(olo[:], rr[:], ohi[:], ALU.subtract))
                emit("vector", v_lo)
            else:
                a0 = sACT.reserve(2 if taps else 1)

                def s_e(sc, ps=ps, w=mmv):
                    sc.wait_ge(sMM.h, w)
                    sw(sc)
                    SA(sc.activation(out=ohi[:], in_=ps[:], func=AF.Copy))
                    if taps:
                        sw(sc)
                        SA(sc.activation(out=rr[:], in_=ps[:], func=AF.Copy))
                emit("scalar", s_e)

                dlo = sDVE.reserve()
                pspool.released_at(sDVE, dlo)

                def v_lo(v, ps=ps, w=a0):
                    v.wait_ge(sACT.h, w)
                    vw(v)
                    VA(v.tensor_tensor(olo[:], ps[:], ohi[:], ALU.subtract))
                emit("vector", v_lo)

            p = tc % 2
            nst = 3 if taps else 2
            stw = dst2.reserve(p, nst)

            def g_st(g, p=p, row0=row0, w=dlo):
                g.wait_ge(sDVE.h, w)
                dst2.attach(p, g.dma_start(out=dsth[row0:row0 + 128, :],
                                           in_=ohi[:]))
                dst2.attach(p, g.dma_start(out=dstl[row0:row0 + 128, :],
                                           in_=olo[:]))
                if taps:
                    dst2.attach(p, g.dma_start(
                        out=taph[tapn][row0:row0 + 128, :], in_=rr[:]))
            emit("gpsimd", g_st)
            emit("scalar", lambda sc, w=stw: pw(sc, w))

        close_scope(sc2)

    p2_og("q")
    p2_og("k")
    p2_og("v")
    close_scope(sc02)

    def p3():
        phase_barrier()
        sc3 = []
        dqk = DStream("p3qk")
        dst3 = DStream("st3", store=True)
        dv_ = DStream("p3v")
        headpool = Pool(2, "p3head")
        qTh = [sbuf(f"p3qh{i}", [128, T], BF16, sc3) for i in range(2)]
        qTl = [sbuf(f"p3ql{i}", [128, T], BF16, sc3) for i in range(2)]
        kTh = [sbuf(f"p3kh{i}", [128, TKV], BF16, sc3) for i in range(2)]
        kTl = [sbuf(f"p3kl{i}", [128, TKV], BF16, sc3) for i in range(2)]
        vh_ = [sbuf(f"p3vh{i}", [128, TKV], BF16, sc3) for i in range(2)]
        vl_ = [sbuf(f"p3vl{i}", [128, TKV], BF16, sc3) for i in range(2)]
        e32 = [sbuf(f"p3e{i}", [128, 512], F32, sc3) for i in range(2)]
        ehi = [sbuf(f"p3eh{i}", [128, 512], BF16, sc3) for i in range(3)]
        elo = [sbuf(f"p3el{i}", [128, 512], BF16, sc3) for i in range(3)]
        den_sb = [sbuf(f"p3dsb{i}", [128, QG], F32, sc3) for i in range(2)]
        lnt = sbuf("p3lnt", [128, QG], F32, sc3)
        rcpb = sbuf("p3rcpb", [128, QG], F32, sc3)
        aof_ev = sbuf("p3aof", [128, QG], F32, sc3)
        aoh_ev = sbuf("p3aoh", [128, QG], BF16, sc3)
        aol_ev = sbuf("p3aol", [128, QG], BF16, sc3)
        scpool = Pool(3, "p3sc")
        ps_s = [psum(f"p3ps{i}", [128, 512], sc3) for i in range(3)]
        epool = Pool(3, "p3ep")
        aopool = Pool(2, "p3ao")
        ps_ao = [psum(f"p3pao{i}", [128, QG], sc3) for i in range(2)]
        ps_dn = [psum(f"p3pdn{i}", [128, 512], sc3) for i in range(2)]

        e32_free = {0: None, 1: None}
        dn_free = {}
        gci = 0
        prev_stw = None
        prev_dlo2 = None
        pending_evac = None
        for h in range(NH):
            hslot, hwv = headpool.acquire()
            trw = dqk.reserve(hslot, 4)

            def hload_t(sc, h=h, hslot=hslot, hwv=hwv):
                pw(sc, hwv)
                c0 = h * 128
                dqk.attach(hslot, sc.dma_start(out=qTh[hslot][:],
                                               in_=s_qh[:, c0:c0 + 128],
                                               transpose=True))
                dqk.attach(hslot, sc.dma_start(out=qTl[hslot][:],
                                               in_=s_ql[:, c0:c0 + 128],
                                               transpose=True))
                dqk.attach(hslot, sc.dma_start(out=kTh[hslot][:],
                                               in_=s_kh[:, c0:c0 + 128],
                                               transpose=True))
                dqk.attach(hslot, sc.dma_start(out=kTl[hslot][:],
                                               in_=s_kl[:, c0:c0 + 128],
                                               transpose=True))
            emit("sync", hload_t)

            vldw = dv_.reserve(hslot, 2)

            def hload_v(s, h=h, hslot=hslot, hwv=hwv):
                pw(s, hwv)
                c0 = h * 128
                src_h = s_vh[:, c0:c0 + 128].rearrange("(kc t) d -> t kc d", t=128)
                src_l = s_vl[:, c0:c0 + 128].rearrange("(kc t) d -> t kc d", t=128)
                dst_h = vh_[hslot][:].rearrange("p (kc d) -> p kc d", d=128)
                dst_l = vl_[hslot][:].rearrange("p (kc d) -> p kc d", d=128)
                dv_.attach(hslot, s.dma_start(out=dst_h, in_=src_h))
                dv_.attach(hslot, s.dma_start(out=dst_l, in_=src_l))
            emit("sync", hload_v)

            pv_done_prev = None
            for qg in range(NQG):
                qc0, qc1 = qg * QGC, (qg + 1) * QGC - 1
                kvcs = list(range(qc0, qc1 + WKC + 1))
                aoslot, aowv = aopool.acquire()

                def zf(te, aowv=aowv, aoslot=aoslot):
                    pw(te, aowv)
                    te.matmul(ps_ao[aoslot][:, 0:QG], ones_bf[:],
                              zro_bf[:, 0:QG], start=True, stop=False)
                emit("tensor", zf)

                def emit_pv(info, is_last, aowv=aowv, hslot=hslot,
                            vldw=vldw, qc0=qc0, aoslot=aoslot):
                    ci, kvc, eslot, ew, qa, qb, dnf = info
                    nc_ = (qb - qa + 1) * 128
                    pvv = sMM.reserve()
                    epool.released_at(sMM, pvv)

                    def t_pv(te, ci=ci, kvc=kvc, eslot=eslot, ew=ew, qa=qa,
                             qb=qb, nc_=nc_, is_last=is_last, qc0=qc0,
                             dnf=dnf, aoslot=aoslot):
                        te.wait_ge(sDVE.h, ew)
                        pw(te, vldw)
                        if dnf is not None:
                            te.wait_ge(sDVE.h, dnf)
                        off = (qa - qc0) * 128
                        vs_h = vh_[hslot][:, kvc * 128:(kvc + 1) * 128]
                        vs_l = vl_[hslot][:, kvc * 128:(kvc + 1) * 128]
                        eh = ehi[eslot][:, 0:nc_]
                        el = elo[eslot][:, 0:nc_]
                        te.matmul(ps_ao[aoslot][:, off:off + nc_], vs_h, eh,
                                  start=False, stop=False)
                        te.matmul(ps_ao[aoslot][:, off:off + nc_], vs_h, el,
                                  start=False, stop=False)
                        te.matmul(ps_ao[aoslot][:, off:off + nc_], vs_l, eh,
                                  start=False, stop=is_last)
                        te.matmul(ps_dn[ci % 2][:, 0:nc_], ones_kv[:], eh,
                                  start=True, stop=False)
                        lastm = te.matmul(ps_dn[ci % 2][:, 0:nc_], ones_kv[:],
                                          el, start=False, stop=True)
                        sMM.attach(lastm)
                    emit("tensor", t_pv)

                    dnv = sDVE.reserve(2 if ci == 0 else 1)
                    dpar = qg % 2

                    def v_dn(v, ci=ci, qa=qa, nc_=nc_, w=pvv, qc0=qc0,
                             dpar=dpar):
                        v.wait_ge(sMM.h, w)
                        off = (qa - qc0) * 128
                        if ci == 0:
                            vw(v)
                            VA(v.memset(den_sb[dpar][:], 0.0))
                        vw(v)
                        VA(v.tensor_tensor(den_sb[dpar][:, off:off + nc_],
                                           ps_dn[ci % 2][:, 0:nc_],
                                           den_sb[dpar][:, off:off + nc_],
                                           ALU.add))
                    emit("vector", v_dn)
                    return pvv, dnv

                pend = []
                pv_done = None
                dn_last = None
                for ci, kvc in enumerate(kvcs):
                    if ci == 1 and pending_evac is not None:
                        prev_stw, prev_dlo2 = pending_evac()
                        pending_evac = None
                    qa = max(qc0, kvc - WKC)
                    qb = min(qc1, kvc)
                    nc_ = (qb - qa + 1) * 128
                    cslot, cwv = scpool.acquire()
                    mmv = sMM.reserve()

                    def t_sc(te, kvc=kvc, qa=qa, qb=qb, nc_=nc_, hslot=hslot,
                             cslot=cslot, cwv=cwv, trw=trw):
                        pw(te, cwv)
                        pw(te, trw)
                        ps = ps_s[cslot]
                        kh = kTh[hslot][:, kvc * 128:(kvc + 1) * 128]
                        kl = kTl[hslot][:, kvc * 128:(kvc + 1) * 128]
                        qh = qTh[hslot][:, qa * 128:qa * 128 + nc_]
                        ql = qTl[hslot][:, qa * 128:qa * 128 + nc_]
                        te.matmul(ps[:, 0:nc_], kh, qh, start=True, stop=False)
                        te.matmul(ps[:, 0:nc_], kh, ql, start=False, stop=False)
                        if kvc <= qb:
                            o = (kvc - qa) * 128
                            te.matmul(ps[:, o:o + 128], mW_b[:], id_bf[:],
                                      start=False, stop=False)
                        if kvc - WKC >= qa:
                            o = (kvc - WKC - qa) * 128
                            te.matmul(ps[:, o:o + 128], mC_b[:], id_bf[:],
                                      start=False, stop=False)
                        if kvc < HC:
                            te.matmul(ps[:, 0:nc_],
                                      mV_b[0:1, kvc * 128:(kvc + 1) * 128],
                                      onesr_b[:, 0:nc_],
                                      start=False, stop=False)
                        last = te.matmul(ps[:, 0:nc_], kl, qh,
                                         start=False, stop=True)
                        sMM.attach(last)
                    emit("tensor", t_sc)

                    eslot, ewv = epool.acquire()
                    asp = sACT.reserve(2)
                    scpool.released_at(sACT, asp)
                    par = gci % 2
                    e2f = e32_free[par]

                    def s_split(sc, cslot=cslot, eslot=eslot, ewv=ewv,
                                nc_=nc_, w=mmv, e2f=e2f, par=par):
                        sc.wait_ge(sMM.h, w)
                        pw(sc, ewv)
                        if e2f is not None:
                            sc.wait_ge(sDVE.h, e2f)
                        sw(sc)
                        SA(sc.activation(out=e32[par][:, 0:nc_],
                                         in_=ps_s[cslot][:, 0:nc_],
                                         func=AF.Exp))
                        sw(sc)
                        SA(sc.activation(out=ehi[eslot][:, 0:nc_],
                                         in_=ps_s[cslot][:, 0:nc_],
                                         func=AF.Exp))
                    emit("scalar", s_split)

                    dlo = sDVE.reserve(1)

                    def v_lo(v, eslot=eslot, nc_=nc_, w=asp, par=par):
                        v.wait_ge(sACT.h, w)
                        vw(v)
                        VA(v.tensor_tensor(elo[eslot][:, 0:nc_],
                                           e32[par][:, 0:nc_],
                                           ehi[eslot][:, 0:nc_], ALU.subtract))
                    emit("vector", v_lo)
                    e32_free[par] = dlo
                    gci += 1

                    if len(pend) == 2:
                        first = pend.pop(0)
                        _, dnv = emit_pv(first, False)
                        dn_free[first[0] % 2] = dnv
                        dn_last = dnv
                    dnf = dn_free.get(ci % 2)
                    pend.append((ci, kvc, eslot, dlo, qa, qb, dnf))
                while len(pend) > 1:
                    first = pend.pop(0)
                    _, dnv = emit_pv(first, False)
                    dn_free[first[0] % 2] = dnv
                    dn_last = dnv
                lastc = pend.pop(0)
                pv_done, dn_last = emit_pv(lastc, True)
                dn_free[lastc[0] % 2] = dn_last

                def make_evac(h=h, qg=qg, pv_done=pv_done, aoslot=aoslot,
                              pstw=prev_stw, pdlo2=prev_dlo2, dn_last=dn_last):
                    def do_emit():
                        dpar = qg % 2
                        arc = sACT.reserve(2)

                        def s_rc(sc, w=dn_last, pdlo2=pdlo2):
                            sc.wait_ge(sDVE.h, w)
                            if pdlo2 is not None:
                                sc.wait_ge(sDVE.h, pdlo2)
                            sw(sc)
                            SA(sc.activation(out=lnt[:], in_=den_sb[dpar][:],
                                             func=AF.Ln))
                            sw(sc)
                            SA(sc.activation(out=rcpb[:], in_=lnt[:],
                                             func=AF.Exp, scale=-1.0))
                        emit("scalar", s_rc)

                        dlo2 = sDVE.reserve(3)
                        aopool.released_at(sDVE, dlo2)

                        def v_ev(v, w=pv_done, pstw=pstw, aoslot=aoslot,
                                 arc=arc):
                            v.wait_ge(sMM.h, w)
                            v.wait_ge(sACT.h, arc)
                            if pstw is not None:
                                pw(v, pstw)
                            vw(v)
                            VA(v.tensor_tensor(aof_ev[:], ps_ao[aoslot][:],
                                               rcpb[:], ALU.mult))
                            vw(v)
                            VA(v.tensor_copy(aoh_ev[:], aof_ev[:]))
                            vw(v)
                            VA(v.tensor_tensor(aol_ev[:], aof_ev[:], aoh_ev[:],
                                               ALU.subtract))
                        emit("vector", v_ev)

                        p = (h * NQG + qg) % 2
                        nst = 3 if taps else 2
                        stw = dst3.reserve(p, nst)

                        def g_st(g, h=h, qg=qg, p=p, w=dlo2):
                            g.wait_ge(sDVE.h, w)
                            r0 = h * 128
                            c0 = qg * QG
                            dst3.attach(p, g.dma_start(
                                out=s_aoh[r0:r0 + 128, c0:c0 + QG],
                                in_=aoh_ev[:]))
                            dst3.attach(p, g.dma_start(
                                out=s_aol[r0:r0 + 128, c0:c0 + QG],
                                in_=aol_ev[:]))
                            if taps:
                                dst3.attach(p, g.dma_start(
                                    out=taph["t_ao"][r0:r0 + 128, c0:c0 + QG],
                                    in_=aof_ev[:]))
                        emit("gpsimd", g_st)
                        return stw, dlo2
                    return do_emit

                pending_evac = make_evac()
                pv_done_prev = pv_done

            headpool.released_at(sMM, pv_done_prev)

        if pending_evac is not None:
            prev_stw, prev_dlo2 = pending_evac()
            pending_evac = None

        close_scope(sc3)

    p3()
    close_scope(sc03)

    def p4():
        phase_barrier()
        sc4 = []
        dstat = DStream("p4s")
        dx = DStream("p4x")
        dst4 = DStream("st4", store=True)
        whi, wlo, wready = load_w_resident("wo", woT, D, sc4)
        statpool = Pool(2, "p4st", init=(sDVE, wready[0]))
        sth = [sbuf(f"p4sh{i}", [128, D], BF16, sc4) for i in range(2)]
        stl = [sbuf(f"p4sl{i}", [128, D], BF16, sc4) for i in range(2)]
        pspool = Pool(2, "p4ps")
        ps4 = [psum(f"p4ps{i}", [128, D], sc4) for i in range(2)]
        xpool = Pool(2, "p4x", init=(sDVE, wready[0]))
        xt = [sbuf(f"p4x{i}", [128, D], F32, sc4) for i in range(2)]
        attnb = sbuf("p4attn", [128, D], F32, sc4)
        x2b = sbuf("p4x2", [128, D], F32, sc4)
        lnh = sbuf("p4lnh", [128, D], BF16, sc4)
        lnl = sbuf("p4lnl", [128, D], BF16, sc4)
        st6 = sbuf("p4st6", [128, 6 * NBN], F32, sc4)
        mv = sbuf("p4mv", [128, 2], F32, sc4)
        r3_t = sbuf("p4r3", [128, 1], F32, sc4)
        b3_t = sbuf("p4b3", [128, 1], F32, sc4)
        r2_t = sbuf("p4r2", [128, 1], F32, sc4)
        b2_t = sbuf("p4b2", [128, 1], F32, sc4)

        for tc in range(TC):
            sslot, swv = statpool.acquire()
            ldw = dstat.reserve(sslot, 2)

            def ld(s, tc=tc, sslot=sslot, swv=swv):
                pw(s, swv)
                src_h = s_aoh[:, tc * 128:(tc + 1) * 128] \
                    .rearrange("(dc d) t -> d dc t", d=128)
                src_l = s_aol[:, tc * 128:(tc + 1) * 128] \
                    .rearrange("(dc d) t -> d dc t", d=128)
                dst_h = sth[sslot][:].rearrange("p (dc t) -> p dc t", t=128)
                dst_l = stl[sslot][:].rearrange("p (dc t) -> p dc t", t=128)
                dstat.attach(sslot, s.dma_start(out=dst_h, in_=src_h))
                dstat.attach(sslot, s.dma_start(out=dst_l, in_=src_l))
            emit("sync", ld)

            xslot, xwv = xpool.acquire()
            xw = dx.reserve(xslot)

            def ldx(s, tc=tc, xslot=xslot, xwv=xwv):
                pw(s, xwv)
                r0 = (HC + tc) * 128
                dx.attach(xslot, s.dma_start(out=xt[xslot][:],
                                             in_=xh[r0:r0 + 128, :]))
            emit("sync", ldx)

            pslot, pwv = pspool.acquire()
            mmv = sMM.reserve()
            statpool.released_at(sMM, mmv)

            def mms(te, sslot=sslot, pslot=pslot, pwv=pwv, ldw=ldw):
                pw(te, pwv)
                pw(te, ldw)
                te.wait_ge(sDVE.h, wready[0])
                te.wait_ge(sACT.h, wready[1])
                groups = [(sth[sslot][:, dc * 128:(dc + 1) * 128],
                           stl[sslot][:, dc * 128:(dc + 1) * 128],
                           whi[dc][:], wlo[dc][:]) for dc in range(NDC)]
                sMM.attach(emit_mm_tc(te, ps4[pslot][:], D, groups))
            emit("tensor", mms)
            ps = ps4[pslot]

            a0 = sACT.reserve()
            pspool.released_at(sACT, a0)

            def s_c(sc, ps=ps, w=mmv):
                sc.wait_ge(sMM.h, w)
                sw(sc)
                SA(sc.activation(out=attnb[:], in_=ps[:], func=AF.Copy))
            emit("scalar", s_c)

            d0 = sDVE.reserve(NBN + 1)

            def v_st3(v, w=a0):
                v.wait_ge(sACT.h, w)
                bn_pair(v, attnb[:], st6, mv)
            emit("vector", v_st3)

            a1 = sACT.reserve()

            def s_r3(sc, w=d0):
                sc.wait_ge(sDVE.h, w)
                sw(sc)
                SA(sc.activation(out=r3_t[:], in_=mv[:, 1:2], func=AF.Sqrt,
                                 bias=epsln_t[:]))
            emit("scalar", s_r3)

            d1 = sDVE.reserve(4 + NBN + 1)
            xpool.released_at(sDVE, d1 - (NBN + 1))

            def v_x2(v, xslot=xslot, w=a1, xw=xw):
                v.wait_ge(sACT.h, w)
                pw(v, xw)
                vw(v)
                VA(v.reciprocal(r3_t[:], r3_t[:]))
                vw(v)
                VA(v.tensor_scalar(b3_t[:], mv[:, 0:1], r3_t[:], -1.0,
                                   ALU.mult, ALU.mult))
                vw(v)
                VA(v.tensor_scalar_add(x2b[:], xt[xslot][:], b3_t[:]))
                vw(v)
                VA(v.scalar_tensor_tensor(x2b[:], attnb[:], r3_t[:], x2b[:],
                                          ALU.mult, ALU.add))
                bn_pair(v, x2b[:], st6, mv)
            emit("vector", v_x2)

            a2 = sACT.reserve()

            def s_r2(sc, w=d1):
                sc.wait_ge(sDVE.h, w)
                sw(sc)
                SA(sc.activation(out=r2_t[:], in_=mv[:, 1:2], func=AF.Sqrt,
                                 bias=epsln_t[:]))
            emit("scalar", s_r2)

            d2 = sDVE.reserve(2)

            def v_b2(v, w=a2):
                v.wait_ge(sACT.h, w)
                vw(v)
                VA(v.reciprocal(r2_t[:], r2_t[:]))
                vw(v)
                VA(v.tensor_scalar(b2_t[:], mv[:, 0:1], r2_t[:], -1.0,
                                   ALU.mult, ALU.mult))
            emit("vector", v_b2)

            a3 = sACT.reserve(2)

            def s_ln2(sc, w=d2):
                sc.wait_ge(sDVE.h, w)
                sw(sc)
                SA(sc.activation(out=attnb[:], in_=x2b[:], func=AF.Identity,
                                 scale=r2_t[:], bias=b2_t[:]))
                sw(sc)
                SA(sc.activation(out=lnh[:], in_=attnb[:], func=AF.Copy))
            emit("scalar", s_ln2)

            dlo = sDVE.reserve()

            def v_fin(v, w=a3):
                v.wait_ge(sACT.h, w)
                vw(v)
                VA(v.tensor_tensor(lnl[:], attnb[:], lnh[:], ALU.subtract))
            emit("vector", v_fin)

            p = tc % 2
            nst = 4 if taps else 3
            stw = dst4.reserve(p, nst)

            def g_st(g, tc=tc, p=p, w=dlo):
                g.wait_ge(sDVE.h, w)
                r0 = tc * 128
                dst4.attach(p, g.dma_start(out=s_x2[r0:r0 + 128, :], in_=x2b[:]))
                dst4.attach(p, g.dma_start(out=s_ln2h[r0:r0 + 128, :], in_=lnh[:]))
                dst4.attach(p, g.dma_start(out=s_ln2l[r0:r0 + 128, :], in_=lnl[:]))
                if taps:
                    dst4.attach(p, g.dma_start(out=taph["t_x2"][r0:r0 + 128, :],
                                               in_=x2b[:]))
            emit("gpsimd", g_st)
            emit("scalar", lambda sc, w=stw: pw(sc, w))

        close_scope(sc4)

    p4()

    def p5a():
        phase_barrier()
        sc5 = []
        dmv = DStream("p5mv")
        wa_h = [sbuf(f"wa_h{i}", [128, HA], BF16, sc5) for i in range(NDC)]
        wa_l = [sbuf(f"wa_l{i}", [128, HA], BF16, sc5) for i in range(NDC)]
        wr_h = [sbuf(f"wr_h{i}", [128, E], BF16, sc5) for i in range(NDC)]
        wr_l = [sbuf(f"wr_l{i}", [128, E], BF16, sc5) for i in range(NDC)]
        tmp = []
        wfa = [sbuf(f"p5wfa{i}", [128, HA], F32, tmp) for i in range(2)]
        wfr = [sbuf(f"p5wfr{i}", [128, E], F32, tmp) for i in range(2)]
        dwa = DStream("p5wa")
        wapool = Pool(2, "p5wa", init=(sDVE, sDVE.c))
        for dc in range(NDC):
            slot, wv = wapool.acquire()
            ww = dwa.reserve(slot, 2)

            def ldw(s, dc=dc, slot=slot, wv=wv):
                pw(s, wv)
                dwa.attach(slot, s.dma_start(out=wfa[slot][:],
                                             in_=waT[dc * 128:(dc + 1) * 128, :]))
                dwa.attach(slot, s.dma_start(out=wfr[slot][:],
                                             in_=wrT[dc * 128:(dc + 1) * 128, :]))
            emit("sync", ldw)

            aw = sACT.reserve(2)

            def s_h(sc, dc=dc, slot=slot, w=ww):
                pw(sc, w)
                sw(sc)
                SA(sc.activation(out=wa_h[dc][:], in_=wfa[slot][:], func=AF.Copy))
                sw(sc)
                SA(sc.activation(out=wr_h[dc][:], in_=wfr[slot][:], func=AF.Copy))
            emit("scalar", s_h)

            dvv = sDVE.reserve(2)
            wapool.released_at(sDVE, dvv)

            def v_l(v, dc=dc, slot=slot, w=aw):
                v.wait_ge(sACT.h, w)
                vw(v)
                VA(v.tensor_tensor(wa_l[dc][:], wfa[slot][:], wa_h[dc][:],
                                   ALU.subtract))
                vw(v)
                VA(v.tensor_tensor(wr_l[dc][:], wfr[slot][:], wr_h[dc][:],
                                   ALU.subtract))
            emit("vector", v_l)
        w_rdy = (sDVE.c, sACT.c)
        close_scope(tmp)

        dst5 = DStream("st5", store=True)
        mvpool = Pool(2, "p5mv", init=(sDVE, w_rdy[0]))
        mv_h = [sbuf(f"p5mh{i}", [128, T], BF16, sc5) for i in range(2)]
        mv_l = [sbuf(f"p5ml{i}", [128, T], BF16, sc5) for i in range(2)]
        ps_ha = psum("p5pha", [128, T], sc5)
        ps_lg = psum("p5plg", [8, T], sc5)

        ocs = [(i * 512, min(512, T - i * 512)) for i in range((T + 511) // 512)]
        for dc in range(NDC):
            mslot, mwv = mvpool.acquire()
            trw = dmv.reserve(mslot, 2)

            def tl(sc, dc=dc, mslot=mslot, mwv=mwv):
                pw(sc, mwv)
                c0 = dc * 128
                dmv.attach(mslot, sc.dma_start(out=mv_h[mslot][:],
                                               in_=s_ln2h[:, c0:c0 + 128],
                                               transpose=True))
                dmv.attach(mslot, sc.dma_start(out=mv_l[mslot][:],
                                               in_=s_ln2l[:, c0:c0 + 128],
                                               transpose=True))
            emit("sync", tl)

            mm5v = sMM.reserve()
            mvpool.released_at(sMM, mm5v)

            def mm5(te, dc=dc, mslot=mslot, trw=trw):
                pw(te, trw)
                if dc == 0:
                    te.wait_ge(sDVE.h, w_rdy[0])
                    te.wait_ge(sACT.h, w_rdy[1])
                last = None
                for pi, (lt, rt) in enumerate(
                        ((wa_h[dc][:], mv_h[mslot][:]),
                         (wa_h[dc][:], mv_l[mslot][:]),
                         (wa_l[dc][:], mv_h[mslot][:]))):
                    for (o0, on) in ocs:
                        last = te.matmul(ps_ha[:, o0:o0 + on], lt,
                                         rt[:, o0:o0 + on],
                                         start=(dc == 0 and pi == 0),
                                         stop=(dc == NDC - 1 and pi == 2))
                for pi, (lt, rt) in enumerate(
                        ((wr_h[dc][:], mv_h[mslot][:]),
                         (wr_h[dc][:], mv_l[mslot][:]),
                         (wr_l[dc][:], mv_h[mslot][:]))):
                    for (o0, on) in ocs:
                        last = te.matmul(ps_lg[:, o0:o0 + on], lt,
                                         rt[:, o0:o0 + on],
                                         start=(dc == 0 and pi == 0),
                                         stop=(dc == NDC - 1 and pi == 2))
                sMM.attach(last)
            emit("tensor", mm5)
        mm_done = sMM.c

        haf = sbuf("p5haf", [128, T], F32, sc5)
        lgf = sbuf("p5lgf", [32, T], F32, sc5)
        lgT = sbuf("p5lgT", [128, TC * 32], F32, sc5)
        gateb = sbuf("p5gb", [128, TC * 32], BF16, sc5)
        g8f = sbuf("p5g8f", [32, T], BF16, sc5)
        gfm = sbuf("p5gfm", [128, T], BF16, sc5)
        mx_t = sbuf("p5mx", [128, 1], F32, sc5)

        dz = sDVE.reserve(2)

        def v_zero(v):
            vw(v)
            VA(v.memset(lgf[:], 0.0))
            vw(v)
            VA(v.memset(gateb[:], 0.0))
        emit("vector", v_zero)

        aev = sACT.reserve(2)

        def s_ev(sc, w=mm_done, wz=dz):
            sc.wait_ge(sMM.h, w)
            sc.wait_ge(sDVE.h, wz)
            sw(sc)
            SA(sc.activation(out=haf[:], in_=ps_ha[:], func=AF.Copy))
            sw(sc)
            SA(sc.activation(out=lgf[0:8, :], in_=ps_lg[:], func=AF.Copy))
        emit("scalar", s_ev)

        dgate = sDVE.reserve(2 * (T // 32) + 2 * TC)

        def v_gate(v, w=aev):
            v.wait_ge(sACT.h, w)
            for j in range(T // 32):
                tcj, rj = j // 4, (j % 4) * 32
                vw(v)
                VA(v.transpose(lgT[rj:rj + 32, tcj * 32:tcj * 32 + 32],
                               lgf[0:32, j * 32:(j + 1) * 32]))
            for tc in range(TC):
                vw(v)
                VA(v.reduce_max(mx_t[:], lgT[:, tc * 32:tc * 32 + E], axis=AX))
                vw(v)
                VA(v.tensor_scalar(gateb[:, tc * 32:tc * 32 + E],
                                   lgT[:, tc * 32:tc * 32 + E],
                                   mx_t[:], 2.0, ALU.is_ge, ALU.mult))
            for j in range(T // 32):
                tcj, rj = j // 4, (j % 4) * 32
                vw(v)
                VA(v.transpose(g8f[0:32, j * 32:(j + 1) * 32],
                               gateb[rj:rj + 32, tcj * 32:tcj * 32 + 32]))
        emit("vector", v_gate)

        gexp = dst5.reserve(0, R)

        def g_gate(g, w=dgate):
            g.wait_ge(sDVE.h, w)
            for r in range(R):
                dst5.attach(0, g.dma_start(out=gfm[r * E:(r + 1) * E, :],
                                           in_=g8f[0:E, :]))
        emit("gpsimd", g_gate)

        dhag = sDVE.reserve()

        def v_hag(v, w=gexp):
            pw(v, w)
            vw(v)
            VA(v.tensor_tensor(haf[:], haf[:], gfm[:], ALU.mult))
        emit("vector", v_hag)

        if taps:
            dst5.reserve(1, 2)

            def g_tap(g, w=dhag):
                g.wait_ge(sDVE.h, w)
                dst5.attach(1, g.dma_start(out=taph["t_gate"][:, :], in_=gfm[:]))
                dst5.attach(1, g.dma_start(out=taph["t_ha"][:, :], in_=haf[:]))
            emit("gpsimd", g_tap)

        ah = sACT.reserve()

        def s_hi(sc, w=dhag):
            sc.wait_ge(sDVE.h, w)
            sw(sc)
            SA(sc.activation(out=hag_hi[:], in_=haf[:], func=AF.Copy))
        emit("scalar", s_hi)

        hag_rdy = sDVE.reserve()

        def v_lo(v, w=ah):
            v.wait_ge(sACT.h, w)
            vw(v)
            VA(v.tensor_tensor(hag_lo[:], haf[:], hag_hi[:], ALU.subtract))
        emit("vector", v_lo)
        close_scope(sc5)
        return hag_rdy

    phase_barrier()
    sc56 = []
    hag_hi = sbuf("hag_hi", [128, T], BF16, sc56)
    hag_lo = sbuf("hag_lo", [128, T], BF16, sc56)
    wb_whi, wb_wlo, wb_ready = load_w_resident("wb", wbT, D, sc56)
    bw_h = sbuf("bw_h", [128, D], BF16, sc56)
    bw_l = sbuf("bw_l", [128, D], BF16, sc56)
    b_hi = sbuf("b_hi", [1, D], BF16, sc56)
    b_lo = sbuf("b_lo", [1, D], BF16, sc56)
    tmp56 = []
    bwf = sbuf("p5bwf", [128, D], F32, tmp56)
    b_f32 = sbuf("p5bf32", [1, D], F32, tmp56)

    bww = sLD.reserve(32)

    def ldbw(s, w=wb_ready[0]):
        s.wait_ge(sDVE.h, w)
        sLD.attach_dma(s.dma_start(out=bwf[:], in_=bwT[:, :]))
        sLD.attach_dma(s.dma_start(out=b_f32[:], in_=b_base[:]))
    emit("sync", ldbw)

    abw = sACT.reserve(2)

    def s_bw(sc, w=bww):
        sc.wait_ge(sLD.h, w)
        sw(sc)
        SA(sc.activation(out=bw_h[:], in_=bwf[:], func=AF.Copy))
        sw(sc)
        SA(sc.activation(out=b_hi[:], in_=b_f32[:], func=AF.Copy))
    emit("scalar", s_bw)

    bw_rdy = sDVE.reserve(2)

    def v_bw(v, w=abw):
        v.wait_ge(sACT.h, w)
        vw(v)
        VA(v.tensor_tensor(bw_l[:], bwf[:], bw_h[:], ALU.subtract))
        vw(v)
        VA(v.tensor_tensor(b_lo[:], b_f32[:], b_hi[:], ALU.subtract))
    emit("vector", v_bw)
    close_scope(tmp56)
    hag_rdy = p5a()

    def p5b():
        phase_barrier()
        sc6 = []
        dstat = DStream("p5s")
        dx2 = DStream("p5x")
        whi, wlo, wready = wb_whi, wb_wlo, wb_ready

        dsto = DStream("sto", store=True)
        statpool = Pool(2, "p5st", init=(sDVE, bw_rdy))
        sth = [sbuf(f"p5sh{i}", [128, D], BF16, sc6) for i in range(2)]
        stl = [sbuf(f"p5sl{i}", [128, D], BF16, sc6) for i in range(2)]
        pspool = Pool(2, "p5ps")
        ps5 = [psum(f"p5ps{i}", [128, D], sc6) for i in range(2)]
        x2pool = Pool(2, "p5x2", init=(sDVE, bw_rdy))
        x2t = [sbuf(f"p5x2{i}", [128, D], F32, sc6) for i in range(2)]
        ffb = sbuf("p5ffb", [128, D], F32, sc6)
        outb = sbuf("p5out", [128, D], F32, sc6)
        st6 = sbuf("p5st6", [128, 6 * NBN], F32, sc6)
        mv = sbuf("p5mv", [128, 2], F32, sc6)
        r4_t = sbuf("p5r4", [128, 1], F32, sc6)
        b4_t = sbuf("p5b4", [128, 1], F32, sc6)

        for tc in range(TC):
            sslot, swv = statpool.acquire()
            trw = dstat.reserve(sslot, 2)

            def tl(sc, tc=tc, sslot=sslot, swv=swv):
                pw(sc, swv)
                r0 = tc * 128
                src_h = s_ln2h[r0:r0 + 128, :].rearrange("t (dc d) -> t dc d", d=128)
                src_l = s_ln2l[r0:r0 + 128, :].rearrange("t (dc d) -> t dc d", d=128)
                dst_h = sth[sslot][:].rearrange("p (dc t) -> p dc t", t=128)
                dst_l = stl[sslot][:].rearrange("p (dc t) -> p dc t", t=128)
                dstat.attach(sslot, sc.dma_start(out=dst_h, in_=src_h, transpose=True))
                dstat.attach(sslot, sc.dma_start(out=dst_l, in_=src_l, transpose=True))
            emit("sync", tl)

            xslot, xwv = x2pool.acquire()
            xw = dx2.reserve(xslot)

            def ldx(s, tc=tc, xslot=xslot, xwv=xwv):
                pw(s, xwv)
                r0 = tc * 128
                dx2.attach(xslot, s.dma_start(out=x2t[xslot][:],
                                              in_=s_x2[r0:r0 + 128, :]))
            emit("sync", ldx)

            pslot, pwv = pspool.acquire()
            mmv = sMM.reserve()
            statpool.released_at(sMM, mmv)
            dvw = max(wready[0], bw_rdy, hag_rdy)

            def mms(te, tc=tc, sslot=sslot, pslot=pslot, pwv=pwv, trw=trw,
                    dvw=dvw):
                pw(te, pwv)
                pw(te, trw)
                te.wait_ge(sDVE.h, dvw)
                te.wait_ge(sACT.h, wready[1])
                groups = [(sth[sslot][:, dc * 128:(dc + 1) * 128],
                           stl[sslot][:, dc * 128:(dc + 1) * 128],
                           whi[dc][:], wlo[dc][:]) for dc in range(NDC)]
                groups.append((hag_hi[:, tc * 128:(tc + 1) * 128],
                               hag_lo[:, tc * 128:(tc + 1) * 128],
                               bw_h[:], bw_l[:]))
                groups.append((ones_bf[:], None, b_hi[:], b_lo[:]))
                sMM.attach(emit_mm_tc(te, ps5[pslot][:], D, groups))
            emit("tensor", mms)
            ps = ps5[pslot]

            a0 = sACT.reserve()
            pspool.released_at(sACT, a0)

            def s_c(sc, ps=ps, w=mmv):
                sc.wait_ge(sMM.h, w)
                sw(sc)
                SA(sc.activation(out=ffb[:], in_=ps[:], func=AF.Copy))
            emit("scalar", s_c)

            d0 = sDVE.reserve(NBN + 1)

            def v_st(v, w=a0):
                v.wait_ge(sACT.h, w)
                bn_pair(v, ffb[:], st6, mv)
            emit("vector", v_st)

            a1 = sACT.reserve()

            def s_r4(sc, w=d0):
                sc.wait_ge(sDVE.h, w)
                sw(sc)
                SA(sc.activation(out=r4_t[:], in_=mv[:, 1:2], func=AF.Sqrt,
                                 bias=epsln_t[:]))
            emit("scalar", s_r4)

            dout_ = sDVE.reserve(4)
            x2pool.released_at(sDVE, dout_)

            def v_out(v, xslot=xslot, w=a1, xw=xw):
                v.wait_ge(sACT.h, w)
                pw(v, xw)
                vw(v)
                VA(v.reciprocal(r4_t[:], r4_t[:]))
                vw(v)
                VA(v.tensor_scalar(b4_t[:], mv[:, 0:1], r4_t[:], -1.0,
                                   ALU.mult, ALU.mult))
                vw(v)
                VA(v.tensor_scalar_add(outb[:], x2t[xslot][:], b4_t[:]))
                vw(v)
                VA(v.scalar_tensor_tensor(outb[:], ffb[:], r4_t[:], outb[:],
                                          ALU.mult, ALU.add))
            emit("vector", v_out)

            p = tc % 2
            ow = dsto.reserve(p)

            def g_out(g, tc=tc, p=p, w=dout_):
                g.wait_ge(sDVE.h, w)
                r0 = tc * 128
                dsto.attach(p, g.dma_start(out=y[r0:r0 + 128, :], in_=outb[:]))
            emit("gpsimd", g_out)
            emit("scalar", lambda sc, w=ow: pw(sc, w))
            emit("vector", lambda v, w=ow: pw(v, w))

        close_scope(sc6)

    p5b()
    close_scope(sc56)

    fin_waits = [(c, c.c) for c in STORE_CTRS if c.c > 0]

    def fin(g, ws=fin_waits):
        for c, v in ws:
            g.wait_ge(c.h, v)
    emit("gpsimd", fin)

    with nc.Block() as block:
        @block.sync
        def _(sync):
            for f in ops["sync"]:
                f(sync)

        @block.scalar
        def _(scalar):
            for f in ops["scalar"]:
                f(scalar)

        @block.vector
        def _(vector):
            for f in ops["vector"]:
                f(vector)

        @block.tensor
        def _(tensor):
            for f in ops["tensor"]:
                f(tensor)

        @block.gpsimd
        def _(gpsimd):
            for f in ops["gpsimd"]:
                f(gpsimd)

    for c in all_ctrs:
        assert c.att == c.c, f"sem {c.name}: attached {c.att} != reserved {c.c}"

    close_scope(glob)
    return nc


def stage_core_inputs(cfg, core_id, x_full, w_qkv, q_norm_w, k_norm_w, w_o,
                      W_base, b_base_v, router_W, A, Bw):
    D = cfg["D"]; NH = cfg["NH"]
    T = cfg["T_OWN"]; TH = cfg["T_HALO"]; E = cfg["E"]; R = cfg["R"]
    KC = (T + TH) // 128
    NJ = (D // NH) // 2
    b, half = core_id // 2, core_id % 2

    own0 = half * T
    x_own = np.ascontiguousarray(x_full[b, own0:own0 + T, :], dtype=np.float32)
    if half >= 1:
        x_halo = np.ascontiguousarray(x_full[b, own0 - TH:own0, :],
                                      dtype=np.float32)
        posv = np.arange(own0 - TH, own0 + T, dtype=np.float32)
        mv = np.zeros((128, TH), dtype=np.float32)
    else:
        x_halo = np.zeros((TH, D), dtype=np.float32)
        posv = np.concatenate([np.zeros(TH, dtype=np.float32),
                               np.arange(0, T, dtype=np.float32)])
        mv = np.full((128, TH), MASKVAL, dtype=np.float32)

    xh = np.concatenate([x_halo, x_own], axis=0)
    posr = np.ascontiguousarray(posv.reshape(KC, 128).T)

    i = np.arange(128)[:, None]
    j = np.arange(128)[None, :]
    mW_ = np.where(j >= i + 1, 0.0, MASKVAL).astype(np.float32)
    mC_ = np.where(j <= i, 0.0, MASKVAL).astype(np.float32)
    jr = np.arange(NJ, dtype=np.float32)[None, :]

    return {
        "xh": xh,
        "wqT": np.ascontiguousarray(w_qkv[0:D, :].T, dtype=np.float32),
        "wkT": np.ascontiguousarray(w_qkv[D:2 * D, :].T, dtype=np.float32),
        "wvT": np.ascontiguousarray(w_qkv[2 * D:3 * D, :].T, dtype=np.float32),
        "woT": np.ascontiguousarray(w_o.T, dtype=np.float32),
        "wbT": np.ascontiguousarray(W_base.T, dtype=np.float32),
        "waT": np.ascontiguousarray(A.transpose(2, 1, 0).reshape(D, R * E)),
        "wrT": np.ascontiguousarray(router_W.T, dtype=np.float32),
        "bwT": np.ascontiguousarray(Bw.transpose(2, 0, 1).reshape(R * E, D)),
        "b_base": np.ascontiguousarray(b_base_v.reshape(1, D), dtype=np.float32),
        "qnw": np.ascontiguousarray(q_norm_w.reshape(1, D), dtype=np.float32),
        "knw": np.ascontiguousarray(k_norm_w.reshape(1, D), dtype=np.float32),
        "posr": posr,
        "jrow": np.ascontiguousarray(jr, dtype=np.float32),
        "mW": mW_, "mC": mC_, "mV": mv,
        "ones_row": np.ones((1, 128), dtype=np.float32),
        "ident": np.eye(128, dtype=np.float32),
    }


def kernel(L0_attn_hidden_states, L0_attn_w_qkv, L0_attn_q_norm_w,
           L0_attn_k_norm_w, L0_attn_w_o, L0_ffn_W_base, L0_ffn_b_base,
           L0_ffn_router_W, L0_ffn_A, L0_ffn_Bw):
    cfg = real_cfg()
    x = np.asarray(L0_attn_hidden_states, dtype=np.float32)
    args = (x, np.asarray(L0_attn_w_qkv, np.float32),
            np.asarray(L0_attn_q_norm_w, np.float32),
            np.asarray(L0_attn_k_norm_w, np.float32),
            np.asarray(L0_attn_w_o, np.float32),
            np.asarray(L0_ffn_W_base, np.float32),
            np.asarray(L0_ffn_b_base, np.float32),
            np.asarray(L0_ffn_router_W, np.float32),
            np.asarray(L0_ffn_A, np.float32),
            np.asarray(L0_ffn_Bw, np.float32))
    in_maps = [stage_core_inputs(cfg, c, *args) for c in range(8)]
    nc = build_nc(cfg)
    res = run_bass_kernel_spmd(nc, in_maps, core_ids=list(range(8)))
    B, S, D = x.shape
    out = np.empty((B, S, D), dtype=np.float32)
    T = cfg["T_OWN"]
    for c in range(8):
        b, half = c // 2, c % 2
        out[b, half * T:(half + 1) * T, :] = res.results[c]["y"]
    return out


# version 29
# speedup vs baseline: 1.4376x; 1.0042x over previous
import math
import numpy as np
import concourse.bass as bass
import concourse.mybir as mybir
from concourse.bass_utils import run_bass_kernel_spmd

F32 = mybir.dt.float32
BF16 = mybir.dt.bfloat16
AF = mybir.ActivationFunctionType
ALU = mybir.AluOpType
AX = mybir.AxisListType.X

MASKVAL = -30000.0
EPS_LN = 1e-5
EPS_RMS = 1e-6


def real_cfg():
    return dict(D=2048, NH=16, WINDOW=1024, T_OWN=2048, T_HALO=1024,
                E=8, R=16, debug_taps=False)


def mini_cfg():
    return dict(D=256, NH=2, WINDOW=256, T_OWN=512, T_HALO=256,
                E=8, R=16, debug_taps=True)


class Ctr:
    def __init__(self, handle, name):
        self.h = handle
        self.name = name
        self.c = 0
        self.att = 0

    def reserve(self, n=1):
        self.c += n
        return self.c

    def reserve_dma(self):
        return self.reserve(16)

    def attach(self, instr, n=1):
        self.att += n
        instr.then_inc(self.h, n)
        return instr

    def attach_dma(self, instr):
        return self.attach(instr, 16)


class Pool:

    def __init__(self, nbuf, name, init=None):
        self.nbuf = nbuf
        self.name = name
        self.uses = 0
        self.init = init
        self.rel = []

    def acquire(self):
        u = self.uses
        self.uses += 1
        if u >= self.nbuf:
            assert len(self.rel) > u - self.nbuf, \
                f"pool {self.name}: use {u} before release {u - self.nbuf}"
            w = self.rel[u - self.nbuf]
        else:
            w = self.init
        return u % self.nbuf, w

    def released_at(self, ctr, value):
        self.rel.append((ctr, value))


def build_nc(cfg):
    D = cfg["D"]; NH = cfg["NH"]; W = cfg["WINDOW"]
    T = cfg["T_OWN"]; TH = cfg["T_HALO"]; E = cfg["E"]; R = cfg["R"]
    HA = E * R
    HD = D // NH
    assert HD == 128 and HA == 128 and W == TH
    NDC = D // 128
    TC = T // 128
    HC = TH // 128
    KC = TC + HC
    TKV = T + TH
    WKC = W // 128
    KSPAN = WKC + 1
    KW = KSPAN * 128
    KWP = ((KW + 511) // 512) * 512
    NJ = HD // 2
    taps = cfg["debug_taps"]
    QG = min(512, T)
    NQG = T // QG
    QGC = QG // 128
    CSC = 1.0 / math.sqrt(HD)

    nc = bass.Bass("TRN2", target_bir_lowering=False, debug=False)

    def din(name, shape, dt=F32):
        return nc.dram_tensor(name, list(shape), dt, kind="ExternalInput").ap()

    def dscr(name, shape, dt):
        return nc.dram_tensor(name, list(shape), dt, kind="Internal").ap()

    def dout(name, shape, dt=F32):
        return nc.dram_tensor(name, list(shape), dt, kind="ExternalOutput").ap()

    xh = din("xh", [TKV, D])
    wqT = din("wqT", [D, D]); wkT = din("wkT", [D, D]); wvT = din("wvT", [D, D])
    woT = din("woT", [D, D]); wbT = din("wbT", [D, D])
    waT = din("waT", [D, HA])
    wrT = din("wrT", [D, E])
    bwT = din("bwT", [HA, D])
    b_base = din("b_base", [1, D])
    qnw = din("qnw", [1, D]); knw = din("knw", [1, D])
    posr = din("posr", [128, KC])
    jrow = din("jrow", [1, NJ])
    mW = din("mW", [128, 128]); mC = din("mC", [128, 128])
    mV = din("mV", [128, TH])
    ones_row = din("ones_row", [1, 128])
    ident = din("ident", [128, 128])

    s_ln1h = dscr("s_ln1h", [TKV, D], BF16); s_ln1l = dscr("s_ln1l", [TKV, D], BF16)
    s_qh = dscr("s_qh", [T, D], BF16); s_ql = dscr("s_ql", [T, D], BF16)
    s_kh = dscr("s_kh", [TKV, D], BF16); s_kl = dscr("s_kl", [TKV, D], BF16)
    s_vh = dscr("s_vh", [TKV, D], BF16); s_vl = dscr("s_vl", [TKV, D], BF16)
    s_aoh = dscr("s_aoh", [D, T], BF16); s_aol = dscr("s_aol", [D, T], BF16)
    s_x2 = dscr("s_x2", [T, D], F32)
    s_ln2h = dscr("s_ln2h", [T, D], BF16); s_ln2l = dscr("s_ln2l", [T, D], BF16)

    y = dout("y", [T, D])
    taph = {}
    if taps:
        for nm, shp in (("t_ln1", [TKV, D]), ("t_q", [T, D]), ("t_k", [TKV, D]),
                        ("t_v", [TKV, D]), ("t_ao", [D, T]), ("t_x2", [T, D]),
                        ("t_gate", [128, T]), ("t_ha", [128, T])):
            taph[nm] = dout(nm, shp)

    ops = {"sync": [], "scalar": [], "vector": [], "tensor": [], "gpsimd": []}

    def emit(eng, fn):
        ops[eng].append(fn)

    glob = []

    def sbuf(name, shape, dt, scope):
        g = nc.sbuf_tensor(name, list(shape), dt)
        t = g.__enter__()
        scope.append(g)
        return t

    def psum(name, shape, scope, dt=F32):
        g = nc.psum_tensor(name, list(shape), dt)
        t = g.__enter__()
        scope.append(g)
        return t

    def close_scope(scope):
        for g in reversed(scope):
            g.__exit__(None, None, None)
        scope.clear()

    all_ctrs = []

    def sem(name):
        g = nc.semaphore(name)
        h = g.__enter__()
        glob.append(g)
        c = Ctr(h, name)
        all_ctrs.append(c)
        return c

    sLD = sem("sLD")
    sST = sem("sST")
    sMM = sem("sMM")
    sACT = sem("sACT")
    sDVE = sem("sDVE")
    sOUT = sem("sOUT")

    STORE_CTRS = []

    class DStream:

        def __init__(self, name, n=2, store=False):
            self.s = [sem(f"{name}{i}") for i in range(n)]
            if store:
                STORE_CTRS.extend(self.s)

        def reserve(self, slot, ndma=1):
            return (self.s[slot], self.s[slot].reserve(16 * ndma))

        def attach(self, slot, instr):
            return self.s[slot].attach(instr, 16)

    def phase_barrier():
        waits = [(c, c.c) for c in STORE_CTRS if c.c > 0]

        def b(e, ws=waits):
            for c, v in ws:
                e.wait_ge(c.h, v)
        emit("sync", b)
        emit("scalar", b)

    def SA(instr):
        return sACT.attach(instr)

    def VA(instr):
        return sDVE.attach(instr)

    def sw(sc):
        sc.wait_ge(sACT.h, sACT.att)

    def vw(v):
        v.wait_ge(sDVE.h, sDVE.att)

    def pw(eng, w):
        if w is not None:
            eng.wait_ge(w[0].h, w[1])

    NBN = (D + 511) // 512

    def bn_pair(v, srcbuf, st6, mv):
        for c in range(NBN):
            vw(v)
            VA(v.bn_stats(st6[:, c * 6:(c + 1) * 6],
                          srcbuf[:, c * 512:min((c + 1) * 512, D)]))
        vw(v)
        VA(v.bn_aggr(mv[:], st6[:, 0:6 * NBN]))

    ones_bf = sbuf("ones_bf", [1, 128], BF16, glob)
    epsln_t = sbuf("epsln_t", [128, 1], F32, glob)
    rmssc_t = sbuf("rmssc_t", [128, 1], F32, glob)
    rmseps_t = sbuf("rmseps_t", [128, 1], F32, glob)
    zro_bf = sbuf("zro_bf", [1, 512], BF16, glob)
    qsc_t = sbuf("qsc_t", [128, 1], F32, glob)
    qeps_t = sbuf("qeps_t", [128, 1], F32, glob)
    sc03 = []
    mW_t = sbuf("mW_t", [128, 128], F32, sc03)
    mC_t = sbuf("mC_t", [128, 128], F32, sc03)
    mV_t = sbuf("mV_t", [128, TH], F32, sc03)
    id_bf = sbuf("id_bf", [128, 128], BF16, sc03)
    mW_b = sbuf("mW_b", [128, 128], BF16, sc03)
    mC_b = sbuf("mC_b", [128, 128], BF16, sc03)
    mV_b = sbuf("mV_b", [128, TH], BF16, sc03)
    idf_t = sbuf("idf_t", [128, 128], F32, sc03)
    onesr_b = sbuf("onesr_b", [1, 512], BF16, sc03)
    ones_kv = sbuf("ones_kv", [128, 128], BF16, sc03)
    sc02 = []
    cos_t = sbuf("cos_t", [128, KC * NJ], F32, sc02)
    sin_t = sbuf("sin_t", [128, KC * NJ], F32, sc02)

    sDVE.reserve(8)

    def v_consts(v):
        VA(v.memset(zro_bf[:], 0.0))
        VA(v.memset(epsln_t[:], EPS_LN))
        VA(v.memset(rmssc_t[:], 1.0 / D))
        VA(v.memset(rmseps_t[:], EPS_RMS))
        VA(v.memset(qsc_t[:], float(HD) / D))
        VA(v.memset(qeps_t[:], float(HD) * EPS_RMS))
        VA(v.memset(onesr_b[:], 1.0))
        VA(v.memset(ones_kv[:], 1.0))
    emit("vector", v_consts)

    def p0():
        sc0 = []
        post = sbuf("post", [128, KC], F32, sc0)
        jbc = sbuf("jbc", [128, NJ], F32, sc0)
        ones_f = sbuf("ones_f", [1, 128], F32, sc0)
        r_ang = sbuf("r_ang", [128, KC * NJ], F32, sc0)
        r_s1 = sbuf("r_s1", [128, KC * NJ], F32, sc0)
        r_s2 = sbuf("r_s2", [128, KC * NJ], F32, sc0)
        r_s3 = sbuf("r_s3", [128, KC * NJ], F32, sc0)
        c_a = sbuf("c_a", [128, 1], F32, sc0)

        w0 = [sLD.reserve_dma() for _ in range(7)][-1]

        def ld(s):
            sLD.attach_dma(s.dma_start(out=ones_f[:], in_=ones_row[:]))
            sLD.attach_dma(s.dma_start(out=idf_t[:], in_=ident[:]))
            sLD.attach_dma(s.dma_start(out=post[:], in_=posr[:]))
            sLD.attach_dma(s.dma_start(out=jbc[:], in_=jrow[:].broadcast_to((128, NJ))))
            sLD.attach_dma(s.dma_start(out=mW_t[:], in_=mW[:]))
            sLD.attach_dma(s.dma_start(out=mC_t[:], in_=mC[:]))
            sLD.attach_dma(s.dma_start(out=mV_t[:], in_=mV[:]))
        emit("sync", ld)

        a_if = sACT.reserve(9)

        def sc_c(sc):
            sc.wait_ge(sLD.h, w0)
            SA(sc.activation(out=ones_bf[:], in_=ones_f[:], func=AF.Copy))
            SA(sc.activation(out=id_bf[:], in_=idf_t[:], func=AF.Copy))
            SA(sc.activation(out=mW_b[:], in_=mW_t[:], func=AF.Copy))
            SA(sc.activation(out=mC_b[:], in_=mC_t[:], func=AF.Copy))
            SA(sc.activation(out=mV_b[:], in_=mV_t[:], func=AF.Copy))
            SA(sc.activation(out=c_a[:], in_=jbc[:, 0:1], func=AF.Copy,
                             scale=0.0, bias=10000.0))
            sw(sc)
            SA(sc.activation(out=c_a[:], in_=c_a[:], func=AF.Ln))
            sw(sc)
            SA(sc.mul(c_a[:], c_a[:], -1.0 / NJ))
            sw(sc)
            SA(sc.activation(out=r_s1[:, 0:NJ], in_=jbc[:], func=AF.Exp,
                             scale=c_a[:]))
        emit("scalar", sc_c)

        mmax = max(0, int(math.ceil(math.log2(max(TKV, 7) / (2 * math.pi)))))
        sDVE.reserve(1 + 3 * (mmax + 1))

        def v_ang(v):
            v.wait_ge(sACT.h, a_if)
            ang = r_ang[:].rearrange("p (kc j) -> p kc j", j=NJ)
            vw(v)
            VA(v.tensor_tensor(
                ang,
                post[:].unsqueeze(2).broadcast_to((128, KC, NJ)),
                r_s1[:, 0:NJ].unsqueeze(1).broadcast_to((128, KC, NJ)),
                ALU.mult))
            for m in range(mmax, -1, -1):
                c = 2 * math.pi * (2 ** m)
                vw(v)
                VA(v.tensor_single_scalar(r_s1[:], r_ang[:], c, ALU.is_ge))
                vw(v)
                VA(v.tensor_scalar(r_s1[:], r_s1[:], c, None, ALU.mult))
                vw(v)
                VA(v.tensor_tensor(r_ang[:], r_ang[:], r_s1[:], ALU.subtract))
        emit("vector", v_ang)

        def make_table(shift, out_tab):
            dv = sDVE.reserve(14)

            def vf(v):
                vw(v)
                VA(v.tensor_scalar(r_s1[:], r_ang[:], shift, None, ALU.add))
                vw(v)
                VA(v.tensor_single_scalar(r_s2[:], r_s1[:], 2 * math.pi, ALU.is_ge))
                vw(v)
                VA(v.tensor_scalar(r_s2[:], r_s2[:], 2 * math.pi, None, ALU.mult))
                vw(v)
                VA(v.tensor_tensor(r_s1[:], r_s1[:], r_s2[:], ALU.subtract))
                vw(v)
                VA(v.tensor_scalar(r_s1[:], r_s1[:], math.pi, None, ALU.subtract))
                vw(v)
                VA(v.tensor_tensor(r_s2[:], r_s1[:], r_s1[:], ALU.mult))
                vw(v)
                VA(v.tensor_single_scalar(r_s2[:], r_s2[:],
                                          (math.pi / 2) ** 2, ALU.is_ge))
                vw(v)
                VA(v.tensor_single_scalar(r_s3[:], r_s1[:], 0.0, ALU.is_ge))
                vw(v)
                VA(v.tensor_scalar(r_s3[:], r_s3[:], 2.0, -1.0, ALU.mult, ALU.add))
                vw(v)
                VA(v.tensor_tensor(r_s3[:], r_s3[:], r_s2[:], ALU.mult))
                vw(v)
                VA(v.tensor_scalar(r_s3[:], r_s3[:], math.pi, None, ALU.mult))
                vw(v)
                VA(v.tensor_scalar(r_s2[:], r_s2[:], -2.0, 1.0, ALU.mult, ALU.add))
                vw(v)
                VA(v.tensor_tensor(r_s1[:], r_s1[:], r_s2[:], ALU.mult))
                vw(v)
                VA(v.tensor_tensor(r_s1[:], r_s1[:], r_s3[:], ALU.add))
            emit("vector", vf)
            av = sACT.reserve()

            def sf(sc):
                sc.wait_ge(sDVE.h, dv)
                sw(sc)
                SA(sc.activation(out=out_tab[:], in_=r_s1[:], func=AF.Sin,
                                 scale=-1.0))
            emit("scalar", sf)
            emit("vector", lambda v, w=av: v.wait_ge(sACT.h, w))

        make_table(0.0, sin_t)
        make_table(math.pi / 2, cos_t)
        fin = sACT.c
        emit("sync", lambda s, w=fin: s.wait_ge(sACT.h, w))
        close_scope(sc0)

    p0()

    def p1():
        phase_barrier()
        sc1 = []
        NP1 = 3
        dsx = DStream("p1x", n=NP1)
        dst1 = DStream("st1", n=NP1, store=True)
        xpool = Pool(NP1, "p1x")
        x_in = [sbuf(f"p1x{i}", [128, D], F32, sc1) for i in range(NP1)]
        lnf = [sbuf(f"p1lnf{i}", [128, D], F32, sc1) for i in range(NP1)]
        lnh = [sbuf(f"p1lnh{i}", [128, D], BF16, sc1) for i in range(NP1)]
        lnl = [sbuf(f"p1lnl{i}", [128, D], BF16, sc1) for i in range(NP1)]
        st6 = [sbuf(f"p1st6{i}", [128, 6 * NBN], F32, sc1) for i in range(NP1)]
        mv = [sbuf(f"p1mv{i}", [128, 2], F32, sc1) for i in range(NP1)]
        r_t = [sbuf(f"p1r{i}", [128, 1], F32, sc1) for i in range(NP1)]
        b_t = [sbuf(f"p1b{i}", [128, 1], F32, sc1) for i in range(NP1)]
        stw_par = [None] * NP1

        for t in range(KC):
            p = t % NP1
            slot, wv = xpool.acquire()
            xw = dsx.reserve(slot)

            def ld(s, t=t, slot=slot, wv=wv):
                pw(s, wv)
                dsx.attach(slot, s.dma_start(out=x_in[slot][:],
                                             in_=xh[t * 128:(t + 1) * 128, :]))
            emit("sync", ld)

            spw = stw_par[p]
            d0 = sDVE.reserve(NBN + 1)

            def v_st(v, slot=slot, p=p, w=xw, spw=spw):
                pw(v, w)
                pw(v, spw)
                bn_pair(v, x_in[slot][:], st6[p], mv[p])
            emit("vector", v_st)

            a0 = sACT.reserve()

            def s_r(sc, p=p, w=d0):
                sc.wait_ge(sDVE.h, w)
                sw(sc)
                SA(sc.activation(out=r_t[p][:], in_=mv[p][:, 1:2], func=AF.Sqrt,
                                 bias=epsln_t[:]))
            emit("scalar", s_r)

            d1 = sDVE.reserve(2)

            def v_b(v, p=p, w=a0):
                v.wait_ge(sACT.h, w)
                vw(v)
                VA(v.reciprocal(r_t[p][:], r_t[p][:]))
                vw(v)
                VA(v.tensor_scalar(b_t[p][:], mv[p][:, 0:1], r_t[p][:], -1.0,
                                   ALU.mult, ALU.mult))
            emit("vector", v_b)

            a1 = sACT.reserve(2)
            xpool.released_at(sACT, a1 - 1)

            def s_ln(sc, slot=slot, p=p, w=d1):
                sc.wait_ge(sDVE.h, w)
                sw(sc)
                SA(sc.activation(out=lnf[p][:], in_=x_in[slot][:],
                                 func=AF.Identity,
                                 scale=r_t[p][:], bias=b_t[p][:]))
                sw(sc)
                SA(sc.activation(out=lnh[p][:], in_=lnf[p][:], func=AF.Copy))
            emit("scalar", s_ln)

            dlo = sDVE.reserve()

            def v_fin(v, p=p, w=a1):
                v.wait_ge(sACT.h, w)
                vw(v)
                VA(v.tensor_tensor(lnl[p][:], lnf[p][:], lnh[p][:],
                                   ALU.subtract))
            emit("vector", v_fin)

            nst = 3 if taps else 2
            stw = dst1.reserve(p, nst)
            stw_par[p] = stw

            def g_st(g, t=t, p=p, w=dlo):
                g.wait_ge(sDVE.h, w)
                dst1.attach(p, g.dma_start(out=s_ln1h[t * 128:(t + 1) * 128, :],
                                           in_=lnh[p][:]))
                dst1.attach(p, g.dma_start(out=s_ln1l[t * 128:(t + 1) * 128, :],
                                           in_=lnl[p][:]))
                if taps:
                    dst1.attach(p, g.dma_start(
                        out=taph["t_ln1"][t * 128:(t + 1) * 128, :],
                        in_=lnf[p][:]))
            emit("gpsimd", g_st)

        close_scope(sc1)

    p1()

    def load_w_resident(pref, src, ncols, scope):
        whi = [sbuf(f"{pref}h{i}", [128, ncols], BF16, scope) for i in range(NDC)]
        wlo = [sbuf(f"{pref}l{i}", [128, ncols], BF16, scope) for i in range(NDC)]
        tmp = []
        nws = 4
        wsl = [sbuf(f"{pref}t{i}", [128, ncols], F32, tmp) for i in range(nws)]
        dsw = DStream(pref + "d", n=nws)
        pl = Pool(nws, pref)
        for dc in range(NDC):
            slot, wv = pl.acquire()
            ww = dsw.reserve(slot)

            def ld(s, dc=dc, slot=slot, wv=wv):
                pw(s, wv)
                dsw.attach(slot, s.dma_start(out=wsl[slot][:],
                                             in_=src[dc * 128:(dc + 1) * 128, :]))
            emit("sync", ld)

            aw = sACT.reserve()

            def s_h(sc, dc=dc, slot=slot, w=ww):
                pw(sc, w)
                sw(sc)
                SA(sc.activation(out=whi[dc][:], in_=wsl[slot][:], func=AF.Copy))
            emit("scalar", s_h)

            dv = sDVE.reserve()
            pl.released_at(sDVE, dv)

            def v_l(v, dc=dc, slot=slot, w=aw):
                v.wait_ge(sACT.h, w)
                vw(v)
                VA(v.tensor_tensor(wlo[dc][:], wsl[slot][:], whi[dc][:],
                                   ALU.subtract))
            emit("vector", v_l)
        ready = (sDVE.c, sACT.c)
        close_scope(tmp)
        return whi, wlo, ready

    def emit_mm_tc(te, ps_ap, n_out, groups):
        ocs = [(i * 512, min(512, n_out - i * 512))
               for i in range((n_out + 511) // 512)]
        plan = []
        for (lh, ll, rh, rl) in groups:
            plan.append((lh, rh))
            plan.append((lh, rl))
            if ll is not None:
                plan.append((ll, rh))
        last = None
        first = [True] * len(ocs)
        for pi, (lt, rt) in enumerate(plan):
            for oi, (o0, on) in enumerate(ocs):
                st = first[oi]
                first[oi] = False
                last = te.matmul(ps_ap[:, o0:o0 + on], lt, rt[:, o0:o0 + on],
                                 start=st, stop=(pi == len(plan) - 1))
        return last

    def p2_og(og):
        phase_barrier()
        sc2 = []
        dstat = DStream(f"p2{og}s")
        dst2 = DStream(f"st2{og}", store=True)
        wsrc = {"q": wqT, "k": wkT, "v": wvT}[og]
        whi, wlo, wready = load_w_resident(f"w{og}", wsrc, D, sc2)

        wrep_w = None
        if og in ("q", "k"):
            wrep = sbuf(f"p2{og}rep", [128, D], F32, sc2)
            wrep_w = sLD.reserve_dma()

            def ldrep(s, og=og, w=wready[0]):
                s.wait_ge(sDVE.h, w)
                wsrc2 = qnw if og == "q" else knw
                sLD.attach_dma(s.dma_start(out=wrep[:],
                                           in_=wsrc2[:].broadcast_to((128, D))))
            emit("sync", ldrep)

        statpool = Pool(2, f"p2{og}st", init=(sDVE, wready[0]))
        sth = [sbuf(f"p2{og}sh{i}", [128, D], BF16, sc2) for i in range(2)]
        stl = [sbuf(f"p2{og}sl{i}", [128, D], BF16, sc2) for i in range(2)]
        pspool = Pool(2, f"p2{og}ps")
        ps2 = [psum(f"p2{og}ps{i}", [128, D], sc2) for i in range(2)]
        qn = sbuf(f"p2{og}qn", [128, D], F32, sc2)
        rr = sbuf(f"p2{og}rr", [128, D], F32, sc2)
        ohi = sbuf(f"p2{og}oh", [128, D], BF16, sc2)
        olo = sbuf(f"p2{og}ol", [128, D], BF16, sc2)
        sq_t = sbuf(f"p2{og}sq", [128, 1], F32, sc2)
        rq_t = sbuf(f"p2{og}rq", [128, 1], F32, sc2)

        tcs = list(range(HC, KC)) if og == "q" else list(range(KC))
        dsth, dstl = {"q": (s_qh, s_ql), "k": (s_kh, s_kl),
                      "v": (s_vh, s_vl)}[og]
        tapn = {"q": "t_q", "k": "t_k", "v": "t_v"}[og]

        for tc in tcs:
            sslot, swv = statpool.acquire()
            trw = dstat.reserve(sslot, 2)

            def tl(sc, tc=tc, sslot=sslot, swv=swv):
                pw(sc, swv)
                r0 = tc * 128
                src_h = s_ln1h[r0:r0 + 128, :].rearrange("t (dc d) -> t dc d", d=128)
                src_l = s_ln1l[r0:r0 + 128, :].rearrange("t (dc d) -> t dc d", d=128)
                dst_h = sth[sslot][:].rearrange("p (dc t) -> p dc t", t=128)
                dst_l = stl[sslot][:].rearrange("p (dc t) -> p dc t", t=128)
                dstat.attach(sslot, sc.dma_start(out=dst_h, in_=src_h, transpose=True))
                dstat.attach(sslot, sc.dma_start(out=dst_l, in_=src_l, transpose=True))
            emit("sync", tl)

            pslot, pwv = pspool.acquire()
            mmv = sMM.reserve()
            statpool.released_at(sMM, mmv)

            def mms(te, sslot=sslot, pslot=pslot, pwv=pwv, trw=trw):
                pw(te, pwv)
                pw(te, trw)
                te.wait_ge(sDVE.h, wready[0])
                te.wait_ge(sACT.h, wready[1])
                groups = [(sth[sslot][:, dc * 128:(dc + 1) * 128],
                           stl[sslot][:, dc * 128:(dc + 1) * 128],
                           whi[dc][:], wlo[dc][:]) for dc in range(NDC)]
                sMM.attach(emit_mm_tc(te, ps2[pslot][:], D, groups))
            emit("tensor", mms)
            ps = ps2[pslot]
            row0 = (tc - HC) * 128 if og == "q" else tc * 128

            if og in ("q", "k"):
                a0 = sACT.reserve(2)

                def s_e(sc, og=og, ps=ps, w=mmv):
                    sc.wait_ge(sMM.h, w)
                    sw(sc)
                    SA(sc.activation(out=rr[:], in_=ps[:], func=AF.Square,
                                     accum_out=sq_t[:]))
                    sw(sc)
                    if og == "q":
                        SA(sc.activation(out=rq_t[:], in_=sq_t[:], func=AF.Sqrt,
                                         scale=qsc_t[:], bias=qeps_t[:]))
                    else:
                        SA(sc.activation(out=rq_t[:], in_=sq_t[:], func=AF.Sqrt,
                                         scale=rmssc_t[:], bias=rmseps_t[:]))
                emit("scalar", s_e)

                d0 = sDVE.reserve(8)
                pspool.released_at(sDVE, d0 - 6)

                def v_e(v, tc=tc, ps=ps, w=a0, ww=wrep_w):
                    v.wait_ge(sACT.h, w)
                    v.wait_ge(sLD.h, ww)
                    vw(v)
                    VA(v.reciprocal(rq_t[:], rq_t[:]))
                    vw(v)
                    VA(v.scalar_tensor_tensor(qn[:], ps[:], rq_t[:], wrep[:],
                                              ALU.mult, ALU.mult))

                    def hv(buf, half):
                        vvw = buf[:].rearrange("p (h two j) -> p h two j",
                                               two=2, j=NJ)
                        return vvw[:, :, half, :]
                    cosv = cos_t[:, tc * NJ:(tc + 1) * NJ] \
                        .unsqueeze(1).broadcast_to((128, NH, NJ))
                    sinv = sin_t[:, tc * NJ:(tc + 1) * NJ] \
                        .unsqueeze(1).broadcast_to((128, NH, NJ))
                    qt, qb = hv(qn, 0), hv(qn, 1)
                    rt_, rb = hv(rr, 0), hv(rr, 1)
                    vw(v); VA(v.tensor_tensor(rt_, qb, sinv, ALU.mult))
                    vw(v); VA(v.tensor_tensor(rb, qb, cosv, ALU.mult))
                    vw(v); VA(v.tensor_tensor(qb, qt, sinv, ALU.mult))
                    vw(v); VA(v.tensor_tensor(rb, rb, qb, ALU.add))
                    vw(v); VA(v.tensor_tensor(qb, qt, cosv, ALU.mult))
                    vw(v); VA(v.tensor_tensor(rt_, qb, rt_, ALU.subtract))
                emit("vector", v_e)

                a1 = sACT.reserve()

                def s_hi(sc, w=d0):
                    sc.wait_ge(sDVE.h, w)
                    sw(sc)
                    SA(sc.activation(out=ohi[:], in_=rr[:], func=AF.Copy))
                emit("scalar", s_hi)

                dlo = sDVE.reserve()

                def v_lo(v, w=a1):
                    v.wait_ge(sACT.h, w)
                    vw(v)
                    VA(v.tensor_tensor(olo[:], rr[:], ohi[:], ALU.subtract))
                emit("vector", v_lo)
            else:
                a0 = sACT.reserve(2 if taps else 1)

                def s_e(sc, ps=ps, w=mmv):
                    sc.wait_ge(sMM.h, w)
                    sw(sc)
                    SA(sc.activation(out=ohi[:], in_=ps[:], func=AF.Copy))
                    if taps:
                        sw(sc)
                        SA(sc.activation(out=rr[:], in_=ps[:], func=AF.Copy))
                emit("scalar", s_e)

                dlo = sDVE.reserve()
                pspool.released_at(sDVE, dlo)

                def v_lo(v, ps=ps, w=a0):
                    v.wait_ge(sACT.h, w)
                    vw(v)
                    VA(v.tensor_tensor(olo[:], ps[:], ohi[:], ALU.subtract))
                emit("vector", v_lo)

            p = tc % 2
            nst = 3 if taps else 2
            stw = dst2.reserve(p, nst)

            def g_st(g, p=p, row0=row0, w=dlo):
                g.wait_ge(sDVE.h, w)
                dst2.attach(p, g.dma_start(out=dsth[row0:row0 + 128, :],
                                           in_=ohi[:]))
                dst2.attach(p, g.dma_start(out=dstl[row0:row0 + 128, :],
                                           in_=olo[:]))
                if taps:
                    dst2.attach(p, g.dma_start(
                        out=taph[tapn][row0:row0 + 128, :], in_=rr[:]))
            emit("gpsimd", g_st)
            emit("scalar", lambda sc, w=stw: pw(sc, w))

        close_scope(sc2)

    p2_og("q")
    p2_og("k")
    p2_og("v")
    close_scope(sc02)

    def p3():
        phase_barrier()
        sc3 = []
        dqk = DStream("p3qk", n=3)
        dst3 = DStream("st3", store=True)
        dv_ = DStream("p3v", n=3)
        NHB = 3
        headpool = Pool(NHB, "p3head")
        qTh = [sbuf(f"p3qh{i}", [128, T], BF16, sc3) for i in range(NHB)]
        qTl = [sbuf(f"p3ql{i}", [128, T], BF16, sc3) for i in range(NHB)]
        kTh = [sbuf(f"p3kh{i}", [128, TKV], BF16, sc3) for i in range(NHB)]
        kTl = [sbuf(f"p3kl{i}", [128, TKV], BF16, sc3) for i in range(NHB)]
        vh_ = [sbuf(f"p3vh{i}", [128, TKV], BF16, sc3) for i in range(NHB)]
        vl_ = [sbuf(f"p3vl{i}", [128, TKV], BF16, sc3) for i in range(NHB)]
        e32 = [sbuf(f"p3e{i}", [128, 512], F32, sc3) for i in range(2)]
        ehi = [sbuf(f"p3eh{i}", [128, 512], BF16, sc3) for i in range(3)]
        elo = [sbuf(f"p3el{i}", [128, 512], BF16, sc3) for i in range(3)]
        den_sb = [sbuf(f"p3dsb{i}", [128, QG], F32, sc3) for i in range(2)]
        lnt = sbuf("p3lnt", [128, QG], F32, sc3)
        rcpb = sbuf("p3rcpb", [128, QG], F32, sc3)
        aof_ev = sbuf("p3aof", [128, QG], F32, sc3)
        aoh_ev = sbuf("p3aoh", [128, QG], BF16, sc3)
        aol_ev = sbuf("p3aol", [128, QG], BF16, sc3)
        scpool = Pool(3, "p3sc")
        ps_s = [psum(f"p3ps{i}", [128, 512], sc3) for i in range(3)]
        epool = Pool(3, "p3ep")
        aopool = Pool(2, "p3ao")
        ps_ao = [psum(f"p3pao{i}", [128, QG], sc3) for i in range(2)]
        ps_dn = [psum(f"p3pdn{i}", [128, 512], sc3) for i in range(2)]

        e32_free = {0: None, 1: None}
        dn_free = {}
        gci = 0
        prev_stw = None
        prev_dlo2 = None
        pending_evac = None
        for h in range(NH):
            hslot, hwv = headpool.acquire()
            trw = dqk.reserve(hslot, 4)

            def hload_t(sc, h=h, hslot=hslot, hwv=hwv):
                pw(sc, hwv)
                c0 = h * 128
                dqk.attach(hslot, sc.dma_start(out=qTh[hslot][:],
                                               in_=s_qh[:, c0:c0 + 128],
                                               transpose=True))
                dqk.attach(hslot, sc.dma_start(out=qTl[hslot][:],
                                               in_=s_ql[:, c0:c0 + 128],
                                               transpose=True))
                dqk.attach(hslot, sc.dma_start(out=kTh[hslot][:],
                                               in_=s_kh[:, c0:c0 + 128],
                                               transpose=True))
                dqk.attach(hslot, sc.dma_start(out=kTl[hslot][:],
                                               in_=s_kl[:, c0:c0 + 128],
                                               transpose=True))
            emit("sync", hload_t)

            vldw = dv_.reserve(hslot, 2)

            def hload_v(s, h=h, hslot=hslot, hwv=hwv):
                pw(s, hwv)
                c0 = h * 128
                src_h = s_vh[:, c0:c0 + 128].rearrange("(kc t) d -> t kc d", t=128)
                src_l = s_vl[:, c0:c0 + 128].rearrange("(kc t) d -> t kc d", t=128)
                dst_h = vh_[hslot][:].rearrange("p (kc d) -> p kc d", d=128)
                dst_l = vl_[hslot][:].rearrange("p (kc d) -> p kc d", d=128)
                dv_.attach(hslot, s.dma_start(out=dst_h, in_=src_h))
                dv_.attach(hslot, s.dma_start(out=dst_l, in_=src_l))
            emit("sync", hload_v)

            pv_done_prev = None
            for qg in range(NQG):
                qc0, qc1 = qg * QGC, (qg + 1) * QGC - 1
                kvcs = list(range(qc0, qc1 + WKC + 1))
                aoslot, aowv = aopool.acquire()

                def zf(te, aowv=aowv, aoslot=aoslot):
                    pw(te, aowv)
                    te.matmul(ps_ao[aoslot][:, 0:QG], ones_bf[:],
                              zro_bf[:, 0:QG], start=True, stop=False)
                emit("tensor", zf)

                def emit_pv(info, is_last, aowv=aowv, hslot=hslot,
                            vldw=vldw, qc0=qc0, aoslot=aoslot):
                    ci, kvc, eslot, ew, qa, qb, dnf = info
                    nc_ = (qb - qa + 1) * 128
                    pvv = sMM.reserve()
                    epool.released_at(sMM, pvv)

                    def t_pv(te, ci=ci, kvc=kvc, eslot=eslot, ew=ew, qa=qa,
                             qb=qb, nc_=nc_, is_last=is_last, qc0=qc0,
                             dnf=dnf, aoslot=aoslot):
                        te.wait_ge(sDVE.h, ew)
                        pw(te, vldw)
                        if dnf is not None:
                            te.wait_ge(sDVE.h, dnf)
                        off = (qa - qc0) * 128
                        vs_h = vh_[hslot][:, kvc * 128:(kvc + 1) * 128]
                        vs_l = vl_[hslot][:, kvc * 128:(kvc + 1) * 128]
                        eh = ehi[eslot][:, 0:nc_]
                        el = elo[eslot][:, 0:nc_]
                        te.matmul(ps_ao[aoslot][:, off:off + nc_], vs_h, eh,
                                  start=False, stop=False)
                        te.matmul(ps_ao[aoslot][:, off:off + nc_], vs_h, el,
                                  start=False, stop=False)
                        te.matmul(ps_ao[aoslot][:, off:off + nc_], vs_l, eh,
                                  start=False, stop=is_last)
                        te.matmul(ps_dn[ci % 2][:, 0:nc_], ones_kv[:], eh,
                                  start=True, stop=False)
                        lastm = te.matmul(ps_dn[ci % 2][:, 0:nc_], ones_kv[:],
                                          el, start=False, stop=True)
                        sMM.attach(lastm)
                    emit("tensor", t_pv)

                    dnv = sDVE.reserve(2 if ci == 0 else 1)
                    dpar = qg % 2

                    def v_dn(v, ci=ci, qa=qa, nc_=nc_, w=pvv, qc0=qc0,
                             dpar=dpar):
                        v.wait_ge(sMM.h, w)
                        off = (qa - qc0) * 128
                        if ci == 0:
                            vw(v)
                            VA(v.memset(den_sb[dpar][:], 0.0))
                        vw(v)
                        VA(v.tensor_tensor(den_sb[dpar][:, off:off + nc_],
                                           ps_dn[ci % 2][:, 0:nc_],
                                           den_sb[dpar][:, off:off + nc_],
                                           ALU.add))
                    emit("vector", v_dn)
                    return pvv, dnv

                pend = []
                pv_done = None
                dn_last = None
                for ci, kvc in enumerate(kvcs):
                    if ci == 1 and pending_evac is not None:
                        prev_stw, prev_dlo2 = pending_evac()
                        pending_evac = None
                    qa = max(qc0, kvc - WKC)
                    qb = min(qc1, kvc)
                    nc_ = (qb - qa + 1) * 128
                    cslot, cwv = scpool.acquire()
                    mmv = sMM.reserve()

                    def t_sc(te, kvc=kvc, qa=qa, qb=qb, nc_=nc_, hslot=hslot,
                             cslot=cslot, cwv=cwv, trw=trw):
                        pw(te, cwv)
                        pw(te, trw)
                        ps = ps_s[cslot]
                        kh = kTh[hslot][:, kvc * 128:(kvc + 1) * 128]
                        kl = kTl[hslot][:, kvc * 128:(kvc + 1) * 128]
                        qh = qTh[hslot][:, qa * 128:qa * 128 + nc_]
                        ql = qTl[hslot][:, qa * 128:qa * 128 + nc_]
                        te.matmul(ps[:, 0:nc_], kh, qh, start=True, stop=False)
                        te.matmul(ps[:, 0:nc_], kh, ql, start=False, stop=False)
                        if kvc <= qb:
                            o = (kvc - qa) * 128
                            te.matmul(ps[:, o:o + 128], mW_b[:], id_bf[:],
                                      start=False, stop=False)
                        if kvc - WKC >= qa:
                            o = (kvc - WKC - qa) * 128
                            te.matmul(ps[:, o:o + 128], mC_b[:], id_bf[:],
                                      start=False, stop=False)
                        if kvc < HC:
                            te.matmul(ps[:, 0:nc_],
                                      mV_b[0:1, kvc * 128:(kvc + 1) * 128],
                                      onesr_b[:, 0:nc_],
                                      start=False, stop=False)
                        last = te.matmul(ps[:, 0:nc_], kl, qh,
                                         start=False, stop=True)
                        sMM.attach(last)
                    emit("tensor", t_sc)

                    eslot, ewv = epool.acquire()
                    asp = sACT.reserve(2)
                    scpool.released_at(sACT, asp)
                    par = gci % 2
                    e2f = e32_free[par]

                    def s_split(sc, cslot=cslot, eslot=eslot, ewv=ewv,
                                nc_=nc_, w=mmv, e2f=e2f, par=par):
                        sc.wait_ge(sMM.h, w)
                        pw(sc, ewv)
                        if e2f is not None:
                            sc.wait_ge(sDVE.h, e2f)
                        sw(sc)
                        SA(sc.activation(out=e32[par][:, 0:nc_],
                                         in_=ps_s[cslot][:, 0:nc_],
                                         func=AF.Exp))
                        sw(sc)
                        SA(sc.activation(out=ehi[eslot][:, 0:nc_],
                                         in_=ps_s[cslot][:, 0:nc_],
                                         func=AF.Exp))
                    emit("scalar", s_split)

                    dlo = sDVE.reserve(1)

                    def v_lo(v, eslot=eslot, nc_=nc_, w=asp, par=par):
                        v.wait_ge(sACT.h, w)
                        vw(v)
                        VA(v.tensor_tensor(elo[eslot][:, 0:nc_],
                                           e32[par][:, 0:nc_],
                                           ehi[eslot][:, 0:nc_], ALU.subtract))
                    emit("vector", v_lo)
                    e32_free[par] = dlo
                    gci += 1

                    if len(pend) == 2:
                        first = pend.pop(0)
                        _, dnv = emit_pv(first, False)
                        dn_free[first[0] % 2] = dnv
                        dn_last = dnv
                    dnf = dn_free.get(ci % 2)
                    pend.append((ci, kvc, eslot, dlo, qa, qb, dnf))
                while len(pend) > 1:
                    first = pend.pop(0)
                    _, dnv = emit_pv(first, False)
                    dn_free[first[0] % 2] = dnv
                    dn_last = dnv
                lastc = pend.pop(0)
                pv_done, dn_last = emit_pv(lastc, True)
                dn_free[lastc[0] % 2] = dn_last

                def make_evac(h=h, qg=qg, pv_done=pv_done, aoslot=aoslot,
                              pstw=prev_stw, pdlo2=prev_dlo2, dn_last=dn_last):
                    def do_emit():
                        dpar = qg % 2
                        arc = sACT.reserve(2)

                        def s_rc(sc, w=dn_last, pdlo2=pdlo2):
                            sc.wait_ge(sDVE.h, w)
                            if pdlo2 is not None:
                                sc.wait_ge(sDVE.h, pdlo2)
                            sw(sc)
                            SA(sc.activation(out=lnt[:], in_=den_sb[dpar][:],
                                             func=AF.Ln))
                            sw(sc)
                            SA(sc.activation(out=rcpb[:], in_=lnt[:],
                                             func=AF.Exp, scale=-1.0))
                        emit("scalar", s_rc)

                        dlo2 = sDVE.reserve(3)
                        aopool.released_at(sDVE, dlo2)

                        def v_ev(v, w=pv_done, pstw=pstw, aoslot=aoslot,
                                 arc=arc):
                            v.wait_ge(sMM.h, w)
                            v.wait_ge(sACT.h, arc)
                            if pstw is not None:
                                pw(v, pstw)
                            vw(v)
                            VA(v.tensor_tensor(aof_ev[:], ps_ao[aoslot][:],
                                               rcpb[:], ALU.mult))
                            vw(v)
                            VA(v.tensor_copy(aoh_ev[:], aof_ev[:]))
                            vw(v)
                            VA(v.tensor_tensor(aol_ev[:], aof_ev[:], aoh_ev[:],
                                               ALU.subtract))
                        emit("vector", v_ev)

                        p = (h * NQG + qg) % 2
                        nst = 3 if taps else 2
                        stw = dst3.reserve(p, nst)

                        def g_st(g, h=h, qg=qg, p=p, w=dlo2):
                            g.wait_ge(sDVE.h, w)
                            r0 = h * 128
                            c0 = qg * QG
                            dst3.attach(p, g.dma_start(
                                out=s_aoh[r0:r0 + 128, c0:c0 + QG],
                                in_=aoh_ev[:]))
                            dst3.attach(p, g.dma_start(
                                out=s_aol[r0:r0 + 128, c0:c0 + QG],
                                in_=aol_ev[:]))
                            if taps:
                                dst3.attach(p, g.dma_start(
                                    out=taph["t_ao"][r0:r0 + 128, c0:c0 + QG],
                                    in_=aof_ev[:]))
                        emit("gpsimd", g_st)
                        return stw, dlo2
                    return do_emit

                pending_evac = make_evac()
                pv_done_prev = pv_done

            headpool.released_at(sMM, pv_done_prev)

        if pending_evac is not None:
            prev_stw, prev_dlo2 = pending_evac()
            pending_evac = None

        close_scope(sc3)

    p3()
    close_scope(sc03)

    def p4():
        phase_barrier()
        sc4 = []
        dstat = DStream("p4s")
        dx = DStream("p4x")
        dst4 = DStream("st4", store=True)
        whi, wlo, wready = load_w_resident("wo", woT, D, sc4)
        statpool = Pool(2, "p4st", init=(sDVE, wready[0]))
        sth = [sbuf(f"p4sh{i}", [128, D], BF16, sc4) for i in range(2)]
        stl = [sbuf(f"p4sl{i}", [128, D], BF16, sc4) for i in range(2)]
        pspool = Pool(2, "p4ps")
        ps4 = [psum(f"p4ps{i}", [128, D], sc4) for i in range(2)]
        xpool = Pool(2, "p4x", init=(sDVE, wready[0]))
        xt = [sbuf(f"p4x{i}", [128, D], F32, sc4) for i in range(2)]
        attnb = sbuf("p4attn", [128, D], F32, sc4)
        x2b = sbuf("p4x2", [128, D], F32, sc4)
        lnh = sbuf("p4lnh", [128, D], BF16, sc4)
        lnl = sbuf("p4lnl", [128, D], BF16, sc4)
        st6 = sbuf("p4st6", [128, 6 * NBN], F32, sc4)
        mv = sbuf("p4mv", [128, 2], F32, sc4)
        r3_t = sbuf("p4r3", [128, 1], F32, sc4)
        b3_t = sbuf("p4b3", [128, 1], F32, sc4)
        r2_t = sbuf("p4r2", [128, 1], F32, sc4)
        b2_t = sbuf("p4b2", [128, 1], F32, sc4)

        for tc in range(TC):
            sslot, swv = statpool.acquire()
            ldw = dstat.reserve(sslot, 2)

            def ld(s, tc=tc, sslot=sslot, swv=swv):
                pw(s, swv)
                src_h = s_aoh[:, tc * 128:(tc + 1) * 128] \
                    .rearrange("(dc d) t -> d dc t", d=128)
                src_l = s_aol[:, tc * 128:(tc + 1) * 128] \
                    .rearrange("(dc d) t -> d dc t", d=128)
                dst_h = sth[sslot][:].rearrange("p (dc t) -> p dc t", t=128)
                dst_l = stl[sslot][:].rearrange("p (dc t) -> p dc t", t=128)
                dstat.attach(sslot, s.dma_start(out=dst_h, in_=src_h))
                dstat.attach(sslot, s.dma_start(out=dst_l, in_=src_l))
            emit("sync", ld)

            xslot, xwv = xpool.acquire()
            xw = dx.reserve(xslot)

            def ldx(s, tc=tc, xslot=xslot, xwv=xwv):
                pw(s, xwv)
                r0 = (HC + tc) * 128
                dx.attach(xslot, s.dma_start(out=xt[xslot][:],
                                             in_=xh[r0:r0 + 128, :]))
            emit("sync", ldx)

            pslot, pwv = pspool.acquire()
            mmv = sMM.reserve()
            statpool.released_at(sMM, mmv)

            def mms(te, sslot=sslot, pslot=pslot, pwv=pwv, ldw=ldw):
                pw(te, pwv)
                pw(te, ldw)
                te.wait_ge(sDVE.h, wready[0])
                te.wait_ge(sACT.h, wready[1])
                groups = [(sth[sslot][:, dc * 128:(dc + 1) * 128],
                           stl[sslot][:, dc * 128:(dc + 1) * 128],
                           whi[dc][:], wlo[dc][:]) for dc in range(NDC)]
                sMM.attach(emit_mm_tc(te, ps4[pslot][:], D, groups))
            emit("tensor", mms)
            ps = ps4[pslot]

            a0 = sACT.reserve()
            pspool.released_at(sACT, a0)

            def s_c(sc, ps=ps, w=mmv):
                sc.wait_ge(sMM.h, w)
                sw(sc)
                SA(sc.activation(out=attnb[:], in_=ps[:], func=AF.Copy))
            emit("scalar", s_c)

            d0 = sDVE.reserve(NBN + 1)

            def v_st3(v, w=a0):
                v.wait_ge(sACT.h, w)
                bn_pair(v, attnb[:], st6, mv)
            emit("vector", v_st3)

            a1 = sACT.reserve()

            def s_r3(sc, w=d0):
                sc.wait_ge(sDVE.h, w)
                sw(sc)
                SA(sc.activation(out=r3_t[:], in_=mv[:, 1:2], func=AF.Sqrt,
                                 bias=epsln_t[:]))
            emit("scalar", s_r3)

            d1 = sDVE.reserve(4 + NBN + 1)
            xpool.released_at(sDVE, d1 - (NBN + 1))

            def v_x2(v, xslot=xslot, w=a1, xw=xw):
                v.wait_ge(sACT.h, w)
                pw(v, xw)
                vw(v)
                VA(v.reciprocal(r3_t[:], r3_t[:]))
                vw(v)
                VA(v.tensor_scalar(b3_t[:], mv[:, 0:1], r3_t[:], -1.0,
                                   ALU.mult, ALU.mult))
                vw(v)
                VA(v.tensor_scalar_add(x2b[:], xt[xslot][:], b3_t[:]))
                vw(v)
                VA(v.scalar_tensor_tensor(x2b[:], attnb[:], r3_t[:], x2b[:],
                                          ALU.mult, ALU.add))
                bn_pair(v, x2b[:], st6, mv)
            emit("vector", v_x2)

            a2 = sACT.reserve()

            def s_r2(sc, w=d1):
                sc.wait_ge(sDVE.h, w)
                sw(sc)
                SA(sc.activation(out=r2_t[:], in_=mv[:, 1:2], func=AF.Sqrt,
                                 bias=epsln_t[:]))
            emit("scalar", s_r2)

            d2 = sDVE.reserve(2)

            def v_b2(v, w=a2):
                v.wait_ge(sACT.h, w)
                vw(v)
                VA(v.reciprocal(r2_t[:], r2_t[:]))
                vw(v)
                VA(v.tensor_scalar(b2_t[:], mv[:, 0:1], r2_t[:], -1.0,
                                   ALU.mult, ALU.mult))
            emit("vector", v_b2)

            a3 = sACT.reserve(2)

            def s_ln2(sc, w=d2):
                sc.wait_ge(sDVE.h, w)
                sw(sc)
                SA(sc.activation(out=attnb[:], in_=x2b[:], func=AF.Identity,
                                 scale=r2_t[:], bias=b2_t[:]))
                sw(sc)
                SA(sc.activation(out=lnh[:], in_=attnb[:], func=AF.Copy))
            emit("scalar", s_ln2)

            dlo = sDVE.reserve()

            def v_fin(v, w=a3):
                v.wait_ge(sACT.h, w)
                vw(v)
                VA(v.tensor_tensor(lnl[:], attnb[:], lnh[:], ALU.subtract))
            emit("vector", v_fin)

            p = tc % 2
            nst = 4 if taps else 3
            stw = dst4.reserve(p, nst)

            def g_st(g, tc=tc, p=p, w=dlo):
                g.wait_ge(sDVE.h, w)
                r0 = tc * 128
                dst4.attach(p, g.dma_start(out=s_x2[r0:r0 + 128, :], in_=x2b[:]))
                dst4.attach(p, g.dma_start(out=s_ln2h[r0:r0 + 128, :], in_=lnh[:]))
                dst4.attach(p, g.dma_start(out=s_ln2l[r0:r0 + 128, :], in_=lnl[:]))
                if taps:
                    dst4.attach(p, g.dma_start(out=taph["t_x2"][r0:r0 + 128, :],
                                               in_=x2b[:]))
            emit("gpsimd", g_st)
            emit("scalar", lambda sc, w=stw: pw(sc, w))

        close_scope(sc4)

    p4()

    def p5a():
        phase_barrier()
        sc5 = []
        dmv = DStream("p5mv")
        wa_h = [sbuf(f"wa_h{i}", [128, HA], BF16, sc5) for i in range(NDC)]
        wa_l = [sbuf(f"wa_l{i}", [128, HA], BF16, sc5) for i in range(NDC)]
        wr_h = [sbuf(f"wr_h{i}", [128, E], BF16, sc5) for i in range(NDC)]
        wr_l = [sbuf(f"wr_l{i}", [128, E], BF16, sc5) for i in range(NDC)]
        tmp = []
        wfa = [sbuf(f"p5wfa{i}", [128, HA], F32, tmp) for i in range(2)]
        wfr = [sbuf(f"p5wfr{i}", [128, E], F32, tmp) for i in range(2)]
        dwa = DStream("p5wa")
        wapool = Pool(2, "p5wa", init=(sDVE, sDVE.c))
        for dc in range(NDC):
            slot, wv = wapool.acquire()
            ww = dwa.reserve(slot, 2)

            def ldw(s, dc=dc, slot=slot, wv=wv):
                pw(s, wv)
                dwa.attach(slot, s.dma_start(out=wfa[slot][:],
                                             in_=waT[dc * 128:(dc + 1) * 128, :]))
                dwa.attach(slot, s.dma_start(out=wfr[slot][:],
                                             in_=wrT[dc * 128:(dc + 1) * 128, :]))
            emit("sync", ldw)

            aw = sACT.reserve(2)

            def s_h(sc, dc=dc, slot=slot, w=ww):
                pw(sc, w)
                sw(sc)
                SA(sc.activation(out=wa_h[dc][:], in_=wfa[slot][:], func=AF.Copy))
                sw(sc)
                SA(sc.activation(out=wr_h[dc][:], in_=wfr[slot][:], func=AF.Copy))
            emit("scalar", s_h)

            dvv = sDVE.reserve(2)
            wapool.released_at(sDVE, dvv)

            def v_l(v, dc=dc, slot=slot, w=aw):
                v.wait_ge(sACT.h, w)
                vw(v)
                VA(v.tensor_tensor(wa_l[dc][:], wfa[slot][:], wa_h[dc][:],
                                   ALU.subtract))
                vw(v)
                VA(v.tensor_tensor(wr_l[dc][:], wfr[slot][:], wr_h[dc][:],
                                   ALU.subtract))
            emit("vector", v_l)
        w_rdy = (sDVE.c, sACT.c)
        close_scope(tmp)

        dst5 = DStream("st5", store=True)
        mvpool = Pool(2, "p5mv", init=(sDVE, w_rdy[0]))
        mv_h = [sbuf(f"p5mh{i}", [128, T], BF16, sc5) for i in range(2)]
        mv_l = [sbuf(f"p5ml{i}", [128, T], BF16, sc5) for i in range(2)]
        ps_ha = psum("p5pha", [128, T], sc5)
        ps_lg = psum("p5plg", [8, T], sc5)

        ocs = [(i * 512, min(512, T - i * 512)) for i in range((T + 511) // 512)]
        for dc in range(NDC):
            mslot, mwv = mvpool.acquire()
            trw = dmv.reserve(mslot, 2)

            def tl(sc, dc=dc, mslot=mslot, mwv=mwv):
                pw(sc, mwv)
                c0 = dc * 128
                dmv.attach(mslot, sc.dma_start(out=mv_h[mslot][:],
                                               in_=s_ln2h[:, c0:c0 + 128],
                                               transpose=True))
                dmv.attach(mslot, sc.dma_start(out=mv_l[mslot][:],
                                               in_=s_ln2l[:, c0:c0 + 128],
                                               transpose=True))
            emit("sync", tl)

            mm5v = sMM.reserve()
            mvpool.released_at(sMM, mm5v)

            def mm5(te, dc=dc, mslot=mslot, trw=trw):
                pw(te, trw)
                if dc == 0:
                    te.wait_ge(sDVE.h, w_rdy[0])
                    te.wait_ge(sACT.h, w_rdy[1])
                last = None
                for pi, (lt, rt) in enumerate(
                        ((wa_h[dc][:], mv_h[mslot][:]),
                         (wa_h[dc][:], mv_l[mslot][:]),
                         (wa_l[dc][:], mv_h[mslot][:]))):
                    for (o0, on) in ocs:
                        last = te.matmul(ps_ha[:, o0:o0 + on], lt,
                                         rt[:, o0:o0 + on],
                                         start=(dc == 0 and pi == 0),
                                         stop=(dc == NDC - 1 and pi == 2))
                for pi, (lt, rt) in enumerate(
                        ((wr_h[dc][:], mv_h[mslot][:]),
                         (wr_h[dc][:], mv_l[mslot][:]),
                         (wr_l[dc][:], mv_h[mslot][:]))):
                    for (o0, on) in ocs:
                        last = te.matmul(ps_lg[:, o0:o0 + on], lt,
                                         rt[:, o0:o0 + on],
                                         start=(dc == 0 and pi == 0),
                                         stop=(dc == NDC - 1 and pi == 2))
                sMM.attach(last)
            emit("tensor", mm5)
        mm_done = sMM.c

        haf = sbuf("p5haf", [128, T], F32, sc5)
        lgf = sbuf("p5lgf", [32, T], F32, sc5)
        lgT = sbuf("p5lgT", [128, TC * 32], F32, sc5)
        gateb = sbuf("p5gb", [128, TC * 32], BF16, sc5)
        g8f = sbuf("p5g8f", [32, T], BF16, sc5)
        gfm = sbuf("p5gfm", [128, T], BF16, sc5)
        mx_t = sbuf("p5mx", [128, 1], F32, sc5)

        dz = sDVE.reserve(2)

        def v_zero(v):
            vw(v)
            VA(v.memset(lgf[:], 0.0))
            vw(v)
            VA(v.memset(gateb[:], 0.0))
        emit("vector", v_zero)

        aev = sACT.reserve(2)

        def s_ev(sc, w=mm_done, wz=dz):
            sc.wait_ge(sMM.h, w)
            sc.wait_ge(sDVE.h, wz)
            sw(sc)
            SA(sc.activation(out=haf[:], in_=ps_ha[:], func=AF.Copy))
            sw(sc)
            SA(sc.activation(out=lgf[0:8, :], in_=ps_lg[:], func=AF.Copy))
        emit("scalar", s_ev)

        dgate = sDVE.reserve(2 * (T // 32) + 2 * TC)

        def v_gate(v, w=aev):
            v.wait_ge(sACT.h, w)
            for j in range(T // 32):
                tcj, rj = j // 4, (j % 4) * 32
                vw(v)
                VA(v.transpose(lgT[rj:rj + 32, tcj * 32:tcj * 32 + 32],
                               lgf[0:32, j * 32:(j + 1) * 32]))
            for tc in range(TC):
                vw(v)
                VA(v.reduce_max(mx_t[:], lgT[:, tc * 32:tc * 32 + E], axis=AX))
                vw(v)
                VA(v.tensor_scalar(gateb[:, tc * 32:tc * 32 + E],
                                   lgT[:, tc * 32:tc * 32 + E],
                                   mx_t[:], 2.0, ALU.is_ge, ALU.mult))
            for j in range(T // 32):
                tcj, rj = j // 4, (j % 4) * 32
                vw(v)
                VA(v.transpose(g8f[0:32, j * 32:(j + 1) * 32],
                               gateb[rj:rj + 32, tcj * 32:tcj * 32 + 32]))
        emit("vector", v_gate)

        gexp = dst5.reserve(0, R)

        def g_gate(g, w=dgate):
            g.wait_ge(sDVE.h, w)
            for r in range(R):
                dst5.attach(0, g.dma_start(out=gfm[r * E:(r + 1) * E, :],
                                           in_=g8f[0:E, :]))
        emit("gpsimd", g_gate)

        dhag = sDVE.reserve()

        def v_hag(v, w=gexp):
            pw(v, w)
            vw(v)
            VA(v.tensor_tensor(haf[:], haf[:], gfm[:], ALU.mult))
        emit("vector", v_hag)

        if taps:
            dst5.reserve(1, 2)

            def g_tap(g, w=dhag):
                g.wait_ge(sDVE.h, w)
                dst5.attach(1, g.dma_start(out=taph["t_gate"][:, :], in_=gfm[:]))
                dst5.attach(1, g.dma_start(out=taph["t_ha"][:, :], in_=haf[:]))
            emit("gpsimd", g_tap)

        ah = sACT.reserve()

        def s_hi(sc, w=dhag):
            sc.wait_ge(sDVE.h, w)
            sw(sc)
            SA(sc.activation(out=hag_hi[:], in_=haf[:], func=AF.Copy))
        emit("scalar", s_hi)

        hag_rdy = sDVE.reserve()

        def v_lo(v, w=ah):
            v.wait_ge(sACT.h, w)
            vw(v)
            VA(v.tensor_tensor(hag_lo[:], haf[:], hag_hi[:], ALU.subtract))
        emit("vector", v_lo)
        close_scope(sc5)
        return hag_rdy

    phase_barrier()
    sc56 = []
    hag_hi = sbuf("hag_hi", [128, T], BF16, sc56)
    hag_lo = sbuf("hag_lo", [128, T], BF16, sc56)
    wb_whi, wb_wlo, wb_ready = load_w_resident("wb", wbT, D, sc56)
    bw_h = sbuf("bw_h", [128, D], BF16, sc56)
    bw_l = sbuf("bw_l", [128, D], BF16, sc56)
    b_hi = sbuf("b_hi", [1, D], BF16, sc56)
    b_lo = sbuf("b_lo", [1, D], BF16, sc56)
    tmp56 = []
    bwf = sbuf("p5bwf", [128, D], F32, tmp56)
    b_f32 = sbuf("p5bf32", [1, D], F32, tmp56)

    bww = sLD.reserve(32)

    def ldbw(s, w=wb_ready[0]):
        s.wait_ge(sDVE.h, w)
        sLD.attach_dma(s.dma_start(out=bwf[:], in_=bwT[:, :]))
        sLD.attach_dma(s.dma_start(out=b_f32[:], in_=b_base[:]))
    emit("sync", ldbw)

    abw = sACT.reserve(2)

    def s_bw(sc, w=bww):
        sc.wait_ge(sLD.h, w)
        sw(sc)
        SA(sc.activation(out=bw_h[:], in_=bwf[:], func=AF.Copy))
        sw(sc)
        SA(sc.activation(out=b_hi[:], in_=b_f32[:], func=AF.Copy))
    emit("scalar", s_bw)

    bw_rdy = sDVE.reserve(2)

    def v_bw(v, w=abw):
        v.wait_ge(sACT.h, w)
        vw(v)
        VA(v.tensor_tensor(bw_l[:], bwf[:], bw_h[:], ALU.subtract))
        vw(v)
        VA(v.tensor_tensor(b_lo[:], b_f32[:], b_hi[:], ALU.subtract))
    emit("vector", v_bw)
    close_scope(tmp56)
    hag_rdy = p5a()

    def p5b():
        phase_barrier()
        sc6 = []
        dstat = DStream("p5s")
        dx2 = DStream("p5x")
        whi, wlo, wready = wb_whi, wb_wlo, wb_ready

        dsto = DStream("sto", store=True)
        statpool = Pool(2, "p5st", init=(sDVE, bw_rdy))
        sth = [sbuf(f"p5sh{i}", [128, D], BF16, sc6) for i in range(2)]
        stl = [sbuf(f"p5sl{i}", [128, D], BF16, sc6) for i in range(2)]
        pspool = Pool(2, "p5ps")
        ps5 = [psum(f"p5ps{i}", [128, D], sc6) for i in range(2)]
        x2pool = Pool(2, "p5x2", init=(sDVE, bw_rdy))
        x2t = [sbuf(f"p5x2{i}", [128, D], F32, sc6) for i in range(2)]
        ffb = sbuf("p5ffb", [128, D], F32, sc6)
        outb = sbuf("p5out", [128, D], F32, sc6)
        st6 = sbuf("p5st6", [128, 6 * NBN], F32, sc6)
        mv = sbuf("p5mv", [128, 2], F32, sc6)
        r4_t = sbuf("p5r4", [128, 1], F32, sc6)
        b4_t = sbuf("p5b4", [128, 1], F32, sc6)

        for tc in range(TC):
            sslot, swv = statpool.acquire()
            trw = dstat.reserve(sslot, 2)

            def tl(sc, tc=tc, sslot=sslot, swv=swv):
                pw(sc, swv)
                r0 = tc * 128
                src_h = s_ln2h[r0:r0 + 128, :].rearrange("t (dc d) -> t dc d", d=128)
                src_l = s_ln2l[r0:r0 + 128, :].rearrange("t (dc d) -> t dc d", d=128)
                dst_h = sth[sslot][:].rearrange("p (dc t) -> p dc t", t=128)
                dst_l = stl[sslot][:].rearrange("p (dc t) -> p dc t", t=128)
                dstat.attach(sslot, sc.dma_start(out=dst_h, in_=src_h, transpose=True))
                dstat.attach(sslot, sc.dma_start(out=dst_l, in_=src_l, transpose=True))
            emit("sync", tl)

            xslot, xwv = x2pool.acquire()
            xw = dx2.reserve(xslot)

            def ldx(s, tc=tc, xslot=xslot, xwv=xwv):
                pw(s, xwv)
                r0 = tc * 128
                dx2.attach(xslot, s.dma_start(out=x2t[xslot][:],
                                              in_=s_x2[r0:r0 + 128, :]))
            emit("sync", ldx)

            pslot, pwv = pspool.acquire()
            mmv = sMM.reserve()
            statpool.released_at(sMM, mmv)
            dvw = max(wready[0], bw_rdy, hag_rdy)

            def mms(te, tc=tc, sslot=sslot, pslot=pslot, pwv=pwv, trw=trw,
                    dvw=dvw):
                pw(te, pwv)
                pw(te, trw)
                te.wait_ge(sDVE.h, dvw)
                te.wait_ge(sACT.h, wready[1])
                groups = [(sth[sslot][:, dc * 128:(dc + 1) * 128],
                           stl[sslot][:, dc * 128:(dc + 1) * 128],
                           whi[dc][:], wlo[dc][:]) for dc in range(NDC)]
                groups.append((hag_hi[:, tc * 128:(tc + 1) * 128],
                               hag_lo[:, tc * 128:(tc + 1) * 128],
                               bw_h[:], bw_l[:]))
                groups.append((ones_bf[:], None, b_hi[:], b_lo[:]))
                sMM.attach(emit_mm_tc(te, ps5[pslot][:], D, groups))
            emit("tensor", mms)
            ps = ps5[pslot]

            a0 = sACT.reserve()
            pspool.released_at(sACT, a0)

            def s_c(sc, ps=ps, w=mmv):
                sc.wait_ge(sMM.h, w)
                sw(sc)
                SA(sc.activation(out=ffb[:], in_=ps[:], func=AF.Copy))
            emit("scalar", s_c)

            d0 = sDVE.reserve(NBN + 1)

            def v_st(v, w=a0):
                v.wait_ge(sACT.h, w)
                bn_pair(v, ffb[:], st6, mv)
            emit("vector", v_st)

            a1 = sACT.reserve()

            def s_r4(sc, w=d0):
                sc.wait_ge(sDVE.h, w)
                sw(sc)
                SA(sc.activation(out=r4_t[:], in_=mv[:, 1:2], func=AF.Sqrt,
                                 bias=epsln_t[:]))
            emit("scalar", s_r4)

            dout_ = sDVE.reserve(4)
            x2pool.released_at(sDVE, dout_)

            def v_out(v, xslot=xslot, w=a1, xw=xw):
                v.wait_ge(sACT.h, w)
                pw(v, xw)
                vw(v)
                VA(v.reciprocal(r4_t[:], r4_t[:]))
                vw(v)
                VA(v.tensor_scalar(b4_t[:], mv[:, 0:1], r4_t[:], -1.0,
                                   ALU.mult, ALU.mult))
                vw(v)
                VA(v.tensor_scalar_add(outb[:], x2t[xslot][:], b4_t[:]))
                vw(v)
                VA(v.scalar_tensor_tensor(outb[:], ffb[:], r4_t[:], outb[:],
                                          ALU.mult, ALU.add))
            emit("vector", v_out)

            p = tc % 2
            ow = dsto.reserve(p)

            def g_out(g, tc=tc, p=p, w=dout_):
                g.wait_ge(sDVE.h, w)
                r0 = tc * 128
                dsto.attach(p, g.dma_start(out=y[r0:r0 + 128, :], in_=outb[:]))
            emit("gpsimd", g_out)
            emit("scalar", lambda sc, w=ow: pw(sc, w))
            emit("vector", lambda v, w=ow: pw(v, w))

        close_scope(sc6)

    p5b()
    close_scope(sc56)

    fin_waits = [(c, c.c) for c in STORE_CTRS if c.c > 0]

    def fin(g, ws=fin_waits):
        for c, v in ws:
            g.wait_ge(c.h, v)
    emit("gpsimd", fin)

    with nc.Block() as block:
        @block.sync
        def _(sync):
            for f in ops["sync"]:
                f(sync)

        @block.scalar
        def _(scalar):
            for f in ops["scalar"]:
                f(scalar)

        @block.vector
        def _(vector):
            for f in ops["vector"]:
                f(vector)

        @block.tensor
        def _(tensor):
            for f in ops["tensor"]:
                f(tensor)

        @block.gpsimd
        def _(gpsimd):
            for f in ops["gpsimd"]:
                f(gpsimd)

    for c in all_ctrs:
        assert c.att == c.c, f"sem {c.name}: attached {c.att} != reserved {c.c}"

    close_scope(glob)
    return nc


def stage_core_inputs(cfg, core_id, x_full, w_qkv, q_norm_w, k_norm_w, w_o,
                      W_base, b_base_v, router_W, A, Bw):
    D = cfg["D"]; NH = cfg["NH"]
    T = cfg["T_OWN"]; TH = cfg["T_HALO"]; E = cfg["E"]; R = cfg["R"]
    KC = (T + TH) // 128
    NJ = (D // NH) // 2
    b, half = core_id // 2, core_id % 2

    own0 = half * T
    x_own = np.ascontiguousarray(x_full[b, own0:own0 + T, :], dtype=np.float32)
    if half >= 1:
        x_halo = np.ascontiguousarray(x_full[b, own0 - TH:own0, :],
                                      dtype=np.float32)
        posv = np.arange(own0 - TH, own0 + T, dtype=np.float32)
        mv = np.zeros((128, TH), dtype=np.float32)
    else:
        x_halo = np.zeros((TH, D), dtype=np.float32)
        posv = np.concatenate([np.zeros(TH, dtype=np.float32),
                               np.arange(0, T, dtype=np.float32)])
        mv = np.full((128, TH), MASKVAL, dtype=np.float32)

    xh = np.concatenate([x_halo, x_own], axis=0)
    posr = np.ascontiguousarray(posv.reshape(KC, 128).T)

    i = np.arange(128)[:, None]
    j = np.arange(128)[None, :]
    mW_ = np.where(j >= i + 1, 0.0, MASKVAL).astype(np.float32)
    mC_ = np.where(j <= i, 0.0, MASKVAL).astype(np.float32)
    jr = np.arange(NJ, dtype=np.float32)[None, :]

    return {
        "xh": xh,
        "wqT": np.ascontiguousarray(w_qkv[0:D, :].T, dtype=np.float32),
        "wkT": np.ascontiguousarray(w_qkv[D:2 * D, :].T, dtype=np.float32),
        "wvT": np.ascontiguousarray(w_qkv[2 * D:3 * D, :].T, dtype=np.float32),
        "woT": np.ascontiguousarray(w_o.T, dtype=np.float32),
        "wbT": np.ascontiguousarray(W_base.T, dtype=np.float32),
        "waT": np.ascontiguousarray(A.transpose(2, 1, 0).reshape(D, R * E)),
        "wrT": np.ascontiguousarray(router_W.T, dtype=np.float32),
        "bwT": np.ascontiguousarray(Bw.transpose(2, 0, 1).reshape(R * E, D)),
        "b_base": np.ascontiguousarray(b_base_v.reshape(1, D), dtype=np.float32),
        "qnw": np.ascontiguousarray(q_norm_w.reshape(1, D), dtype=np.float32),
        "knw": np.ascontiguousarray(k_norm_w.reshape(1, D), dtype=np.float32),
        "posr": posr,
        "jrow": np.ascontiguousarray(jr, dtype=np.float32),
        "mW": mW_, "mC": mC_, "mV": mv,
        "ones_row": np.ones((1, 128), dtype=np.float32),
        "ident": np.eye(128, dtype=np.float32),
    }


def kernel(L0_attn_hidden_states, L0_attn_w_qkv, L0_attn_q_norm_w,
           L0_attn_k_norm_w, L0_attn_w_o, L0_ffn_W_base, L0_ffn_b_base,
           L0_ffn_router_W, L0_ffn_A, L0_ffn_Bw):
    cfg = real_cfg()
    x = np.asarray(L0_attn_hidden_states, dtype=np.float32)
    args = (x, np.asarray(L0_attn_w_qkv, np.float32),
            np.asarray(L0_attn_q_norm_w, np.float32),
            np.asarray(L0_attn_k_norm_w, np.float32),
            np.asarray(L0_attn_w_o, np.float32),
            np.asarray(L0_ffn_W_base, np.float32),
            np.asarray(L0_ffn_b_base, np.float32),
            np.asarray(L0_ffn_router_W, np.float32),
            np.asarray(L0_ffn_A, np.float32),
            np.asarray(L0_ffn_Bw, np.float32))
    in_maps = [stage_core_inputs(cfg, c, *args) for c in range(8)]
    nc = build_nc(cfg)
    res = run_bass_kernel_spmd(nc, in_maps, core_ids=list(range(8)))
    B, S, D = x.shape
    out = np.empty((B, S, D), dtype=np.float32)
    T = cfg["T_OWN"]
    for c in range(8):
        b, half = c // 2, c % 2
        out[b, half * T:(half + 1) * T, :] = res.results[c]["y"]
    return out
